# version 01
# baseline (speedup 1.0000x reference)
import contextlib
import math
import numpy as np
import concourse.bass as bass
import concourse.mybir as mybir
from concourse.bass_utils import run_bass_kernel_spmd

F32 = mybir.dt.float32
BF16 = mybir.dt.bfloat16
AF = mybir.ActivationFunctionType
ALU = mybir.AluOpType
AX = mybir.AxisListType

NCORE = 8
TPC = 2048
NTILE = TPC // 128
D = 1024
DC = D // 128
T_ALL = 16384


class Prog:

    def __init__(self):
        self.nc = bass.Bass("TRN2", target_bir_lowering=False)
        self.ctx = contextlib.ExitStack()
        self._sems = {}
        self._cnt = {}
        self._cvecs = {}

    def sem(self, name):
        if name not in self._sems:
            self._sems[name] = self.ctx.enter_context(self.nc.semaphore(name))
            self._cnt[name] = 0
        return self._sems[name]

    def inc(self, name, inst, n=1):
        inst.then_inc(self.sem(name), n)
        self._cnt[name] += n
        return inst

    def wait(self, eng, name):
        eng.wait_ge(self.sem(name), self._cnt[name])

    def _op(self, eng, chan, fname, *args, **kw):
        self.wait(eng, chan)
        inst = getattr(eng, fname)(*args, **kw)
        self.inc(chan, inst, 1)
        return inst

    def V(self, fname, *args, **kw):
        return self._op(self.nc.vector, "V", fname, *args, **kw)

    def S(self, fname, *args, **kw):
        return self._op(self.nc.scalar, "S", fname, *args, **kw)

    def G(self, fname, *args, **kw):
        return self._op(self.nc.gpsimd, "G", fname, *args, **kw)

    def mm(self, out, lhsT, rhs, start, stop, inc=None, **kw):
        i = self.nc.tensor.matmul(out, lhsT, rhs, start=start, stop=stop, **kw)
        if inc or (inc is None and stop):
            self.inc("T", i, 1)
        return i

    def tr(self, out, in_, ident):
        i = self.nc.tensor.transpose(out, in_, ident)
        self.inc("T", i, 1)
        return i

    def dma(self, name, eng, out, in_, wait_chans=(), **kw):
        for ch in wait_chans:
            self.wait(eng, ch)
        i = eng.dma_start(out=out, in_=in_, **kw)
        self.inc(name, i, 16)
        return i

    def sbuf(self, name, shape, dtype):
        return self.ctx.enter_context(self.nc.sbuf_tensor(name, shape, dtype))

    def psum(self, name, shape, dtype=F32):
        return self.ctx.enter_context(self.nc.psum_tensor(name, shape, dtype))

    def din(self, name, shape, dtype=F32):
        return self.nc.dram_tensor(name, shape, dtype, kind="ExternalInput").ap()

    def dout(self, name, shape, dtype=F32):
        return self.nc.dram_tensor(name, shape, dtype, kind="ExternalOutput").ap()

    def close(self):
        self.ctx.close()

    def cvec(self, val, parts=128):
        key = (val, parts)
        if key not in self._cvecs:
            t = self.sbuf(f"cv_{len(self._cvecs)}", [parts, 1], F32)
            self.V("memset", t[:], val)
            self._cvecs[key] = t
        return self._cvecs[key]


def make_consts(p):
    ident = p.sbuf("c_ident", [128, 128], F32)
    iota_p = p.sbuf("c_iotap", [128, 1], F32)
    iota_f = p.sbuf("c_iotaf", [128, 128], F32)
    ones = p.sbuf("c_ones", [128, 1], F32)
    ones_row = p.sbuf("c_onesrow", [1, 128], F32)
    p.G("iota", iota_p[:], pattern=[[0, 1]], base=0, channel_multiplier=1,
        allow_small_or_imprecise_dtypes=True)
    p.G("iota", iota_f[:], pattern=[[1, 128]], base=0, channel_multiplier=0,
        allow_small_or_imprecise_dtypes=True)
    p.V("memset", ones[:], 1.0)
    p.V("memset", ones_row[:], 1.0)
    p.wait(p.nc.vector, "G")
    p.V("tensor_scalar", ident[:], iota_f[:], iota_p[:], None, op0=ALU.is_equal)
    return ident, iota_p, ones, ones_row


def emit_ln_stats(p, x_sb, ntok, a_row, b_row, scr, tag="", ps_a=None, ps_b=None, mu_row=None):
    nc = p.nc
    ones = p.cvec(1.0)
    if ps_a is None:
        ps_a = p.psum("lnA" + tag, [1, 512])[:]
        ps_b = p.psum("lnB" + tag, [1, 512])[:]
    if mu_row is None:
        mu_row = p.sbuf("lnmu" + tag, [1, ntok], F32)[:]
    nG = ntok // 512
    p.wait(nc.tensor, "V")
    for g in range(nG):
        sl = slice(g * 512, (g + 1) * 512)
        for c in range(DC):
            p.mm(ps_a, ones[:], x_sb[:, c, sl], start=(c == 0), stop=(c == DC - 1))
        for c in range(DC):
            p.wait(nc.scalar, "T")
            p.S("activation", out=scr[:, 0:512], in_=x_sb[:, c, sl], func=AF.Square)
            p.wait(nc.tensor, "S")
            p.mm(ps_b, ones[:], scr[:, 0:512], start=(c == 0), stop=(c == DC - 1),
                 inc=True)
        p.wait(nc.vector, "T")
        p.V("tensor_scalar_mul", mu_row[0:1, sl], ps_a, 1.0 / D)
        p.V("tensor_scalar_mul", b_row[:, sl], ps_b, 1.0 / D)
        if g < nG - 1:
            p.wait(nc.tensor, "V")
    p.wait(nc.scalar, "V")
    p.S("activation", out=scr[0:1, 0:ntok], in_=mu_row[0:1, 0:ntok], func=AF.Square)
    p.wait(nc.vector, "S")
    p.V("tensor_sub", b_row[:], b_row[:], scr[0:1, 0:ntok])
    eps = p.cvec(1e-5, 1)
    p.wait(nc.scalar, "V")
    p.S("activation", out=scr[0:1, 0:ntok], in_=b_row[:], func=AF.Sqrt, bias=eps[0:1, 0:1])
    p.wait(nc.vector, "S")
    p.V("reciprocal", a_row[:], scr[0:1, 0:ntok])
    p.V("scalar_tensor_tensor", b_row[:], mu_row[0:1, 0:ntok], -1.0, a_row[:],
        op0=ALU.mult, op1=ALU.mult)


def emit_row_to_col(p, row, col, ident, ntile, tag="", ps=None):
    nc = p.nc
    if ps is None:
        ps = p.psum("r2c" + tag, [128, ntile])[:]
    p.wait(nc.tensor, "V")
    for i in range(ntile):
        p.tr(ps[:, i:i + 1], row[:, i * 128:(i + 1) * 128], ident[0:1, 0:1])
    p.wait(nc.vector, "T")
    p.V("tensor_copy", col[:, :], ps[:, 0:ntile])


def emit_bcast_rows(p, rows, out_sb, ones_row, nrow_parts=128, tag="", ps=None):
    nc = p.nc
    n = rows.shape[-1] if hasattr(rows, "shape") else None
    n = out_sb.shape[-1]
    if ps is None:
        ps = p.psum("bc" + tag, [nrow_parts, 512])[:]
    p.wait(nc.tensor, "V")
    for g in range((n + 511) // 512):
        sl = slice(g * 512, min((g + 1) * 512, n))
        w = sl.stop - sl.start
        p.mm(ps[:, 0:w], ones_row[0:1, 0:nrow_parts], rows[:, sl], start=True, stop=True)
        p.wait(nc.vector, "T")
        p.V("tensor_copy", out_sb[:, sl], ps[:, 0:w])
        if g * 512 + 512 < n:
            p.wait(nc.tensor, "V")
    return out_sb



def build_l0_route():
    p = Prog()
    nc = p.nc
    x = p.din("x_fm", [D, TPC])
    rt = p.din("r_T", [D, 8])
    ep = p.dout("ep", [2, 128, NTILE])

    x_sb = p.sbuf("x_sb", [128, DC, TPC], F32)
    r_sb = p.sbuf("r_sb", [128, DC, 8], F32)
    scr = p.sbuf("scr", [128, TPC], F32)
    a_row = p.sbuf("a_row", [1, TPC], F32)
    b_row = p.sbuf("b_row", [1, TPC], F32)
    a_col = p.sbuf("a_col", [128, NTILE], F32)
    b_col = p.sbuf("b_col", [128, NTILE], F32)
    lg = p.sbuf("lg", [128, NTILE, 8], F32)
    csr = p.sbuf("csr", [128, 8], F32)
    iota8 = p.sbuf("iota8", [128, 8], F32)
    mx = p.sbuf("mx", [128, NTILE], F32)
    e_col = p.sbuf("e_col", [128, NTILE], F32)
    s_col = p.sbuf("s_col", [128, NTILE], F32)
    exps = p.sbuf("exps", [128, 8], F32)
    eq = p.sbuf("eq", [128, 8], F32)
    csr_row = p.sbuf("csr_row", [1, 8], F32)

    p.dma("ld", nc.sync, x_sb[:], x.rearrange("(c p) t -> p c t", p=128))
    p.dma("ld", nc.sync, r_sb[:], rt.rearrange("(c p) e -> p c e", p=128))
    for e in (nc.tensor, nc.scalar, nc.vector, nc.gpsimd):
        p.wait(e, "ld")

    ident, iota_p, ones, ones_row = make_consts(p)
    p.G("iota", iota8[:], pattern=[[1, 8]], base=99, channel_multiplier=0,
        allow_small_or_imprecise_dtypes=True)

    emit_ln_stats(p, x_sb, TPC, a_row, b_row, scr)

    ps_lg = p.psum("ps_lg", [128, NTILE, 8])
    for it in range(NTILE):
        for c in range(DC):
            p.mm(ps_lg[:, it, :], x_sb[:, c, it * 128:(it + 1) * 128], r_sb[:, c, :],
                 start=(c == 0), stop=(c == DC - 1))
    ps_c = p.psum("ps_c", [1, 8])
    for c in range(DC):
        p.mm(ps_c[:], ones[:], r_sb[:, c, :], start=(c == 0), stop=(c == DC - 1))
    p.wait(nc.vector, "T")
    p.V("tensor_copy", csr_row[:], ps_c[:])
    ps_b8 = p.psum("ps_b8", [128, 8])
    p.wait(nc.tensor, "V")
    p.mm(ps_b8[:], ones_row[:], csr_row[:], start=True, stop=True)
    p.wait(nc.vector, "T")
    p.V("tensor_copy", csr[:], ps_b8[:])

    emit_row_to_col(p, a_row, a_col, ident, NTILE, tag="a")
    emit_row_to_col(p, b_row, b_col, ident, NTILE, tag="b")

    p.wait(nc.vector, "G")
    for it in range(NTILE):
        p.V("tensor_scalar_mul", lg[:, it, :], ps_lg[:, it, :], a_col[:, it:it + 1])
        p.V("scalar_tensor_tensor", lg[:, it, :], csr[:], b_col[:, it:it + 1],
            lg[:, it, :], op0=ALU.mult, op1=ALU.add)
        p.V("tensor_reduce", mx[:, it:it + 1], lg[:, it, :], axis=AX.X, op=ALU.max,
            negate=True)
        p.wait(nc.scalar, "V")
        p.S("activation", out=exps[:], in_=lg[:, it, :], func=AF.Exp,
            bias=mx[:, it:it + 1], accum_out=s_col[:, it:it + 1])
        p.V("tensor_scalar", eq[:], lg[:, it, :], mx[:, it:it + 1], None, op0=ALU.add)
        p.V("tensor_scalar", eq[:], eq[:], 0.0, None, op0=ALU.is_equal)
        p.V("scalar_tensor_tensor", eq[:], eq[:], -99.0, iota8[:],
            op0=ALU.mult, op1=ALU.add)
        p.V("tensor_reduce", e_col[:, it:it + 1], eq[:], axis=AX.X, op=ALU.min)

    p.dma("st", nc.gpsimd, ep[0], e_col[:], wait_chans=("V",))
    p.dma("st", nc.gpsimd, ep[1], s_col[:], wait_chans=("S",))
    p.wait(nc.gpsimd, "st")
    return p



def _shard_x(x):
    out = []
    for c in range(NCORE):
        b, q = c // 4, c % 4
        out.append(np.ascontiguousarray(x[b, q * TPC:(q + 1) * TPC, :].T))
    return out


def kernel(**inputs):
    shards = _shard_x(inputs["L0_attn_x"])
    cache = {}
    shards = run_l0_ffn(shards, inputs, cache)
    shards = run_l0_attn(shards, inputs, cache)
    shards = run_l0_ffn(shards, inputs, cache)
    shards = run_l1_ffn(shards, inputs, cache)
    shards = run_l1_attn(shards, inputs, cache)
    shards = run_l1_ffn(shards, inputs, cache)
    shards = run_l2_ffn(shards, inputs, cache)
    shards = run_l2_attn(shards, inputs, cache)
    shards = run_l2_ffn(shards, inputs, cache)
    shards = run_l3_ffn(shards, inputs, cache)
    shards = run_l3_attn(shards, inputs, cache)
    shards = run_l3_ffn(shards, inputs, cache)
    out = np.empty((2, 8192, 1024), dtype=np.float32)
    for c in range(NCORE):
        b, q = c // 4, c % 4
        out[b, q * TPC:(q + 1) * TPC, :] = shards[c].T
    return out


CAPACITY = 2560


def emit_moe_rank(p, e_all_d, s_all_d, e_col, s_col, idx_col, fe_col, arena, ones_row, ps_bc=None):
    nc = p.nc
    A = arena
    e_bc = A[:, 0 * TPC:1 * TPC]
    s_bc = A[:, 1 * TPC:2 * TPC]
    scrap = A[:, 2 * TPC:3 * TPC]
    beq = A[:, 3 * TPC:4 * TPC]
    jiota = A[:, 4 * TPC:5 * TPC]
    rank = p.sbuf("rank_col", [128, NTILE], F32)
    r1 = p.sbuf("r1_col", [128, 1], F32)
    r2 = p.sbuf("r2_col", [128, 1], F32)
    ish = p.sbuf("ish_col", [128, 1], F32)

    p.G("iota", jiota[:, :], pattern=[[1, TPC]], base=0, channel_multiplier=0,
        allow_small_or_imprecise_dtypes=True)
    p.V("memset", rank[:], 0.0)
    for jc in range(NCORE):
        sl = slice(jc * TPC, (jc + 1) * TPC)
        p.dma("rk", nc.sync, e_bc, e_all_d[0:1, sl].partition_broadcast(128),
              wait_chans=("V",))
        p.dma("rk", nc.sync, s_bc, s_all_d[0:1, sl].partition_broadcast(128))
        p.wait(nc.vector, "rk")
        for it in range(NTILE):
            p.V("tensor_scalar", beq[:, :], e_bc, e_col[:, it:it + 1], None,
                op0=ALU.is_equal)
            p.V("scalar_tensor_tensor", scrap[:, :], s_bc, s_col[:, it:it + 1],
                beq[:, :], op0=ALU.is_lt, op1=ALU.mult, accum_out=r1[:])
            p.V("scalar_tensor_tensor", beq[:, :], s_bc, s_col[:, it:it + 1],
                beq[:, :], op0=ALU.is_equal, op1=ALU.mult)
            p.V("tensor_scalar", ish[:], idx_col[:, it:it + 1], float(jc * TPC), None,
                op0=ALU.subtract)
            p.V("scalar_tensor_tensor", scrap[:, :], jiota, ish[:], beq[:, :],
                op0=ALU.is_lt, op1=ALU.mult, accum_out=r2[:])
            p.V("tensor_add", r1[:], r1[:], r2[:])
            p.V("tensor_add", rank[:, it:it + 1], rank[:, it:it + 1], r1[:])
    p.V("tensor_scalar", fe_col[:], rank[:], float(CAPACITY), None, op0=ALU.is_lt)
    p.V("tensor_mul", fe_col[:], fe_col[:], e_col[:])
    return fe_col



def build_l0_moe():
    p = Prog()
    nc = p.nc
    x = p.din("x_fm", [D, TPC])
    ep_d = p.din("ep", [2, 128, NTILE])
    idx_d = p.din("idx", [128, NTILE])
    e_all_d = p.din("e_all", [1, T_ALL])
    s_all_d = p.din("s_all", [1, T_ALL])
    w1_d = p.din("W1", [8, D, 4096])
    w2_d = p.din("W2", [8, 4096, D])
    b1_d = p.din("b1c", [128, 8, 32])
    b2_d = p.din("b2c", [128, 8, 8])
    xo = p.dout("x_out", [D, TPC])
    scratch_d = nc.dram_tensor("fe_scr", [NTILE, 128], F32, kind="Internal").ap()

    x_sb = p.sbuf("x_sb", [128, DC, TPC], F32)
    ln_bf = p.sbuf("ln_bf", [128, DC, TPC], BF16)
    arena = p.sbuf("arena", [128, 5 * TPC], F32)
    w1buf = p.sbuf("w1buf", [128, DC, 512], BF16)
    h_tg = p.sbuf("h_tg", [128, 32, 512], BF16)
    w2buf = p.sbuf("w2buf", [128, 1, 32, 128], BF16)
    a_row = p.sbuf("a_row", [1, TPC], F32)
    b_row = p.sbuf("b_row", [1, TPC], F32)
    ec = p.sbuf("ec", [128, NTILE], F32)
    sc = p.sbuf("sc", [128, NTILE], F32)
    ic = p.sbuf("ic", [128, NTILE], F32)
    fe = p.sbuf("fe", [128, NTILE], F32)
    fe_t = p.sbuf("fe_t", [NTILE, 128], F32)
    b1_sb = p.sbuf("b1_sb", [128, 8, 32], F32)
    b2_sb = p.sbuf("b2_sb", [128, 8, 8], F32)

    psP = p.psum("psP", [128, 2, 512])
    psH = p.psum("psH", [128, 2, 512])
    psY = p.psum("psY", [128, 512])

    p.dma("ld", nc.sync, x_sb[:], x.rearrange("(c p) t -> p c t", p=128))
    p.dma("ld", nc.sync, ec[:], ep_d[0])
    p.dma("ld", nc.sync, sc[:], ep_d[1])
    p.dma("ld", nc.sync, ic[:], idx_d)
    p.dma("ld", nc.sync, b1_sb[:], b1_d)
    p.dma("ld", nc.sync, b2_sb[:], b2_d)
    for e in (nc.tensor, nc.scalar, nc.vector, nc.gpsimd):
        p.wait(e, "ld")

    ident, iota_p, ones, ones_row = make_consts(p)

    scr = arena[:, 2 * TPC:3 * TPC]
    emit_ln_stats(p, x_sb, TPC, a_row, b_row, scr,
                  ps_a=psP[0:1, 0, :], ps_b=psP[0:1, 1, :],
                  mu_row=arena[0:1, 3 * TPC:3 * TPC + TPC])
    a_bc = arena[:, 0 * TPC:1 * TPC]
    b_bc = arena[:, 1 * TPC:2 * TPC]
    emit_bcast_rows(p, a_row[:], a_bc, ones_row, ps=psP[:, 0, :], tag="a")
    emit_bcast_rows(p, b_row[:], b_bc, ones_row, ps=psP[:, 0, :], tag="b")
    for c in range(DC):
        p.V("tensor_mul", scr[:, :], x_sb[:, c, :], a_bc)
        p.V("tensor_add", ln_bf[:, c, :], scr[:, :], b_bc)

    emit_moe_rank(p, e_all_d, s_all_d, ec, sc, ic, fe, arena, ones_row, ps_bc=psP[:])

    p.wait(nc.tensor, "V")
    p.tr(psP[0:NTILE, 0, 0:128], fe[:, :], ident[:])
    p.wait(nc.vector, "T")
    p.V("tensor_copy", fe_t[:, :], psP[0:NTILE, 0, 0:128])
    p.dma("fe1", nc.gpsimd, scratch_d, fe_t[:], wait_chans=("V",))
    p.wait(nc.gpsimd, "fe1")
    fe_bc = arena[:, 0 * TPC:1 * TPC]
    p.dma("fe2", nc.gpsimd, fe_bc,
          scratch_d.rearrange("i p -> (i p)").unsqueeze(0).partition_broadcast(128),
          wait_chans=("V",))
    for e in (nc.tensor, nc.vector):
        p.wait(e, "fe2")

    mask_bc = arena[:, 1 * TPC:2 * TPC]
    evac = arena[:, 2 * TPC:3 * TPC]

    for e in range(8):
        p.V("tensor_scalar", mask_bc[:, :], fe_bc, float(e), None, op0=ALU.is_equal)
        for tg in range(4):
            tsl = slice(tg * 512, (tg + 1) * 512)
            for q in range(8):
                p.dma("w1", nc.gpsimd, w1buf[:],
                      w1_d[e].rearrange("(kc p) f -> p kc f", p=128)
                      [:, :, q * 512:(q + 1) * 512],
                      wait_chans=("T",))
                p.wait(nc.tensor, "w1")
                for f8 in range(4):
                    fc = q * 4 + f8
                    hp = psH[:, fc % 2, :]
                    p.wait(nc.tensor, "S")
                    for kc in range(DC):
                        p.mm(hp, w1buf[:, kc, f8 * 128:(f8 + 1) * 128],
                             ln_bf[:, kc, tsl], start=(kc == 0), stop=(kc == DC - 1))
                    p.wait(nc.scalar, "T")
                    p.S("activation", out=h_tg[:, fc, :], in_=hp, func=AF.Relu,
                        bias=b1_sb[:, e, fc:fc + 1])
            for oc in range(8):
                par = 0
                p.dma("w2", nc.gpsimd, w2buf[:, par],
                      w2_d[e].rearrange("(fc p) o -> p fc o", p=128)
                      [:, :, oc * 128:(oc + 1) * 128],
                      wait_chans=("T",))
                p.wait(nc.tensor, "w2")
                p.wait(nc.tensor, "V")
                p.wait(nc.tensor, "S")
                for fc in range(32):
                    p.mm(psY[:], w2buf[:, par, fc, :], h_tg[:, fc, :],
                         start=(fc == 0), stop=(fc == 31))
                p.wait(nc.vector, "T")
                p.V("tensor_scalar", evac[:, 0:512], psY[:], b2_sb[:, e, oc:oc + 1],
                    None, op0=ALU.add)
                p.V("tensor_mul", evac[:, 0:512], evac[:, 0:512], mask_bc[:, tsl])
                p.V("scalar_tensor_tensor", x_sb[:, oc, tsl], evac[:, 0:512], 0.5,
                    x_sb[:, oc, tsl], op0=ALU.mult, op1=ALU.add)

    p.dma("st", nc.sync, xo.rearrange("(c p) t -> p c t", p=128), x_sb[:],
          wait_chans=("V",))
    p.wait(nc.sync, "st")
    return p


def _col_of(v, c):
    return np.ascontiguousarray(v[c * TPC:(c + 1) * TPC].reshape(NTILE, 128).T)


def run_l0_ffn(x_shards, inputs, cache):
    rt = np.ascontiguousarray(inputs["L0_ffn_router_w"].T)
    if "l0_route" not in cache:
        cache["l0_route"] = build_l0_route()
    pA = cache["l0_route"]
    in_maps = [{"x_fm": x_shards[c], "r_T": rt} for c in range(NCORE)]
    resA = run_bass_kernel_spmd(pA.nc, in_maps, core_ids=list(range(NCORE)))
    e_all = np.concatenate([r["ep"][0].T.reshape(-1) for r in resA.results])
    s_all = np.concatenate([r["ep"][1].T.reshape(-1) for r in resA.results])

    if "l0_moe" not in cache:
        cache["l0_moe"] = build_l0_moe()
    pB = cache["l0_moe"]
    idx = np.arange(T_ALL, dtype=np.float32)
    b1c = np.ascontiguousarray(
        inputs["L0_ffn_b1"].reshape(8, 32, 128).transpose(2, 0, 1))
    b2c = np.ascontiguousarray(
        inputs["L0_ffn_b2"].reshape(8, 8, 128).transpose(2, 0, 1))
    in_maps = []
    for c in range(NCORE):
        in_maps.append({
            "x_fm": x_shards[c],
            "ep": np.stack([_col_of(e_all, c), _col_of(s_all, c)]),
            "idx": _col_of(idx, c),
            "e_all": e_all[None, :], "s_all": s_all[None, :],
            "W1": inputs["L0_ffn_W1"], "W2": inputs["L0_ffn_W2"],
            "b1c": b1c, "b2c": b2c,
        })
    resB = run_bass_kernel_spmd(pB.nc, in_maps, core_ids=list(range(NCORE)))
    return [np.asarray(r["x_out"]) for r in resB.results]



def emit_gemm_stream(p, w_dram, kdim, odim, act_sb, ntok, ps, evac, wbuf, name,
                     tgw=512):
    nc = p.nc
    KC = kdim // 128
    nog = odim // 128
    ntg = ntok // tgw
    for oc in range(nog):
        par = oc % 2
        p.dma("w" + name, nc.gpsimd, wbuf[:, par, 0:KC],
              w_dram.rearrange("(kc p) o -> p kc o", p=128)[:, :, oc * 128:(oc + 1) * 128],
              wait_chans=("T",))
        p.wait(nc.tensor, "w" + name)
        for tg in range(ntg):
            tsl = slice(tg * tgw, (tg + 1) * tgw)
            p.wait(nc.tensor, "V")
            p.wait(nc.tensor, "S")
            for kc in range(KC):
                p.mm(ps, wbuf[:, par, kc, :], act_sb[:, kc, tsl],
                     start=(kc == 0), stop=(kc == KC - 1))
            evac(oc, tg, ps)




def build_l0_attn_c1():
    p = Prog()
    nc = p.nc
    x = p.din("x_fm", [D, TPC])
    wdq_d = p.din("W_DQ", [D, 4096])
    wdkv_d = p.din("W_DKV", [D, 256])
    wuk_d = p.din("W_UK", [256, D])
    wuv_d = p.din("W_UV", [256, D])
    wkr_d = p.din("W_KR", [D, 32])
    scale_d = p.din("scale_fm", [32, TPC], BF16)
    cq_o = p.dout("cq", [32, 128, TPC], BF16)
    kc_o = p.dout("kc", [DC, 128, TPC], BF16)
    v_o = p.dout("v", [NTILE, 128, D], BF16)
    kr_o = p.dout("kr", [32, TPC], BF16)

    x_sb = p.sbuf("x_sb", [128, DC, TPC], F32)
    ln_bf = p.sbuf("ln_bf", [128, DC, TPC], BF16)
    arena = p.sbuf("arena", [128, 3 * TPC], F32)
    a_row = p.sbuf("a_row", [1, TPC], F32)
    b_row = p.sbuf("b_row", [1, TPC], F32)
    wbuf = p.sbuf("wbuf", [128, 2, DC, 128], BF16)
    wkv_sb = p.sbuf("wkv_sb", [128, DC, 256], BF16)
    ckv_bf = p.sbuf("ckv_bf", [128, 2, TPC], BF16)
    scale_sb = p.sbuf("scale_sb", [32, TPC], BF16)
    orow = p.sbuf("orow", [128, TPC], BF16)
    psP = p.psum("psP", [128, 2, 512])
    psQ = p.psum("psQ", [128, 512])

    p.dma("ld", nc.sync, x_sb[:], x.rearrange("(c p) t -> p c t", p=128))
    p.dma("ld", nc.sync, scale_sb[:], scale_d)
    p.dma("ld", nc.gpsimd, wkv_sb[:], wdkv_d.rearrange("(kc p) o -> p kc o", p=128))
    for e in (nc.tensor, nc.scalar, nc.vector, nc.gpsimd):
        p.wait(e, "ld")

    scr = arena[:, 2 * TPC:3 * TPC]
    emit_ln_stats(p, x_sb, TPC, a_row, b_row, scr,
                  ps_a=psP[0:1, 0, :], ps_b=psP[0:1, 1, :],
                  mu_row=arena[0:1, 0:TPC])
    ones_row = p.sbuf("c_onesrow", [1, 128], F32)
    p.V("memset", ones_row[:], 1.0)
    a_bc = arena[:, 0 * TPC:1 * TPC]
    b_bc = arena[:, 1 * TPC:2 * TPC]
    emit_bcast_rows(p, a_row[:], a_bc, ones_row, ps=psP[:, 0, :], tag="a")
    emit_bcast_rows(p, b_row[:], b_bc, ones_row, ps=psP[:, 0, :], tag="b")
    for c in range(DC):
        p.V("tensor_mul", scr[:, :], x_sb[:, c, :], a_bc)
        p.V("tensor_add", ln_bf[:, c, :], scr[:, :], b_bc)

    def evac_cq(oc, tg, ps):
        p.wait(nc.vector, "T")
        if tg == 0:
            p.wait(nc.vector, "cqo")
        p.V("tensor_copy", orow[:, tg * 512:(tg + 1) * 512], ps)
        if tg == 3:
            p.dma("cqo", nc.sync, cq_o[oc], orow[:, :], wait_chans=("V",))
    emit_gemm_stream(p, wdq_d, D, 4096, ln_bf[:], TPC, psQ[:], evac_cq, wbuf, "dq")

    p.wait(nc.tensor, "ld")
    for oc in range(2):
        for tg in range(4):
            tsl = slice(tg * 512, (tg + 1) * 512)
            p.wait(nc.tensor, "V")
            for kc in range(DC):
                p.mm(psQ[:], wkv_sb[:, kc, oc * 128:(oc + 1) * 128], ln_bf[:, kc, tsl],
                     start=(kc == 0), stop=(kc == DC - 1))
            p.wait(nc.vector, "T")
            p.V("tensor_copy", ckv_bf[:, oc, tsl], psQ[:])

    def evac_kc(oc, tg, ps):
        p.wait(nc.vector, "T")
        if tg == 0:
            p.wait(nc.vector, "kco")
        p.V("tensor_copy", orow[:, tg * 512:(tg + 1) * 512], ps)
        if tg == 3:
            p.dma("kco", nc.sync, kc_o[oc], orow[:, :], wait_chans=("V",))
    wbuf2 = p.sbuf("wbuf2", [128, 2, 2, 128], BF16)
    emit_gemm_stream(p, wuk_d, 256, D, ckv_bf[:], TPC, psQ[:], evac_kc, wbuf2, "uk")

    wuv_sb = p.sbuf("wuv_sb", [128, 2, D], BF16)
    p.dma("wuv", nc.gpsimd, wuv_sb[:], wuv_d.rearrange("(kc p) o -> p kc o", p=128))
    p.wait(nc.tensor, "wuv")
    vt_bf = p.sbuf("vt_bf", [128, D], BF16)
    for it in range(NTILE):
        p.wait(nc.vector, "vo")
        for g in range(2):
            osl = slice(g * 512, (g + 1) * 512)
            p.wait(nc.tensor, "V")
            for kc in range(2):
                p.mm(psQ[:], ckv_bf[:, kc, it * 128:(it + 1) * 128], wuv_sb[:, kc, osl],
                     start=(kc == 0), stop=(kc == 1))
            p.wait(nc.vector, "T")
            p.V("tensor_copy", vt_bf[:, osl], psQ[:])
        p.dma("vo", nc.sync, v_o[it], vt_bf[:], wait_chans=("V",))

    wkr_sb = p.sbuf("wkr_sb", [128, DC, 32], BF16)
    p.dma("wkr", nc.gpsimd, wkr_sb[:], wkr_d.rearrange("(kc p) o -> p kc o", p=128))
    p.wait(nc.tensor, "wkr")
    kr_bf = p.sbuf("kr_bf", [32, TPC], BF16)
    for tg in range(4):
        tsl = slice(tg * 512, (tg + 1) * 512)
        p.wait(nc.tensor, "V")
        for kc in range(DC):
            p.mm(psQ[0:32, :], wkr_sb[:, kc, :], ln_bf[:, kc, tsl],
                 start=(kc == 0), stop=(kc == DC - 1))
        p.wait(nc.vector, "T")
        p.V("tensor_mul", kr_bf[0:32, tsl], psQ[0:32, 0:512], scale_sb[0:32, tsl])
    p.dma("kro", nc.sync, kr_o, kr_bf[:], wait_chans=("V",))

    p.wait(nc.sync, "cqo")
    p.wait(nc.sync, "kco")
    p.wait(nc.sync, "vo")
    p.wait(nc.sync, "kro")
    return p


NKB = 64


def build_l0_attn_d():
    p = Prog()
    nc = p.nc
    S = 8192
    cq_d = p.din("cq", [32, 128, S], BF16)
    kc_d = p.din("kch", [4, 64, S], BF16)
    kr_d = p.din("krf", [32, S], BF16)
    v_d = p.din("vph", [S // 128, 128, 4, 65], BF16)
    wuq_d = p.din("wuqh", [4096, 4 * 64])
    wqr_d = p.din("wqrh", [4096, 4 * 32])
    scale_d = p.din("scale_fm", [32, S], BF16)
    mask_d = p.din("masks", [128, 4, 512], BF16)
    o_o = p.dout("o_out", [4, 64, S], BF16)

    q_sb = p.sbuf("q_sb", [128, 4, S], BF16)
    slab = p.sbuf("slab", [128, 32, 512], BF16)
    wuq_sb = p.sbuf("wuq_sb", [128, 32, 256], BF16)
    wqr_sb = p.sbuf("wqr_sb", [128, 32, 128], BF16)
    k_sb = p.sbuf("k_sb", [128, S], BF16)
    v_sb = p.sbuf("v_sb", [128, NKB, 65], BF16)
    o_sb = p.sbuf("o_sb", [64, S], BF16)
    scale_sb = p.sbuf("scale_sb", [32, S], BF16)
    mask_sb = p.sbuf("mask_sb", [128, 4, 512], BF16)
    p_bf = p.sbuf("p_bf", [128, 2, 512], BF16)
    num_sb = p.sbuf("num_sb", [64, 512], F32)
    den_r = p.sbuf("den_r", [1, 512], F32)
    psS = p.psum("psS", [128, 2, 512])
    psO = p.psum("psO", [65, 512])
    psB = p.psum("psB", [64, 512])

    ones_row = p.sbuf("c_onesrow", [1, 128], F32)
    p.V("memset", ones_row[:], 1.0)
    p.V("memset", q_sb[96:128, :, :].rearrange("p h t -> p (h t)"), 0.0)
    p.V("memset", k_sb[96:128, :], 0.0)

    p.dma("ld", nc.gpsimd, wuq_sb[:], wuq_d.rearrange("(kc p) o -> p kc o", p=128))
    p.dma("ld", nc.gpsimd, wqr_sb[:], wqr_d.rearrange("(kc p) o -> p kc o", p=128))
    p.dma("ld", nc.sync, scale_sb[:], scale_d)
    p.dma("ld", nc.sync, mask_sb[:], mask_d)
    for e in (nc.tensor, nc.scalar, nc.vector, nc.gpsimd):
        p.wait(e, "ld")

    for tg in range(16):
        tsl = slice(tg * 512, (tg + 1) * 512)
        p.dma("cq", nc.sync, slab[:], cq_d[:, :, tsl].rearrange("kc p t -> p kc t"),
              wait_chans=("T",))
        p.wait(nc.tensor, "cq")
        for h in range(4):
            ps = psS[:, 0, :]
            p.wait(nc.tensor, "V")
            for kc in range(32):
                p.mm(ps[0:64, :], wuq_sb[:, kc, h * 64:(h + 1) * 64], slab[:, kc, :],
                     start=(kc == 0), stop=(kc == 31))
            for kc in range(32):
                p.mm(ps[64:96, :], wqr_sb[:, kc, h * 32:(h + 1) * 32], slab[:, kc, :],
                     start=(kc == 0), stop=(kc == 31), inc=(kc == 31))
            p.wait(nc.vector, "T")
            p.V("tensor_copy", q_sb[0:64, h, tsl], ps[0:64, :])
            p.V("tensor_mul", q_sb[64:96, h, tsl], ps[64:96, 0:512], scale_sb[0:32, tsl])

    sc = p.cvec(1.0 / math.sqrt(96.0))
    for h in range(4):
        p.dma("kv", nc.sync, k_sb[0:64, :], kc_d[h], wait_chans=("T",))
        p.dma("kv", nc.sync, k_sb[64:96, :], kr_d[:])
        p.dma("kv", nc.sync, v_sb[:],
              v_d[:, :, h, :].rearrange("kb p o -> p kb o"))
        p.wait(nc.tensor, "kv")
        p.wait(nc.vector, "oo")
        for s in range(16):
            qsl = slice(s * 512, (s + 1) * 512)
            nkb = 4 * (s + 1)
            for kb in range(nkb):
                par = kb % 2
                psT = psS[:, par, :]
                p.wait(nc.tensor, "S")
                p.mm(psT, k_sb[:, kb * 128:(kb + 1) * 128], q_sb[:, h, qsl],
                     start=True, stop=True)
                p.wait(nc.scalar, "T")
                p.S("activation", out=p_bf[:, par, :], in_=psT, func=AF.Exp,
                    scale=sc[:])
                diag = kb - (nkb - 4)
                if diag >= 0:
                    p.wait(nc.vector, "S")
                    p.V("tensor_mul", p_bf[:, par, :], p_bf[:, par, :],
                        mask_sb[:, diag, :])
                    p.wait(nc.tensor, "V")
                else:
                    p.wait(nc.tensor, "S")
                p.mm(psO[:], v_sb[:, kb, :], p_bf[:, par, :],
                     start=(kb == 0), stop=(kb == nkb - 1))
            p.wait(nc.vector, "T")
            p.V("tensor_copy", num_sb[:, :], psO[0:64, :])
            p.V("reciprocal", den_r[:], psO[64:65, :])
            p.wait(nc.tensor, "V")
            p.mm(psB[:], ones_row[0:1, 0:64], den_r[:], start=True, stop=True)
            p.wait(nc.vector, "T")
            p.V("tensor_mul", o_sb[:, qsl], num_sb[:, :], psB[:, :])
        p.dma("oo", nc.sync, o_o[h], o_sb[:], wait_chans=("V",))
        p.wait(nc.sync, "oo")
    return p



def build_l0_attn_e():
    p = Prog()
    nc = p.nc
    x = p.din("x_fm", [D, TPC])
    y_d = p.din("y_fm", [D, TPC], BF16)
    wo_d = p.din("W_O", [D, D])
    xo = p.dout("x_out", [D, TPC])

    x_sb = p.sbuf("x_sb", [128, DC, TPC], F32)
    y_sb = p.sbuf("y_sb", [128, DC, TPC], BF16)
    wbuf = p.sbuf("wbuf", [128, 2, DC, 128], BF16)
    psQ = p.psum("psQ", [128, 512])

    p.dma("ld", nc.sync, x_sb[:], x.rearrange("(c p) t -> p c t", p=128))
    p.dma("ld", nc.sync, y_sb[:], y_d.rearrange("(c p) t -> p c t", p=128))
    for e in (nc.tensor, nc.scalar, nc.vector, nc.gpsimd):
        p.wait(e, "ld")

    def evac(oc, tg, ps):
        tsl = slice(tg * 512, (tg + 1) * 512)
        p.wait(nc.vector, "T")
        p.V("tensor_add", x_sb[:, oc, tsl], x_sb[:, oc, tsl], ps)

    emit_gemm_stream(p, wo_d, D, D, y_sb[:], TPC, psQ[:], evac, wbuf, "wo")
    p.dma("st", nc.sync, xo.rearrange("(c p) t -> p c t", p=128), x_sb[:],
          wait_chans=("V",))
    p.wait(nc.sync, "st")
    return p


def _rope_scale_table():
    T, DHR = 8192, 32
    t = np.arange(T, dtype=np.float32)
    inv = 1.0 / 10000.0 ** (np.arange(0, DHR, 2, dtype=np.float32) / DHR)
    freqs = t[:, None] * inv[None, :]
    emb = np.concatenate([freqs, freqs], -1)
    return (np.cos(emb) + np.sin(emb)).astype(np.float32)


def run_l0_attn(x_shards, inputs, cache):
    import ml_dtypes
    bf16 = ml_dtypes.bfloat16
    scale = _rope_scale_table()
    if "l0c1" not in cache:
        cache["l0c1"] = build_l0_attn_c1()
    pC = cache["l0c1"]
    in_maps = []
    for c in range(NCORE):
        q = c % 4
        sl = scale[q * TPC:(q + 1) * TPC].T.astype(bf16)
        in_maps.append({
            "x_fm": x_shards[c], "W_DQ": inputs["L0_attn_W_DQ"],
            "W_DKV": inputs["L0_attn_W_DKV"], "W_UK": inputs["L0_attn_W_UK"],
            "W_UV": inputs["L0_attn_W_UV"], "W_KR": inputs["L0_attn_W_KR"],
            "scale_fm": np.ascontiguousarray(sl),
        })
    resC = run_bass_kernel_spmd(pC.nc, in_maps, core_ids=list(range(NCORE)))

    cq_b, kc_b, kr_b, vp_b = [], [], [], []
    for b in range(2):
        rs = [resC.results[b * 4 + q] for q in range(4)]
        cq = np.concatenate([np.asarray(r["cq"]) for r in rs], axis=2)
        kcf = np.concatenate([np.asarray(r["kc"]) for r in rs], axis=2)
        kcf = kcf.reshape(D, 8192)
        krf = np.concatenate([np.asarray(r["kr"]) for r in rs], axis=1)
        v = np.concatenate([np.asarray(r["v"]) for r in rs], axis=0)
        vp = np.zeros((64, 128, 16, 65), dtype=bf16)
        vp[:, :, :, :64] = v.reshape(64, 128, 16, 64)
        vp[:, :, :, 64] = np.asarray(1.0, dtype=bf16)
        cq_b.append(cq); kc_b.append(kcf); kr_b.append(krf); vp_b.append(vp)

    if "l0d" not in cache:
        cache["l0d"] = build_l0_attn_d()
    pD = cache["l0d"]
    masks = np.zeros((128, 4, 512), dtype=bf16)
    i = np.arange(128)[:, None]
    j = np.arange(512)[None, :]
    for r in range(4):
        masks[:, r, :] = (j >= i + 128 * r).astype(bf16)
    scale_fmT = np.ascontiguousarray(scale.T.astype(bf16))
    in_maps = []
    for c in range(NCORE):
        b = c // 4
        hs = [(c % 4) * 4 + j2 for j2 in range(4)]
        wuq = np.concatenate([inputs["L0_attn_W_UQ"][:, h * 64:(h + 1) * 64] for h in hs], axis=1)
        wqr = np.concatenate([inputs["L0_attn_W_QR"][:, h * 32:(h + 1) * 32] for h in hs], axis=1)
        kch = np.stack([kc_b[b].reshape(16, 64, 8192)[h] for h in hs])
        vph = np.ascontiguousarray(vp_b[b][:, :, hs, :])
        in_maps.append({
            "cq": cq_b[b], "kch": np.ascontiguousarray(kch), "krf": kr_b[b],
            "vph": vph,
            "wuqh": np.ascontiguousarray(wuq), "wqrh": np.ascontiguousarray(wqr),
            "scale_fm": scale_fmT, "masks": masks,
        })
    resD = run_bass_kernel_spmd(pD.nc, in_maps, core_ids=list(range(NCORE)))

    y_b = []
    for b in range(2):
        y = np.zeros((D, 8192), dtype=bf16)
        for q in range(4):
            r = resD.results[b * 4 + q]
            for j2 in range(4):
                h = q * 4 + j2
                y[h * 64:(h + 1) * 64, :] = np.asarray(r["o_out"][j2])
        y_b.append(y)

    if "l0e" not in cache:
        cache["l0e"] = build_l0_attn_e()
    pE = cache["l0e"]
    in_maps = []
    for c in range(NCORE):
        b, q = c // 4, c % 4
        in_maps.append({
            "x_fm": x_shards[c],
            "y_fm": np.ascontiguousarray(y_b[b][:, q * TPC:(q + 1) * TPC]),
            "W_O": inputs["L0_attn_W_O"],
        })
    resE = run_bass_kernel_spmd(pE.nc, in_maps, core_ids=list(range(NCORE)))
    return [np.asarray(r["x_out"]) for r in resE.results]



def emit_ln_norm(p, x_sb, ln_bf, arena, a_row, b_row, ones_row, psP, ntok=TPC):
    nc = p.nc
    scr = arena[:, 2 * ntok:3 * ntok]
    emit_ln_stats(p, x_sb, ntok, a_row, b_row, scr,
                  ps_a=psP[0:1, 0, :], ps_b=psP[0:1, 1, :],
                  mu_row=arena[0:1, 0:ntok])
    a_bc = arena[:, 0 * ntok:1 * ntok]
    b_bc = arena[:, 1 * ntok:2 * ntok]
    emit_bcast_rows(p, a_row[:], a_bc, ones_row, ps=psP[:, 0, :], tag="lna")
    emit_bcast_rows(p, b_row[:], b_bc, ones_row, ps=psP[:, 0, :], tag="lnb")
    for c in range(DC):
        p.V("tensor_mul", scr[:, :], x_sb[:, c, :], a_bc)
        p.V("tensor_add", ln_bf[:, c, :], scr[:, :], b_bc)



def build_ln_only():
    p = Prog()
    nc = p.nc
    x = p.din("x_fm", [D, TPC])
    lo = p.dout("ln_out", [DC, 128, TPC], BF16)
    x_sb = p.sbuf("x_sb", [128, DC, TPC], F32)
    ln_bf = p.sbuf("ln_bf", [128, DC, TPC], BF16)
    arena = p.sbuf("arena", [128, 3 * TPC], F32)
    a_row = p.sbuf("a_row", [1, TPC], F32)
    b_row = p.sbuf("b_row", [1, TPC], F32)
    ones_row = p.sbuf("c_onesrow", [1, 128], F32)
    psP = p.psum("psP", [128, 2, 512])
    p.dma("ld", nc.sync, x_sb[:], x.rearrange("(c p) t -> p c t", p=128))
    for e in (nc.tensor, nc.scalar, nc.vector, nc.gpsimd):
        p.wait(e, "ld")
    p.V("memset", ones_row[:], 1.0)
    emit_ln_norm(p, x_sb, ln_bf, arena, a_row, b_row, ones_row, psP[:])
    p.dma("st", nc.sync, lo.rearrange("c p t -> p c t"), ln_bf[:], wait_chans=("V",))
    p.wait(nc.sync, "st")
    return p


def build_l2_attn_d():
    p = Prog()
    nc = p.nc
    S = 8192
    lnh_d = p.din("lnh", [4, 64, S], BF16)
    wq_d = p.din("WqT", [64, 64])
    wk_d = p.din("WkT", [64, 64])
    wv_d = p.din("WvT", [64, 64])
    mask_d = p.din("masks", [128, 4, 512], BF16)
    o_o = p.dout("o_out", [4, 64, S], BF16)

    lnh = p.sbuf("lnh_sb", [128, 2, S], BF16)
    wq_sb = p.sbuf("wq_sb", [128, 3, 64], BF16)
    q_sb = p.sbuf("q_sb", [64, S], BF16)
    k_sb = p.sbuf("k_sb", [64, S], BF16)
    v_sb = p.sbuf("v_sb", [128, NKB, 65], BF16)
    o_sb = p.sbuf("o_sb", [64, S], BF16)
    mask_sb = p.sbuf("mask_sb", [128, 4, 512], BF16)
    p_bf = p.sbuf("p_bf", [128, 2, 512], BF16)
    num_sb = p.sbuf("num_sb", [64, 512], F32)
    den_r = p.sbuf("den_r", [1, 512], F32)
    ones_row = p.sbuf("c_onesrow", [1, 128], F32)
    psS = p.psum("psS", [128, 2, 512])
    psO = p.psum("psO", [65, 512])
    psB = p.psum("psB", [64, 512])
    psV = p.psum("psV", [128, 2, 65])

    p.V("memset", ones_row[:], 1.0)
    p.V("memset", v_sb[:, :, 64:65], 1.0)
    p.dma("ld", nc.gpsimd, wq_sb[0:64, 0, :], wq_d)
    p.dma("ld", nc.gpsimd, wq_sb[0:64, 1, :], wk_d)
    p.dma("ld", nc.gpsimd, wq_sb[0:64, 2, :], wv_d)
    p.dma("ld", nc.gpsimd, wq_sb[64:128, 0, :], wq_d)
    p.dma("ld", nc.gpsimd, wq_sb[64:128, 1, :], wk_d)
    p.dma("ld", nc.gpsimd, wq_sb[64:128, 2, :], wv_d)
    p.dma("ld", nc.sync, mask_sb[:], mask_d)
    for e in (nc.tensor, nc.scalar, nc.vector, nc.gpsimd):
        p.wait(e, "ld")

    sc = p.cvec(1.0 / 32.0)
    for hp in range(2):
        p.dma("lh", nc.sync, lnh[0:64, hp, :], lnh_d[2 * hp], wait_chans=("T",))
        p.dma("lh", nc.sync, lnh[64:128, hp, :], lnh_d[2 * hp + 1])
    p.wait(nc.tensor, "lh")
    for h in range(4):
        hb = (h % 2) * 64
        lv = lnh[hb:hb + 64, h // 2, :]
        wsl = wq_sb[hb:hb + 64]
        for g in range(S // 512):
            gsl = slice(g * 512, (g + 1) * 512)
            p.wait(nc.tensor, "V")
            p.mm(psS[0:64, 0, :], wsl[:, 0, :], lv[:, gsl], start=True, stop=True)
            p.mm(psS[0:64, 1, :], wsl[:, 1, :], lv[:, gsl], start=True, stop=True)
            p.wait(nc.vector, "T")
            p.V("tensor_copy", q_sb[:, gsl], psS[0:64, 0, 0:512])
            p.V("tensor_copy", k_sb[:, gsl], psS[0:64, 1, 0:512])
        for kb in range(NKB):
            par = kb % 2
            p.wait(nc.tensor, "V")
            p.mm(psV[:, par, 0:64], lv[:, kb * 128:(kb + 1) * 128], wsl[:, 2, :],
                 start=True, stop=True)
            p.wait(nc.vector, "T")
            p.V("tensor_copy", v_sb[:, kb, 0:64], psV[:, par, 0:64])
        for s in range(16):
            qsl = slice(s * 512, (s + 1) * 512)
            nkb = 4 * (s + 1)
            for kb in range(nkb):
                par = kb % 2
                psT = psS[:, par, :]
                p.wait(nc.tensor, "S")
                p.mm(psT, k_sb[:, kb * 128:(kb + 1) * 128], q_sb[:, qsl],
                     start=True, stop=True)
                p.wait(nc.scalar, "T")
                p.S("activation", out=p_bf[:, par, :], in_=psT, func=AF.Exp,
                    scale=sc[:])
                diag = kb - (nkb - 4)
                if diag >= 0:
                    p.wait(nc.vector, "S")
                    p.V("tensor_mul", p_bf[:, par, :], p_bf[:, par, :],
                        mask_sb[:, diag, :])
                    p.wait(nc.tensor, "V")
                else:
                    p.wait(nc.tensor, "S")
                p.mm(psO[:], v_sb[:, kb, :], p_bf[:, par, :],
                     start=(kb == 0), stop=(kb == nkb - 1))
            p.wait(nc.vector, "T")
            p.V("tensor_copy", num_sb[:, :], psO[0:64, :])
            p.V("reciprocal", den_r[:], psO[64:65, :])
            p.wait(nc.tensor, "V")
            p.mm(psB[:], ones_row[0:1, 0:64], den_r[:], start=True, stop=True)
            p.wait(nc.vector, "T")
            p.V("tensor_mul", o_sb[:, qsl], num_sb[:, :], psB[:, :])
        p.dma("oo", nc.sync, o_o[h], o_sb[:], wait_chans=("V",))
        p.wait(nc.sync, "oo")
        p.wait(nc.vector, "oo")
    return p


def build_proj_res():
    p = Prog()
    nc = p.nc
    x = p.din("x_fm", [D, TPC])
    y_d = p.din("y_fm", [D, TPC], BF16)
    w_d = p.din("W", [D, D])
    b_d = p.din("bcol", [128, DC])
    xo = p.dout("x_out", [D, TPC])

    x_sb = p.sbuf("x_sb", [128, DC, TPC], F32)
    y_sb = p.sbuf("y_sb", [128, DC, TPC], BF16)
    b_sb = p.sbuf("b_sb", [128, DC], F32)
    wbuf = p.sbuf("wbuf", [128, 2, DC, 128], BF16)
    scr = p.sbuf("scr", [128, 512], F32)
    psQ = p.psum("psQ", [128, 512])

    p.dma("ld", nc.sync, x_sb[:], x.rearrange("(c p) t -> p c t", p=128))
    p.dma("ld", nc.sync, y_sb[:], y_d.rearrange("(c p) t -> p c t", p=128))
    p.dma("ld", nc.sync, b_sb[:], b_d)
    for e in (nc.tensor, nc.scalar, nc.vector, nc.gpsimd):
        p.wait(e, "ld")

    def evac(oc, tg, ps):
        tsl = slice(tg * 512, (tg + 1) * 512)
        p.wait(nc.vector, "T")
        p.V("tensor_scalar", scr[:, :], ps, b_sb[:, oc:oc + 1], None, op0=ALU.add)
        p.V("tensor_add", x_sb[:, oc, tsl], x_sb[:, oc, tsl], scr[:, :])

    emit_gemm_stream(p, w_d, D, D, y_sb[:], TPC, psQ[:], evac, wbuf, "w")
    p.dma("st", nc.sync, xo.rearrange("(c p) t -> p c t", p=128), x_sb[:],
          wait_chans=("V",))
    p.wait(nc.sync, "st")
    return p


def run_l2_attn(x_shards, inputs, cache):
    import ml_dtypes
    bf16 = ml_dtypes.bfloat16
    if "ln_only" not in cache:
        cache["ln_only"] = build_ln_only()
    pC = cache["ln_only"]
    in_maps = [{"x_fm": x_shards[c]} for c in range(NCORE)]
    resC = run_bass_kernel_spmd(pC.nc, in_maps, core_ids=list(range(NCORE)))
    ln_b = []
    for b in range(2):
        ln = np.concatenate([np.asarray(resC.results[b * 4 + q]["ln_out"]) for q in range(4)],
                            axis=2)
        ln_b.append(ln.reshape(D, 8192))

    if "l2d" not in cache:
        cache["l2d"] = build_l2_attn_d()
    pD = cache["l2d"]
    masks = np.zeros((128, 4, 512), dtype=bf16)
    i = np.arange(128)[:, None]
    j = np.arange(512)[None, :]
    for r in range(4):
        masks[:, r, :] = (j >= i + 128 * r).astype(bf16)
    in_maps = []
    for c in range(NCORE):
        b = c // 4
        hs = [(c % 4) * 4 + j2 for j2 in range(4)]
        lnh = np.stack([ln_b[b].reshape(16, 64, 8192)[h] for h in hs])
        in_maps.append({
            "lnh": np.ascontiguousarray(lnh),
            "WqT": np.ascontiguousarray(inputs["L2_attn_Wq"].T),
            "WkT": np.ascontiguousarray(inputs["L2_attn_Wk"].T),
            "WvT": np.ascontiguousarray(inputs["L2_attn_Wv"].T),
            "masks": masks,
        })
    resD = run_bass_kernel_spmd(pD.nc, in_maps, core_ids=list(range(NCORE)))
    y_b = []
    for b in range(2):
        y = np.zeros((D, 8192), dtype=bf16)
        for q in range(4):
            r = resD.results[b * 4 + q]
            for j2 in range(4):
                h = q * 4 + j2
                y[h * 64:(h + 1) * 64, :] = np.asarray(r["o_out"][j2])
        y_b.append(y)

    if "proj_res" not in cache:
        cache["proj_res"] = build_proj_res()
    pE = cache["proj_res"]
    bu = inputs["L2_attn_bu"].reshape(DC, 128).T
    in_maps = []
    for c in range(NCORE):
        b, q = c // 4, c % 4
        in_maps.append({
            "x_fm": x_shards[c],
            "y_fm": np.ascontiguousarray(y_b[b][:, q * TPC:(q + 1) * TPC]),
            "W": np.ascontiguousarray(inputs["L2_attn_Wu"].T),
            "bcol": np.ascontiguousarray(bu),
        })
    resE = run_bass_kernel_spmd(pE.nc, in_maps, core_ids=list(range(NCORE)))
    return [np.asarray(r["x_out"]) for r in resE.results]



def build_l2_f1():
    p = Prog()
    nc = p.nc
    x = p.din("x_fm", [D, TPC])
    po = p.dout("part", [128, DC])
    x_sb = p.sbuf("x_sb", [128, DC, TPC], F32)
    arena = p.sbuf("arena", [128, 3 * TPC], F32)
    a_row = p.sbuf("a_row", [1, TPC], F32)
    b_row = p.sbuf("b_row", [1, TPC], F32)
    part = p.sbuf("part_sb1", [128, DC], F32)
    ones_row = p.sbuf("c_onesrow", [1, 128], F32)
    psP = p.psum("psP", [128, 2, 512])
    p.dma("ld", nc.sync, x_sb[:], x.rearrange("(c p) t -> p c t", p=128))
    for e in (nc.tensor, nc.scalar, nc.vector, nc.gpsimd):
        p.wait(e, "ld")
    p.V("memset", ones_row[:], 1.0)
    scr = arena[:, 2 * TPC:3 * TPC]
    emit_ln_stats(p, x_sb, TPC, a_row, b_row, scr,
                  ps_a=psP[0:1, 0, :], ps_b=psP[0:1, 1, :],
                  mu_row=arena[0:1, 0:TPC])
    a_bc = arena[:, 0 * TPC:1 * TPC]
    b_bc = arena[:, 1 * TPC:2 * TPC]
    emit_bcast_rows(p, a_row[:], a_bc, ones_row, ps=psP[:, 0, :], tag="a")
    emit_bcast_rows(p, b_row[:], b_bc, ones_row, ps=psP[:, 0, :], tag="b")
    for c in range(DC):
        p.V("tensor_mul", scr[:, :], x_sb[:, c, :], a_bc)
        p.V("tensor_add", scr[:, :], scr[:, :], b_bc)
        p.V("tensor_reduce", part[:, c:c + 1], scr[:, :], axis=AX.X, op=ALU.add)
    p.dma("st", nc.sync, po, part[:], wait_chans=("V",))
    p.wait(nc.sync, "st")
    return p


def build_l2_f2():
    p = Prog()
    nc = p.nc
    x = p.din("x_fm", [D, TPC])
    part_d = p.din("parts", [4, 128, DC])
    cnw1_d = p.din("cnw1", [D, D])
    cnw2_d = p.din("cnw2", [D, 512])
    cnw3_d = p.din("cnw3", [512, 256])
    cnw4_d = p.din("cnw4", [256, 1])
    cnb1_d = p.din("cnb1", [128, 8])
    cnb2_d = p.din("cnb2", [128, 4])
    cnb3_d = p.din("cnb3", [128, 2])
    cnb4_d = p.din("cnb4", [1, 1])
    gcw1_d = p.din("gcw1", [D, 512])
    gcw2_d = p.din("gcw2", [512, 256])
    gcw3_d = p.din("gcw3", [256, 1])
    gcb1_d = p.din("gcb1", [128, 4])
    gcb2_d = p.din("gcb2", [128, 2])
    gcb3_d = p.din("gcb3", [1, 1])
    ar_d = p.din("ar", [1, 1])
    xo = p.dout("x_out", [D, TPC])

    out32 = p.sbuf("out32", [128, DC, TPC], F32)
    arena = p.sbuf("arena", [128, 3 * TPC], F32)
    out_bf = p.sbuf("out_bf", [128, DC, TPC], BF16)
    h1c = p.sbuf("h1c", [128, DC, TPC], BF16)
    h2c = p.sbuf("h2c", [128, 4, TPC], BF16)
    h3c = p.sbuf("h3c", [128, 2, TPC], BF16)
    a_row = p.sbuf("a_row", [1, TPC], F32)
    b_row = p.sbuf("b_row", [1, TPC], F32)
    part = p.sbuf("part_sb", [128, 4, DC], F32)
    gm_bf = p.sbuf("gm_bf", [128, DC], BF16)
    gm32 = p.sbuf("gm32", [128, DC], F32)
    gcw1_sb = p.sbuf("gcw1_sb", [128, DC, 512], BF16)
    gcw2_sb = p.sbuf("gcw2_sb", [128, 4, 256], BF16)
    gcw3_sb = p.sbuf("gcw3_sb", [128, 2, 1], BF16)
    cnw4_sb = p.sbuf("cnw4_sb", [128, 2, 1], BF16)
    gh1 = p.sbuf("gh1", [128, 4], BF16)
    gh2 = p.sbuf("gh2", [128, 2], BF16)
    gf_sc = p.sbuf("gf_sc", [1, 4], F32)
    bias_sc = p.sbuf("bias_sc", [1, 3], F32)
    cnb1 = p.sbuf("cnb1_sb", [128, 8], F32)
    cnb2 = p.sbuf("cnb2_sb", [128, 4], F32)
    cnb3 = p.sbuf("cnb3_sb", [128, 2], F32)
    gcb1 = p.sbuf("gcb1_sb", [128, 4], F32)
    gcb2 = p.sbuf("gcb2_sb", [128, 2], F32)
    wbuf = p.sbuf("wbuf", [128, 2, DC, 128], BF16)
    ones_row = p.sbuf("c_onesrow", [1, 128], F32)
    psP = p.psum("psP", [128, 2, 512])
    psQ = p.psum("psQ", [128, 512])
    psG = p.psum("psG", [128, 1])

    p.dma("ld", nc.sync, out32[:], x.rearrange("(c p) t -> p c t", p=128))
    p.dma("ld", nc.sync, part[:], part_d.rearrange("q p c -> p q c"))
    p.dma("ld", nc.gpsimd, gcw1_sb[:], gcw1_d.rearrange("(kc p) o -> p kc o", p=128))
    p.dma("ld", nc.gpsimd, gcw2_sb[:], gcw2_d.rearrange("(kc p) o -> p kc o", p=128))
    p.dma("ld", nc.gpsimd, gcw3_sb[:], gcw3_d.rearrange("(kc p) o -> p kc o", p=128), allow_slow_non_contiguous=True)
    p.dma("ld", nc.gpsimd, cnw4_sb[:], cnw4_d.rearrange("(kc p) o -> p kc o", p=128), allow_slow_non_contiguous=True)
    p.dma("ld", nc.sync, cnb1[:], cnb1_d)
    p.dma("ld", nc.sync, cnb2[:], cnb2_d)
    p.dma("ld", nc.sync, cnb3[:], cnb3_d)
    p.dma("ld", nc.sync, bias_sc[0:1, 0:1], cnb4_d)
    p.dma("ld", nc.sync, gcb1[:], gcb1_d)
    p.dma("ld", nc.sync, gcb2[:], gcb2_d)
    p.dma("ld", nc.sync, bias_sc[0:1, 1:2], gcb3_d)
    p.dma("ld", nc.sync, bias_sc[0:1, 2:3], ar_d)
    for e in (nc.tensor, nc.scalar, nc.vector, nc.gpsimd):
        p.wait(e, "ld")
    p.V("memset", ones_row[:], 1.0)

    scr = arena[:, 2 * TPC:3 * TPC]
    emit_ln_stats(p, out32, TPC, a_row, b_row, scr,
                  ps_a=psP[0:1, 0, :], ps_b=psP[0:1, 1, :],
                  mu_row=arena[0:1, 0:TPC])
    a_bc = arena[:, 0 * TPC:1 * TPC]
    b_bc = arena[:, 1 * TPC:2 * TPC]
    emit_bcast_rows(p, a_row[:], a_bc, ones_row, ps=psP[:, 0, :], tag="a")
    emit_bcast_rows(p, b_row[:], b_bc, ones_row, ps=psP[:, 0, :], tag="b")
    for c in range(DC):
        p.V("tensor_mul", out32[:, c, :], out32[:, c, :], a_bc)
        p.V("tensor_add", out32[:, c, :], out32[:, c, :], b_bc)

    tf_row = arena[0:1, 1 * TPC:2 * TPC]
    cp_row = arena[0:1, 2 * TPC:3 * TPC]
    p.V("tensor_add", part[:, 0, :], part[:, 0, :], part[:, 1, :])
    p.V("tensor_add", part[:, 2, :], part[:, 2, :], part[:, 3, :])
    p.V("tensor_add", gm32[:], part[:, 0, :], part[:, 2, :])
    p.V("tensor_scalar_mul", gm32[:], gm32[:], 1.0 / 8192.0)
    p.V("tensor_copy", gm_bf[:], gm32[:])

    p.wait(nc.tensor, "V")
    for oc in range(4):
        for kc in range(DC):
            p.mm(psG[:], gcw1_sb[:, kc, oc * 128:(oc + 1) * 128], gm_bf[:, kc:kc + 1],
                 start=(kc == 0), stop=(kc == DC - 1))
        p.wait(nc.scalar, "T")
        p.S("activation", out=gh1[:, oc:oc + 1], in_=psG[:], func=AF.Gelu,
            bias=gcb1[:, oc:oc + 1])
        p.wait(nc.tensor, "S")
    for oc in range(2):
        for kc in range(4):
            p.mm(psG[:], gcw2_sb[:, kc, oc * 128:(oc + 1) * 128], gh1[:, kc:kc + 1],
                 start=(kc == 0), stop=(kc == 3))
        p.wait(nc.scalar, "T")
        p.S("activation", out=gh2[:, oc:oc + 1], in_=psG[:], func=AF.Gelu,
            bias=gcb2[:, oc:oc + 1])
        p.wait(nc.tensor, "S")
    for kc in range(2):
        p.mm(psG[0:1, :], gcw3_sb[:, kc, :], gh2[:, kc:kc + 1],
             start=(kc == 0), stop=(kc == 1))
    p.wait(nc.scalar, "T")
    p.S("activation", out=gf_sc[0:1, 0:1], in_=psG[0:1, :], func=AF.Sigmoid,
        bias=bias_sc[0:1, 1:2])
    p.wait(nc.vector, "S")
    p.V("tensor_scalar", gf_sc[0:1, 1:2], gf_sc[0:1, 0:1], 0.7, -0.5,
        op0=ALU.mult, op1=ALU.add)
    p.V("tensor_scalar_mul", gf_sc[0:1, 2:3], bias_sc[0:1, 2:3], 2.0)

    for cyc in range(3):
        for c in range(DC):
            p.V("tensor_copy", out_bf[:, c, :], out32[:, c, :])

        def evac_h1(oc, tg, ps):
            tsl = slice(tg * 512, (tg + 1) * 512)
            p.wait(nc.scalar, "T")
            p.S("activation", out=h1c[:, oc, tsl], in_=ps, func=AF.Gelu,
                bias=cnb1[:, oc:oc + 1])
        emit_gemm_stream(p, cnw1_d, D, D, out_bf[:], TPC, psQ[:], evac_h1, wbuf,
                         "c1_%d" % cyc)

        def evac_h2(oc, tg, ps):
            tsl = slice(tg * 512, (tg + 1) * 512)
            p.wait(nc.scalar, "T")
            p.S("activation", out=h2c[:, oc, tsl], in_=ps, func=AF.Gelu,
                bias=cnb2[:, oc:oc + 1])
        emit_gemm_stream(p, cnw2_d, D, 512, h1c[:], TPC, psQ[:], evac_h2, wbuf,
                         "c2_%d" % cyc)

        def evac_h3(oc, tg, ps):
            tsl = slice(tg * 512, (tg + 1) * 512)
            p.wait(nc.scalar, "T")
            p.S("activation", out=h3c[:, oc, tsl], in_=ps, func=AF.Gelu,
                bias=cnb3[:, oc:oc + 1])
        emit_gemm_stream(p, cnw3_d, 512, 256, h2c[:], TPC, psQ[:], evac_h3, wbuf,
                         "c3_%d" % cyc)

        for tg in range(4):
            tsl = slice(tg * 512, (tg + 1) * 512)
            p.wait(nc.tensor, "S")
            for kc in range(2):
                p.mm(psQ[0:1, :], cnw4_sb[:, kc, :], h3c[:, kc, tsl],
                     start=(kc == 0), stop=(kc == 1))
            p.wait(nc.scalar, "T")
            p.S("activation", out=tf_row[0:1, tsl], in_=psQ[0:1, 0:512],
                func=AF.Sigmoid, bias=bias_sc[0:1, 0:1])
        p.wait(nc.vector, "S")
        p.V("tensor_scalar", cp_row[:], tf_row[:], 0.3, gf_sc[0:1, 1:2],
            op0=ALU.mult, op1=ALU.add)
        p.V("tensor_scalar", cp_row[:], cp_row[:], gf_sc[0:1, 2:3], 0.1,
            op0=ALU.mult, op1=ALU.add)
        cp_bc = arena[:, 0 * TPC:1 * TPC]
        emit_bcast_rows(p, cp_row[:], cp_bc, ones_row, ps=psP[:, 0, :],
                        tag="cp%d" % cyc)
        tw = arena[:, 1 * TPC:2 * TPC]
        tw2 = arena[:, 2 * TPC:3 * TPC]
        for c in range(4):
            sgn_top = 1.0 if c < 2 else -1.0
            p.V("tensor_mul", tw[:, :], out32[:, c + 4, :], cp_bc)
            p.V("tensor_mul", tw2[:, :], out32[:, c, :], cp_bc)
            if c < 2:
                p.V("tensor_add", out32[:, c, :], out32[:, c, :], tw[:, :])
                p.V("tensor_sub", out32[:, c + 4, :], out32[:, c + 4, :], tw2[:, :])
            else:
                p.V("tensor_sub", out32[:, c, :], out32[:, c, :], tw[:, :])
                p.V("tensor_add", out32[:, c + 4, :], out32[:, c + 4, :], tw2[:, :])

    xrow = arena[:, 0 * TPC:1 * TPC]
    for c in range(DC):
        p.dma("xl", nc.sync, xrow,
              x.rearrange("(c p) t -> p c t", p=128)[:, c, :], wait_chans=("V",))
        p.wait(nc.vector, "xl")
        p.V("scalar_tensor_tensor", out32[:, c, :], out32[:, c, :], 0.5, xrow,
            op0=ALU.mult, op1=ALU.add)
    p.dma("st", nc.sync, xo.rearrange("(c p) t -> p c t", p=128), out32[:],
          wait_chans=("V",))
    p.wait(nc.sync, "st")
    return p


def run_l2_ffn(x_shards, inputs, cache):
    if "l2f1" not in cache:
        cache["l2f1"] = build_l2_f1()
    p1 = cache["l2f1"]
    res1 = run_bass_kernel_spmd(p1.nc, [{"x_fm": x_shards[c]} for c in range(NCORE)],
                                core_ids=list(range(NCORE)))
    parts = [np.asarray(r["part"]) for r in res1.results]
    if "l2f2" not in cache:
        cache["l2f2"] = build_l2_f2()
    p2 = cache["l2f2"]
    in_maps = []
    for c in range(NCORE):
        b = c // 4
        in_maps.append({
            "x_fm": x_shards[c],
            "parts": np.stack([parts[b * 4 + q] for q in range(4)]),
            "cnw1": inputs["L2_ffn_cn_w1"], "cnw2": inputs["L2_ffn_cn_w2"],
            "cnw3": inputs["L2_ffn_cn_w3"], "cnw4": inputs["L2_ffn_cn_w4"],
            "cnb1": np.ascontiguousarray(inputs["L2_ffn_cn_b1"].reshape(8, 128).T),
            "cnb2": np.ascontiguousarray(inputs["L2_ffn_cn_b2"].reshape(4, 128).T),
            "cnb3": np.ascontiguousarray(inputs["L2_ffn_cn_b3"].reshape(2, 128).T),
            "cnb4": inputs["L2_ffn_cn_b4"].reshape(1, 1),
            "gcw1": inputs["L2_ffn_gc_w1"], "gcw2": inputs["L2_ffn_gc_w2"],
            "gcw3": inputs["L2_ffn_gc_w3"],
            "gcb1": np.ascontiguousarray(inputs["L2_ffn_gc_b1"].reshape(4, 128).T),
            "gcb2": np.ascontiguousarray(inputs["L2_ffn_gc_b2"].reshape(2, 128).T),
            "gcb3": inputs["L2_ffn_gc_b3"].reshape(1, 1),
            "ar": inputs["L2_ffn_adaptive_range"].reshape(1, 1),
        })
    res2 = run_bass_kernel_spmd(p2.nc, in_maps, core_ids=list(range(NCORE)))
    return [np.asarray(r["x_out"]) for r in res2.results]



def build_l3_ffn():
    p = Prog()
    nc = p.nc
    x = p.din("x_fm", [D, TPC])
    wt_d = p.din("w_T", [D, D])
    alpha_d = p.din("alpha", [1, 1])
    b_d = p.din("bcol", [128, DC])
    xo = p.dout("x_out", [D, TPC])

    x_sb = p.sbuf("x_sb", [128, DC, TPC], F32)
    ln_bf = p.sbuf("ln_bf", [128, DC, TPC], BF16)
    arena = p.sbuf("arena", [128, 3 * TPC], F32)
    a_row = p.sbuf("a_row", [1, TPC], F32)
    b_row = p.sbuf("b_row", [1, TPC], F32)
    w_sb = p.sbuf("w_sb", [128, DC, D], F32)
    wq_bf = p.sbuf("wq_bf", [128, DC, D], BF16)
    b_sb = p.sbuf("b_sb", [128, DC], F32)
    asum = p.sbuf("asum", [128, 1], F32)
    dcol = p.sbuf("dcol", [128, 2], F32)
    srow = p.sbuf("srow", [1, 2], F32)
    alpha_sb = p.sbuf("alpha_sb", [1, 1], F32)
    scr5 = p.sbuf("scr5", [128, 512], F32)
    psP = p.psum("psP", [128, 2, 512])
    psQ = p.psum("psQ", [128, 512])

    p.dma("ld", nc.sync, x_sb[:], x.rearrange("(c p) t -> p c t", p=128))
    p.dma("ld", nc.sync, w_sb[:], wt_d.rearrange("(kc p) o -> p kc o", p=128))
    p.dma("ld", nc.sync, b_sb[:], b_d)
    p.dma("ld", nc.sync, alpha_sb[:], alpha_d)
    for e in (nc.tensor, nc.scalar, nc.vector, nc.gpsimd):
        p.wait(e, "ld")
    ident, _, _, ones_row = make_consts(p)

    p.V("tensor_reduce", asum[:], w_sb[:].rearrange("p c o -> p (c o)"), axis=AX.X,
        op=ALU.add, apply_absolute_value=True)
    p.wait(nc.tensor, "V")
    p.tr(psP[0:1, 0, 0:128], asum[:], ident[:])
    p.wait(nc.vector, "T")
    p.V("tensor_reduce", srow[0:1, 0:1], psP[0:1, 0, 0:128], axis=AX.X, op=ALU.add)
    p.V("tensor_scalar_mul", srow[0:1, 0:1], srow[0:1, 0:1], 0.7 / (D * D))
    p.V("tensor_scalar_mul", srow[0:1, 1:2], srow[0:1, 0:1], -1.0)
    p.wait(nc.tensor, "V")
    p.mm(psP[:, 0, 0:2], ones_row[:], srow[:], start=True, stop=True)
    p.wait(nc.vector, "T")
    p.V("tensor_copy", dcol[:], psP[:, 0, 0:2])
    acol = p.sbuf("acol", [128, 1], F32)
    p.mm(psP[:, 0, 2:3], ones_row[:], alpha_sb[:], start=True, stop=True)
    p.wait(nc.vector, "T")
    p.V("tensor_copy", acol[:], psP[:, 0, 2:3])
    nacol = p.sbuf("nacol", [128, 1], F32)
    p.wait(nc.vector, "T")
    p.V("tensor_scalar_mul", nacol[:], acol[:], -1.0)
    for c in range(DC):
        for g in range(2):
            sl = slice(g * 512, (g + 1) * 512)
            p.V("tensor_scalar", scr5[:, :], w_sb[:, c, sl], dcol[:, 0:1], None,
                op0=ALU.is_gt)
            p.V("scalar_tensor_tensor", scr5[:, :], w_sb[:, c, sl], dcol[:, 1:2],
                scr5[:, :], op0=ALU.is_lt, op1=ALU.subtract)
            p.V("tensor_scalar", wq_bf[:, c, sl], scr5[:, :], nacol[:, 0:1], None,
                op0=ALU.mult)

    emit_ln_norm(p, x_sb, ln_bf, arena, a_row, b_row, ones_row, psP[:])

    for oc in range(DC):
        for tg in range(4):
            tsl = slice(tg * 512, (tg + 1) * 512)
            p.wait(nc.tensor, "V")
            for kc in range(DC):
                p.mm(psQ[:], wq_bf[:, kc, oc * 128:(oc + 1) * 128], ln_bf[:, kc, tsl],
                     start=(kc == 0), stop=(kc == DC - 1))
            p.wait(nc.vector, "T")
            p.V("tensor_scalar", scr5[:, :], psQ[:], b_sb[:, oc:oc + 1], None,
                op0=ALU.add)
            p.V("scalar_tensor_tensor", x_sb[:, oc, tsl], scr5[:, :], 0.5,
                x_sb[:, oc, tsl], op0=ALU.mult, op1=ALU.add)
    p.dma("st", nc.sync, xo.rearrange("(c p) t -> p c t", p=128), x_sb[:],
          wait_chans=("V",))
    p.wait(nc.sync, "st")
    return p


_IDENT_CACHE = {}


def ident_view(p):
    if not hasattr(p, "_ident1"):
        t = p.sbuf("ident1", [1, 1], F32)
        p.V("memset", t[:], 1.0)
        p._ident1 = t
    return p._ident1[0:1, 0:1]


def run_l3_ffn(x_shards, inputs, cache):
    if "l3f" not in cache:
        cache["l3f"] = build_l3_ffn()
    pf = cache["l3f"]
    bcol = np.ascontiguousarray(inputs["L3_ffn_bias"].reshape(DC, 128).T)
    in_maps = [{
        "x_fm": x_shards[c],
        "w_T": np.ascontiguousarray(inputs["L3_ffn_weight"].T),
        "alpha": inputs["L3_ffn_alpha"].reshape(1, 1),
        "bcol": bcol,
    } for c in range(NCORE)]
    res = run_bass_kernel_spmd(pf.nc, in_maps, core_ids=list(range(NCORE)))
    return [np.asarray(r["x_out"]) for r in res.results]



def build_l3_a1():
    p = Prog()
    nc = p.nc
    x = p.din("x_fm", [D, TPC])
    wqkv_d = p.din("w_qkv", [D, 3 * D])
    bqkv_d = p.din("bqkv", [128, 24])
    proj_d = p.din("proj", [16, 64, 64])
    qp_o = p.dout("qp", [16, 64, TPC], BF16)
    kpf_o = p.dout("kpf", [16, 64, TPC], BF16)
    kpt_o = p.dout("kpt", [16, NTILE, 128, 64], BF16)
    vt_o = p.dout("vt", [16, NTILE, 128, 65], BF16)
    st_o = p.dout("st", [16, 64, 65])

    x_sb = p.sbuf("x_sb", [128, DC, TPC], F32)
    ln_bf = p.sbuf("ln_bf", [128, DC, TPC], BF16)
    arena = p.sbuf("arena", [128, 3 * TPC], F32)
    a_row = p.sbuf("a_row", [1, TPC], F32)
    b_row = p.sbuf("b_row", [1, TPC], F32)
    qkv_bf = p.sbuf("qkv_bf", [128, 12, TPC], BF16)
    bqkv_sb = p.sbuf("bqkv_sb", [128, 24], F32)
    proj_sb = p.sbuf("proj_sb", [128, 8, 64], BF16)
    wbuf = p.sbuf("wbuf", [128, 2, DC, 128], BF16)
    qp_sb = p.sbuf("qp_sb", [64, TPC], BF16)
    kpf_sb = p.sbuf("kpf_sb", [64, TPC], BF16)
    kpt_sb = p.sbuf("kpt_sb", [128, NTILE, 64], BF16)
    vt_sb = p.sbuf("vt_sb", [128, NTILE, 65], BF16)
    st_sb = p.sbuf("st_sb", [64, 65], F32)
    psP = p.psum("psP", [128, 2, 512])
    psQ = p.psum("psQ", [128, 512])
    psS = p.psum("psS", [64, 65])
    psT = p.psum("psT", [128, 64], BF16)
    ident_bf = p.sbuf("ident_bf", [128, 128], BF16)
    ones_row = None

    p.dma("ld", nc.sync, x_sb[:], x.rearrange("(c p) t -> p c t", p=128))
    p.dma("ld", nc.sync, bqkv_sb[:], bqkv_d)
    p.dma("ld", nc.gpsimd, proj_sb[0:64, :, :],
          proj_d.rearrange("(hp two) d m -> two d hp m", two=2)[0])
    p.dma("ld", nc.gpsimd, proj_sb[64:128, :, :],
          proj_d.rearrange("(hp two) d m -> two d hp m", two=2)[1])
    for e in (nc.tensor, nc.scalar, nc.vector, nc.gpsimd):
        p.wait(e, "ld")
    ident, _, _, ones_row = make_consts(p)
    p.V("tensor_copy", ident_bf[:], ident[:])

    emit_ln_norm(p, x_sb, ln_bf, arena, a_row, b_row, ones_row, psP[:])

    sc18 = p.cvec(1.0 / 8.0)
    for half in range(2):
      for part3 in range(3):
        def evac_qkv(oc, tg, ps, part3=part3, half=half):
            tsl = slice(tg * 512, (tg + 1) * 512)
            p.wait(nc.scalar, "T")
            goc = part3 * 8 + half * 4 + oc
            p.S("activation", out=qkv_bf[:, part3 * 4 + oc, tsl], in_=ps,
                func=AF.Identity, bias=bqkv_sb[:, goc:goc + 1])
        emit_gemm_stream(p, wqkv_d[:, (part3 * 8 + half * 4) * 128:
                                   (part3 * 8 + half * 4 + 4) * 128],
                         D, 512, ln_bf[:], TPC, psQ[:], evac_qkv, wbuf,
                         "qkv%d_%d" % (half, part3))

      for hh in range(8):
        h = half * 8 + hh
        hb = (h % 2) * 64
        prj = proj_sb[hb:hb + 64, :, :]
        pc = (h // 2)
        lc = hh // 2
        qf = qkv_bf[hb:hb + 64, lc, :]
        kf = qkv_bf[hb:hb + 64, 4 + lc, :]
        vf = qkv_bf[hb:hb + 64, 8 + lc, :]
        for g in range(4):
            gsl = slice(g * 512, (g + 1) * 512)
            p.wait(nc.tensor, "S")
            p.mm(psQ[0:64, :], prj[:, pc, :], qf[:, gsl], start=True, stop=True)
            p.wait(nc.scalar, "T")
            p.S("activation", out=qp_sb[:, gsl], in_=psQ[0:64, 0:512], func=AF.Relu,
                scale=sc18[0:64, :])
            p.wait(nc.tensor, "S")
            p.mm(psQ[0:64, :], prj[:, pc, :], kf[:, gsl], start=True, stop=True)
            p.wait(nc.scalar, "T")
            p.S("activation", out=kpf_sb[:, gsl], in_=psQ[0:64, 0:512], func=AF.Relu,
                scale=sc18[0:64, :])
        for it in range(NTILE):
            tsl = slice(it * 128, (it + 1) * 128)
            p.wait(nc.tensor, "S")
            p.wait(nc.tensor, "V")
            p.mm(psP[:, 0, 0:64], kf[:, tsl], prj[:, pc, :], start=True, stop=True)
            p.tr(psT[:, 0:64], vf[:, tsl], ident_bf[hb:hb + 64, hb:hb + 64])
            p.wait(nc.scalar, "T")
            p.S("activation", out=kpt_sb[:, it, :], in_=psP[:, 0, 0:64], func=AF.Relu,
                scale=sc18[:])
            p.wait(nc.vector, "T")
            p.V("tensor_copy", vt_sb[:, it, 0:64], psT[:, 0:64])
        p.V("memset", vt_sb[:, :, 64:65], 1.0)
        p.wait(nc.tensor, "S")
        p.wait(nc.tensor, "V")
        for it in range(NTILE):
            p.mm(psS[:], kpt_sb[:, it, :], vt_sb[:, it, :],
                 start=(it == 0), stop=(it == NTILE - 1))
        p.wait(nc.vector, "T")
        p.V("tensor_copy", st_sb[:], psS[:])
        p.dma("st", nc.sync, qp_o[h], qp_sb[:], wait_chans=("S",))
        p.dma("st", nc.sync, kpf_o[h], kpf_sb[:])
        p.dma("st", nc.sync, kpt_o[h].rearrange("i p m -> p i m"), kpt_sb[:])
        p.dma("st", nc.sync, vt_o[h].rearrange("i p m -> p i m"), vt_sb[:],
              wait_chans=("V",))
        p.dma("st", nc.sync, st_o[h], st_sb[:])
        p.wait(nc.tensor, "st")
        p.wait(nc.scalar, "st")
        p.wait(nc.vector, "st")
    p.wait(nc.sync, "st")
    return p


def build_l3_a2():
    p = Prog()
    nc = p.nc
    x = p.din("x_fm", [D, TPC])
    qp_d = p.din("qp", [16, 64, TPC], BF16)
    kpf_d = p.din("kpf", [16, 64, TPC], BF16)
    kpt_d = p.din("kpt", [16, NTILE, 128, 64], BF16)
    vt_d = p.din("vt", [16, NTILE, 128, 65], BF16)
    prev_d = p.din("prev", [3, 16, 64, 65])
    wout_d = p.din("w_out", [D, D])
    bout_d = p.din("bcol", [128, DC])
    xo = p.dout("x_out", [D, TPC])

    x_sb = p.sbuf("x_sb", [128, DC, TPC], F32)
    qp_sb = p.sbuf("qp_sb", [64, TPC], BF16)
    kpf_sb = p.sbuf("kpf_sb", [64, TPC], BF16)
    kpt_sb = p.sbuf("kpt_sb", [128, NTILE, 64], BF16)
    vt_sb = p.sbuf("vt_sb", [128, NTILE, 65], BF16)
    prev_sb = p.sbuf("prev_sb", [64, 3, 16, 65], F32)
    st32 = p.sbuf("st32", [64, 65], F32)
    st_bf = p.sbuf("st_bf", [64, 65], BF16)
    ao_sb = p.sbuf("ao_sb", [128, DC, TPC], BF16)
    g_bf = p.sbuf("g_bf", [128, 128], BF16)
    num_sb = p.sbuf("num_sb", [64, 128], F32)
    den_r = p.sbuf("den_r", [1, 128], F32)
    tri = p.sbuf("tri", [128, 128], BF16)
    b_sb = p.sbuf("b_sb", [128, DC], F32)
    wbuf = p.sbuf("wbuf", [128, 2, DC, 128], BF16)
    scr5 = p.sbuf("scr5", [128, 512], F32)
    psG = p.psum("psG", [128, 2, 128])
    psA = p.psum("psA", [65, 128])
    psU = p.psum("psU", [64, 65])
    psB = p.psum("psB", [64, 128])
    psQ = p.psum("psQ", [128, 512])

    p.dma("ld", nc.sync, x_sb[:], x.rearrange("(c p) t -> p c t", p=128))
    p.dma("ld", nc.sync, prev_sb[:], prev_d.rearrange("s h d m -> d s h m"))
    p.dma("ld", nc.sync, b_sb[:], bout_d)
    for e in (nc.tensor, nc.scalar, nc.vector, nc.gpsimd):
        p.wait(e, "ld")
    ident, _, ones, ones_row = make_consts(p)
    iota_p2 = p.sbuf("iota_p2", [128, 1], F32)
    iota_f2 = p.sbuf("iota_f2", [128, 128], F32)
    p.G("iota", iota_p2[:], pattern=[[0, 1]], base=0, channel_multiplier=1,
        allow_small_or_imprecise_dtypes=True)
    p.G("iota", iota_f2[:], pattern=[[1, 128]], base=0, channel_multiplier=0,
        allow_small_or_imprecise_dtypes=True)
    p.wait(nc.vector, "G")
    p.V("tensor_scalar", tri[:], iota_f2[:], iota_p2[:], None, op0=ALU.is_ge)
    eps = p.cvec(1e-6, 1)

    for h in range(16):
        p.dma("lh", nc.sync, qp_sb[:], qp_d[h], wait_chans=("T",))
        p.dma("lh", nc.sync, kpf_sb[:], kpf_d[h])
        p.dma("lh", nc.sync, kpt_sb[:], kpt_d[h].rearrange("i p m -> p i m"))
        p.dma("lh", nc.sync, vt_sb[:], vt_d[h].rearrange("i p m -> p i m"))
        p.wait(nc.vector, "lh")
        hsl = [prev_sb[:, s, h, :] for s in range(3)]
        p.V("tensor_add", st32[:], hsl[0], hsl[1])
        p.V("tensor_add", st32[:], st32[:], hsl[2])
        p.wait(nc.vector, "lh")
        for it in range(NTILE):
            tsl = slice(it * 128, (it + 1) * 128)
            p.wait(nc.vector, "T")
            p.V("tensor_copy", st_bf[:], st32[:])
            p.wait(nc.tensor, "V")
            p.wait(nc.tensor, "S")
            p.mm(psG[:, 0, :], kpf_sb[:, tsl], qp_sb[:, tsl], start=True, stop=True)
            p.wait(nc.vector, "T")
            p.V("tensor_mul", g_bf[:], psG[:, 0, 0:128], tri[:])
            p.wait(nc.tensor, "V")
            p.mm(psA[:], vt_sb[:, it, :], g_bf[:], start=True, stop=False)
            p.mm(psA[:], st_bf[:], qp_sb[:, tsl], start=False, stop=True)
            p.wait(nc.vector, "T")
            p.V("tensor_copy", num_sb[:], psA[0:64, :])
            p.V("tensor_scalar", den_r[:], psA[64:65, :], eps[0:1, 0:1], None,
                op0=ALU.add)
            p.V("reciprocal", den_r[:], den_r[:])
            p.wait(nc.tensor, "V")
            p.mm(psB[:], ones_row[0:1, 0:64], den_r[:], start=True, stop=True)
            p.wait(nc.vector, "T")
            hb = (h % 2) * 64
            p.V("tensor_mul", ao_sb[hb:hb + 64, h // 2, tsl], num_sb[:], psB[:, :])
            p.mm(psU[:], kpt_sb[:, it, :], vt_sb[:, it, :], start=True, stop=True)
            p.wait(nc.vector, "T")
            p.V("tensor_add", st32[:], st32[:], psU[:, :])

    def evac(oc, tg, ps):
        tsl = slice(tg * 512, (tg + 1) * 512)
        p.wait(nc.vector, "T")
        p.V("tensor_scalar", scr5[:, :], ps, b_sb[:, oc:oc + 1], None, op0=ALU.add)
        p.V("tensor_add", x_sb[:, oc, tsl], x_sb[:, oc, tsl], scr5[:, :])
    emit_gemm_stream(p, wout_d, D, D, ao_sb[:], TPC, psQ[:], evac, wbuf, "wo")
    p.dma("st", nc.sync, xo.rearrange("(c p) t -> p c t", p=128), x_sb[:],
          wait_chans=("V",))
    p.wait(nc.sync, "st")
    return p


def run_l3_attn(x_shards, inputs, cache):
    if "l3a1" not in cache:
        cache["l3a1"] = build_l3_a1()
    p1 = cache["l3a1"]
    bq = np.ascontiguousarray(inputs["L3_attn_b_qkv"].reshape(24, 128).T)
    in_maps = [{
        "x_fm": x_shards[c], "w_qkv": inputs["L3_attn_w_qkv"],
        "bqkv": bq, "proj": inputs["L3_attn_proj"],
    } for c in range(NCORE)]
    res1 = run_bass_kernel_spmd(p1.nc, in_maps, core_ids=list(range(NCORE)))
    parts = [np.asarray(r["st"]) for r in res1.results]

    if "l3a2" not in cache:
        cache["l3a2"] = build_l3_a2()
    p2 = cache["l3a2"]
    in_maps = []
    for c in range(NCORE):
        b, q = c // 4, c % 4
        prev = np.zeros((3, 16, 64, 65), dtype=np.float32)
        for j2 in range(q):
            prev[j2] = parts[b * 4 + j2]
        r = res1.results[c]
        in_maps.append({
            "x_fm": x_shards[c],
            "qp": np.asarray(r["qp"]), "kpf": np.asarray(r["kpf"]),
            "kpt": np.asarray(r["kpt"]), "vt": np.asarray(r["vt"]),
            "prev": prev,
            "w_out": inputs["L3_attn_w_out"],
            "bcol": np.ascontiguousarray(inputs["L3_attn_b_out"].reshape(DC, 128).T),
        })
    res2 = run_bass_kernel_spmd(p2.nc, in_maps, core_ids=list(range(NCORE)))
    return [np.asarray(r["x_out"]) for r in res2.results]


EXTA = 2688
XCA = EXTA + 1


ILA_GRPS = [(g * 512, 512) for g in range(5)] + [(2560, 128)]


def build_l1_ila():
    p = Prog()
    nc = p.nc
    x_d = p.din("x_ext", [D, XCA])
    wqkv_d = p.din("wqkv", [D, 1536])
    qs_d = p.din("qs", [64, 1])
    ks_d = p.din("ks", [64, 1])
    cosS_d = p.din("cosS", [64, EXTA], BF16)
    sinS_d = p.din("sinS", [64, EXTA], BF16)
    cosI_d = p.din("cosI", [64, EXTA], BF16)
    sinI_d = p.din("sinI", [64, EXTA], BF16)
    mask_d = p.din("maskb", [16, 128, 640], BF16)
    wo_d = p.din("wo", [512, D])
    rt_d = p.din("rotT", [64, 64], BF16)
    xo = p.dout("x_out", [D, TPC])

    A1 = p.sbuf("A1", [128, 8 * XCA], BF16)
    qn = p.sbuf("qn", [64, 8, EXTA], BF16)
    kn = p.sbuf("kn", [64, 8, EXTA], BF16)
    vp = p.sbuf("vp", [128, 4, EXTA], BF16)
    vt = p.sbuf("vt", [128, 21, 8, 65], BF16)
    wbuf = p.sbuf("wbuf", [128, 2, DC, 128], BF16)
    ssq_row = p.sbuf("ssq_row", [1, EXTA], F32)
    r_row = p.sbuf("r_row", [1, EXTA], F32)
    scrS = p.sbuf("scrS", [64, EXTA], BF16)
    p_bf = p.sbuf("p_bf", [128, 640], BF16)
    pt_bf = p.sbuf("pt_bf", [128, 128], BF16)
    num_sb = p.sbuf("num_sb", [64, 128], F32)
    den_r = p.sbuf("den_r", [1, 128], F32)
    xrow = p.sbuf("xrow", [128, 512], F32)
    qs_sb = p.sbuf("qs_sb", [64, 2], F32)
    ident_bf = p.sbuf("ident_bf2", [128, 128], BF16)
    psS = p.psum("psS", [128, 768])
    psQ = p.psum("psQ", [128, 512])
    psA = p.psum("psA", [65, 128])
    psB = p.psum("psB", [64, 128])
    psT = p.psum("psT2", [128, 64], BF16)
    psTP = p.psum("psTP", [128, 128], BF16)
    psR = p.psum("psR", [64, 512])

    xq = A1[:, 0:8 * XCA].rearrange("p (c t) -> p c t", c=8)
    masks = A1[:, 0:16 * 640].rearrange("p (i k) -> p i k", i=16)
    o_fm = A1[:, 16 * 640:16 * 640 + 4 * TPC].rearrange("p (c t) -> p c t", c=4)
    tabs = vp[0:64, :, :]

    p.dma("ld", nc.gpsimd, xq, x_d.rearrange("(c p) t -> p c t", p=128))
    p.dma("ld", nc.sync, qs_sb[:, 0:1], qs_d)
    p.dma("ld", nc.sync, qs_sb[:, 1:2], ks_d)
    rt_bf = p.sbuf("rt_bf", [64, 64], BF16)
    p.dma("ld", nc.sync, rt_bf[:], rt_d)
    for e in (nc.tensor, nc.scalar, nc.vector, nc.gpsimd):
        p.wait(e, "ld")
    ident, _, ones, ones_row = make_consts(p)
    p.V("tensor_copy", ident_bf[:], ident[:])

    def act_fn(kc, tsl):
        off = 1 if kc < 4 else 0
        return xq[:, kc, off + tsl.start: off + tsl.stop]
    for part3 in range(3):
        wsl = wqkv_d[:, part3 * 512:(part3 + 1) * 512]
        if part3 < 2:
            dst = qn if part3 == 0 else kn
            for oc in range(8):
                p.dma("w", nc.gpsimd, wbuf[:, oc % 2, :, 0:64],
                      wsl.rearrange("(kc p) o -> p kc o", p=128)
                      [:, :, oc * 64:(oc + 1) * 64], wait_chans=("T",))
                p.wait(nc.tensor, "w")
                for g0, gw in ILA_GRPS:
                    tsl = slice(g0, g0 + gw)
                    p.wait(nc.tensor, "V")
                    for kc in range(DC):
                        p.mm(psR[:, 0:gw], wbuf[:, oc % 2, kc, 0:64], act_fn(kc, tsl),
                             start=(kc == 0), stop=(kc == DC - 1))
                    p.wait(nc.vector, "T")
                    p.V("tensor_copy", dst[:, oc, tsl], psR[:, 0:gw])
        else:
            for oc in range(4):
                p.dma("w", nc.gpsimd, wbuf[:, oc % 2],
                      wsl.rearrange("(kc p) o -> p kc o", p=128)
                      [:, :, oc * 128:(oc + 1) * 128], wait_chans=("T",))
                p.wait(nc.tensor, "w")
                for g0, gw in ILA_GRPS:
                    tsl = slice(g0, g0 + gw)
                    p.wait(nc.tensor, "V")
                    for kc in range(DC):
                        p.mm(psQ[:, 0:gw], wbuf[:, oc % 2, kc, :], act_fn(kc, tsl),
                             start=(kc == 0), stop=(kc == DC - 1))
                    p.wait(nc.vector, "T")
                    p.V("tensor_copy", vp[:, oc, tsl], psQ[:, 0:gw])

    for h in range(8):
        hb = (h % 2) * 64
        for it in range(21):
            tsl = slice(it * 128, (it + 1) * 128)
            p.wait(nc.tensor, "V")
            p.tr(psT[:, 0:64], vp[hb:hb + 64, h // 2, tsl],
                 ident_bf[hb:hb + 64, hb:hb + 64])
            p.wait(nc.vector, "T")
            p.V("tensor_copy", vt[:, it, h, 0:64], psT[:, 0:64])
    p.V("memset", vt[:, :, :, 64:65], 1.0)

    p.dma("tb", nc.sync, tabs[:, 0, :], cosS_d, wait_chans=("V",))
    p.dma("tb", nc.sync, tabs[:, 1, :], sinS_d)
    p.dma("tb", nc.sync, tabs[:, 2, :], cosI_d)
    p.dma("tb", nc.sync, tabs[:, 3, :], sinI_d)
    p.wait(nc.vector, "tb")
    p.wait(nc.scalar, "tb")

    sq_ones = p.sbuf("sq1b", [64, 1], BF16)
    p.V("memset", sq_ones[:], 1.0)
    clipv = p.cvec(1e-12, 1)
    for which, dst, scol, ctab, stab in ((0, qn, qs_sb[:, 0:1], 0, 1),
                                         (1, kn, qs_sb[:, 1:2], 2, 3)):
        for h in range(8):
            hv = dst[:, h, :]
            p.wait(nc.scalar, "T")
            p.S("activation", out=scrS[:, :], in_=hv, func=AF.Square)
            p.wait(nc.tensor, "S")
            for g0, gw in ILA_GRPS:
                gsl = slice(g0, g0 + gw)
                p.mm(psR[0:1, 0:gw], sq_ones[:], scrS[:, gsl], start=True,
                     stop=True, inc=True)
                p.wait(nc.vector, "T")
                p.V("tensor_copy", ssq_row[0:1, gsl], psR[0:1, 0:gw])
                p.wait(nc.tensor, "V")
            p.wait(nc.scalar, "V")
            p.S("activation", out=r_row[:], in_=ssq_row[:], func=AF.Sqrt)
            p.wait(nc.vector, "S")
            p.V("tensor_scalar", r_row[:], r_row[:], clipv[0:1, 0:1], None, op0=ALU.max)
            p.V("reciprocal", r_row[:], r_row[:])
            p.wait(nc.tensor, "V")
            for g0, gw in ILA_GRPS:
                gsl = slice(g0, g0 + gw)
                p.mm(psR[:, 0:gw], ones_row[0:1, 0:64], r_row[0:1, gsl], start=True,
                     stop=True, inc=True)
                p.wait(nc.vector, "T")
                p.V("tensor_mul", hv[:, gsl], hv[:, gsl], psR[:, 0:gw])
                p.wait(nc.tensor, "V")
            p.V("tensor_scalar", hv[:, :], hv[:, :], scol, None, op0=ALU.mult)
            ct = tabs[:, ctab, :]
            st = tabs[:, stab, :]
            for g0, gw in ILA_GRPS:
                gsl = slice(g0, g0 + gw)
                p.wait(nc.tensor, "V")
                p.mm(psR[:, 0:gw], rt_bf[:], hv[:, gsl], start=True, stop=True,
                     inc=True)
                p.wait(nc.vector, "T")
                p.V("tensor_mul", scrS[:, gsl], psR[:, 0:gw], st[:, gsl])
                p.V("tensor_mul", hv[:, gsl], hv[:, gsl], ct[:, gsl])
                p.V("tensor_add", hv[:, gsl], hv[:, gsl], scrS[:, gsl])

    p.dma("mk", nc.sync, masks, mask_d.rearrange("i p k -> p i k"), wait_chans=("V",))
    for e in (nc.vector, nc.scalar, nc.tensor):
        p.wait(e, "mk")

    sc8 = p.cvec(8.0)
    b8 = p.sbuf("b8col", [128, 1], F32)
    mcol = p.sbuf("mcol", [128, 1], F32)
    for h in range(8):
        hb = (h % 2) * 64
        for i in range(16):
            qsl = slice(640 + i * 128, 640 + (i + 1) * 128)
            k0 = i * 128 + 128
            p.wait(nc.tensor, "V")
            p.wait(nc.tensor, "S")
            p.mm(psS[:, 0:512], qn[:, h, qsl], kn[:, h, k0:k0 + 512],
                 start=True, stop=True, inc=False)
            p.mm(psS[:, 512:640], qn[:, h, qsl], kn[:, h, k0 + 512:k0 + 640],
                 start=True, stop=True)
            p.wait(nc.vector, "T")
            p.V("tensor_add", psS[:, 0:640], psS[:, 0:640], masks[:, i, :])
            p.V("tensor_reduce", mcol[:], psS[:, 0:640], axis=AX.X, op=ALU.max,
                negate=True)
            p.wait(nc.vector, "S")
            p.V("tensor_scalar_mul", b8[:], mcol[:], 8.0)
            p.wait(nc.scalar, "V")
            p.S("activation", out=p_bf[:, :], in_=psS[:, 0:640], func=AF.Exp,
                scale=sc8[:], bias=b8[:])
            for kb in range(5):
                p.wait(nc.tensor, "S" if kb == 0 else "V")
                p.tr(psTP[:, :], p_bf[:, kb * 128:(kb + 1) * 128], ident_bf[:])
                p.wait(nc.vector, "T")
                p.V("tensor_copy", pt_bf[:, :], psTP[:, :])
                p.wait(nc.tensor, "V")
                p.mm(psA[:], vt[:, i + 1 + kb, h, :], pt_bf[:, 0:128],
                     start=(kb == 0), stop=(kb == 4))
            p.wait(nc.vector, "T")
            p.V("tensor_copy", num_sb[:], psA[0:64, :])
            p.V("reciprocal", den_r[:], psA[64:65, :])
            p.wait(nc.tensor, "V")
            p.mm(psB[:], ones_row[0:1, 0:64], den_r[:], start=True, stop=True)
            p.wait(nc.vector, "T")
            p.V("tensor_mul", o_fm[hb:hb + 64, h // 2, i * 128:(i + 1) * 128],
                num_sb[:], psB[:, :])

    for oc in range(DC):
        p.dma("w2", nc.gpsimd, wbuf[:, oc % 2, 0:4],
              wo_d.rearrange("(kc p) o -> p kc o", p=128)[:, :, oc * 128:(oc + 1) * 128],
              wait_chans=("T",))
        p.wait(nc.tensor, "w2")
        for tg in range(4):
            tsl = slice(tg * 512, (tg + 1) * 512)
            p.dma("xl", nc.sync, xrow[:],
                  x_d.rearrange("(c p) t -> p c t", p=128)[:, oc, 641 + tg * 512:
                                                           641 + (tg + 1) * 512],
                  wait_chans=("V",))
            p.wait(nc.tensor, "V")
            for kc in range(4):
                p.mm(psQ[:], wbuf[:, oc % 2, kc, :], o_fm[:, kc, tsl],
                     start=(kc == 0), stop=(kc == 3))
            p.wait(nc.vector, "T")
            p.wait(nc.vector, "xl")
            p.V("tensor_add", xrow[:, :], xrow[:, :], psQ[:, 0:512])
            p.dma("st", nc.sync, xo.rearrange("(c p) t -> p c t", p=128)[:, oc, tsl],
                  xrow[:], wait_chans=("V",))
            p.wait(nc.vector, "st")
    p.wait(nc.sync, "st")
    return p


HIDP = 2816


def build_l1_ilb():
    p = Prog()
    nc = p.nc
    x_d = p.din("x_ext", [D, TPC + 1])
    w1a_d = p.din("w1a", [D, HIDP])
    w1g_d = p.din("w1g", [D, HIDP])
    w2_d = p.din("w2p", [HIDP, D])
    b1a_d = p.din("b1a", [128, 22])
    b1g_d = p.din("b1g", [128, 22])
    b2_d = p.din("b2c", [128, DC])
    lg_d = p.din("lgc", [128, DC])
    lb_d = p.din("lbc", [128, DC])
    xo = p.dout("x_out", [D, TPC])

    x_sb = p.sbuf("x_sb", [128, DC, TPC + 1], F32)
    lnf = p.sbuf("lnf", [128, DC, TPC], BF16)
    geglu = p.sbuf("geglu", [128, 22, 512], BF16)
    gel_s = p.sbuf("gel_s", [128, 512], F32)
    a_s = p.sbuf("a_s", [128, 512], F32)
    arena = p.sbuf("arena", [128, 3 * TPC], F32)
    a_row = p.sbuf("a_row", [1, TPC], F32)
    b_row = p.sbuf("b_row", [1, TPC], F32)
    wbuf = p.sbuf("wbuf", [128, 2, DC, 128], BF16)
    w2buf = p.sbuf("w2buf", [128, 2, 22, 128], BF16)
    bcols = p.sbuf("bcols", [128, 22 * 2 + DC * 3], F32)
    scr5 = p.sbuf("scr5", [128, 512], F32)
    psP = p.psum("psP", [128, 2, 512])
    psQ = p.psum("psQ", [128, 512])
    psH = p.psum("psH", [128, 2, 512])

    b1a = bcols[:, 0:22]
    b1g = bcols[:, 22:44]
    b2c = bcols[:, 44:44 + DC]
    lgc = bcols[:, 44 + DC:44 + 2 * DC]
    lbc = bcols[:, 44 + 2 * DC:44 + 3 * DC]

    p.dma("ld", nc.sync, x_sb[:], x_d.rearrange("(c p) t -> p c t", p=128))
    p.dma("ld", nc.sync, b1a, b1a_d)
    p.dma("ld", nc.sync, b1g, b1g_d)
    p.dma("ld", nc.sync, b2c, b2_d)
    p.dma("ld", nc.sync, lgc, lg_d)
    p.dma("ld", nc.sync, lbc, lb_d)
    for e in (nc.tensor, nc.scalar, nc.vector, nc.gpsimd):
        p.wait(e, "ld")
    ident, _, ones, ones_row = make_consts(p)

    def hview(c, tsl):
        off = 1 if c < 4 else 0
        return x_sb[:, c, off + tsl.start: off + tsl.stop]

    scr = arena[:, 2 * TPC:3 * TPC]
    mu_row = arena[0:1, 0:TPC]
    ps_a, ps_b = psP[0:1, 0, :], psP[0:1, 1, :]
    p.wait(nc.tensor, "V")
    for g in range(4):
        sl = slice(g * 512, (g + 1) * 512)
        for c in range(DC):
            p.mm(ps_a, ones[:], hview(c, sl), start=(c == 0), stop=(c == DC - 1))
        for c in range(DC):
            p.wait(nc.scalar, "T")
            p.S("activation", out=scr[:, 0:512], in_=hview(c, sl), func=AF.Square)
            p.wait(nc.tensor, "S")
            p.mm(ps_b, ones[:], scr[:, 0:512], start=(c == 0), stop=(c == DC - 1),
                 inc=True)
        p.wait(nc.vector, "T")
        p.V("tensor_scalar_mul", mu_row[0:1, sl], ps_a, 1.0 / D)
        p.V("tensor_scalar_mul", b_row[:, sl], ps_b, 1.0 / D)
        if g < 3:
            p.wait(nc.tensor, "V")
    p.wait(nc.scalar, "V")
    p.S("activation", out=scr[0:1, 0:TPC], in_=mu_row[0:1, 0:TPC], func=AF.Square)
    p.wait(nc.vector, "S")
    p.V("tensor_sub", b_row[:], b_row[:], scr[0:1, 0:TPC])
    eps = p.cvec(1e-5, 1)
    p.wait(nc.scalar, "V")
    p.S("activation", out=scr[0:1, 0:TPC], in_=b_row[:], func=AF.Sqrt,
        bias=eps[0:1, 0:1])
    p.wait(nc.vector, "S")
    p.V("reciprocal", a_row[:], scr[0:1, 0:TPC])
    p.V("scalar_tensor_tensor", b_row[:], mu_row[0:1, 0:TPC], -1.0, a_row[:],
        op0=ALU.mult, op1=ALU.mult)
    a_bc = arena[:, 0 * TPC:1 * TPC]
    b_bc = arena[:, 1 * TPC:2 * TPC]
    emit_bcast_rows(p, a_row[:], a_bc, ones_row, ps=psP[:, 0, :], tag="a")
    emit_bcast_rows(p, b_row[:], b_bc, ones_row, ps=psP[:, 0, :], tag="b")
    full = slice(0, TPC)
    for c in range(DC):
        p.V("tensor_mul", scr[:, 0:TPC], hview(c, full), a_bc)
        p.V("tensor_add", scr[:, 0:TPC], scr[:, 0:TPC], b_bc)
        p.V("tensor_scalar", lnf[:, c, :], scr[:, 0:TPC], lgc[:, c:c + 1],
            lbc[:, c:c + 1], op0=ALU.mult, op1=ALU.add)

    for tg in range(4):
        tsl = slice(tg * 512, (tg + 1) * 512)
        for fc in range(22):
            para = fc % 2
            p.dma("wa", nc.gpsimd, wbuf[:, para],
                  w1a_d.rearrange("(kc p) o -> p kc o", p=128)
                  [:, :, fc * 128:(fc + 1) * 128], wait_chans=("T",))
            p.dma("wg", nc.gpsimd, w2buf[:, para, 0:DC],
                  w1g_d.rearrange("(kc p) o -> p kc o", p=128)
                  [:, :, fc * 128:(fc + 1) * 128], wait_chans=("T",))
            p.wait(nc.tensor, "wa")
            p.wait(nc.tensor, "wg")
            p.wait(nc.tensor, "S")
            p.wait(nc.tensor, "V")
            for kc in range(DC):
                p.mm(psH[:, 0, :], wbuf[:, para, kc, :], lnf[:, kc, tsl],
                     start=(kc == 0), stop=(kc == DC - 1))
            for kc in range(DC):
                p.mm(psH[:, 1, :], w2buf[:, para, kc, :], lnf[:, kc, tsl],
                     start=(kc == 0), stop=(kc == DC - 1), inc=(kc == DC - 1))
            p.wait(nc.scalar, "T")
            p.S("activation", out=gel_s[:, :], in_=psH[:, 1, :], func=AF.Gelu,
                bias=b1g[:, fc:fc + 1])
            p.wait(nc.vector, "T")
            p.wait(nc.vector, "S")
            p.V("tensor_scalar", a_s[:, :], psH[:, 0, :], b1a[:, fc:fc + 1], None,
                op0=ALU.add)
            p.V("tensor_mul", geglu[:, fc, :], a_s[:, :], gel_s[:, :])
        for oc in range(DC):
            paro = oc % 2
            p.dma("w2", nc.gpsimd, w2buf[:, paro],
                  w2_d.rearrange("(kc p) o -> p kc o", p=128)
                  [:, :, oc * 128:(oc + 1) * 128], wait_chans=("T",))
            p.wait(nc.tensor, "w2")
            p.wait(nc.tensor, "V")
            for kc in range(22):
                p.mm(psQ[:], w2buf[:, paro, kc, :], geglu[:, kc, :],
                     start=(kc == 0), stop=(kc == 21))
            p.wait(nc.vector, "T")
            p.V("tensor_scalar", scr5[:, :], psQ[:], b2c[:, oc:oc + 1], None,
                op0=ALU.add)
            p.V("tensor_add", x_sb[:, oc, 1 + tsl.start:1 + tsl.stop],
                x_sb[:, oc, 1 + tsl.start:1 + tsl.stop], scr5[:, :])
    p.dma("st", nc.sync, xo.rearrange("(c p) t -> p c t", p=128), x_sb[:, :, 1:],
          wait_chans=("V",))
    p.wait(nc.sync, "st")
    return p


def build_ln_f32():
    p = Prog()
    nc = p.nc
    x = p.din("x_fm", [D, TPC])
    lo = p.dout("ln_out", [D, TPC])
    x_sb = p.sbuf("x_sb", [128, DC, TPC], F32)
    arena = p.sbuf("arena", [128, 3 * TPC], F32)
    a_row = p.sbuf("a_row", [1, TPC], F32)
    b_row = p.sbuf("b_row", [1, TPC], F32)
    ones_row = p.sbuf("c_onesrow", [1, 128], F32)
    psP = p.psum("psP", [128, 2, 512])
    p.dma("ld", nc.sync, x_sb[:], x.rearrange("(c p) t -> p c t", p=128))
    for e in (nc.tensor, nc.scalar, nc.vector, nc.gpsimd):
        p.wait(e, "ld")
    p.V("memset", ones_row[:], 1.0)
    scr = arena[:, 2 * TPC:3 * TPC]
    emit_ln_stats(p, x_sb, TPC, a_row, b_row, scr,
                  ps_a=psP[0:1, 0, :], ps_b=psP[0:1, 1, :],
                  mu_row=arena[0:1, 0:TPC])
    a_bc = arena[:, 0 * TPC:1 * TPC]
    b_bc = arena[:, 1 * TPC:2 * TPC]
    emit_bcast_rows(p, a_row[:], a_bc, ones_row, ps=psP[:, 0, :], tag="a")
    emit_bcast_rows(p, b_row[:], b_bc, ones_row, ps=psP[:, 0, :], tag="b")
    for c in range(DC):
        p.V("tensor_mul", x_sb[:, c, :], x_sb[:, c, :], a_bc)
        p.V("tensor_add", x_sb[:, c, :], x_sb[:, c, :], b_bc)
    p.dma("st", nc.sync, lo.rearrange("(c p) t -> p c t", p=128), x_sb[:],
          wait_chans=("V",))
    p.wait(nc.sync, "st")
    return p


def build_axpy():
    p = Prog()
    nc = p.nc
    x = p.din("x_fm", [D, TPC])
    y = p.din("y32", [D, TPC])
    xo = p.dout("x_out", [D, TPC])
    x_sb = p.sbuf("x_sb", [128, DC, TPC], F32)
    y_sb = p.sbuf("y_sb", [128, DC, TPC], F32)
    p.dma("ld", nc.sync, x_sb[:], x.rearrange("(c p) t -> p c t", p=128))
    p.dma("ld", nc.sync, y_sb[:], y.rearrange("(c p) t -> p c t", p=128))
    p.wait(nc.vector, "ld")
    for c in range(DC):
        p.V("scalar_tensor_tensor", x_sb[:, c, :], y_sb[:, c, :], 0.5, x_sb[:, c, :],
            op0=ALU.mult, op1=ALU.add)
    p.dma("st", nc.sync, xo.rearrange("(c p) t -> p c t", p=128), x_sb[:],
          wait_chans=("V",))
    p.wait(nc.sync, "st")
    return p


def _xpos_tables(ext_lo, n=8192):
    DH = 64
    W2 = 256
    t = np.arange(ext_lo, ext_lo + EXTA, dtype=np.float32)
    t = np.maximum(t, 0.0)
    half = np.arange(0, DH, 2, dtype=np.float32)
    inv = 1.0 / 10000.0 ** (half / DH)
    fr = t[:, None] * inv[None, :]
    fr = np.concatenate([fr, fr], 1)
    base = (half + 0.4 * DH) / (1.4 * DH)
    power = (t - n // 2) / float(W2)
    sc = base[None, :] ** power[:, None]
    sc = np.concatenate([sc, sc], 1)
    cos, sin = np.cos(fr), np.sin(fr)
    return (cos * sc).T, (sin * sc).T, (cos / sc).T, (sin / sc).T


def run_l1_ffn(x_shards, inputs, cache, call_tag=""):
    import ml_dtypes
    bf16 = ml_dtypes.bfloat16
    if "ln_f32" not in cache:
        cache["ln_f32"] = build_ln_f32()
    pL = cache["ln_f32"]
    res = run_bass_kernel_spmd(pL.nc, [{"x_fm": x_shards[c]} for c in range(NCORE)],
                               core_ids=list(range(NCORE)))
    y_b = [np.concatenate([np.asarray(res.results[b * 4 + q]["ln_out"]) for q in range(4)],
                          axis=1) for b in range(2)]

    if "il_a" not in cache:
        cache["il_a"] = build_l1_ila()
    if "il_b" not in cache:
        cache["il_b"] = build_l1_ilb()
    pA, pB = cache["il_a"], cache["il_b"]

    for l in range(2):
        wqkv = inputs["L1_ffn_wqkv"][l]
        qs = inputs["L1_ffn_q_scale"][l].reshape(64, 1)
        ks = inputs["L1_ffn_k_scale"][l].reshape(64, 1)
        wo = inputs["L1_ffn_wo"][l]
        in_maps = []
        for c in range(NCORE):
            b, q = c // 4, c % 4
            G = q * TPC
            xe = np.zeros((D, XCA), dtype=np.float32)
            lo = G - 641
            s0 = max(0, -lo)
            xe[:, s0:] = y_b[b][:, max(lo, 0):G + TPC]
            cS, sS, cI, sI = _xpos_tables(G - 640)
            mb = np.zeros((16, 128, 640), dtype=np.float32)
            pp = np.arange(128)[:, None]
            jj = np.arange(640)[None, :]
            for i in range(16):
                valid = (jj >= pp) & (jj <= pp + 512) & (G + 128 * i - 512 + jj >= 0)
                mb[i] = np.where(valid, 0.0, -1e30)
            rotT = np.zeros((64, 64), dtype=np.float32)
            for o in range(32):
                rotT[32 + o, o] = -1.0
                rotT[o, 32 + o] = 1.0
            in_maps.append({
                "x_ext": xe, "wqkv": wqkv, "qs": qs, "ks": ks,
                "cosS": cS.astype(bf16), "sinS": sS.astype(bf16),
                "cosI": cI.astype(bf16), "sinI": sI.astype(bf16),
                "maskb": mb.astype(bf16), "wo": wo, "rotT": rotT.astype(bf16),
            })
        resA = run_bass_kernel_spmd(pA.nc, in_maps, core_ids=list(range(NCORE)))
        for b in range(2):
            y_b[b] = np.concatenate(
                [np.asarray(resA.results[b * 4 + q]["x_out"]) for q in range(4)], axis=1)
        w1 = inputs["L1_ffn_w1"][l]
        w1a = np.zeros((D, HIDP), dtype=np.float32); w1a[:, :2730] = w1[:, :2730]
        w1g = np.zeros((D, HIDP), dtype=np.float32); w1g[:, :2730] = w1[:, 2730:]
        w2p = np.zeros((HIDP, D), dtype=np.float32)
        w2p[:2730] = inputs["L1_ffn_w2"][l]
        b1 = inputs["L1_ffn_b1"][l]
        b1a = np.zeros((128, 22), dtype=np.float32)
        b1g = np.zeros((128, 22), dtype=np.float32)
        b1a.T.reshape(-1)[:2730] = b1[:2730]
        b1g.T.reshape(-1)[:2730] = b1[2730:]
        in_maps = []
        for c in range(NCORE):
            b, q = c // 4, c % 4
            G = q * TPC
            xe = np.zeros((D, TPC + 1), dtype=np.float32)
            lo = G - 1
            s0 = max(0, -lo)
            xe[:, s0:] = y_b[b][:, max(lo, 0):G + TPC]
            in_maps.append({
                "x_ext": xe, "w1a": w1a, "w1g": w1g, "w2p": w2p,
                "b1a": b1a, "b1g": b1g,
                "b2c": np.ascontiguousarray(inputs["L1_ffn_b2"][l].reshape(DC, 128).T),
                "lgc": np.ascontiguousarray(inputs["L1_ffn_ln_g"][l].reshape(DC, 128).T),
                "lbc": np.ascontiguousarray(inputs["L1_ffn_ln_b"][l].reshape(DC, 128).T),
            })
        resB = run_bass_kernel_spmd(pB.nc, in_maps, core_ids=list(range(NCORE)))
        for b in range(2):
            y_b[b] = np.concatenate(
                [np.asarray(resB.results[b * 4 + q]["x_out"]) for q in range(4)], axis=1)

    if "axpy" not in cache:
        cache["axpy"] = build_axpy()
    pX = cache["axpy"]
    in_maps = []
    for c in range(NCORE):
        b, q = c // 4, c % 4
        in_maps.append({"x_fm": x_shards[c],
                        "y32": np.ascontiguousarray(y_b[b][:, q * TPC:(q + 1) * TPC])})
    resX = run_bass_kernel_spmd(pX.nc, in_maps, core_ids=list(range(NCORE)))
    return [np.asarray(r["x_out"]) for r in resX.results]


XCL = 2080


def build_l1_attn_main(debug=False):
    p = Prog()
    nc = p.nc
    x_d = p.din("x_fm", [D, TPC])
    ln_d = p.din("ln_ext", [D, XCL])
    ln0_d = p.din("ln0", [D, 1])
    wloc_d = p.din("wlocT", [D, 3 * D])
    bloc_d = p.din("bloc", [128, 24])
    wcls_d = p.din("wclsT", [D, 3 * D])
    bcls_d = p.din("bcls", [128, 24])
    woL_d = p.din("woLT", [D, D])
    boL_d = p.din("boL", [128, DC])
    mask_d = p.din("maskb", [16, 128, 160], BF16)
    xo = p.dout("x_out", [D, TPC])
    po = p.dout("part", [16, 65])
    if debug:
        qp_o = p.dout("qp_o", [DC, 128, TPC], BF16)
        kp_o = p.dout("kp_o", [DC, 128, XCL], BF16)
        vt_o = p.dout("vt_o", [17, 128, 16, 65], BF16)
        o_o = p.dout("o_o", [DC, 128, TPC], BF16)

    lnx = p.sbuf("lnx", [128, DC, XCL], BF16)
    kp = p.sbuf("kp", [128, DC, XCL], BF16)
    vt = p.sbuf("vt", [128, 17, 16, 65], BF16)
    qp = p.sbuf("qp", [128, DC, TPC], BF16)
    o_fm = p.sbuf("o_fm", [128, DC, TPC], BF16)
    masks = p.sbuf("masks", [128, 16, 160], BF16)
    wbuf = p.sbuf("wbuf", [128, 2, DC, 128], BF16)
    bcols = p.sbuf("bcols", [128, 24 * 2 + DC], F32)
    vrow_bf = p.sbuf("vrow_bf", [128, 512], BF16)
    p_bf = p.sbuf("p_bf", [128, 160], BF16)
    pt_bf = p.sbuf("pt_bf", [128, 128], BF16)
    num_sb = p.sbuf("num_sb", [64, 128], F32)
    den_r = p.sbuf("den_r", [1, 128], F32)
    mcol = p.sbuf("mcol", [128, 1], F32)
    b8 = p.sbuf("b8c", [128, 1], F32)
    xrow = p.sbuf("xrow", [128, 512], F32)
    qcls_row = p.sbuf("qcls_row", [1, D], F32)
    qcol = p.sbuf("qcol", [128, DC], BF16)
    se = p.sbuf("se", [128, NTILE, 16], BF16)
    part_sb = p.sbuf("part_sb", [65, 16], F32)
    ident_bf = p.sbuf("ident_bf3", [128, 128], BF16)
    ln0_sb = p.sbuf("ln0_sb", [128, DC], BF16)
    psS = p.psum("psS", [128, 512])
    psA = p.psum("psA", [65, 128])
    psB = p.psum("psB", [64, 128])
    psTP = p.psum("psTP", [128, 128], BF16)
    psQ = p.psum("psQ", [128, 512])
    psC = p.psum("psC", [128, 16])

    p.dma("ld", nc.gpsimd, lnx[:], ln_d.rearrange("(c p) t -> p c t", p=128))
    p.dma("ld", nc.gpsimd, ln0_sb[:], ln0_d.rearrange("(c p) o -> p c o", p=128), allow_slow_non_contiguous=True)
    p.dma("ld", nc.sync, bcols[:, 0:24], bloc_d)
    p.dma("ld", nc.sync, bcols[:, 24:48], bcls_d)
    p.dma("ld", nc.sync, bcols[:, 48:48 + DC], boL_d)
    p.dma("ld", nc.sync, masks[:], mask_d.rearrange("i p k -> p i k"))
    for e in (nc.tensor, nc.scalar, nc.vector, nc.gpsimd):
        p.wait(e, "ld")
    ident, _, ones, ones_row = make_consts(p)
    p.V("tensor_copy", ident_bf[:], ident[:])

    grps_ext = [(0, 512), (512, 512), (1024, 512), (1536, 512), (2048, 32)]
    grps_own = [(g * 512, 512) for g in range(4)]

    def proj_fm(w_d, col0, dst, grps, bofs, name, aoff=0):
        for oc in range(DC):
            p.dma("w" + name, nc.gpsimd, wbuf[:, oc % 2],
                  w_d.rearrange("(kc p) o -> p kc o", p=128)
                  [:, :, col0 + oc * 128:col0 + (oc + 1) * 128], wait_chans=("T",))
            p.wait(nc.tensor, "w" + name)
            for g0, gw in grps:
                p.wait(nc.tensor, "S")
                for kc in range(DC):
                    p.mm(psQ[:, 0:gw], wbuf[:, oc % 2, kc, :],
                         lnx[:, kc, aoff + g0:aoff + g0 + gw],
                         start=(kc == 0), stop=(kc == DC - 1))
                p.wait(nc.scalar, "T")
                p.S("activation", out=dst[:, oc, g0:g0 + gw], in_=psQ[:, 0:gw],
                    func=AF.Identity, bias=bcols[:, bofs + oc:bofs + oc + 1])

    def v_tiles(w_d, grps, bofs, name, ntl, aoff=0):
        for oc in range(DC):
            p.dma("wv" + name, nc.gpsimd, wbuf[:, oc % 2],
                  w_d.rearrange("(kc p) o -> p kc o", p=128)
                  [:, :, 2 * D + oc * 128:2 * D + (oc + 1) * 128], wait_chans=("T",))
            p.wait(nc.tensor, "wv" + name)
            for g0, gw in grps:
                p.wait(nc.tensor, "S")
                p.wait(nc.tensor, "V")
                for kc in range(DC):
                    p.mm(psQ[:, 0:gw], wbuf[:, oc % 2, kc, :],
                         lnx[:, kc, aoff + g0:aoff + g0 + gw],
                         start=(kc == 0), stop=(kc == DC - 1))
                p.wait(nc.scalar, "T")
                p.S("activation", out=vrow_bf[:, 0:gw], in_=psQ[:, 0:gw],
                    func=AF.Identity, bias=bcols[:, bofs + 16 + oc:bofs + 17 + oc])
                nt = (gw + 127) // 128
                for t2 in range(nt):
                    it = (g0 + t2 * 128) // 128
                    tw = min(128, gw - t2 * 128)
                    p.wait(nc.tensor, "S")
                    p.wait(nc.tensor, "V")
                    p.tr(psTP[0:tw, :], vrow_bf[:, t2 * 128:t2 * 128 + tw],
                         ident_bf[:])
                    p.wait(nc.vector, "T")
                    p.V("tensor_copy", vt[0:tw, it, 2 * oc, 0:64], psTP[0:tw, 0:64])
                    p.V("tensor_copy", vt[0:tw, it, 2 * oc + 1, 0:64],
                        psTP[0:tw, 64:128])
        p.V("memset", vt[:, :, :, 64:65], 1.0)

    proj_fm(wloc_d, D, kp, grps_ext, 8, "kl")
    v_tiles(wloc_d, grps_ext, 0, "l", 17)
    proj_fm(wloc_d, 0, qp, grps_own, 0, "ql", aoff=16)

    sc18 = p.cvec(0.125)
    for h in range(16):
        hb = (h % 2) * 64
        for i in range(16):
            p.wait(nc.tensor, "V")
            p.wait(nc.tensor, "S")
            k0 = i * 128
            p.mm(psS[:, 0:160], qp[hb:hb + 64, h // 2, i * 128:(i + 1) * 128],
                 kp[hb:hb + 64, h // 2, k0:k0 + 160], start=True, stop=True)
            p.wait(nc.vector, "T")
            p.V("tensor_add", psS[:, 0:160], psS[:, 0:160], masks[:, i, :])
            p.V("tensor_reduce", mcol[:], psS[:, 0:160], axis=AX.X, op=ALU.max,
                negate=True)
            p.wait(nc.vector, "S")
            p.V("tensor_scalar_mul", b8[:], mcol[:], 0.125)
            p.wait(nc.scalar, "V")
            p.S("activation", out=p_bf[:, :], in_=psS[:, 0:160], func=AF.Exp,
                scale=sc18[:], bias=b8[:])
            for kb in range(2):
                p.wait(nc.tensor, "S" if kb == 0 else "V")
                kw = 128 if kb == 0 else 32
                p.tr(psTP[0:kw, :], p_bf[:, kb * 128:kb * 128 + kw], ident_bf[:])
                p.wait(nc.vector, "T")
                p.V("tensor_copy", pt_bf[0:kw, :], psTP[0:kw, 0:128])
                p.wait(nc.tensor, "V")
                p.mm(psA[:], vt[0:kw, i + kb, h, :], pt_bf[0:kw, 0:128],
                     start=(kb == 0), stop=(kb == 1))
            p.wait(nc.vector, "T")
            p.V("tensor_copy", num_sb[:], psA[0:64, :])
            p.V("reciprocal", den_r[:], psA[64:65, :])
            p.wait(nc.tensor, "V")
            p.mm(psB[:], ones_row[0:1, 0:64], den_r[:], start=True, stop=True)
            p.wait(nc.vector, "T")
            p.V("tensor_mul", o_fm[hb:hb + 64, h // 2, i * 128:(i + 1) * 128],
                num_sb[:], psB[:, :])

    if debug:
        p.dma("dbg", nc.sync, qp_o.rearrange("c p t -> p c t"), qp[:],
              wait_chans=("V", "S"))
        p.dma("dbg", nc.sync, kp_o.rearrange("c p t -> p c t"), kp[:])
        p.dma("dbg", nc.sync, vt_o.rearrange("i p h m -> p i h m"), vt[:])
        p.dma("dbg", nc.sync, o_o.rearrange("c p t -> p c t"), o_fm[:])
        p.wait(nc.sync, "dbg")
    for oc in range(DC):
        p.dma("wo2", nc.gpsimd, wbuf[:, oc % 2],
              woL_d.rearrange("(kc p) o -> p kc o", p=128)
              [:, :, oc * 128:(oc + 1) * 128], wait_chans=("T",))
        p.wait(nc.tensor, "wo2")
        for tg in range(4):
            tsl = slice(tg * 512, (tg + 1) * 512)
            p.dma("xl", nc.sync, xrow[:],
                  x_d.rearrange("(c p) t -> p c t", p=128)[:, oc, tsl],
                  wait_chans=("V",))
            p.wait(nc.tensor, "V")
            for kc in range(DC):
                p.mm(psQ[:], wbuf[:, oc % 2, kc, :], o_fm[:, kc, tsl],
                     start=(kc == 0), stop=(kc == DC - 1))
            p.wait(nc.vector, "T")
            p.wait(nc.vector, "xl")
            p.V("tensor_scalar", scr5f(p)[:, :], psQ[:], bcols[:, 48 + oc:49 + oc],
                None, op0=ALU.add)
            p.V("tensor_add", xrow[:, :], xrow[:, :], scr5f(p)[:, :])
            p.dma("st", nc.sync, xo.rearrange("(c p) t -> p c t", p=128)[:, oc, tsl],
                  xrow[:], wait_chans=("V",))
            p.wait(nc.vector, "st")

    wqc_buf = p.sbuf("wqc_buf", [128, DC, 512], BF16)
    p.wait(nc.tensor, "st")
    for g in range(2):
        gsl = slice(g * 512, (g + 1) * 512)
        p.dma("wc", nc.gpsimd, wqc_buf[:],
              wcls_d.rearrange("(kc p) o -> p kc o", p=128)[:, :, gsl],
              wait_chans=("T",))
        p.wait(nc.tensor, "wc")
        p.wait(nc.tensor, "V")
        for kc in range(DC):
            p.mm(psQ[0:1, :], ln0_sb[:, kc:kc + 1], wqc_buf[:, kc, :],
                 start=(kc == 0), stop=(kc == DC - 1))
        p.wait(nc.vector, "T")
        p.V("tensor_copy", qcls_row[0:1, gsl], psQ[0:1, 0:512])
    p.wait(nc.tensor, "V")
    for cch in range(DC):
        p.tr(psQ[0:1, 0:128], bcols[:, 24 + cch:25 + cch], ident[:])
        p.wait(nc.vector, "T")
        p.V("tensor_add", qcls_row[0:1, cch * 128:(cch + 1) * 128],
            qcls_row[0:1, cch * 128:(cch + 1) * 128], psQ[0:1, 0:128])
        p.wait(nc.tensor, "V")
    for cch in range(DC):
        p.wait(nc.tensor, "V")
        p.tr(psQ[:, 0:1], qcls_row[0:1, cch * 128:(cch + 1) * 128], ident[0:1, 0:1])
        p.wait(nc.vector, "T")
        p.V("tensor_copy", qcol[:, cch:cch + 1], psQ[:, 0:1])
    proj_fm(wcls_d, D, qp, grps_own, 32, "kc", aoff=16)
    v_tiles(wcls_d, grps_own, 24, "c", 16, aoff=16)
    for h in range(16):
        hb = (h % 2) * 64
        for it in range(NTILE):
            p.wait(nc.tensor, "S")
            p.mm(psS[:, 0:1],
                 qp[hb:hb + 64, h // 2, it * 128:(it + 1) * 128],
                 qcol[hb:hb + 64, h // 2:h // 2 + 1], start=True, stop=True)
            p.wait(nc.scalar, "T")
            p.S("activation", out=se[:, it, h:h + 1], in_=psS[:, 0:1], func=AF.Exp,
                scale=sc18[:])
    p.wait(nc.tensor, "S")
    for h in range(16):
        for it in range(NTILE):
            p.mm(psA[:, 0:1], vt[:, it, h, :], se[:, it, h:h + 1],
                 start=(it == 0), stop=(it == NTILE - 1))
        p.wait(nc.vector, "T")
        p.V("tensor_copy", part_sb[:, h:h + 1], psA[:, 0:1])
        p.wait(nc.tensor, "V")
    p.dma("po", nc.sync, po.rearrange("h m -> m h"), part_sb[:], wait_chans=("V",), allow_slow_non_contiguous=True)
    p.wait(nc.sync, "po")
    return p


def scr5f(p):
    if not hasattr(p, "_scr5f"):
        p._scr5f = p.sbuf("scr5f", [128, 512], F32)
    return p._scr5f


def build_l1_attn_combine():
    p = Prog()
    nc = p.nc
    part_d = p.din("parts", [4, 16, 65])
    wo_d = p.din("woC", [D, D])
    bo_d = p.din("boC", [128, DC])
    x0_d = p.din("x0", [D, 1])
    xo = p.dout("x0_new", [D, 1])

    parts = p.sbuf("parts_sb", [65, 4, 16], F32)
    ssum = p.sbuf("ssum", [65, 16], F32)
    dr = p.sbuf("dr", [1, 16], F32)
    ctx_mh = p.sbuf("ctx_mh", [64, 16], F32)
    ctx_col = p.sbuf("ctx_col", [128, DC], BF16)
    w_sb = p.sbuf("w_sb", [128, DC, D], BF16)
    bo_sb = p.sbuf("bo_sb", [128, DC], F32)
    x0_sb = p.sbuf("x0_sb", [128, DC], F32)
    y0 = p.sbuf("y0", [128, DC], F32)
    ones_row = p.sbuf("c_onesrow2", [1, 128], F32)
    psB = p.psum("psB", [64, 16])
    psQ = p.psum("psQ", [128, 1])

    p.dma("ld", nc.sync, parts[:], part_d.rearrange("s h m -> m s h"),
          allow_slow_non_contiguous=True)
    p.dma("ld", nc.gpsimd, w_sb[:], wo_d.rearrange("(kc p) o -> p kc o", p=128))
    p.dma("ld", nc.sync, bo_sb[:], bo_d)
    p.dma("ld", nc.sync, x0_sb[:], x0_d.rearrange("(c p) o -> p c o", p=128), allow_slow_non_contiguous=True)
    for e in (nc.tensor, nc.scalar, nc.vector, nc.gpsimd):
        p.wait(e, "ld")
    p.V("memset", ones_row[:], 1.0)
    p.V("tensor_add", ssum[:], parts[:, 0, :], parts[:, 1, :])
    p.V("tensor_add", ssum[:], ssum[:], parts[:, 2, :])
    p.V("tensor_add", ssum[:], ssum[:], parts[:, 3, :])
    p.V("reciprocal", dr[:], ssum[64:65, :])
    p.wait(nc.tensor, "V")
    p.mm(psB[:], ones_row[0:1, 0:64], dr[:], start=True, stop=True)
    p.wait(nc.vector, "T")
    p.V("tensor_mul", ctx_mh[:], ssum[0:64, :], psB[:, :])
    odd_tmp = p.sbuf("odd_tmp", [64, DC], F32)
    for c in range(DC):
        p.V("tensor_copy", ctx_col[0:64, c:c + 1], ctx_mh[:, 2 * c:2 * c + 1])
        p.V("tensor_copy", odd_tmp[:, c:c + 1], ctx_mh[:, 2 * c + 1:2 * c + 2])
    p.dma("mv", nc.gpsimd, ctx_col[64:128, :], odd_tmp[:], wait_chans=("V",))
    for e in (nc.tensor, nc.vector):
        p.wait(e, "mv")
    for oc in range(DC):
        for kc in range(DC):
            p.mm(psQ[:], w_sb[:, kc, oc * 128:(oc + 1) * 128], ctx_col[:, kc:kc + 1],
                 start=(kc == 0), stop=(kc == DC - 1))
        p.wait(nc.vector, "T")
        p.V("tensor_add", y0[:, oc:oc + 1], psQ[:, 0:1], bo_sb[:, oc:oc + 1])
        p.V("tensor_add", y0[:, oc:oc + 1], y0[:, oc:oc + 1], x0_sb[:, oc:oc + 1])
        p.wait(nc.tensor, "V")
    p.dma("st", nc.sync, xo.rearrange("(c p) o -> p c o", p=128), y0[:],
          wait_chans=("V",), allow_slow_non_contiguous=True)
    p.wait(nc.sync, "st")
    return p


def run_l1_attn(x_shards, inputs, cache):
    import ml_dtypes
    bf16 = ml_dtypes.bfloat16
    if "ln_f32" not in cache:
        cache["ln_f32"] = build_ln_f32()
    pL = cache["ln_f32"]
    res = run_bass_kernel_spmd(pL.nc, [{"x_fm": x_shards[c]} for c in range(NCORE)],
                               core_ids=list(range(NCORE)))
    ln_b = [np.concatenate([np.asarray(res.results[b * 4 + q]["ln_out"]) for q in range(4)],
                           axis=1) for b in range(2)]

    if "l1am" not in cache:
        cache["l1am"] = build_l1_attn_main()
    pM = cache["l1am"]
    wlocT = np.ascontiguousarray(inputs["L1_attn_in_w_loc"].T)
    wclsT = np.ascontiguousarray(inputs["L1_attn_in_w_cls"].T)
    bloc = np.ascontiguousarray(inputs["L1_attn_in_b_loc"].reshape(24, 128).T)
    bcls = np.ascontiguousarray(inputs["L1_attn_in_b_cls"].reshape(24, 128).T)
    woLT = np.ascontiguousarray(inputs["L1_attn_out_w_loc"].T)
    boL = np.ascontiguousarray(inputs["L1_attn_out_b_loc"].reshape(DC, 128).T)
    in_maps = []
    for c in range(NCORE):
        b, q = c // 4, c % 4
        G = q * TPC
        xe = np.zeros((D, XCL), dtype=np.float32)
        lo, hi = G - 16, G + 2064
        s0 = max(0, -lo)
        e0 = XCL - max(0, hi - 8192)
        xe[:, s0:e0] = ln_b[b][:, max(lo, 0):min(hi, 8192)]
        mb = np.zeros((16, 128, 160), dtype=np.float32)
        pp = np.arange(128)[:, None]; jj = np.arange(160)[None, :]
        for i in range(16):
            kglob = G + 128 * i - 16 + jj
            qglob = G + 128 * i + pp
            valid = (kglob >= qglob - 16) & (kglob <= qglob + 16) & \
                    (kglob >= 1) & (kglob < 8192)
            mb[i] = np.where(valid, 0.0, -1e30)
        in_maps.append({
            "x_fm": x_shards[c], "ln_ext": xe,
            "ln0": np.ascontiguousarray(ln_b[b][:, 0:1]),
            "wlocT": wlocT, "bloc": bloc, "wclsT": wclsT, "bcls": bcls,
            "woLT": woLT, "boL": boL, "maskb": mb.astype(bf16),
        })
    resM = run_bass_kernel_spmd(pM.nc, in_maps, core_ids=list(range(NCORE)))

    if "l1ac" not in cache:
        cache["l1ac"] = build_l1_attn_combine()
    pC = cache["l1ac"]
    woC = np.ascontiguousarray(inputs["L1_attn_out_w_cls"].T)
    boC = np.ascontiguousarray(inputs["L1_attn_out_b_cls"].reshape(DC, 128).T)
    in_maps = []
    for c in range(NCORE):
        b = c // 4
        parts = np.stack([np.asarray(resM.results[b * 4 + q]["part"]) for q in range(4)])
        in_maps.append({
            "parts": parts, "woC": woC, "boC": boC,
            "x0": np.ascontiguousarray(x_shards[b * 4][:, 0:1]),
        })
    resC = run_bass_kernel_spmd(pC.nc, in_maps, core_ids=list(range(NCORE)))

    new_shards = [np.asarray(r["x_out"]).copy() for r in resM.results]
    for b in range(2):
        new_shards[b * 4][:, 0:1] = np.asarray(resC.results[b * 4]["x0_new"])
    return new_shards


# version 02
# speedup vs baseline: 1.1701x; 1.1701x over previous
import contextlib
import math
import numpy as np
import concourse.bass as bass
from concourse.bass import IndirectOffsetOnAxis
import concourse.mybir as mybir
from concourse.bass_utils import run_bass_kernel_spmd

F32 = mybir.dt.float32
BF16 = mybir.dt.bfloat16
AF = mybir.ActivationFunctionType
ALU = mybir.AluOpType
AX = mybir.AxisListType

NCORE = 8
TPC = 2048
NTILE = TPC // 128
D = 1024
DC = D // 128
T_ALL = 16384


class Prog:

    def __init__(self):
        self.nc = bass.Bass("TRN2", target_bir_lowering=False)
        self.ctx = contextlib.ExitStack()
        self._sems = {}
        self._cnt = {}
        self._cvecs = {}

    def sem(self, name):
        if name not in self._sems:
            self._sems[name] = self.ctx.enter_context(self.nc.semaphore(name))
            self._cnt[name] = 0
        return self._sems[name]

    def inc(self, name, inst, n=1):
        inst.then_inc(self.sem(name), n)
        self._cnt[name] += n
        return inst

    def wait(self, eng, name):
        eng.wait_ge(self.sem(name), self._cnt[name])

    def _op(self, eng, chan, fname, *args, **kw):
        self.wait(eng, chan)
        inst = getattr(eng, fname)(*args, **kw)
        self.inc(chan, inst, 1)
        return inst

    def V(self, fname, *args, **kw):
        return self._op(self.nc.vector, "V", fname, *args, **kw)

    def S(self, fname, *args, **kw):
        return self._op(self.nc.scalar, "S", fname, *args, **kw)

    def G(self, fname, *args, **kw):
        return self._op(self.nc.gpsimd, "G", fname, *args, **kw)

    def mm(self, out, lhsT, rhs, start, stop, inc=None, **kw):
        i = self.nc.tensor.matmul(out, lhsT, rhs, start=start, stop=stop, **kw)
        if inc or (inc is None and stop):
            self.inc("T", i, 1)
        return i

    def tr(self, out, in_, ident):
        i = self.nc.tensor.transpose(out, in_, ident)
        self.inc("T", i, 1)
        return i

    def dma(self, name, eng, out, in_, wait_chans=(), **kw):
        for ch in wait_chans:
            self.wait(eng, ch)
        i = eng.dma_start(out=out, in_=in_, **kw)
        self.inc(name, i, 16)
        return i

    def sbuf(self, name, shape, dtype):
        return self.ctx.enter_context(self.nc.sbuf_tensor(name, shape, dtype))

    def psum(self, name, shape, dtype=F32):
        return self.ctx.enter_context(self.nc.psum_tensor(name, shape, dtype))

    def din(self, name, shape, dtype=F32):
        return self.nc.dram_tensor(name, shape, dtype, kind="ExternalInput").ap()

    def dout(self, name, shape, dtype=F32):
        return self.nc.dram_tensor(name, shape, dtype, kind="ExternalOutput").ap()

    def close(self):
        self.ctx.close()

    def cvec(self, val, parts=128):
        key = (val, parts)
        if key not in self._cvecs:
            t = self.sbuf(f"cv_{len(self._cvecs)}", [parts, 1], F32)
            self.V("memset", t[:], val)
            self._cvecs[key] = t
        return self._cvecs[key]


def make_consts(p):
    ident = p.sbuf("c_ident", [128, 128], F32)
    iota_p = p.sbuf("c_iotap", [128, 1], F32)
    iota_f = p.sbuf("c_iotaf", [128, 128], F32)
    ones = p.sbuf("c_ones", [128, 1], F32)
    ones_row = p.sbuf("c_onesrow", [1, 128], F32)
    p.G("iota", iota_p[:], pattern=[[0, 1]], base=0, channel_multiplier=1,
        allow_small_or_imprecise_dtypes=True)
    p.G("iota", iota_f[:], pattern=[[1, 128]], base=0, channel_multiplier=0,
        allow_small_or_imprecise_dtypes=True)
    p.V("memset", ones[:], 1.0)
    p.V("memset", ones_row[:], 1.0)
    p.wait(p.nc.vector, "G")
    p.V("tensor_scalar", ident[:], iota_f[:], iota_p[:], None, op0=ALU.is_equal)
    return ident, iota_p, ones, ones_row


def emit_ln_stats(p, x_sb, ntok, a_row, b_row, scr, tag="", ps_a=None, ps_b=None, mu_row=None):
    nc = p.nc
    ones = p.cvec(1.0)
    if ps_a is None:
        ps_a = p.psum("lnA" + tag, [1, 512])[:]
        ps_b = p.psum("lnB" + tag, [1, 512])[:]
    if mu_row is None:
        mu_row = p.sbuf("lnmu" + tag, [1, ntok], F32)[:]
    nG = ntok // 512
    p.wait(nc.tensor, "V")
    for g in range(nG):
        sl = slice(g * 512, (g + 1) * 512)
        for c in range(DC):
            p.mm(ps_a, ones[:], x_sb[:, c, sl], start=(c == 0), stop=(c == DC - 1))
        for c in range(DC):
            p.wait(nc.scalar, "T")
            p.S("activation", out=scr[:, 0:512], in_=x_sb[:, c, sl], func=AF.Square)
            p.wait(nc.tensor, "S")
            p.mm(ps_b, ones[:], scr[:, 0:512], start=(c == 0), stop=(c == DC - 1),
                 inc=True)
        p.wait(nc.vector, "T")
        p.V("tensor_scalar_mul", mu_row[0:1, sl], ps_a, 1.0 / D)
        p.V("tensor_scalar_mul", b_row[:, sl], ps_b, 1.0 / D)
        if g < nG - 1:
            p.wait(nc.tensor, "V")
    p.wait(nc.scalar, "V")
    p.S("activation", out=scr[0:1, 0:ntok], in_=mu_row[0:1, 0:ntok], func=AF.Square)
    p.wait(nc.vector, "S")
    p.V("tensor_sub", b_row[:], b_row[:], scr[0:1, 0:ntok])
    eps = p.cvec(1e-5, 1)
    p.wait(nc.scalar, "V")
    p.S("activation", out=scr[0:1, 0:ntok], in_=b_row[:], func=AF.Sqrt, bias=eps[0:1, 0:1])
    p.wait(nc.vector, "S")
    p.V("reciprocal", a_row[:], scr[0:1, 0:ntok])
    p.V("scalar_tensor_tensor", b_row[:], mu_row[0:1, 0:ntok], -1.0, a_row[:],
        op0=ALU.mult, op1=ALU.mult)


def emit_row_to_col(p, row, col, ident, ntile, tag="", ps=None):
    nc = p.nc
    if ps is None:
        ps = p.psum("r2c" + tag, [128, ntile])[:]
    p.wait(nc.tensor, "V")
    for i in range(ntile):
        p.tr(ps[:, i:i + 1], row[:, i * 128:(i + 1) * 128], ident[0:1, 0:1])
    p.wait(nc.vector, "T")
    p.V("tensor_copy", col[:, :], ps[:, 0:ntile])


def emit_bcast_rows(p, rows, out_sb, ones_row, nrow_parts=128, tag="", ps=None):
    nc = p.nc
    n = rows.shape[-1] if hasattr(rows, "shape") else None
    n = out_sb.shape[-1]
    if ps is None:
        ps = p.psum("bc" + tag, [nrow_parts, 512])[:]
    p.wait(nc.tensor, "V")
    for g in range((n + 511) // 512):
        sl = slice(g * 512, min((g + 1) * 512, n))
        w = sl.stop - sl.start
        p.mm(ps[:, 0:w], ones_row[0:1, 0:nrow_parts], rows[:, sl], start=True, stop=True)
        p.wait(nc.vector, "T")
        p.V("tensor_copy", out_sb[:, sl], ps[:, 0:w])
        if g * 512 + 512 < n:
            p.wait(nc.tensor, "V")
    return out_sb



def build_l0_route():
    p = Prog()
    nc = p.nc
    x = p.din("x_fm", [D, TPC])
    rt = p.din("r_T", [D, 8])
    ep = p.dout("ep", [2, 128, NTILE])

    x_sb = p.sbuf("x_sb", [128, DC, TPC], F32)
    r_sb = p.sbuf("r_sb", [128, DC, 8], F32)
    scr = p.sbuf("scr", [128, TPC], F32)
    a_row = p.sbuf("a_row", [1, TPC], F32)
    b_row = p.sbuf("b_row", [1, TPC], F32)
    a_col = p.sbuf("a_col", [128, NTILE], F32)
    b_col = p.sbuf("b_col", [128, NTILE], F32)
    lg = p.sbuf("lg", [128, NTILE, 8], F32)
    csr = p.sbuf("csr", [128, 8], F32)
    iota8 = p.sbuf("iota8", [128, 8], F32)
    mx = p.sbuf("mx", [128, NTILE], F32)
    e_col = p.sbuf("e_col", [128, NTILE], F32)
    s_col = p.sbuf("s_col", [128, NTILE], F32)
    exps = p.sbuf("exps", [128, 8], F32)
    eq = p.sbuf("eq", [128, 8], F32)
    csr_row = p.sbuf("csr_row", [1, 8], F32)

    p.dma("ld", nc.sync, x_sb[:], x.rearrange("(c p) t -> p c t", p=128))
    p.dma("ld", nc.sync, r_sb[:], rt.rearrange("(c p) e -> p c e", p=128))
    for e in (nc.tensor, nc.scalar, nc.vector, nc.gpsimd):
        p.wait(e, "ld")

    ident, iota_p, ones, ones_row = make_consts(p)
    p.G("iota", iota8[:], pattern=[[1, 8]], base=99, channel_multiplier=0,
        allow_small_or_imprecise_dtypes=True)

    emit_ln_stats(p, x_sb, TPC, a_row, b_row, scr)

    ps_lg = p.psum("ps_lg", [128, NTILE, 8])
    for it in range(NTILE):
        for c in range(DC):
            p.mm(ps_lg[:, it, :], x_sb[:, c, it * 128:(it + 1) * 128], r_sb[:, c, :],
                 start=(c == 0), stop=(c == DC - 1))
    ps_c = p.psum("ps_c", [1, 8])
    for c in range(DC):
        p.mm(ps_c[:], ones[:], r_sb[:, c, :], start=(c == 0), stop=(c == DC - 1))
    p.wait(nc.vector, "T")
    p.V("tensor_copy", csr_row[:], ps_c[:])
    ps_b8 = p.psum("ps_b8", [128, 8])
    p.wait(nc.tensor, "V")
    p.mm(ps_b8[:], ones_row[:], csr_row[:], start=True, stop=True)
    p.wait(nc.vector, "T")
    p.V("tensor_copy", csr[:], ps_b8[:])

    emit_row_to_col(p, a_row, a_col, ident, NTILE, tag="a")
    emit_row_to_col(p, b_row, b_col, ident, NTILE, tag="b")

    p.wait(nc.vector, "G")
    for it in range(NTILE):
        p.V("tensor_scalar_mul", lg[:, it, :], ps_lg[:, it, :], a_col[:, it:it + 1])
        p.V("scalar_tensor_tensor", lg[:, it, :], csr[:], b_col[:, it:it + 1],
            lg[:, it, :], op0=ALU.mult, op1=ALU.add)
        p.V("tensor_reduce", mx[:, it:it + 1], lg[:, it, :], axis=AX.X, op=ALU.max,
            negate=True)
        p.wait(nc.scalar, "V")
        p.S("activation", out=exps[:], in_=lg[:, it, :], func=AF.Exp,
            bias=mx[:, it:it + 1], accum_out=s_col[:, it:it + 1])
        p.V("tensor_scalar", eq[:], lg[:, it, :], mx[:, it:it + 1], None, op0=ALU.add)
        p.V("tensor_scalar", eq[:], eq[:], 0.0, None, op0=ALU.is_equal)
        p.V("scalar_tensor_tensor", eq[:], eq[:], -99.0, iota8[:],
            op0=ALU.mult, op1=ALU.add)
        p.V("tensor_reduce", e_col[:, it:it + 1], eq[:], axis=AX.X, op=ALU.min)

    p.dma("st", nc.gpsimd, ep[0], e_col[:], wait_chans=("V",))
    p.dma("st", nc.gpsimd, ep[1], s_col[:], wait_chans=("S",))
    p.wait(nc.gpsimd, "st")
    return p



def _shard_x(x):
    out = []
    for c in range(NCORE):
        b, q = c // 4, c % 4
        out.append(np.ascontiguousarray(x[b, q * TPC:(q + 1) * TPC, :].T))
    return out


def kernel(**inputs):
    shards = _shard_x(inputs["L0_attn_x"])
    cache = {}
    shards = run_l0_ffn(shards, inputs, cache)
    shards = run_l0_attn(shards, inputs, cache)
    shards = run_l0_ffn(shards, inputs, cache)
    shards = run_l1_ffn(shards, inputs, cache)
    shards = run_l1_attn(shards, inputs, cache)
    shards = run_l1_ffn(shards, inputs, cache)
    shards = run_l2_ffn(shards, inputs, cache)
    shards = run_l2_attn(shards, inputs, cache)
    shards = run_l2_ffn(shards, inputs, cache)
    shards = run_l3_ffn(shards, inputs, cache)
    shards = run_l3_attn(shards, inputs, cache)
    shards = run_l3_ffn(shards, inputs, cache)
    out = np.empty((2, 8192, 1024), dtype=np.float32)
    for c in range(NCORE):
        b, q = c // 4, c % 4
        out[b, q * TPC:(q + 1) * TPC, :] = shards[c].T
    return out


CAPACITY = 2560


def emit_moe_rank(p, e_all_d, s_all_d, e_col, s_col, idx_col, fe_col, arena, ones_row, ps_bc=None):
    nc = p.nc
    A = arena
    e_bc = A[:, 0 * TPC:1 * TPC]
    s_bc = A[:, 1 * TPC:2 * TPC]
    scrap = A[:, 2 * TPC:3 * TPC]
    beq = A[:, 3 * TPC:4 * TPC]
    jiota = A[:, 4 * TPC:5 * TPC]
    rank = p.sbuf("rank_col", [128, NTILE], F32)
    r1 = p.sbuf("r1_col", [128, 1], F32)
    r2 = p.sbuf("r2_col", [128, 1], F32)
    ish = p.sbuf("ish_col", [128, 1], F32)

    p.G("iota", jiota[:, :], pattern=[[1, TPC]], base=0, channel_multiplier=0,
        allow_small_or_imprecise_dtypes=True)
    p.V("memset", rank[:], 0.0)
    for jc in range(NCORE):
        sl = slice(jc * TPC, (jc + 1) * TPC)
        p.dma("rk", nc.sync, e_bc, e_all_d[0:1, sl].partition_broadcast(128),
              wait_chans=("V",))
        p.dma("rk", nc.sync, s_bc, s_all_d[0:1, sl].partition_broadcast(128))
        p.wait(nc.vector, "rk")
        for it in range(NTILE):
            p.V("tensor_scalar", beq[:, :], e_bc, e_col[:, it:it + 1], None,
                op0=ALU.is_equal)
            p.V("scalar_tensor_tensor", scrap[:, :], s_bc, s_col[:, it:it + 1],
                beq[:, :], op0=ALU.is_lt, op1=ALU.mult, accum_out=r1[:])
            p.V("scalar_tensor_tensor", beq[:, :], s_bc, s_col[:, it:it + 1],
                beq[:, :], op0=ALU.is_equal, op1=ALU.mult)
            p.V("tensor_scalar", ish[:], idx_col[:, it:it + 1], float(jc * TPC), None,
                op0=ALU.subtract)
            p.V("scalar_tensor_tensor", scrap[:, :], jiota, ish[:], beq[:, :],
                op0=ALU.is_lt, op1=ALU.mult, accum_out=r2[:])
            p.V("tensor_add", r1[:], r1[:], r2[:])
            p.V("tensor_add", rank[:, it:it + 1], rank[:, it:it + 1], r1[:])
    p.V("tensor_scalar", fe_col[:], rank[:], float(CAPACITY), None, op0=ALU.is_lt)
    p.V("tensor_mul", fe_col[:], fe_col[:], e_col[:])
    return fe_col



def build_l0_moe():
    p = Prog()
    nc = p.nc
    x = p.din("x_fm", [D, TPC])
    ep_d = p.din("ep", [2, 128, NTILE])
    idx_d = p.din("idx", [128, NTILE])
    e_all_d = p.din("e_all", [1, T_ALL])
    s_all_d = p.din("s_all", [1, T_ALL])
    w1_d = p.din("W1", [8, D, 4096])
    w2_d = p.din("W2", [8, 4096, D])
    b1_d = p.din("b1c", [128, 8, 32])
    b2_d = p.din("b2c", [128, 8, 8])
    xo = p.dout("x_out", [D, TPC])
    scratch_d = nc.dram_tensor("fe_scr", [NTILE, 128], F32, kind="Internal").ap()

    x_sb = p.sbuf("x_sb", [128, DC, TPC], F32)
    ln_bf = p.sbuf("ln_bf", [128, DC, TPC], BF16)
    arena = p.sbuf("arena", [128, 5 * TPC], F32)
    w1buf = p.sbuf("w1buf", [128, DC, 512], BF16)
    h_tg = p.sbuf("h_tg", [128, 32, 512], BF16)
    w2buf = p.sbuf("w2buf", [128, 1, 32, 128], BF16)
    a_row = p.sbuf("a_row", [1, TPC], F32)
    b_row = p.sbuf("b_row", [1, TPC], F32)
    ec = p.sbuf("ec", [128, NTILE], F32)
    sc = p.sbuf("sc", [128, NTILE], F32)
    ic = p.sbuf("ic", [128, NTILE], F32)
    fe = p.sbuf("fe", [128, NTILE], F32)
    fe_t = p.sbuf("fe_t", [NTILE, 128], F32)
    b1_sb = p.sbuf("b1_sb", [128, 8, 32], F32)
    b2_sb = p.sbuf("b2_sb", [128, 8, 8], F32)

    psP = p.psum("psP", [128, 2, 512])
    psH = p.psum("psH", [128, 2, 512])
    psY = p.psum("psY", [128, 512])

    p.dma("ld", nc.sync, x_sb[:], x.rearrange("(c p) t -> p c t", p=128))
    p.dma("ld", nc.sync, ec[:], ep_d[0])
    p.dma("ld", nc.sync, sc[:], ep_d[1])
    p.dma("ld", nc.sync, ic[:], idx_d)
    p.dma("ld", nc.sync, b1_sb[:], b1_d)
    p.dma("ld", nc.sync, b2_sb[:], b2_d)
    for e in (nc.tensor, nc.scalar, nc.vector, nc.gpsimd):
        p.wait(e, "ld")

    ident, iota_p, ones, ones_row = make_consts(p)

    scr = arena[:, 2 * TPC:3 * TPC]
    emit_ln_stats(p, x_sb, TPC, a_row, b_row, scr,
                  ps_a=psP[0:1, 0, :], ps_b=psP[0:1, 1, :],
                  mu_row=arena[0:1, 3 * TPC:3 * TPC + TPC])
    a_bc = arena[:, 0 * TPC:1 * TPC]
    b_bc = arena[:, 1 * TPC:2 * TPC]
    emit_bcast_rows(p, a_row[:], a_bc, ones_row, ps=psP[:, 0, :], tag="a")
    emit_bcast_rows(p, b_row[:], b_bc, ones_row, ps=psP[:, 0, :], tag="b")
    for c in range(DC):
        p.V("tensor_mul", scr[:, :], x_sb[:, c, :], a_bc)
        p.V("tensor_add", ln_bf[:, c, :], scr[:, :], b_bc)

    emit_moe_rank(p, e_all_d, s_all_d, ec, sc, ic, fe, arena, ones_row, ps_bc=psP[:])

    p.wait(nc.tensor, "V")
    p.tr(psP[0:NTILE, 0, 0:128], fe[:, :], ident[:])
    p.wait(nc.vector, "T")
    p.V("tensor_copy", fe_t[:, :], psP[0:NTILE, 0, 0:128])
    p.dma("fe1", nc.gpsimd, scratch_d, fe_t[:], wait_chans=("V",))
    p.wait(nc.gpsimd, "fe1")
    fe_bc = arena[:, 0 * TPC:1 * TPC]
    p.dma("fe2", nc.gpsimd, fe_bc,
          scratch_d.rearrange("i p -> (i p)").unsqueeze(0).partition_broadcast(128),
          wait_chans=("V",))
    for e in (nc.tensor, nc.vector):
        p.wait(e, "fe2")

    mask_bc = arena[:, 1 * TPC:2 * TPC]
    evac = arena[:, 2 * TPC:3 * TPC]

    for e in range(8):
        p.V("tensor_scalar", mask_bc[:, :], fe_bc, float(e), None, op0=ALU.is_equal)
        for tg in range(4):
            tsl = slice(tg * 512, (tg + 1) * 512)
            for q in range(8):
                p.dma("w1", nc.gpsimd, w1buf[:],
                      w1_d[e].rearrange("(kc p) f -> p kc f", p=128)
                      [:, :, q * 512:(q + 1) * 512],
                      wait_chans=("T",))
                p.wait(nc.tensor, "w1")
                for f8 in range(4):
                    fc = q * 4 + f8
                    hp = psH[:, fc % 2, :]
                    p.wait(nc.tensor, "S")
                    for kc in range(DC):
                        p.mm(hp, w1buf[:, kc, f8 * 128:(f8 + 1) * 128],
                             ln_bf[:, kc, tsl], start=(kc == 0), stop=(kc == DC - 1))
                    p.wait(nc.scalar, "T")
                    p.S("activation", out=h_tg[:, fc, :], in_=hp, func=AF.Relu,
                        bias=b1_sb[:, e, fc:fc + 1])
            for oc in range(8):
                par = 0
                p.dma("w2", nc.gpsimd, w2buf[:, par],
                      w2_d[e].rearrange("(fc p) o -> p fc o", p=128)
                      [:, :, oc * 128:(oc + 1) * 128],
                      wait_chans=("T",))
                p.wait(nc.tensor, "w2")
                p.wait(nc.tensor, "V")
                p.wait(nc.tensor, "S")
                for fc in range(32):
                    p.mm(psY[:], w2buf[:, par, fc, :], h_tg[:, fc, :],
                         start=(fc == 0), stop=(fc == 31))
                p.wait(nc.vector, "T")
                p.V("tensor_scalar", evac[:, 0:512], psY[:], b2_sb[:, e, oc:oc + 1],
                    None, op0=ALU.add)
                p.V("tensor_mul", evac[:, 0:512], evac[:, 0:512], mask_bc[:, tsl])
                p.V("scalar_tensor_tensor", x_sb[:, oc, tsl], evac[:, 0:512], 0.5,
                    x_sb[:, oc, tsl], op0=ALU.mult, op1=ALU.add)

    p.dma("st", nc.sync, xo.rearrange("(c p) t -> p c t", p=128), x_sb[:],
          wait_chans=("V",))
    p.wait(nc.sync, "st")
    return p


def _col_of(v, c):
    return np.ascontiguousarray(v[c * TPC:(c + 1) * TPC].reshape(NTILE, 128).T)


def run_l0_ffn(x_shards, inputs, cache):
    rt = np.ascontiguousarray(inputs["L0_ffn_router_w"].T)
    if "l0_route" not in cache:
        cache["l0_route"] = build_l0_route()
    pA = cache["l0_route"]
    in_maps = [{"x_fm": x_shards[c], "r_T": rt} for c in range(NCORE)]
    resA = run_bass_kernel_spmd(pA.nc, in_maps, core_ids=list(range(NCORE)))
    e_all = np.concatenate([r["ep"][0].T.reshape(-1) for r in resA.results])
    s_all = np.concatenate([r["ep"][1].T.reshape(-1) for r in resA.results])

    if "l0_moe" not in cache:
        cache["l0_moe"] = build_l0_moe_sparse()
    pB = cache["l0_moe"]
    idx = np.arange(T_ALL, dtype=np.float32)
    in_maps = []
    for c in range(NCORE):
        in_maps.append({
            "x_fm": x_shards[c],
            "ep": np.stack([_col_of(e_all, c), _col_of(s_all, c)]),
            "idx": _col_of(idx, c),
            "e_all": e_all[None, :], "s_all": s_all[None, :],
            "W1": inputs["L0_ffn_W1"].reshape(8 * D, 4096),
            "W2": inputs["L0_ffn_W2"].reshape(8 * 4096, D),
            "b1r": inputs["L0_ffn_b1"].reshape(256, 128),
            "b2r": inputs["L0_ffn_b2"].reshape(64, 128),
        })
    resB = run_bass_kernel_spmd(pB.nc, in_maps, core_ids=list(range(NCORE)))
    return [np.asarray(r["x_out"]) for r in resB.results]



def emit_gemm_stream(p, w_dram, kdim, odim, act_sb, ntok, ps, evac, wbuf, name,
                     tgw=512):
    nc = p.nc
    KC = kdim // 128
    nog = odim // 128
    ntg = ntok // tgw
    for oc in range(nog):
        par = oc % 2
        p.dma("w" + name, nc.gpsimd, wbuf[:, par, 0:KC],
              w_dram.rearrange("(kc p) o -> p kc o", p=128)[:, :, oc * 128:(oc + 1) * 128],
              wait_chans=("T",))
        p.wait(nc.tensor, "w" + name)
        for tg in range(ntg):
            tsl = slice(tg * tgw, (tg + 1) * tgw)
            p.wait(nc.tensor, "V")
            p.wait(nc.tensor, "S")
            for kc in range(KC):
                p.mm(ps, wbuf[:, par, kc, :], act_sb[:, kc, tsl],
                     start=(kc == 0), stop=(kc == KC - 1))
            evac(oc, tg, ps)




def build_l0_attn_c1():
    p = Prog()
    nc = p.nc
    x = p.din("x_fm", [D, TPC])
    wdq_d = p.din("W_DQ", [D, 4096])
    wdkv_d = p.din("W_DKV", [D, 256])
    wuk_d = p.din("W_UK", [256, D])
    wuv_d = p.din("W_UV", [256, D])
    wkr_d = p.din("W_KR", [D, 32])
    scale_d = p.din("scale_fm", [32, TPC], BF16)
    cq_o = p.dout("cq", [32, 128, TPC], BF16)
    kc_o = p.dout("kc", [DC, 128, TPC], BF16)
    v_o = p.dout("v", [NTILE, 128, D], BF16)
    kr_o = p.dout("kr", [32, TPC], BF16)

    x_sb = p.sbuf("x_sb", [128, DC, TPC], F32)
    ln_bf = p.sbuf("ln_bf", [128, DC, TPC], BF16)
    arena = p.sbuf("arena", [128, 3 * TPC], F32)
    a_row = p.sbuf("a_row", [1, TPC], F32)
    b_row = p.sbuf("b_row", [1, TPC], F32)
    wbuf = p.sbuf("wbuf", [128, 2, DC, 128], BF16)
    wkv_sb = p.sbuf("wkv_sb", [128, DC, 256], BF16)
    ckv_bf = p.sbuf("ckv_bf", [128, 2, TPC], BF16)
    scale_sb = p.sbuf("scale_sb", [32, TPC], BF16)
    orow = p.sbuf("orow", [128, TPC], BF16)
    psP = p.psum("psP", [128, 2, 512])
    psQ = p.psum("psQ", [128, 512])

    p.dma("ld", nc.sync, x_sb[:], x.rearrange("(c p) t -> p c t", p=128))
    p.dma("ld", nc.sync, scale_sb[:], scale_d)
    p.dma("ld", nc.gpsimd, wkv_sb[:], wdkv_d.rearrange("(kc p) o -> p kc o", p=128))
    for e in (nc.tensor, nc.scalar, nc.vector, nc.gpsimd):
        p.wait(e, "ld")

    scr = arena[:, 2 * TPC:3 * TPC]
    emit_ln_stats(p, x_sb, TPC, a_row, b_row, scr,
                  ps_a=psP[0:1, 0, :], ps_b=psP[0:1, 1, :],
                  mu_row=arena[0:1, 0:TPC])
    ones_row = p.sbuf("c_onesrow", [1, 128], F32)
    p.V("memset", ones_row[:], 1.0)
    a_bc = arena[:, 0 * TPC:1 * TPC]
    b_bc = arena[:, 1 * TPC:2 * TPC]
    emit_bcast_rows(p, a_row[:], a_bc, ones_row, ps=psP[:, 0, :], tag="a")
    emit_bcast_rows(p, b_row[:], b_bc, ones_row, ps=psP[:, 0, :], tag="b")
    for c in range(DC):
        p.V("tensor_mul", scr[:, :], x_sb[:, c, :], a_bc)
        p.V("tensor_add", ln_bf[:, c, :], scr[:, :], b_bc)

    def evac_cq(oc, tg, ps):
        p.wait(nc.vector, "T")
        if tg == 0:
            p.wait(nc.vector, "cqo")
        p.V("tensor_copy", orow[:, tg * 512:(tg + 1) * 512], ps)
        if tg == 3:
            p.dma("cqo", nc.sync, cq_o[oc], orow[:, :], wait_chans=("V",))
    emit_gemm_stream(p, wdq_d, D, 4096, ln_bf[:], TPC, psQ[:], evac_cq, wbuf, "dq")

    p.wait(nc.tensor, "ld")
    for oc in range(2):
        for tg in range(4):
            tsl = slice(tg * 512, (tg + 1) * 512)
            p.wait(nc.tensor, "V")
            for kc in range(DC):
                p.mm(psQ[:], wkv_sb[:, kc, oc * 128:(oc + 1) * 128], ln_bf[:, kc, tsl],
                     start=(kc == 0), stop=(kc == DC - 1))
            p.wait(nc.vector, "T")
            p.V("tensor_copy", ckv_bf[:, oc, tsl], psQ[:])

    def evac_kc(oc, tg, ps):
        p.wait(nc.vector, "T")
        if tg == 0:
            p.wait(nc.vector, "kco")
        p.V("tensor_copy", orow[:, tg * 512:(tg + 1) * 512], ps)
        if tg == 3:
            p.dma("kco", nc.sync, kc_o[oc], orow[:, :], wait_chans=("V",))
    wbuf2 = p.sbuf("wbuf2", [128, 2, 2, 128], BF16)
    emit_gemm_stream(p, wuk_d, 256, D, ckv_bf[:], TPC, psQ[:], evac_kc, wbuf2, "uk")

    wuv_sb = p.sbuf("wuv_sb", [128, 2, D], BF16)
    p.dma("wuv", nc.gpsimd, wuv_sb[:], wuv_d.rearrange("(kc p) o -> p kc o", p=128))
    p.wait(nc.tensor, "wuv")
    vt_bf = p.sbuf("vt_bf", [128, D], BF16)
    for it in range(NTILE):
        p.wait(nc.vector, "vo")
        for g in range(2):
            osl = slice(g * 512, (g + 1) * 512)
            p.wait(nc.tensor, "V")
            for kc in range(2):
                p.mm(psQ[:], ckv_bf[:, kc, it * 128:(it + 1) * 128], wuv_sb[:, kc, osl],
                     start=(kc == 0), stop=(kc == 1))
            p.wait(nc.vector, "T")
            p.V("tensor_copy", vt_bf[:, osl], psQ[:])
        p.dma("vo", nc.sync, v_o[it], vt_bf[:], wait_chans=("V",))

    wkr_sb = p.sbuf("wkr_sb", [128, DC, 32], BF16)
    p.dma("wkr", nc.gpsimd, wkr_sb[:], wkr_d.rearrange("(kc p) o -> p kc o", p=128))
    p.wait(nc.tensor, "wkr")
    kr_bf = p.sbuf("kr_bf", [32, TPC], BF16)
    for tg in range(4):
        tsl = slice(tg * 512, (tg + 1) * 512)
        p.wait(nc.tensor, "V")
        for kc in range(DC):
            p.mm(psQ[0:32, :], wkr_sb[:, kc, :], ln_bf[:, kc, tsl],
                 start=(kc == 0), stop=(kc == DC - 1))
        p.wait(nc.vector, "T")
        p.V("tensor_mul", kr_bf[0:32, tsl], psQ[0:32, 0:512], scale_sb[0:32, tsl])
    p.dma("kro", nc.sync, kr_o, kr_bf[:], wait_chans=("V",))

    p.wait(nc.sync, "cqo")
    p.wait(nc.sync, "kco")
    p.wait(nc.sync, "vo")
    p.wait(nc.sync, "kro")
    return p


NKB = 64


def build_l0_attn_d():
    p = Prog()
    nc = p.nc
    S = 8192
    cq_d = p.din("cq", [32, 128, S], BF16)
    kc_d = p.din("kch", [4, 64, S], BF16)
    kr_d = p.din("krf", [32, S], BF16)
    v_d = p.din("vph", [S // 128, 128, 4, 65], BF16)
    wuq_d = p.din("wuqh", [4096, 4 * 64])
    wqr_d = p.din("wqrh", [4096, 4 * 32])
    scale_d = p.din("scale_fm", [32, S], BF16)
    mask_d = p.din("masks", [128, 4, 512], BF16)
    o_o = p.dout("o_out", [4, 64, S], BF16)

    q_sb = p.sbuf("q_sb", [128, 4, S], BF16)
    slab = p.sbuf("slab", [128, 32, 512], BF16)
    wuq_sb = p.sbuf("wuq_sb", [128, 32, 256], BF16)
    wqr_sb = p.sbuf("wqr_sb", [128, 32, 128], BF16)
    k_sb = p.sbuf("k_sb", [128, S], BF16)
    v_sb = p.sbuf("v_sb", [128, NKB, 65], BF16)
    o_sb = p.sbuf("o_sb", [64, S], BF16)
    scale_sb = p.sbuf("scale_sb", [32, S], BF16)
    mask_sb = p.sbuf("mask_sb", [128, 4, 512], BF16)
    p_bf = p.sbuf("p_bf", [128, 2, 512], BF16)
    num_sb = p.sbuf("num_sb", [64, 512], F32)
    den_r = p.sbuf("den_r", [1, 512], F32)
    psS = p.psum("psS", [128, 2, 512])
    psO = p.psum("psO", [65, 512])
    psB = p.psum("psB", [64, 512])

    ones_row = p.sbuf("c_onesrow", [1, 128], F32)
    p.V("memset", ones_row[:], 1.0)
    p.V("memset", q_sb[96:128, :, :].rearrange("p h t -> p (h t)"), 0.0)
    p.V("memset", k_sb[96:128, :], 0.0)

    p.dma("ld", nc.gpsimd, wuq_sb[:], wuq_d.rearrange("(kc p) o -> p kc o", p=128))
    p.dma("ld", nc.gpsimd, wqr_sb[:], wqr_d.rearrange("(kc p) o -> p kc o", p=128))
    p.dma("ld", nc.sync, scale_sb[:], scale_d)
    p.dma("ld", nc.sync, mask_sb[:], mask_d)
    for e in (nc.tensor, nc.scalar, nc.vector, nc.gpsimd):
        p.wait(e, "ld")

    for tg in range(16):
        tsl = slice(tg * 512, (tg + 1) * 512)
        p.dma("cq", nc.sync, slab[:], cq_d[:, :, tsl].rearrange("kc p t -> p kc t"),
              wait_chans=("T",))
        p.wait(nc.tensor, "cq")
        for h in range(4):
            ps = psS[:, 0, :]
            p.wait(nc.tensor, "V")
            for kc in range(32):
                p.mm(ps[0:64, :], wuq_sb[:, kc, h * 64:(h + 1) * 64], slab[:, kc, :],
                     start=(kc == 0), stop=(kc == 31))
            for kc in range(32):
                p.mm(ps[64:96, :], wqr_sb[:, kc, h * 32:(h + 1) * 32], slab[:, kc, :],
                     start=(kc == 0), stop=(kc == 31), inc=(kc == 31))
            p.wait(nc.vector, "T")
            p.V("tensor_copy", q_sb[0:64, h, tsl], ps[0:64, :])
            p.V("tensor_mul", q_sb[64:96, h, tsl], ps[64:96, 0:512], scale_sb[0:32, tsl])

    sc = p.cvec(1.0 / math.sqrt(96.0))
    for h in range(4):
        p.dma("kv", nc.sync, k_sb[0:64, :], kc_d[h], wait_chans=("T",))
        p.dma("kv", nc.sync, k_sb[64:96, :], kr_d[:])
        p.dma("kv", nc.sync, v_sb[:],
              v_d[:, :, h, :].rearrange("kb p o -> p kb o"))
        p.wait(nc.tensor, "kv")
        p.wait(nc.vector, "oo")
        for s in range(16):
            qsl = slice(s * 512, (s + 1) * 512)
            nkb = 4 * (s + 1)
            for kb in range(nkb):
                par = kb % 2
                psT = psS[:, par, :]
                p.wait(nc.tensor, "S")
                p.mm(psT, k_sb[:, kb * 128:(kb + 1) * 128], q_sb[:, h, qsl],
                     start=True, stop=True)
                p.wait(nc.scalar, "T")
                p.S("activation", out=p_bf[:, par, :], in_=psT, func=AF.Exp,
                    scale=sc[:])
                diag = kb - (nkb - 4)
                if diag >= 0:
                    p.wait(nc.vector, "S")
                    p.V("tensor_mul", p_bf[:, par, :], p_bf[:, par, :],
                        mask_sb[:, diag, :])
                    p.wait(nc.tensor, "V")
                else:
                    p.wait(nc.tensor, "S")
                p.mm(psO[:], v_sb[:, kb, :], p_bf[:, par, :],
                     start=(kb == 0), stop=(kb == nkb - 1))
            p.wait(nc.vector, "T")
            p.V("tensor_copy", num_sb[:, :], psO[0:64, :])
            p.V("reciprocal", den_r[:], psO[64:65, :])
            p.wait(nc.tensor, "V")
            p.mm(psB[:], ones_row[0:1, 0:64], den_r[:], start=True, stop=True)
            p.wait(nc.vector, "T")
            p.V("tensor_mul", o_sb[:, qsl], num_sb[:, :], psB[:, :])
        p.dma("oo", nc.sync, o_o[h], o_sb[:], wait_chans=("V",))
        p.wait(nc.sync, "oo")
    return p



def build_l0_attn_e():
    p = Prog()
    nc = p.nc
    x = p.din("x_fm", [D, TPC])
    y_d = p.din("y_fm", [D, TPC], BF16)
    wo_d = p.din("W_O", [D, D])
    xo = p.dout("x_out", [D, TPC])

    x_sb = p.sbuf("x_sb", [128, DC, TPC], F32)
    y_sb = p.sbuf("y_sb", [128, DC, TPC], BF16)
    wbuf = p.sbuf("wbuf", [128, 2, DC, 128], BF16)
    psQ = p.psum("psQ", [128, 512])

    p.dma("ld", nc.sync, x_sb[:], x.rearrange("(c p) t -> p c t", p=128))
    p.dma("ld", nc.sync, y_sb[:], y_d.rearrange("(c p) t -> p c t", p=128))
    for e in (nc.tensor, nc.scalar, nc.vector, nc.gpsimd):
        p.wait(e, "ld")

    def evac(oc, tg, ps):
        tsl = slice(tg * 512, (tg + 1) * 512)
        p.wait(nc.vector, "T")
        p.V("tensor_add", x_sb[:, oc, tsl], x_sb[:, oc, tsl], ps)

    emit_gemm_stream(p, wo_d, D, D, y_sb[:], TPC, psQ[:], evac, wbuf, "wo")
    p.dma("st", nc.sync, xo.rearrange("(c p) t -> p c t", p=128), x_sb[:],
          wait_chans=("V",))
    p.wait(nc.sync, "st")
    return p


def _rope_scale_table():
    T, DHR = 8192, 32
    t = np.arange(T, dtype=np.float32)
    inv = 1.0 / 10000.0 ** (np.arange(0, DHR, 2, dtype=np.float32) / DHR)
    freqs = t[:, None] * inv[None, :]
    emb = np.concatenate([freqs, freqs], -1)
    return (np.cos(emb) + np.sin(emb)).astype(np.float32)


def run_l0_attn(x_shards, inputs, cache):
    import ml_dtypes
    bf16 = ml_dtypes.bfloat16
    scale = _rope_scale_table()
    if "l0c1" not in cache:
        cache["l0c1"] = build_l0_attn_c1()
    pC = cache["l0c1"]
    in_maps = []
    for c in range(NCORE):
        q = c % 4
        sl = scale[q * TPC:(q + 1) * TPC].T.astype(bf16)
        in_maps.append({
            "x_fm": x_shards[c], "W_DQ": inputs["L0_attn_W_DQ"],
            "W_DKV": inputs["L0_attn_W_DKV"], "W_UK": inputs["L0_attn_W_UK"],
            "W_UV": inputs["L0_attn_W_UV"], "W_KR": inputs["L0_attn_W_KR"],
            "scale_fm": np.ascontiguousarray(sl),
        })
    resC = run_bass_kernel_spmd(pC.nc, in_maps, core_ids=list(range(NCORE)))

    cq_b, kc_b, kr_b, vp_b = [], [], [], []
    for b in range(2):
        rs = [resC.results[b * 4 + q] for q in range(4)]
        cq = np.concatenate([np.asarray(r["cq"]) for r in rs], axis=2)
        kcf = np.concatenate([np.asarray(r["kc"]) for r in rs], axis=2)
        kcf = kcf.reshape(D, 8192)
        krf = np.concatenate([np.asarray(r["kr"]) for r in rs], axis=1)
        v = np.concatenate([np.asarray(r["v"]) for r in rs], axis=0)
        vp = np.zeros((64, 128, 16, 65), dtype=bf16)
        vp[:, :, :, :64] = v.reshape(64, 128, 16, 64)
        vp[:, :, :, 64] = np.asarray(1.0, dtype=bf16)
        cq_b.append(cq); kc_b.append(kcf); kr_b.append(krf); vp_b.append(vp)

    if "l0d" not in cache:
        cache["l0d"] = build_l0_attn_d()
    pD = cache["l0d"]
    masks = np.zeros((128, 4, 512), dtype=bf16)
    i = np.arange(128)[:, None]
    j = np.arange(512)[None, :]
    for r in range(4):
        masks[:, r, :] = (j >= i + 128 * r).astype(bf16)
    scale_fmT = np.ascontiguousarray(scale.T.astype(bf16))
    in_maps = []
    for c in range(NCORE):
        b = c // 4
        hs = [(c % 4) * 4 + j2 for j2 in range(4)]
        wuq = np.concatenate([inputs["L0_attn_W_UQ"][:, h * 64:(h + 1) * 64] for h in hs], axis=1)
        wqr = np.concatenate([inputs["L0_attn_W_QR"][:, h * 32:(h + 1) * 32] for h in hs], axis=1)
        kch = np.stack([kc_b[b].reshape(16, 64, 8192)[h] for h in hs])
        vph = np.ascontiguousarray(vp_b[b][:, :, hs, :])
        in_maps.append({
            "cq": cq_b[b], "kch": np.ascontiguousarray(kch), "krf": kr_b[b],
            "vph": vph,
            "wuqh": np.ascontiguousarray(wuq), "wqrh": np.ascontiguousarray(wqr),
            "scale_fm": scale_fmT, "masks": masks,
        })
    resD = run_bass_kernel_spmd(pD.nc, in_maps, core_ids=list(range(NCORE)))

    y_b = []
    for b in range(2):
        y = np.zeros((D, 8192), dtype=bf16)
        for q in range(4):
            r = resD.results[b * 4 + q]
            for j2 in range(4):
                h = q * 4 + j2
                y[h * 64:(h + 1) * 64, :] = np.asarray(r["o_out"][j2])
        y_b.append(y)

    if "l0e" not in cache:
        cache["l0e"] = build_l0_attn_e()
    pE = cache["l0e"]
    in_maps = []
    for c in range(NCORE):
        b, q = c // 4, c % 4
        in_maps.append({
            "x_fm": x_shards[c],
            "y_fm": np.ascontiguousarray(y_b[b][:, q * TPC:(q + 1) * TPC]),
            "W_O": inputs["L0_attn_W_O"],
        })
    resE = run_bass_kernel_spmd(pE.nc, in_maps, core_ids=list(range(NCORE)))
    return [np.asarray(r["x_out"]) for r in resE.results]



def emit_ln_norm(p, x_sb, ln_bf, arena, a_row, b_row, ones_row, psP, ntok=TPC):
    nc = p.nc
    scr = arena[:, 2 * ntok:3 * ntok]
    emit_ln_stats(p, x_sb, ntok, a_row, b_row, scr,
                  ps_a=psP[0:1, 0, :], ps_b=psP[0:1, 1, :],
                  mu_row=arena[0:1, 0:ntok])
    a_bc = arena[:, 0 * ntok:1 * ntok]
    b_bc = arena[:, 1 * ntok:2 * ntok]
    emit_bcast_rows(p, a_row[:], a_bc, ones_row, ps=psP[:, 0, :], tag="lna")
    emit_bcast_rows(p, b_row[:], b_bc, ones_row, ps=psP[:, 0, :], tag="lnb")
    for c in range(DC):
        p.V("tensor_mul", scr[:, :], x_sb[:, c, :], a_bc)
        p.V("tensor_add", ln_bf[:, c, :], scr[:, :], b_bc)



def build_ln_only():
    p = Prog()
    nc = p.nc
    x = p.din("x_fm", [D, TPC])
    lo = p.dout("ln_out", [DC, 128, TPC], BF16)
    x_sb = p.sbuf("x_sb", [128, DC, TPC], F32)
    ln_bf = p.sbuf("ln_bf", [128, DC, TPC], BF16)
    arena = p.sbuf("arena", [128, 3 * TPC], F32)
    a_row = p.sbuf("a_row", [1, TPC], F32)
    b_row = p.sbuf("b_row", [1, TPC], F32)
    ones_row = p.sbuf("c_onesrow", [1, 128], F32)
    psP = p.psum("psP", [128, 2, 512])
    p.dma("ld", nc.sync, x_sb[:], x.rearrange("(c p) t -> p c t", p=128))
    for e in (nc.tensor, nc.scalar, nc.vector, nc.gpsimd):
        p.wait(e, "ld")
    p.V("memset", ones_row[:], 1.0)
    emit_ln_norm(p, x_sb, ln_bf, arena, a_row, b_row, ones_row, psP[:])
    p.dma("st", nc.sync, lo.rearrange("c p t -> p c t"), ln_bf[:], wait_chans=("V",))
    p.wait(nc.sync, "st")
    return p


def build_l2_attn_d():
    p = Prog()
    nc = p.nc
    S = 8192
    lnh_d = p.din("lnh", [4, 64, S], BF16)
    wq_d = p.din("WqT", [64, 64])
    wk_d = p.din("WkT", [64, 64])
    wv_d = p.din("WvT", [64, 64])
    mask_d = p.din("masks", [128, 4, 512], BF16)
    o_o = p.dout("o_out", [4, 64, S], BF16)

    lnh = p.sbuf("lnh_sb", [128, 2, S], BF16)
    wq_sb = p.sbuf("wq_sb", [128, 3, 64], BF16)
    q_sb = p.sbuf("q_sb", [64, S], BF16)
    k_sb = p.sbuf("k_sb", [64, S], BF16)
    v_sb = p.sbuf("v_sb", [128, NKB, 65], BF16)
    o_sb = p.sbuf("o_sb", [64, S], BF16)
    mask_sb = p.sbuf("mask_sb", [128, 4, 512], BF16)
    p_bf = p.sbuf("p_bf", [128, 2, 512], BF16)
    num_sb = p.sbuf("num_sb", [64, 512], F32)
    den_r = p.sbuf("den_r", [1, 512], F32)
    ones_row = p.sbuf("c_onesrow", [1, 128], F32)
    psS = p.psum("psS", [128, 2, 512])
    psO = p.psum("psO", [65, 512])
    psB = p.psum("psB", [64, 512])
    psV = p.psum("psV", [128, 2, 65])

    p.V("memset", ones_row[:], 1.0)
    p.V("memset", v_sb[:, :, 64:65], 1.0)
    p.dma("ld", nc.gpsimd, wq_sb[0:64, 0, :], wq_d)
    p.dma("ld", nc.gpsimd, wq_sb[0:64, 1, :], wk_d)
    p.dma("ld", nc.gpsimd, wq_sb[0:64, 2, :], wv_d)
    p.dma("ld", nc.gpsimd, wq_sb[64:128, 0, :], wq_d)
    p.dma("ld", nc.gpsimd, wq_sb[64:128, 1, :], wk_d)
    p.dma("ld", nc.gpsimd, wq_sb[64:128, 2, :], wv_d)
    p.dma("ld", nc.sync, mask_sb[:], mask_d)
    for e in (nc.tensor, nc.scalar, nc.vector, nc.gpsimd):
        p.wait(e, "ld")

    sc = p.cvec(1.0 / 32.0)
    for hp in range(2):
        p.dma("lh", nc.sync, lnh[0:64, hp, :], lnh_d[2 * hp], wait_chans=("T",))
        p.dma("lh", nc.sync, lnh[64:128, hp, :], lnh_d[2 * hp + 1])
    p.wait(nc.tensor, "lh")
    for h in range(4):
        hb = (h % 2) * 64
        lv = lnh[hb:hb + 64, h // 2, :]
        wsl = wq_sb[hb:hb + 64]
        for g in range(S // 512):
            gsl = slice(g * 512, (g + 1) * 512)
            p.wait(nc.tensor, "V")
            p.mm(psS[0:64, 0, :], wsl[:, 0, :], lv[:, gsl], start=True, stop=True)
            p.mm(psS[0:64, 1, :], wsl[:, 1, :], lv[:, gsl], start=True, stop=True)
            p.wait(nc.vector, "T")
            p.V("tensor_copy", q_sb[:, gsl], psS[0:64, 0, 0:512])
            p.V("tensor_copy", k_sb[:, gsl], psS[0:64, 1, 0:512])
        for kb in range(NKB):
            par = kb % 2
            p.wait(nc.tensor, "V")
            p.mm(psV[:, par, 0:64], lv[:, kb * 128:(kb + 1) * 128], wsl[:, 2, :],
                 start=True, stop=True)
            p.wait(nc.vector, "T")
            p.V("tensor_copy", v_sb[:, kb, 0:64], psV[:, par, 0:64])
        for s in range(16):
            qsl = slice(s * 512, (s + 1) * 512)
            nkb = 4 * (s + 1)
            for kb in range(nkb):
                par = kb % 2
                psT = psS[:, par, :]
                p.wait(nc.tensor, "S")
                p.mm(psT, k_sb[:, kb * 128:(kb + 1) * 128], q_sb[:, qsl],
                     start=True, stop=True)
                p.wait(nc.scalar, "T")
                p.S("activation", out=p_bf[:, par, :], in_=psT, func=AF.Exp,
                    scale=sc[:])
                diag = kb - (nkb - 4)
                if diag >= 0:
                    p.wait(nc.vector, "S")
                    p.V("tensor_mul", p_bf[:, par, :], p_bf[:, par, :],
                        mask_sb[:, diag, :])
                    p.wait(nc.tensor, "V")
                else:
                    p.wait(nc.tensor, "S")
                p.mm(psO[:], v_sb[:, kb, :], p_bf[:, par, :],
                     start=(kb == 0), stop=(kb == nkb - 1))
            p.wait(nc.vector, "T")
            p.V("tensor_copy", num_sb[:, :], psO[0:64, :])
            p.V("reciprocal", den_r[:], psO[64:65, :])
            p.wait(nc.tensor, "V")
            p.mm(psB[:], ones_row[0:1, 0:64], den_r[:], start=True, stop=True)
            p.wait(nc.vector, "T")
            p.V("tensor_mul", o_sb[:, qsl], num_sb[:, :], psB[:, :])
        p.dma("oo", nc.sync, o_o[h], o_sb[:], wait_chans=("V",))
        p.wait(nc.sync, "oo")
        p.wait(nc.vector, "oo")
    return p


def build_proj_res():
    p = Prog()
    nc = p.nc
    x = p.din("x_fm", [D, TPC])
    y_d = p.din("y_fm", [D, TPC], BF16)
    w_d = p.din("W", [D, D])
    b_d = p.din("bcol", [128, DC])
    xo = p.dout("x_out", [D, TPC])

    x_sb = p.sbuf("x_sb", [128, DC, TPC], F32)
    y_sb = p.sbuf("y_sb", [128, DC, TPC], BF16)
    b_sb = p.sbuf("b_sb", [128, DC], F32)
    wbuf = p.sbuf("wbuf", [128, 2, DC, 128], BF16)
    scr = p.sbuf("scr", [128, 512], F32)
    psQ = p.psum("psQ", [128, 512])

    p.dma("ld", nc.sync, x_sb[:], x.rearrange("(c p) t -> p c t", p=128))
    p.dma("ld", nc.sync, y_sb[:], y_d.rearrange("(c p) t -> p c t", p=128))
    p.dma("ld", nc.sync, b_sb[:], b_d)
    for e in (nc.tensor, nc.scalar, nc.vector, nc.gpsimd):
        p.wait(e, "ld")

    def evac(oc, tg, ps):
        tsl = slice(tg * 512, (tg + 1) * 512)
        p.wait(nc.vector, "T")
        p.V("tensor_scalar", scr[:, :], ps, b_sb[:, oc:oc + 1], None, op0=ALU.add)
        p.V("tensor_add", x_sb[:, oc, tsl], x_sb[:, oc, tsl], scr[:, :])

    emit_gemm_stream(p, w_d, D, D, y_sb[:], TPC, psQ[:], evac, wbuf, "w")
    p.dma("st", nc.sync, xo.rearrange("(c p) t -> p c t", p=128), x_sb[:],
          wait_chans=("V",))
    p.wait(nc.sync, "st")
    return p


def run_l2_attn(x_shards, inputs, cache):
    import ml_dtypes
    bf16 = ml_dtypes.bfloat16
    if "ln_only" not in cache:
        cache["ln_only"] = build_ln_only()
    pC = cache["ln_only"]
    in_maps = [{"x_fm": x_shards[c]} for c in range(NCORE)]
    resC = run_bass_kernel_spmd(pC.nc, in_maps, core_ids=list(range(NCORE)))
    ln_b = []
    for b in range(2):
        ln = np.concatenate([np.asarray(resC.results[b * 4 + q]["ln_out"]) for q in range(4)],
                            axis=2)
        ln_b.append(ln.reshape(D, 8192))

    if "l2d" not in cache:
        cache["l2d"] = build_l2_attn_d()
    pD = cache["l2d"]
    masks = np.zeros((128, 4, 512), dtype=bf16)
    i = np.arange(128)[:, None]
    j = np.arange(512)[None, :]
    for r in range(4):
        masks[:, r, :] = (j >= i + 128 * r).astype(bf16)
    in_maps = []
    for c in range(NCORE):
        b = c // 4
        hs = [(c % 4) * 4 + j2 for j2 in range(4)]
        lnh = np.stack([ln_b[b].reshape(16, 64, 8192)[h] for h in hs])
        in_maps.append({
            "lnh": np.ascontiguousarray(lnh),
            "WqT": np.ascontiguousarray(inputs["L2_attn_Wq"].T),
            "WkT": np.ascontiguousarray(inputs["L2_attn_Wk"].T),
            "WvT": np.ascontiguousarray(inputs["L2_attn_Wv"].T),
            "masks": masks,
        })
    resD = run_bass_kernel_spmd(pD.nc, in_maps, core_ids=list(range(NCORE)))
    y_b = []
    for b in range(2):
        y = np.zeros((D, 8192), dtype=bf16)
        for q in range(4):
            r = resD.results[b * 4 + q]
            for j2 in range(4):
                h = q * 4 + j2
                y[h * 64:(h + 1) * 64, :] = np.asarray(r["o_out"][j2])
        y_b.append(y)

    if "proj_res" not in cache:
        cache["proj_res"] = build_proj_res()
    pE = cache["proj_res"]
    bu = inputs["L2_attn_bu"].reshape(DC, 128).T
    in_maps = []
    for c in range(NCORE):
        b, q = c // 4, c % 4
        in_maps.append({
            "x_fm": x_shards[c],
            "y_fm": np.ascontiguousarray(y_b[b][:, q * TPC:(q + 1) * TPC]),
            "W": np.ascontiguousarray(inputs["L2_attn_Wu"].T),
            "bcol": np.ascontiguousarray(bu),
        })
    resE = run_bass_kernel_spmd(pE.nc, in_maps, core_ids=list(range(NCORE)))
    return [np.asarray(r["x_out"]) for r in resE.results]



def build_l2_f1():
    p = Prog()
    nc = p.nc
    x = p.din("x_fm", [D, TPC])
    po = p.dout("part", [128, DC])
    x_sb = p.sbuf("x_sb", [128, DC, TPC], F32)
    arena = p.sbuf("arena", [128, 3 * TPC], F32)
    a_row = p.sbuf("a_row", [1, TPC], F32)
    b_row = p.sbuf("b_row", [1, TPC], F32)
    part = p.sbuf("part_sb1", [128, DC], F32)
    ones_row = p.sbuf("c_onesrow", [1, 128], F32)
    psP = p.psum("psP", [128, 2, 512])
    p.dma("ld", nc.sync, x_sb[:], x.rearrange("(c p) t -> p c t", p=128))
    for e in (nc.tensor, nc.scalar, nc.vector, nc.gpsimd):
        p.wait(e, "ld")
    p.V("memset", ones_row[:], 1.0)
    scr = arena[:, 2 * TPC:3 * TPC]
    emit_ln_stats(p, x_sb, TPC, a_row, b_row, scr,
                  ps_a=psP[0:1, 0, :], ps_b=psP[0:1, 1, :],
                  mu_row=arena[0:1, 0:TPC])
    a_bc = arena[:, 0 * TPC:1 * TPC]
    b_bc = arena[:, 1 * TPC:2 * TPC]
    emit_bcast_rows(p, a_row[:], a_bc, ones_row, ps=psP[:, 0, :], tag="a")
    emit_bcast_rows(p, b_row[:], b_bc, ones_row, ps=psP[:, 0, :], tag="b")
    for c in range(DC):
        p.V("tensor_mul", scr[:, :], x_sb[:, c, :], a_bc)
        p.V("tensor_add", scr[:, :], scr[:, :], b_bc)
        p.V("tensor_reduce", part[:, c:c + 1], scr[:, :], axis=AX.X, op=ALU.add)
    p.dma("st", nc.sync, po, part[:], wait_chans=("V",))
    p.wait(nc.sync, "st")
    return p


def build_l2_f2():
    p = Prog()
    nc = p.nc
    x = p.din("x_fm", [D, TPC])
    part_d = p.din("parts", [4, 128, DC])
    cnw1_d = p.din("cnw1", [D, D])
    cnw2_d = p.din("cnw2", [D, 512])
    cnw3_d = p.din("cnw3", [512, 256])
    cnw4_d = p.din("cnw4", [256, 1])
    cnb1_d = p.din("cnb1", [128, 8])
    cnb2_d = p.din("cnb2", [128, 4])
    cnb3_d = p.din("cnb3", [128, 2])
    cnb4_d = p.din("cnb4", [1, 1])
    gcw1_d = p.din("gcw1", [D, 512])
    gcw2_d = p.din("gcw2", [512, 256])
    gcw3_d = p.din("gcw3", [256, 1])
    gcb1_d = p.din("gcb1", [128, 4])
    gcb2_d = p.din("gcb2", [128, 2])
    gcb3_d = p.din("gcb3", [1, 1])
    ar_d = p.din("ar", [1, 1])
    xo = p.dout("x_out", [D, TPC])

    out32 = p.sbuf("out32", [128, DC, TPC], F32)
    arena = p.sbuf("arena", [128, 3 * TPC], F32)
    out_bf = p.sbuf("out_bf", [128, DC, TPC], BF16)
    h1c = p.sbuf("h1c", [128, DC, TPC], BF16)
    h2c = p.sbuf("h2c", [128, 4, TPC], BF16)
    h3c = p.sbuf("h3c", [128, 2, TPC], BF16)
    a_row = p.sbuf("a_row", [1, TPC], F32)
    b_row = p.sbuf("b_row", [1, TPC], F32)
    part = p.sbuf("part_sb", [128, 4, DC], F32)
    gm_bf = p.sbuf("gm_bf", [128, DC], BF16)
    gm32 = p.sbuf("gm32", [128, DC], F32)
    gcw1_sb = p.sbuf("gcw1_sb", [128, DC, 512], BF16)
    gcw2_sb = p.sbuf("gcw2_sb", [128, 4, 256], BF16)
    gcw3_sb = p.sbuf("gcw3_sb", [128, 2, 1], BF16)
    cnw4_sb = p.sbuf("cnw4_sb", [128, 2, 1], BF16)
    gh1 = p.sbuf("gh1", [128, 4], BF16)
    gh2 = p.sbuf("gh2", [128, 2], BF16)
    gf_sc = p.sbuf("gf_sc", [1, 4], F32)
    bias_sc = p.sbuf("bias_sc", [1, 3], F32)
    cnb1 = p.sbuf("cnb1_sb", [128, 8], F32)
    cnb2 = p.sbuf("cnb2_sb", [128, 4], F32)
    cnb3 = p.sbuf("cnb3_sb", [128, 2], F32)
    gcb1 = p.sbuf("gcb1_sb", [128, 4], F32)
    gcb2 = p.sbuf("gcb2_sb", [128, 2], F32)
    wbuf = p.sbuf("wbuf", [128, 2, DC, 128], BF16)
    ones_row = p.sbuf("c_onesrow", [1, 128], F32)
    psP = p.psum("psP", [128, 2, 512])
    psQ = p.psum("psQ", [128, 512])
    psG = p.psum("psG", [128, 1])

    p.dma("ld", nc.sync, out32[:], x.rearrange("(c p) t -> p c t", p=128))
    p.dma("ld", nc.sync, part[:], part_d.rearrange("q p c -> p q c"))
    p.dma("ld", nc.gpsimd, gcw1_sb[:], gcw1_d.rearrange("(kc p) o -> p kc o", p=128))
    p.dma("ld", nc.gpsimd, gcw2_sb[:], gcw2_d.rearrange("(kc p) o -> p kc o", p=128))
    p.dma("ld", nc.gpsimd, gcw3_sb[:], gcw3_d.rearrange("(kc p) o -> p kc o", p=128), allow_slow_non_contiguous=True)
    p.dma("ld", nc.gpsimd, cnw4_sb[:], cnw4_d.rearrange("(kc p) o -> p kc o", p=128), allow_slow_non_contiguous=True)
    p.dma("ld", nc.sync, cnb1[:], cnb1_d)
    p.dma("ld", nc.sync, cnb2[:], cnb2_d)
    p.dma("ld", nc.sync, cnb3[:], cnb3_d)
    p.dma("ld", nc.sync, bias_sc[0:1, 0:1], cnb4_d)
    p.dma("ld", nc.sync, gcb1[:], gcb1_d)
    p.dma("ld", nc.sync, gcb2[:], gcb2_d)
    p.dma("ld", nc.sync, bias_sc[0:1, 1:2], gcb3_d)
    p.dma("ld", nc.sync, bias_sc[0:1, 2:3], ar_d)
    for e in (nc.tensor, nc.scalar, nc.vector, nc.gpsimd):
        p.wait(e, "ld")
    p.V("memset", ones_row[:], 1.0)

    scr = arena[:, 2 * TPC:3 * TPC]
    emit_ln_stats(p, out32, TPC, a_row, b_row, scr,
                  ps_a=psP[0:1, 0, :], ps_b=psP[0:1, 1, :],
                  mu_row=arena[0:1, 0:TPC])
    a_bc = arena[:, 0 * TPC:1 * TPC]
    b_bc = arena[:, 1 * TPC:2 * TPC]
    emit_bcast_rows(p, a_row[:], a_bc, ones_row, ps=psP[:, 0, :], tag="a")
    emit_bcast_rows(p, b_row[:], b_bc, ones_row, ps=psP[:, 0, :], tag="b")
    for c in range(DC):
        p.V("tensor_mul", out32[:, c, :], out32[:, c, :], a_bc)
        p.V("tensor_add", out32[:, c, :], out32[:, c, :], b_bc)

    tf_row = arena[0:1, 1 * TPC:2 * TPC]
    cp_row = arena[0:1, 2 * TPC:3 * TPC]
    p.V("tensor_add", part[:, 0, :], part[:, 0, :], part[:, 1, :])
    p.V("tensor_add", part[:, 2, :], part[:, 2, :], part[:, 3, :])
    p.V("tensor_add", gm32[:], part[:, 0, :], part[:, 2, :])
    p.V("tensor_scalar_mul", gm32[:], gm32[:], 1.0 / 8192.0)
    p.V("tensor_copy", gm_bf[:], gm32[:])

    p.wait(nc.tensor, "V")
    for oc in range(4):
        for kc in range(DC):
            p.mm(psG[:], gcw1_sb[:, kc, oc * 128:(oc + 1) * 128], gm_bf[:, kc:kc + 1],
                 start=(kc == 0), stop=(kc == DC - 1))
        p.wait(nc.scalar, "T")
        p.S("activation", out=gh1[:, oc:oc + 1], in_=psG[:], func=AF.Gelu,
            bias=gcb1[:, oc:oc + 1])
        p.wait(nc.tensor, "S")
    for oc in range(2):
        for kc in range(4):
            p.mm(psG[:], gcw2_sb[:, kc, oc * 128:(oc + 1) * 128], gh1[:, kc:kc + 1],
                 start=(kc == 0), stop=(kc == 3))
        p.wait(nc.scalar, "T")
        p.S("activation", out=gh2[:, oc:oc + 1], in_=psG[:], func=AF.Gelu,
            bias=gcb2[:, oc:oc + 1])
        p.wait(nc.tensor, "S")
    for kc in range(2):
        p.mm(psG[0:1, :], gcw3_sb[:, kc, :], gh2[:, kc:kc + 1],
             start=(kc == 0), stop=(kc == 1))
    p.wait(nc.scalar, "T")
    p.S("activation", out=gf_sc[0:1, 0:1], in_=psG[0:1, :], func=AF.Sigmoid,
        bias=bias_sc[0:1, 1:2])
    p.wait(nc.vector, "S")
    p.V("tensor_scalar", gf_sc[0:1, 1:2], gf_sc[0:1, 0:1], 0.7, -0.5,
        op0=ALU.mult, op1=ALU.add)
    p.V("tensor_scalar_mul", gf_sc[0:1, 2:3], bias_sc[0:1, 2:3], 2.0)

    for cyc in range(3):
        for c in range(DC):
            p.V("tensor_copy", out_bf[:, c, :], out32[:, c, :])

        def evac_h1(oc, tg, ps):
            tsl = slice(tg * 512, (tg + 1) * 512)
            p.wait(nc.scalar, "T")
            p.S("activation", out=h1c[:, oc, tsl], in_=ps, func=AF.Gelu,
                bias=cnb1[:, oc:oc + 1])
        emit_gemm_stream(p, cnw1_d, D, D, out_bf[:], TPC, psQ[:], evac_h1, wbuf,
                         "c1_%d" % cyc)

        def evac_h2(oc, tg, ps):
            tsl = slice(tg * 512, (tg + 1) * 512)
            p.wait(nc.scalar, "T")
            p.S("activation", out=h2c[:, oc, tsl], in_=ps, func=AF.Gelu,
                bias=cnb2[:, oc:oc + 1])
        emit_gemm_stream(p, cnw2_d, D, 512, h1c[:], TPC, psQ[:], evac_h2, wbuf,
                         "c2_%d" % cyc)

        def evac_h3(oc, tg, ps):
            tsl = slice(tg * 512, (tg + 1) * 512)
            p.wait(nc.scalar, "T")
            p.S("activation", out=h3c[:, oc, tsl], in_=ps, func=AF.Gelu,
                bias=cnb3[:, oc:oc + 1])
        emit_gemm_stream(p, cnw3_d, 512, 256, h2c[:], TPC, psQ[:], evac_h3, wbuf,
                         "c3_%d" % cyc)

        for tg in range(4):
            tsl = slice(tg * 512, (tg + 1) * 512)
            p.wait(nc.tensor, "S")
            for kc in range(2):
                p.mm(psQ[0:1, :], cnw4_sb[:, kc, :], h3c[:, kc, tsl],
                     start=(kc == 0), stop=(kc == 1))
            p.wait(nc.scalar, "T")
            p.S("activation", out=tf_row[0:1, tsl], in_=psQ[0:1, 0:512],
                func=AF.Sigmoid, bias=bias_sc[0:1, 0:1])
        p.wait(nc.vector, "S")
        p.V("tensor_scalar", cp_row[:], tf_row[:], 0.3, gf_sc[0:1, 1:2],
            op0=ALU.mult, op1=ALU.add)
        p.V("tensor_scalar", cp_row[:], cp_row[:], gf_sc[0:1, 2:3], 0.1,
            op0=ALU.mult, op1=ALU.add)
        cp_bc = arena[:, 0 * TPC:1 * TPC]
        emit_bcast_rows(p, cp_row[:], cp_bc, ones_row, ps=psP[:, 0, :],
                        tag="cp%d" % cyc)
        tw = arena[:, 1 * TPC:2 * TPC]
        tw2 = arena[:, 2 * TPC:3 * TPC]
        for c in range(4):
            sgn_top = 1.0 if c < 2 else -1.0
            p.V("tensor_mul", tw[:, :], out32[:, c + 4, :], cp_bc)
            p.V("tensor_mul", tw2[:, :], out32[:, c, :], cp_bc)
            if c < 2:
                p.V("tensor_add", out32[:, c, :], out32[:, c, :], tw[:, :])
                p.V("tensor_sub", out32[:, c + 4, :], out32[:, c + 4, :], tw2[:, :])
            else:
                p.V("tensor_sub", out32[:, c, :], out32[:, c, :], tw[:, :])
                p.V("tensor_add", out32[:, c + 4, :], out32[:, c + 4, :], tw2[:, :])

    xrow = arena[:, 0 * TPC:1 * TPC]
    for c in range(DC):
        p.dma("xl", nc.sync, xrow,
              x.rearrange("(c p) t -> p c t", p=128)[:, c, :], wait_chans=("V",))
        p.wait(nc.vector, "xl")
        p.V("scalar_tensor_tensor", out32[:, c, :], out32[:, c, :], 0.5, xrow,
            op0=ALU.mult, op1=ALU.add)
    p.dma("st", nc.sync, xo.rearrange("(c p) t -> p c t", p=128), out32[:],
          wait_chans=("V",))
    p.wait(nc.sync, "st")
    return p


def run_l2_ffn(x_shards, inputs, cache):
    if "l2f1" not in cache:
        cache["l2f1"] = build_l2_f1()
    p1 = cache["l2f1"]
    res1 = run_bass_kernel_spmd(p1.nc, [{"x_fm": x_shards[c]} for c in range(NCORE)],
                                core_ids=list(range(NCORE)))
    parts = [np.asarray(r["part"]) for r in res1.results]
    if "l2f2" not in cache:
        cache["l2f2"] = build_l2_f2()
    p2 = cache["l2f2"]
    in_maps = []
    for c in range(NCORE):
        b = c // 4
        in_maps.append({
            "x_fm": x_shards[c],
            "parts": np.stack([parts[b * 4 + q] for q in range(4)]),
            "cnw1": inputs["L2_ffn_cn_w1"], "cnw2": inputs["L2_ffn_cn_w2"],
            "cnw3": inputs["L2_ffn_cn_w3"], "cnw4": inputs["L2_ffn_cn_w4"],
            "cnb1": np.ascontiguousarray(inputs["L2_ffn_cn_b1"].reshape(8, 128).T),
            "cnb2": np.ascontiguousarray(inputs["L2_ffn_cn_b2"].reshape(4, 128).T),
            "cnb3": np.ascontiguousarray(inputs["L2_ffn_cn_b3"].reshape(2, 128).T),
            "cnb4": inputs["L2_ffn_cn_b4"].reshape(1, 1),
            "gcw1": inputs["L2_ffn_gc_w1"], "gcw2": inputs["L2_ffn_gc_w2"],
            "gcw3": inputs["L2_ffn_gc_w3"],
            "gcb1": np.ascontiguousarray(inputs["L2_ffn_gc_b1"].reshape(4, 128).T),
            "gcb2": np.ascontiguousarray(inputs["L2_ffn_gc_b2"].reshape(2, 128).T),
            "gcb3": inputs["L2_ffn_gc_b3"].reshape(1, 1),
            "ar": inputs["L2_ffn_adaptive_range"].reshape(1, 1),
        })
    res2 = run_bass_kernel_spmd(p2.nc, in_maps, core_ids=list(range(NCORE)))
    return [np.asarray(r["x_out"]) for r in res2.results]



def build_l3_ffn():
    p = Prog()
    nc = p.nc
    x = p.din("x_fm", [D, TPC])
    wt_d = p.din("w_T", [D, D])
    alpha_d = p.din("alpha", [1, 1])
    b_d = p.din("bcol", [128, DC])
    xo = p.dout("x_out", [D, TPC])

    x_sb = p.sbuf("x_sb", [128, DC, TPC], F32)
    ln_bf = p.sbuf("ln_bf", [128, DC, TPC], BF16)
    arena = p.sbuf("arena", [128, 3 * TPC], F32)
    a_row = p.sbuf("a_row", [1, TPC], F32)
    b_row = p.sbuf("b_row", [1, TPC], F32)
    w_sb = p.sbuf("w_sb", [128, DC, D], F32)
    wq_bf = p.sbuf("wq_bf", [128, DC, D], BF16)
    b_sb = p.sbuf("b_sb", [128, DC], F32)
    asum = p.sbuf("asum", [128, 1], F32)
    dcol = p.sbuf("dcol", [128, 2], F32)
    srow = p.sbuf("srow", [1, 2], F32)
    alpha_sb = p.sbuf("alpha_sb", [1, 1], F32)
    scr5 = p.sbuf("scr5", [128, 512], F32)
    psP = p.psum("psP", [128, 2, 512])
    psQ = p.psum("psQ", [128, 512])

    p.dma("ld", nc.sync, x_sb[:], x.rearrange("(c p) t -> p c t", p=128))
    p.dma("ld", nc.sync, w_sb[:], wt_d.rearrange("(kc p) o -> p kc o", p=128))
    p.dma("ld", nc.sync, b_sb[:], b_d)
    p.dma("ld", nc.sync, alpha_sb[:], alpha_d)
    for e in (nc.tensor, nc.scalar, nc.vector, nc.gpsimd):
        p.wait(e, "ld")
    ident, _, _, ones_row = make_consts(p)

    p.V("tensor_reduce", asum[:], w_sb[:].rearrange("p c o -> p (c o)"), axis=AX.X,
        op=ALU.add, apply_absolute_value=True)
    p.wait(nc.tensor, "V")
    p.tr(psP[0:1, 0, 0:128], asum[:], ident[:])
    p.wait(nc.vector, "T")
    p.V("tensor_reduce", srow[0:1, 0:1], psP[0:1, 0, 0:128], axis=AX.X, op=ALU.add)
    p.V("tensor_scalar_mul", srow[0:1, 0:1], srow[0:1, 0:1], 0.7 / (D * D))
    p.V("tensor_scalar_mul", srow[0:1, 1:2], srow[0:1, 0:1], -1.0)
    p.wait(nc.tensor, "V")
    p.mm(psP[:, 0, 0:2], ones_row[:], srow[:], start=True, stop=True)
    p.wait(nc.vector, "T")
    p.V("tensor_copy", dcol[:], psP[:, 0, 0:2])
    acol = p.sbuf("acol", [128, 1], F32)
    p.mm(psP[:, 0, 2:3], ones_row[:], alpha_sb[:], start=True, stop=True)
    p.wait(nc.vector, "T")
    p.V("tensor_copy", acol[:], psP[:, 0, 2:3])
    nacol = p.sbuf("nacol", [128, 1], F32)
    p.wait(nc.vector, "T")
    p.V("tensor_scalar_mul", nacol[:], acol[:], -1.0)
    for c in range(DC):
        for g in range(2):
            sl = slice(g * 512, (g + 1) * 512)
            p.V("tensor_scalar", scr5[:, :], w_sb[:, c, sl], dcol[:, 0:1], None,
                op0=ALU.is_gt)
            p.V("scalar_tensor_tensor", scr5[:, :], w_sb[:, c, sl], dcol[:, 1:2],
                scr5[:, :], op0=ALU.is_lt, op1=ALU.subtract)
            p.V("tensor_scalar", wq_bf[:, c, sl], scr5[:, :], nacol[:, 0:1], None,
                op0=ALU.mult)

    emit_ln_norm(p, x_sb, ln_bf, arena, a_row, b_row, ones_row, psP[:])

    for oc in range(DC):
        for tg in range(4):
            tsl = slice(tg * 512, (tg + 1) * 512)
            p.wait(nc.tensor, "V")
            for kc in range(DC):
                p.mm(psQ[:], wq_bf[:, kc, oc * 128:(oc + 1) * 128], ln_bf[:, kc, tsl],
                     start=(kc == 0), stop=(kc == DC - 1))
            p.wait(nc.vector, "T")
            p.V("tensor_scalar", scr5[:, :], psQ[:], b_sb[:, oc:oc + 1], None,
                op0=ALU.add)
            p.V("scalar_tensor_tensor", x_sb[:, oc, tsl], scr5[:, :], 0.5,
                x_sb[:, oc, tsl], op0=ALU.mult, op1=ALU.add)
    p.dma("st", nc.sync, xo.rearrange("(c p) t -> p c t", p=128), x_sb[:],
          wait_chans=("V",))
    p.wait(nc.sync, "st")
    return p


_IDENT_CACHE = {}


def ident_view(p):
    if not hasattr(p, "_ident1"):
        t = p.sbuf("ident1", [1, 1], F32)
        p.V("memset", t[:], 1.0)
        p._ident1 = t
    return p._ident1[0:1, 0:1]


def run_l3_ffn(x_shards, inputs, cache):
    if "l3f" not in cache:
        cache["l3f"] = build_l3_ffn()
    pf = cache["l3f"]
    bcol = np.ascontiguousarray(inputs["L3_ffn_bias"].reshape(DC, 128).T)
    in_maps = [{
        "x_fm": x_shards[c],
        "w_T": np.ascontiguousarray(inputs["L3_ffn_weight"].T),
        "alpha": inputs["L3_ffn_alpha"].reshape(1, 1),
        "bcol": bcol,
    } for c in range(NCORE)]
    res = run_bass_kernel_spmd(pf.nc, in_maps, core_ids=list(range(NCORE)))
    return [np.asarray(r["x_out"]) for r in res.results]



def build_l3_a1():
    p = Prog()
    nc = p.nc
    x = p.din("x_fm", [D, TPC])
    wqkv_d = p.din("w_qkv", [D, 3 * D])
    bqkv_d = p.din("bqkv", [128, 24])
    proj_d = p.din("proj", [16, 64, 64])
    qp_o = p.dout("qp", [16, 64, TPC], BF16)
    kpf_o = p.dout("kpf", [16, 64, TPC], BF16)
    kpt_o = p.dout("kpt", [16, NTILE, 128, 64], BF16)
    vt_o = p.dout("vt", [16, NTILE, 128, 65], BF16)
    st_o = p.dout("st", [16, 64, 65])

    x_sb = p.sbuf("x_sb", [128, DC, TPC], F32)
    ln_bf = p.sbuf("ln_bf", [128, DC, TPC], BF16)
    arena = p.sbuf("arena", [128, 3 * TPC], F32)
    a_row = p.sbuf("a_row", [1, TPC], F32)
    b_row = p.sbuf("b_row", [1, TPC], F32)
    qkv_bf = p.sbuf("qkv_bf", [128, 12, TPC], BF16)
    bqkv_sb = p.sbuf("bqkv_sb", [128, 24], F32)
    proj_sb = p.sbuf("proj_sb", [128, 8, 64], BF16)
    wbuf = p.sbuf("wbuf", [128, 2, DC, 128], BF16)
    qp_sb = p.sbuf("qp_sb", [64, TPC], BF16)
    kpf_sb = p.sbuf("kpf_sb", [64, TPC], BF16)
    kpt_sb = p.sbuf("kpt_sb", [128, NTILE, 64], BF16)
    vt_sb = p.sbuf("vt_sb", [128, NTILE, 65], BF16)
    st_sb = p.sbuf("st_sb", [64, 65], F32)
    psP = p.psum("psP", [128, 2, 512])
    psQ = p.psum("psQ", [128, 512])
    psS = p.psum("psS", [64, 65])
    psT = p.psum("psT", [128, 64], BF16)
    ident_bf = p.sbuf("ident_bf", [128, 128], BF16)
    ones_row = None

    p.dma("ld", nc.sync, x_sb[:], x.rearrange("(c p) t -> p c t", p=128))
    p.dma("ld", nc.sync, bqkv_sb[:], bqkv_d)
    p.dma("ld", nc.gpsimd, proj_sb[0:64, :, :],
          proj_d.rearrange("(hp two) d m -> two d hp m", two=2)[0])
    p.dma("ld", nc.gpsimd, proj_sb[64:128, :, :],
          proj_d.rearrange("(hp two) d m -> two d hp m", two=2)[1])
    for e in (nc.tensor, nc.scalar, nc.vector, nc.gpsimd):
        p.wait(e, "ld")
    ident, _, _, ones_row = make_consts(p)
    p.V("tensor_copy", ident_bf[:], ident[:])

    emit_ln_norm(p, x_sb, ln_bf, arena, a_row, b_row, ones_row, psP[:])

    sc18 = p.cvec(1.0 / 8.0)
    for half in range(2):
      for part3 in range(3):
        def evac_qkv(oc, tg, ps, part3=part3, half=half):
            tsl = slice(tg * 512, (tg + 1) * 512)
            p.wait(nc.scalar, "T")
            goc = part3 * 8 + half * 4 + oc
            p.S("activation", out=qkv_bf[:, part3 * 4 + oc, tsl], in_=ps,
                func=AF.Identity, bias=bqkv_sb[:, goc:goc + 1])
        emit_gemm_stream(p, wqkv_d[:, (part3 * 8 + half * 4) * 128:
                                   (part3 * 8 + half * 4 + 4) * 128],
                         D, 512, ln_bf[:], TPC, psQ[:], evac_qkv, wbuf,
                         "qkv%d_%d" % (half, part3))

      for hh in range(8):
        h = half * 8 + hh
        hb = (h % 2) * 64
        prj = proj_sb[hb:hb + 64, :, :]
        pc = (h // 2)
        lc = hh // 2
        qf = qkv_bf[hb:hb + 64, lc, :]
        kf = qkv_bf[hb:hb + 64, 4 + lc, :]
        vf = qkv_bf[hb:hb + 64, 8 + lc, :]
        for g in range(4):
            gsl = slice(g * 512, (g + 1) * 512)
            p.wait(nc.tensor, "S")
            p.mm(psQ[0:64, :], prj[:, pc, :], qf[:, gsl], start=True, stop=True)
            p.wait(nc.scalar, "T")
            p.S("activation", out=qp_sb[:, gsl], in_=psQ[0:64, 0:512], func=AF.Relu,
                scale=sc18[0:64, :])
            p.wait(nc.tensor, "S")
            p.mm(psQ[0:64, :], prj[:, pc, :], kf[:, gsl], start=True, stop=True)
            p.wait(nc.scalar, "T")
            p.S("activation", out=kpf_sb[:, gsl], in_=psQ[0:64, 0:512], func=AF.Relu,
                scale=sc18[0:64, :])
        for it in range(NTILE):
            tsl = slice(it * 128, (it + 1) * 128)
            p.wait(nc.tensor, "S")
            p.wait(nc.tensor, "V")
            p.mm(psP[:, 0, 0:64], kf[:, tsl], prj[:, pc, :], start=True, stop=True)
            p.tr(psT[:, 0:64], vf[:, tsl], ident_bf[hb:hb + 64, hb:hb + 64])
            p.wait(nc.scalar, "T")
            p.S("activation", out=kpt_sb[:, it, :], in_=psP[:, 0, 0:64], func=AF.Relu,
                scale=sc18[:])
            p.wait(nc.vector, "T")
            p.V("tensor_copy", vt_sb[:, it, 0:64], psT[:, 0:64])
        p.V("memset", vt_sb[:, :, 64:65], 1.0)
        p.wait(nc.tensor, "S")
        p.wait(nc.tensor, "V")
        for it in range(NTILE):
            p.mm(psS[:], kpt_sb[:, it, :], vt_sb[:, it, :],
                 start=(it == 0), stop=(it == NTILE - 1))
        p.wait(nc.vector, "T")
        p.V("tensor_copy", st_sb[:], psS[:])
        p.dma("st", nc.sync, qp_o[h], qp_sb[:], wait_chans=("S",))
        p.dma("st", nc.sync, kpf_o[h], kpf_sb[:])
        p.dma("st", nc.sync, kpt_o[h].rearrange("i p m -> p i m"), kpt_sb[:])
        p.dma("st", nc.sync, vt_o[h].rearrange("i p m -> p i m"), vt_sb[:],
              wait_chans=("V",))
        p.dma("st", nc.sync, st_o[h], st_sb[:])
        p.wait(nc.tensor, "st")
        p.wait(nc.scalar, "st")
        p.wait(nc.vector, "st")
    p.wait(nc.sync, "st")
    return p


def build_l3_a2():
    p = Prog()
    nc = p.nc
    x = p.din("x_fm", [D, TPC])
    qp_d = p.din("qp", [16, 64, TPC], BF16)
    kpf_d = p.din("kpf", [16, 64, TPC], BF16)
    kpt_d = p.din("kpt", [16, NTILE, 128, 64], BF16)
    vt_d = p.din("vt", [16, NTILE, 128, 65], BF16)
    prev_d = p.din("prev", [3, 16, 64, 65])
    wout_d = p.din("w_out", [D, D])
    bout_d = p.din("bcol", [128, DC])
    xo = p.dout("x_out", [D, TPC])

    x_sb = p.sbuf("x_sb", [128, DC, TPC], F32)
    qp_sb = p.sbuf("qp_sb", [64, TPC], BF16)
    kpf_sb = p.sbuf("kpf_sb", [64, TPC], BF16)
    kpt_sb = p.sbuf("kpt_sb", [128, NTILE, 64], BF16)
    vt_sb = p.sbuf("vt_sb", [128, NTILE, 65], BF16)
    prev_sb = p.sbuf("prev_sb", [64, 3, 16, 65], F32)
    st32 = p.sbuf("st32", [64, 65], F32)
    st_bf = p.sbuf("st_bf", [64, 65], BF16)
    ao_sb = p.sbuf("ao_sb", [128, DC, TPC], BF16)
    g_bf = p.sbuf("g_bf", [128, 128], BF16)
    num_sb = p.sbuf("num_sb", [64, 128], F32)
    den_r = p.sbuf("den_r", [1, 128], F32)
    tri = p.sbuf("tri", [128, 128], BF16)
    b_sb = p.sbuf("b_sb", [128, DC], F32)
    wbuf = p.sbuf("wbuf", [128, 2, DC, 128], BF16)
    scr5 = p.sbuf("scr5", [128, 512], F32)
    psG = p.psum("psG", [128, 2, 128])
    psA = p.psum("psA", [65, 128])
    psU = p.psum("psU", [64, 65])
    psB = p.psum("psB", [64, 128])
    psQ = p.psum("psQ", [128, 512])

    p.dma("ld", nc.sync, x_sb[:], x.rearrange("(c p) t -> p c t", p=128))
    p.dma("ld", nc.sync, prev_sb[:], prev_d.rearrange("s h d m -> d s h m"))
    p.dma("ld", nc.sync, b_sb[:], bout_d)
    for e in (nc.tensor, nc.scalar, nc.vector, nc.gpsimd):
        p.wait(e, "ld")
    ident, _, ones, ones_row = make_consts(p)
    iota_p2 = p.sbuf("iota_p2", [128, 1], F32)
    iota_f2 = p.sbuf("iota_f2", [128, 128], F32)
    p.G("iota", iota_p2[:], pattern=[[0, 1]], base=0, channel_multiplier=1,
        allow_small_or_imprecise_dtypes=True)
    p.G("iota", iota_f2[:], pattern=[[1, 128]], base=0, channel_multiplier=0,
        allow_small_or_imprecise_dtypes=True)
    p.wait(nc.vector, "G")
    p.V("tensor_scalar", tri[:], iota_f2[:], iota_p2[:], None, op0=ALU.is_ge)
    eps = p.cvec(1e-6, 1)

    for h in range(16):
        p.dma("lh", nc.sync, qp_sb[:], qp_d[h], wait_chans=("T",))
        p.dma("lh", nc.sync, kpf_sb[:], kpf_d[h])
        p.dma("lh", nc.sync, kpt_sb[:], kpt_d[h].rearrange("i p m -> p i m"))
        p.dma("lh", nc.sync, vt_sb[:], vt_d[h].rearrange("i p m -> p i m"))
        p.wait(nc.vector, "lh")
        hsl = [prev_sb[:, s, h, :] for s in range(3)]
        p.V("tensor_add", st32[:], hsl[0], hsl[1])
        p.V("tensor_add", st32[:], st32[:], hsl[2])
        p.wait(nc.vector, "lh")
        for it in range(NTILE):
            tsl = slice(it * 128, (it + 1) * 128)
            p.wait(nc.vector, "T")
            p.V("tensor_copy", st_bf[:], st32[:])
            p.wait(nc.tensor, "V")
            p.wait(nc.tensor, "S")
            p.mm(psG[:, 0, :], kpf_sb[:, tsl], qp_sb[:, tsl], start=True, stop=True)
            p.wait(nc.vector, "T")
            p.V("tensor_mul", g_bf[:], psG[:, 0, 0:128], tri[:])
            p.wait(nc.tensor, "V")
            p.mm(psA[:], vt_sb[:, it, :], g_bf[:], start=True, stop=False)
            p.mm(psA[:], st_bf[:], qp_sb[:, tsl], start=False, stop=True)
            p.wait(nc.vector, "T")
            p.V("tensor_copy", num_sb[:], psA[0:64, :])
            p.V("tensor_scalar", den_r[:], psA[64:65, :], eps[0:1, 0:1], None,
                op0=ALU.add)
            p.V("reciprocal", den_r[:], den_r[:])
            p.wait(nc.tensor, "V")
            p.mm(psB[:], ones_row[0:1, 0:64], den_r[:], start=True, stop=True)
            p.wait(nc.vector, "T")
            hb = (h % 2) * 64
            p.V("tensor_mul", ao_sb[hb:hb + 64, h // 2, tsl], num_sb[:], psB[:, :])
            p.mm(psU[:], kpt_sb[:, it, :], vt_sb[:, it, :], start=True, stop=True)
            p.wait(nc.vector, "T")
            p.V("tensor_add", st32[:], st32[:], psU[:, :])

    def evac(oc, tg, ps):
        tsl = slice(tg * 512, (tg + 1) * 512)
        p.wait(nc.vector, "T")
        p.V("tensor_scalar", scr5[:, :], ps, b_sb[:, oc:oc + 1], None, op0=ALU.add)
        p.V("tensor_add", x_sb[:, oc, tsl], x_sb[:, oc, tsl], scr5[:, :])
    emit_gemm_stream(p, wout_d, D, D, ao_sb[:], TPC, psQ[:], evac, wbuf, "wo")
    p.dma("st", nc.sync, xo.rearrange("(c p) t -> p c t", p=128), x_sb[:],
          wait_chans=("V",))
    p.wait(nc.sync, "st")
    return p


def run_l3_attn(x_shards, inputs, cache):
    if "l3a1" not in cache:
        cache["l3a1"] = build_l3_a1()
    p1 = cache["l3a1"]
    bq = np.ascontiguousarray(inputs["L3_attn_b_qkv"].reshape(24, 128).T)
    in_maps = [{
        "x_fm": x_shards[c], "w_qkv": inputs["L3_attn_w_qkv"],
        "bqkv": bq, "proj": inputs["L3_attn_proj"],
    } for c in range(NCORE)]
    res1 = run_bass_kernel_spmd(p1.nc, in_maps, core_ids=list(range(NCORE)))
    parts = [np.asarray(r["st"]) for r in res1.results]

    if "l3a2" not in cache:
        cache["l3a2"] = build_l3_a2()
    p2 = cache["l3a2"]
    in_maps = []
    for c in range(NCORE):
        b, q = c // 4, c % 4
        prev = np.zeros((3, 16, 64, 65), dtype=np.float32)
        for j2 in range(q):
            prev[j2] = parts[b * 4 + j2]
        r = res1.results[c]
        in_maps.append({
            "x_fm": x_shards[c],
            "qp": np.asarray(r["qp"]), "kpf": np.asarray(r["kpf"]),
            "kpt": np.asarray(r["kpt"]), "vt": np.asarray(r["vt"]),
            "prev": prev,
            "w_out": inputs["L3_attn_w_out"],
            "bcol": np.ascontiguousarray(inputs["L3_attn_b_out"].reshape(DC, 128).T),
        })
    res2 = run_bass_kernel_spmd(p2.nc, in_maps, core_ids=list(range(NCORE)))
    return [np.asarray(r["x_out"]) for r in res2.results]


EXTA = 2688
XCA = EXTA + 1


ILA_GRPS = [(g * 512, 512) for g in range(5)] + [(2560, 128)]


def build_l1_ila():
    p = Prog()
    nc = p.nc
    x_d = p.din("x_ext", [D, XCA])
    wqkv_d = p.din("wqkv", [D, 1536])
    qs_d = p.din("qs", [64, 1])
    ks_d = p.din("ks", [64, 1])
    cosS_d = p.din("cosS", [64, EXTA], BF16)
    sinS_d = p.din("sinS", [64, EXTA], BF16)
    cosI_d = p.din("cosI", [64, EXTA], BF16)
    sinI_d = p.din("sinI", [64, EXTA], BF16)
    mask_d = p.din("maskb", [16, 128, 640], BF16)
    wo_d = p.din("wo", [512, D])
    rt_d = p.din("rotT", [64, 64], BF16)
    xo = p.dout("x_out", [D, TPC])

    A1 = p.sbuf("A1", [128, 8 * XCA], BF16)
    qn = p.sbuf("qn", [64, 8, EXTA], BF16)
    kn = p.sbuf("kn", [64, 8, EXTA], BF16)
    vp = p.sbuf("vp", [128, 4, EXTA], BF16)
    vt = p.sbuf("vt", [128, 21, 8, 65], BF16)
    wbuf = p.sbuf("wbuf", [128, 2, DC, 128], BF16)
    ssq_row = p.sbuf("ssq_row", [1, EXTA], F32)
    r_row = p.sbuf("r_row", [1, EXTA], F32)
    scrS = p.sbuf("scrS", [64, EXTA], BF16)
    p_bf = p.sbuf("p_bf", [128, 640], BF16)
    pt_bf = p.sbuf("pt_bf", [128, 128], BF16)
    num_sb = p.sbuf("num_sb", [64, 128], F32)
    den_r = p.sbuf("den_r", [1, 128], F32)
    xrow = p.sbuf("xrow", [128, 512], F32)
    qs_sb = p.sbuf("qs_sb", [64, 2], F32)
    ident_bf = p.sbuf("ident_bf2", [128, 128], BF16)
    psS = p.psum("psS", [128, 768])
    psQ = p.psum("psQ", [128, 512])
    psA = p.psum("psA", [65, 128])
    psB = p.psum("psB", [64, 128])
    psT = p.psum("psT2", [128, 64], BF16)
    psTP = p.psum("psTP", [128, 128], BF16)
    psR = p.psum("psR", [64, 512])

    xq = A1[:, 0:8 * XCA].rearrange("p (c t) -> p c t", c=8)
    masks = A1[:, 0:16 * 640].rearrange("p (i k) -> p i k", i=16)
    o_fm = A1[:, 16 * 640:16 * 640 + 4 * TPC].rearrange("p (c t) -> p c t", c=4)
    tabs = vp[0:64, :, :]

    p.dma("ld", nc.gpsimd, xq, x_d.rearrange("(c p) t -> p c t", p=128))
    p.dma("ld", nc.sync, qs_sb[:, 0:1], qs_d)
    p.dma("ld", nc.sync, qs_sb[:, 1:2], ks_d)
    rt_bf = p.sbuf("rt_bf", [64, 64], BF16)
    p.dma("ld", nc.sync, rt_bf[:], rt_d)
    for e in (nc.tensor, nc.scalar, nc.vector, nc.gpsimd):
        p.wait(e, "ld")
    ident, _, ones, ones_row = make_consts(p)
    p.V("tensor_copy", ident_bf[:], ident[:])

    def act_fn(kc, tsl):
        off = 1 if kc < 4 else 0
        return xq[:, kc, off + tsl.start: off + tsl.stop]
    for part3 in range(3):
        wsl = wqkv_d[:, part3 * 512:(part3 + 1) * 512]
        if part3 < 2:
            dst = qn if part3 == 0 else kn
            for oc in range(8):
                p.dma("w", nc.gpsimd, wbuf[:, oc % 2, :, 0:64],
                      wsl.rearrange("(kc p) o -> p kc o", p=128)
                      [:, :, oc * 64:(oc + 1) * 64], wait_chans=("T",))
                p.wait(nc.tensor, "w")
                for g0, gw in ILA_GRPS:
                    tsl = slice(g0, g0 + gw)
                    p.wait(nc.tensor, "V")
                    for kc in range(DC):
                        p.mm(psR[:, 0:gw], wbuf[:, oc % 2, kc, 0:64], act_fn(kc, tsl),
                             start=(kc == 0), stop=(kc == DC - 1))
                    p.wait(nc.vector, "T")
                    p.V("tensor_copy", dst[:, oc, tsl], psR[:, 0:gw])
        else:
            for oc in range(4):
                p.dma("w", nc.gpsimd, wbuf[:, oc % 2],
                      wsl.rearrange("(kc p) o -> p kc o", p=128)
                      [:, :, oc * 128:(oc + 1) * 128], wait_chans=("T",))
                p.wait(nc.tensor, "w")
                for g0, gw in ILA_GRPS:
                    tsl = slice(g0, g0 + gw)
                    p.wait(nc.tensor, "V")
                    for kc in range(DC):
                        p.mm(psQ[:, 0:gw], wbuf[:, oc % 2, kc, :], act_fn(kc, tsl),
                             start=(kc == 0), stop=(kc == DC - 1))
                    p.wait(nc.vector, "T")
                    p.V("tensor_copy", vp[:, oc, tsl], psQ[:, 0:gw])

    for h in range(8):
        hb = (h % 2) * 64
        for it in range(21):
            tsl = slice(it * 128, (it + 1) * 128)
            p.wait(nc.tensor, "V")
            p.tr(psT[:, 0:64], vp[hb:hb + 64, h // 2, tsl],
                 ident_bf[hb:hb + 64, hb:hb + 64])
            p.wait(nc.vector, "T")
            p.V("tensor_copy", vt[:, it, h, 0:64], psT[:, 0:64])
    p.V("memset", vt[:, :, :, 64:65], 1.0)

    p.dma("tb", nc.sync, tabs[:, 0, :], cosS_d, wait_chans=("V",))
    p.dma("tb", nc.sync, tabs[:, 1, :], sinS_d)
    p.dma("tb", nc.sync, tabs[:, 2, :], cosI_d)
    p.dma("tb", nc.sync, tabs[:, 3, :], sinI_d)
    p.wait(nc.vector, "tb")
    p.wait(nc.scalar, "tb")

    sq_ones = p.sbuf("sq1b", [64, 1], BF16)
    p.V("memset", sq_ones[:], 1.0)
    clipv = p.cvec(1e-12, 1)
    for which, dst, scol, ctab, stab in ((0, qn, qs_sb[:, 0:1], 0, 1),
                                         (1, kn, qs_sb[:, 1:2], 2, 3)):
        for h in range(8):
            hv = dst[:, h, :]
            p.wait(nc.scalar, "T")
            p.S("activation", out=scrS[:, :], in_=hv, func=AF.Square)
            p.wait(nc.tensor, "S")
            for g0, gw in ILA_GRPS:
                gsl = slice(g0, g0 + gw)
                p.mm(psR[0:1, 0:gw], sq_ones[:], scrS[:, gsl], start=True,
                     stop=True, inc=True)
                p.wait(nc.vector, "T")
                p.V("tensor_copy", ssq_row[0:1, gsl], psR[0:1, 0:gw])
                p.wait(nc.tensor, "V")
            p.wait(nc.scalar, "V")
            p.S("activation", out=r_row[:], in_=ssq_row[:], func=AF.Sqrt)
            p.wait(nc.vector, "S")
            p.V("tensor_scalar", r_row[:], r_row[:], clipv[0:1, 0:1], None, op0=ALU.max)
            p.V("reciprocal", r_row[:], r_row[:])
            p.wait(nc.tensor, "V")
            for g0, gw in ILA_GRPS:
                gsl = slice(g0, g0 + gw)
                p.mm(psR[:, 0:gw], ones_row[0:1, 0:64], r_row[0:1, gsl], start=True,
                     stop=True, inc=True)
                p.wait(nc.vector, "T")
                p.V("tensor_mul", hv[:, gsl], hv[:, gsl], psR[:, 0:gw])
                p.wait(nc.tensor, "V")
            p.V("tensor_scalar", hv[:, :], hv[:, :], scol, None, op0=ALU.mult)
            ct = tabs[:, ctab, :]
            st = tabs[:, stab, :]
            for g0, gw in ILA_GRPS:
                gsl = slice(g0, g0 + gw)
                p.wait(nc.tensor, "V")
                p.mm(psR[:, 0:gw], rt_bf[:], hv[:, gsl], start=True, stop=True,
                     inc=True)
                p.wait(nc.vector, "T")
                p.V("tensor_mul", scrS[:, gsl], psR[:, 0:gw], st[:, gsl])
                p.V("tensor_mul", hv[:, gsl], hv[:, gsl], ct[:, gsl])
                p.V("tensor_add", hv[:, gsl], hv[:, gsl], scrS[:, gsl])

    p.dma("mk", nc.sync, masks, mask_d.rearrange("i p k -> p i k"), wait_chans=("V",))
    for e in (nc.vector, nc.scalar, nc.tensor):
        p.wait(e, "mk")

    sc8 = p.cvec(8.0)
    b8 = p.sbuf("b8col", [128, 1], F32)
    mcol = p.sbuf("mcol", [128, 1], F32)
    for h in range(8):
        hb = (h % 2) * 64
        for i in range(16):
            qsl = slice(640 + i * 128, 640 + (i + 1) * 128)
            k0 = i * 128 + 128
            p.wait(nc.tensor, "V")
            p.wait(nc.tensor, "S")
            p.mm(psS[:, 0:512], qn[:, h, qsl], kn[:, h, k0:k0 + 512],
                 start=True, stop=True, inc=False)
            p.mm(psS[:, 512:640], qn[:, h, qsl], kn[:, h, k0 + 512:k0 + 640],
                 start=True, stop=True)
            p.wait(nc.vector, "T")
            p.V("tensor_add", psS[:, 0:640], psS[:, 0:640], masks[:, i, :])
            p.V("tensor_reduce", mcol[:], psS[:, 0:640], axis=AX.X, op=ALU.max,
                negate=True)
            p.wait(nc.vector, "S")
            p.V("tensor_scalar_mul", b8[:], mcol[:], 8.0)
            p.wait(nc.scalar, "V")
            p.S("activation", out=p_bf[:, :], in_=psS[:, 0:640], func=AF.Exp,
                scale=sc8[:], bias=b8[:])
            for kb in range(5):
                p.wait(nc.tensor, "S" if kb == 0 else "V")
                p.tr(psTP[:, :], p_bf[:, kb * 128:(kb + 1) * 128], ident_bf[:])
                p.wait(nc.vector, "T")
                p.V("tensor_copy", pt_bf[:, :], psTP[:, :])
                p.wait(nc.tensor, "V")
                p.mm(psA[:], vt[:, i + 1 + kb, h, :], pt_bf[:, 0:128],
                     start=(kb == 0), stop=(kb == 4))
            p.wait(nc.vector, "T")
            p.V("tensor_copy", num_sb[:], psA[0:64, :])
            p.V("reciprocal", den_r[:], psA[64:65, :])
            p.wait(nc.tensor, "V")
            p.mm(psB[:], ones_row[0:1, 0:64], den_r[:], start=True, stop=True)
            p.wait(nc.vector, "T")
            p.V("tensor_mul", o_fm[hb:hb + 64, h // 2, i * 128:(i + 1) * 128],
                num_sb[:], psB[:, :])

    for oc in range(DC):
        p.dma("w2", nc.gpsimd, wbuf[:, oc % 2, 0:4],
              wo_d.rearrange("(kc p) o -> p kc o", p=128)[:, :, oc * 128:(oc + 1) * 128],
              wait_chans=("T",))
        p.wait(nc.tensor, "w2")
        for tg in range(4):
            tsl = slice(tg * 512, (tg + 1) * 512)
            p.dma("xl", nc.sync, xrow[:],
                  x_d.rearrange("(c p) t -> p c t", p=128)[:, oc, 641 + tg * 512:
                                                           641 + (tg + 1) * 512],
                  wait_chans=("V",))
            p.wait(nc.tensor, "V")
            for kc in range(4):
                p.mm(psQ[:], wbuf[:, oc % 2, kc, :], o_fm[:, kc, tsl],
                     start=(kc == 0), stop=(kc == 3))
            p.wait(nc.vector, "T")
            p.wait(nc.vector, "xl")
            p.V("tensor_add", xrow[:, :], xrow[:, :], psQ[:, 0:512])
            p.dma("st", nc.sync, xo.rearrange("(c p) t -> p c t", p=128)[:, oc, tsl],
                  xrow[:], wait_chans=("V",))
            p.wait(nc.vector, "st")
    p.wait(nc.sync, "st")
    return p


HIDP = 2816


def build_l1_ilb():
    p = Prog()
    nc = p.nc
    x_d = p.din("x_ext", [D, TPC + 1])
    w1a_d = p.din("w1a", [D, HIDP])
    w1g_d = p.din("w1g", [D, HIDP])
    w2_d = p.din("w2p", [HIDP, D])
    b1a_d = p.din("b1a", [128, 22])
    b1g_d = p.din("b1g", [128, 22])
    b2_d = p.din("b2c", [128, DC])
    lg_d = p.din("lgc", [128, DC])
    lb_d = p.din("lbc", [128, DC])
    xo = p.dout("x_out", [D, TPC])

    x_sb = p.sbuf("x_sb", [128, DC, TPC + 1], F32)
    lnf = p.sbuf("lnf", [128, DC, TPC], BF16)
    geglu = p.sbuf("geglu", [128, 22, 512], BF16)
    gel_s = p.sbuf("gel_s", [128, 512], F32)
    a_s = p.sbuf("a_s", [128, 512], F32)
    arena = p.sbuf("arena", [128, 3 * TPC], F32)
    a_row = p.sbuf("a_row", [1, TPC], F32)
    b_row = p.sbuf("b_row", [1, TPC], F32)
    wbuf = p.sbuf("wbuf", [128, 2, DC, 128], BF16)
    w2buf = p.sbuf("w2buf", [128, 2, 22, 128], BF16)
    bcols = p.sbuf("bcols", [128, 22 * 2 + DC * 3], F32)
    scr5 = p.sbuf("scr5", [128, 512], F32)
    psP = p.psum("psP", [128, 2, 512])
    psQ = p.psum("psQ", [128, 512])
    psH = p.psum("psH", [128, 2, 512])

    b1a = bcols[:, 0:22]
    b1g = bcols[:, 22:44]
    b2c = bcols[:, 44:44 + DC]
    lgc = bcols[:, 44 + DC:44 + 2 * DC]
    lbc = bcols[:, 44 + 2 * DC:44 + 3 * DC]

    p.dma("ld", nc.sync, x_sb[:], x_d.rearrange("(c p) t -> p c t", p=128))
    p.dma("ld", nc.sync, b1a, b1a_d)
    p.dma("ld", nc.sync, b1g, b1g_d)
    p.dma("ld", nc.sync, b2c, b2_d)
    p.dma("ld", nc.sync, lgc, lg_d)
    p.dma("ld", nc.sync, lbc, lb_d)
    for e in (nc.tensor, nc.scalar, nc.vector, nc.gpsimd):
        p.wait(e, "ld")
    ident, _, ones, ones_row = make_consts(p)

    def hview(c, tsl):
        off = 1 if c < 4 else 0
        return x_sb[:, c, off + tsl.start: off + tsl.stop]

    scr = arena[:, 2 * TPC:3 * TPC]
    mu_row = arena[0:1, 0:TPC]
    ps_a, ps_b = psP[0:1, 0, :], psP[0:1, 1, :]
    p.wait(nc.tensor, "V")
    for g in range(4):
        sl = slice(g * 512, (g + 1) * 512)
        for c in range(DC):
            p.mm(ps_a, ones[:], hview(c, sl), start=(c == 0), stop=(c == DC - 1))
        for c in range(DC):
            p.wait(nc.scalar, "T")
            p.S("activation", out=scr[:, 0:512], in_=hview(c, sl), func=AF.Square)
            p.wait(nc.tensor, "S")
            p.mm(ps_b, ones[:], scr[:, 0:512], start=(c == 0), stop=(c == DC - 1),
                 inc=True)
        p.wait(nc.vector, "T")
        p.V("tensor_scalar_mul", mu_row[0:1, sl], ps_a, 1.0 / D)
        p.V("tensor_scalar_mul", b_row[:, sl], ps_b, 1.0 / D)
        if g < 3:
            p.wait(nc.tensor, "V")
    p.wait(nc.scalar, "V")
    p.S("activation", out=scr[0:1, 0:TPC], in_=mu_row[0:1, 0:TPC], func=AF.Square)
    p.wait(nc.vector, "S")
    p.V("tensor_sub", b_row[:], b_row[:], scr[0:1, 0:TPC])
    eps = p.cvec(1e-5, 1)
    p.wait(nc.scalar, "V")
    p.S("activation", out=scr[0:1, 0:TPC], in_=b_row[:], func=AF.Sqrt,
        bias=eps[0:1, 0:1])
    p.wait(nc.vector, "S")
    p.V("reciprocal", a_row[:], scr[0:1, 0:TPC])
    p.V("scalar_tensor_tensor", b_row[:], mu_row[0:1, 0:TPC], -1.0, a_row[:],
        op0=ALU.mult, op1=ALU.mult)
    a_bc = arena[:, 0 * TPC:1 * TPC]
    b_bc = arena[:, 1 * TPC:2 * TPC]
    emit_bcast_rows(p, a_row[:], a_bc, ones_row, ps=psP[:, 0, :], tag="a")
    emit_bcast_rows(p, b_row[:], b_bc, ones_row, ps=psP[:, 0, :], tag="b")
    full = slice(0, TPC)
    for c in range(DC):
        p.V("tensor_mul", scr[:, 0:TPC], hview(c, full), a_bc)
        p.V("tensor_add", scr[:, 0:TPC], scr[:, 0:TPC], b_bc)
        p.V("tensor_scalar", lnf[:, c, :], scr[:, 0:TPC], lgc[:, c:c + 1],
            lbc[:, c:c + 1], op0=ALU.mult, op1=ALU.add)

    for tg in range(4):
        tsl = slice(tg * 512, (tg + 1) * 512)
        for fc in range(22):
            para = fc % 2
            p.dma("wa", nc.gpsimd, wbuf[:, para],
                  w1a_d.rearrange("(kc p) o -> p kc o", p=128)
                  [:, :, fc * 128:(fc + 1) * 128], wait_chans=("T",))
            p.dma("wg", nc.gpsimd, w2buf[:, para, 0:DC],
                  w1g_d.rearrange("(kc p) o -> p kc o", p=128)
                  [:, :, fc * 128:(fc + 1) * 128], wait_chans=("T",))
            p.wait(nc.tensor, "wa")
            p.wait(nc.tensor, "wg")
            p.wait(nc.tensor, "S")
            p.wait(nc.tensor, "V")
            for kc in range(DC):
                p.mm(psH[:, 0, :], wbuf[:, para, kc, :], lnf[:, kc, tsl],
                     start=(kc == 0), stop=(kc == DC - 1))
            for kc in range(DC):
                p.mm(psH[:, 1, :], w2buf[:, para, kc, :], lnf[:, kc, tsl],
                     start=(kc == 0), stop=(kc == DC - 1), inc=(kc == DC - 1))
            p.wait(nc.scalar, "T")
            p.S("activation", out=gel_s[:, :], in_=psH[:, 1, :], func=AF.Gelu,
                bias=b1g[:, fc:fc + 1])
            p.wait(nc.vector, "T")
            p.wait(nc.vector, "S")
            p.V("tensor_scalar", a_s[:, :], psH[:, 0, :], b1a[:, fc:fc + 1], None,
                op0=ALU.add)
            p.V("tensor_mul", geglu[:, fc, :], a_s[:, :], gel_s[:, :])
        for oc in range(DC):
            paro = oc % 2
            p.dma("w2", nc.gpsimd, w2buf[:, paro],
                  w2_d.rearrange("(kc p) o -> p kc o", p=128)
                  [:, :, oc * 128:(oc + 1) * 128], wait_chans=("T",))
            p.wait(nc.tensor, "w2")
            p.wait(nc.tensor, "V")
            for kc in range(22):
                p.mm(psQ[:], w2buf[:, paro, kc, :], geglu[:, kc, :],
                     start=(kc == 0), stop=(kc == 21))
            p.wait(nc.vector, "T")
            p.V("tensor_scalar", scr5[:, :], psQ[:], b2c[:, oc:oc + 1], None,
                op0=ALU.add)
            p.V("tensor_add", x_sb[:, oc, 1 + tsl.start:1 + tsl.stop],
                x_sb[:, oc, 1 + tsl.start:1 + tsl.stop], scr5[:, :])
    p.dma("st", nc.sync, xo.rearrange("(c p) t -> p c t", p=128), x_sb[:, :, 1:],
          wait_chans=("V",))
    p.wait(nc.sync, "st")
    return p


def build_ln_f32():
    p = Prog()
    nc = p.nc
    x = p.din("x_fm", [D, TPC])
    lo = p.dout("ln_out", [D, TPC])
    x_sb = p.sbuf("x_sb", [128, DC, TPC], F32)
    arena = p.sbuf("arena", [128, 3 * TPC], F32)
    a_row = p.sbuf("a_row", [1, TPC], F32)
    b_row = p.sbuf("b_row", [1, TPC], F32)
    ones_row = p.sbuf("c_onesrow", [1, 128], F32)
    psP = p.psum("psP", [128, 2, 512])
    p.dma("ld", nc.sync, x_sb[:], x.rearrange("(c p) t -> p c t", p=128))
    for e in (nc.tensor, nc.scalar, nc.vector, nc.gpsimd):
        p.wait(e, "ld")
    p.V("memset", ones_row[:], 1.0)
    scr = arena[:, 2 * TPC:3 * TPC]
    emit_ln_stats(p, x_sb, TPC, a_row, b_row, scr,
                  ps_a=psP[0:1, 0, :], ps_b=psP[0:1, 1, :],
                  mu_row=arena[0:1, 0:TPC])
    a_bc = arena[:, 0 * TPC:1 * TPC]
    b_bc = arena[:, 1 * TPC:2 * TPC]
    emit_bcast_rows(p, a_row[:], a_bc, ones_row, ps=psP[:, 0, :], tag="a")
    emit_bcast_rows(p, b_row[:], b_bc, ones_row, ps=psP[:, 0, :], tag="b")
    for c in range(DC):
        p.V("tensor_mul", x_sb[:, c, :], x_sb[:, c, :], a_bc)
        p.V("tensor_add", x_sb[:, c, :], x_sb[:, c, :], b_bc)
    p.dma("st", nc.sync, lo.rearrange("(c p) t -> p c t", p=128), x_sb[:],
          wait_chans=("V",))
    p.wait(nc.sync, "st")
    return p


def build_axpy():
    p = Prog()
    nc = p.nc
    x = p.din("x_fm", [D, TPC])
    y = p.din("y32", [D, TPC])
    xo = p.dout("x_out", [D, TPC])
    x_sb = p.sbuf("x_sb", [128, DC, TPC], F32)
    y_sb = p.sbuf("y_sb", [128, DC, TPC], F32)
    p.dma("ld", nc.sync, x_sb[:], x.rearrange("(c p) t -> p c t", p=128))
    p.dma("ld", nc.sync, y_sb[:], y.rearrange("(c p) t -> p c t", p=128))
    p.wait(nc.vector, "ld")
    for c in range(DC):
        p.V("scalar_tensor_tensor", x_sb[:, c, :], y_sb[:, c, :], 0.5, x_sb[:, c, :],
            op0=ALU.mult, op1=ALU.add)
    p.dma("st", nc.sync, xo.rearrange("(c p) t -> p c t", p=128), x_sb[:],
          wait_chans=("V",))
    p.wait(nc.sync, "st")
    return p


def _xpos_tables(ext_lo, n=8192):
    DH = 64
    W2 = 256
    t = np.arange(ext_lo, ext_lo + EXTA, dtype=np.float32)
    t = np.maximum(t, 0.0)
    half = np.arange(0, DH, 2, dtype=np.float32)
    inv = 1.0 / 10000.0 ** (half / DH)
    fr = t[:, None] * inv[None, :]
    fr = np.concatenate([fr, fr], 1)
    base = (half + 0.4 * DH) / (1.4 * DH)
    power = (t - n // 2) / float(W2)
    sc = base[None, :] ** power[:, None]
    sc = np.concatenate([sc, sc], 1)
    cos, sin = np.cos(fr), np.sin(fr)
    return (cos * sc).T, (sin * sc).T, (cos / sc).T, (sin / sc).T


def run_l1_ffn(x_shards, inputs, cache, call_tag=""):
    import ml_dtypes
    bf16 = ml_dtypes.bfloat16
    if "ln_f32" not in cache:
        cache["ln_f32"] = build_ln_f32()
    pL = cache["ln_f32"]
    res = run_bass_kernel_spmd(pL.nc, [{"x_fm": x_shards[c]} for c in range(NCORE)],
                               core_ids=list(range(NCORE)))
    y_b = [np.concatenate([np.asarray(res.results[b * 4 + q]["ln_out"]) for q in range(4)],
                          axis=1) for b in range(2)]

    if "il_a" not in cache:
        cache["il_a"] = build_l1_ila()
    if "il_b" not in cache:
        cache["il_b"] = build_l1_ilb()
    pA, pB = cache["il_a"], cache["il_b"]

    for l in range(2):
        wqkv = inputs["L1_ffn_wqkv"][l]
        qs = inputs["L1_ffn_q_scale"][l].reshape(64, 1)
        ks = inputs["L1_ffn_k_scale"][l].reshape(64, 1)
        wo = inputs["L1_ffn_wo"][l]
        in_maps = []
        for c in range(NCORE):
            b, q = c // 4, c % 4
            G = q * TPC
            xe = np.zeros((D, XCA), dtype=np.float32)
            lo = G - 641
            s0 = max(0, -lo)
            xe[:, s0:] = y_b[b][:, max(lo, 0):G + TPC]
            cS, sS, cI, sI = _xpos_tables(G - 640)
            mb = np.zeros((16, 128, 640), dtype=np.float32)
            pp = np.arange(128)[:, None]
            jj = np.arange(640)[None, :]
            for i in range(16):
                valid = (jj >= pp) & (jj <= pp + 512) & (G + 128 * i - 512 + jj >= 0)
                mb[i] = np.where(valid, 0.0, -1e30)
            rotT = np.zeros((64, 64), dtype=np.float32)
            for o in range(32):
                rotT[32 + o, o] = -1.0
                rotT[o, 32 + o] = 1.0
            in_maps.append({
                "x_ext": xe, "wqkv": wqkv, "qs": qs, "ks": ks,
                "cosS": cS.astype(bf16), "sinS": sS.astype(bf16),
                "cosI": cI.astype(bf16), "sinI": sI.astype(bf16),
                "maskb": mb.astype(bf16), "wo": wo, "rotT": rotT.astype(bf16),
            })
        resA = run_bass_kernel_spmd(pA.nc, in_maps, core_ids=list(range(NCORE)))
        for b in range(2):
            y_b[b] = np.concatenate(
                [np.asarray(resA.results[b * 4 + q]["x_out"]) for q in range(4)], axis=1)
        w1 = inputs["L1_ffn_w1"][l]
        w1a = np.zeros((D, HIDP), dtype=np.float32); w1a[:, :2730] = w1[:, :2730]
        w1g = np.zeros((D, HIDP), dtype=np.float32); w1g[:, :2730] = w1[:, 2730:]
        w2p = np.zeros((HIDP, D), dtype=np.float32)
        w2p[:2730] = inputs["L1_ffn_w2"][l]
        b1 = inputs["L1_ffn_b1"][l]
        b1a = np.zeros((128, 22), dtype=np.float32)
        b1g = np.zeros((128, 22), dtype=np.float32)
        b1a.T.reshape(-1)[:2730] = b1[:2730]
        b1g.T.reshape(-1)[:2730] = b1[2730:]
        in_maps = []
        for c in range(NCORE):
            b, q = c // 4, c % 4
            G = q * TPC
            xe = np.zeros((D, TPC + 1), dtype=np.float32)
            lo = G - 1
            s0 = max(0, -lo)
            xe[:, s0:] = y_b[b][:, max(lo, 0):G + TPC]
            in_maps.append({
                "x_ext": xe, "w1a": w1a, "w1g": w1g, "w2p": w2p,
                "b1a": b1a, "b1g": b1g,
                "b2c": np.ascontiguousarray(inputs["L1_ffn_b2"][l].reshape(DC, 128).T),
                "lgc": np.ascontiguousarray(inputs["L1_ffn_ln_g"][l].reshape(DC, 128).T),
                "lbc": np.ascontiguousarray(inputs["L1_ffn_ln_b"][l].reshape(DC, 128).T),
            })
        resB = run_bass_kernel_spmd(pB.nc, in_maps, core_ids=list(range(NCORE)))
        for b in range(2):
            y_b[b] = np.concatenate(
                [np.asarray(resB.results[b * 4 + q]["x_out"]) for q in range(4)], axis=1)

    if "axpy" not in cache:
        cache["axpy"] = build_axpy()
    pX = cache["axpy"]
    in_maps = []
    for c in range(NCORE):
        b, q = c // 4, c % 4
        in_maps.append({"x_fm": x_shards[c],
                        "y32": np.ascontiguousarray(y_b[b][:, q * TPC:(q + 1) * TPC])})
    resX = run_bass_kernel_spmd(pX.nc, in_maps, core_ids=list(range(NCORE)))
    return [np.asarray(r["x_out"]) for r in resX.results]


XCL = 2080


def build_l1_attn_main(debug=False):
    p = Prog()
    nc = p.nc
    x_d = p.din("x_fm", [D, TPC])
    ln_d = p.din("ln_ext", [D, XCL])
    ln0_d = p.din("ln0", [D, 1])
    wloc_d = p.din("wlocT", [D, 3 * D])
    bloc_d = p.din("bloc", [128, 24])
    wcls_d = p.din("wclsT", [D, 3 * D])
    bcls_d = p.din("bcls", [128, 24])
    woL_d = p.din("woLT", [D, D])
    boL_d = p.din("boL", [128, DC])
    mask_d = p.din("maskb", [16, 128, 160], BF16)
    xo = p.dout("x_out", [D, TPC])
    po = p.dout("part", [16, 65])
    if debug:
        qp_o = p.dout("qp_o", [DC, 128, TPC], BF16)
        kp_o = p.dout("kp_o", [DC, 128, XCL], BF16)
        vt_o = p.dout("vt_o", [17, 128, 16, 65], BF16)
        o_o = p.dout("o_o", [DC, 128, TPC], BF16)

    lnx = p.sbuf("lnx", [128, DC, XCL], BF16)
    kp = p.sbuf("kp", [128, DC, XCL], BF16)
    vt = p.sbuf("vt", [128, 17, 16, 65], BF16)
    qp = p.sbuf("qp", [128, DC, TPC], BF16)
    o_fm = p.sbuf("o_fm", [128, DC, TPC], BF16)
    masks = p.sbuf("masks", [128, 16, 160], BF16)
    wbuf = p.sbuf("wbuf", [128, 2, DC, 128], BF16)
    bcols = p.sbuf("bcols", [128, 24 * 2 + DC], F32)
    vrow_bf = p.sbuf("vrow_bf", [128, 512], BF16)
    p_bf = p.sbuf("p_bf", [128, 160], BF16)
    pt_bf = p.sbuf("pt_bf", [128, 128], BF16)
    num_sb = p.sbuf("num_sb", [64, 128], F32)
    den_r = p.sbuf("den_r", [1, 128], F32)
    mcol = p.sbuf("mcol", [128, 1], F32)
    b8 = p.sbuf("b8c", [128, 1], F32)
    xrow = p.sbuf("xrow", [128, 512], F32)
    qcls_row = p.sbuf("qcls_row", [1, D], F32)
    qcol = p.sbuf("qcol", [128, DC], BF16)
    se = p.sbuf("se", [128, NTILE, 16], BF16)
    part_sb = p.sbuf("part_sb", [65, 16], F32)
    ident_bf = p.sbuf("ident_bf3", [128, 128], BF16)
    ln0_sb = p.sbuf("ln0_sb", [128, DC], BF16)
    psS = p.psum("psS", [128, 512])
    psA = p.psum("psA", [65, 128])
    psB = p.psum("psB", [64, 128])
    psTP = p.psum("psTP", [128, 128], BF16)
    psQ = p.psum("psQ", [128, 512])
    psC = p.psum("psC", [128, 16])

    p.dma("ld", nc.gpsimd, lnx[:], ln_d.rearrange("(c p) t -> p c t", p=128))
    p.dma("ld", nc.gpsimd, ln0_sb[:], ln0_d.rearrange("(c p) o -> p c o", p=128), allow_slow_non_contiguous=True)
    p.dma("ld", nc.sync, bcols[:, 0:24], bloc_d)
    p.dma("ld", nc.sync, bcols[:, 24:48], bcls_d)
    p.dma("ld", nc.sync, bcols[:, 48:48 + DC], boL_d)
    p.dma("ld", nc.sync, masks[:], mask_d.rearrange("i p k -> p i k"))
    for e in (nc.tensor, nc.scalar, nc.vector, nc.gpsimd):
        p.wait(e, "ld")
    ident, _, ones, ones_row = make_consts(p)
    p.V("tensor_copy", ident_bf[:], ident[:])

    grps_ext = [(0, 512), (512, 512), (1024, 512), (1536, 512), (2048, 32)]
    grps_own = [(g * 512, 512) for g in range(4)]

    def proj_fm(w_d, col0, dst, grps, bofs, name, aoff=0):
        for oc in range(DC):
            p.dma("w" + name, nc.gpsimd, wbuf[:, oc % 2],
                  w_d.rearrange("(kc p) o -> p kc o", p=128)
                  [:, :, col0 + oc * 128:col0 + (oc + 1) * 128], wait_chans=("T",))
            p.wait(nc.tensor, "w" + name)
            for g0, gw in grps:
                p.wait(nc.tensor, "S")
                for kc in range(DC):
                    p.mm(psQ[:, 0:gw], wbuf[:, oc % 2, kc, :],
                         lnx[:, kc, aoff + g0:aoff + g0 + gw],
                         start=(kc == 0), stop=(kc == DC - 1))
                p.wait(nc.scalar, "T")
                p.S("activation", out=dst[:, oc, g0:g0 + gw], in_=psQ[:, 0:gw],
                    func=AF.Identity, bias=bcols[:, bofs + oc:bofs + oc + 1])

    def v_tiles(w_d, grps, bofs, name, ntl, aoff=0):
        for oc in range(DC):
            p.dma("wv" + name, nc.gpsimd, wbuf[:, oc % 2],
                  w_d.rearrange("(kc p) o -> p kc o", p=128)
                  [:, :, 2 * D + oc * 128:2 * D + (oc + 1) * 128], wait_chans=("T",))
            p.wait(nc.tensor, "wv" + name)
            for g0, gw in grps:
                p.wait(nc.tensor, "S")
                p.wait(nc.tensor, "V")
                for kc in range(DC):
                    p.mm(psQ[:, 0:gw], wbuf[:, oc % 2, kc, :],
                         lnx[:, kc, aoff + g0:aoff + g0 + gw],
                         start=(kc == 0), stop=(kc == DC - 1))
                p.wait(nc.scalar, "T")
                p.S("activation", out=vrow_bf[:, 0:gw], in_=psQ[:, 0:gw],
                    func=AF.Identity, bias=bcols[:, bofs + 16 + oc:bofs + 17 + oc])
                nt = (gw + 127) // 128
                for t2 in range(nt):
                    it = (g0 + t2 * 128) // 128
                    tw = min(128, gw - t2 * 128)
                    p.wait(nc.tensor, "S")
                    p.wait(nc.tensor, "V")
                    p.tr(psTP[0:tw, :], vrow_bf[:, t2 * 128:t2 * 128 + tw],
                         ident_bf[:])
                    p.wait(nc.vector, "T")
                    p.V("tensor_copy", vt[0:tw, it, 2 * oc, 0:64], psTP[0:tw, 0:64])
                    p.V("tensor_copy", vt[0:tw, it, 2 * oc + 1, 0:64],
                        psTP[0:tw, 64:128])
        p.V("memset", vt[:, :, :, 64:65], 1.0)

    proj_fm(wloc_d, D, kp, grps_ext, 8, "kl")
    v_tiles(wloc_d, grps_ext, 0, "l", 17)
    proj_fm(wloc_d, 0, qp, grps_own, 0, "ql", aoff=16)

    sc18 = p.cvec(0.125)
    for h in range(16):
        hb = (h % 2) * 64
        for i in range(16):
            p.wait(nc.tensor, "V")
            p.wait(nc.tensor, "S")
            k0 = i * 128
            p.mm(psS[:, 0:160], qp[hb:hb + 64, h // 2, i * 128:(i + 1) * 128],
                 kp[hb:hb + 64, h // 2, k0:k0 + 160], start=True, stop=True)
            p.wait(nc.vector, "T")
            p.V("tensor_add", psS[:, 0:160], psS[:, 0:160], masks[:, i, :])
            p.V("tensor_reduce", mcol[:], psS[:, 0:160], axis=AX.X, op=ALU.max,
                negate=True)
            p.wait(nc.vector, "S")
            p.V("tensor_scalar_mul", b8[:], mcol[:], 0.125)
            p.wait(nc.scalar, "V")
            p.S("activation", out=p_bf[:, :], in_=psS[:, 0:160], func=AF.Exp,
                scale=sc18[:], bias=b8[:])
            for kb in range(2):
                p.wait(nc.tensor, "S" if kb == 0 else "V")
                kw = 128 if kb == 0 else 32
                p.tr(psTP[0:kw, :], p_bf[:, kb * 128:kb * 128 + kw], ident_bf[:])
                p.wait(nc.vector, "T")
                p.V("tensor_copy", pt_bf[0:kw, :], psTP[0:kw, 0:128])
                p.wait(nc.tensor, "V")
                p.mm(psA[:], vt[0:kw, i + kb, h, :], pt_bf[0:kw, 0:128],
                     start=(kb == 0), stop=(kb == 1))
            p.wait(nc.vector, "T")
            p.V("tensor_copy", num_sb[:], psA[0:64, :])
            p.V("reciprocal", den_r[:], psA[64:65, :])
            p.wait(nc.tensor, "V")
            p.mm(psB[:], ones_row[0:1, 0:64], den_r[:], start=True, stop=True)
            p.wait(nc.vector, "T")
            p.V("tensor_mul", o_fm[hb:hb + 64, h // 2, i * 128:(i + 1) * 128],
                num_sb[:], psB[:, :])

    if debug:
        p.dma("dbg", nc.sync, qp_o.rearrange("c p t -> p c t"), qp[:],
              wait_chans=("V", "S"))
        p.dma("dbg", nc.sync, kp_o.rearrange("c p t -> p c t"), kp[:])
        p.dma("dbg", nc.sync, vt_o.rearrange("i p h m -> p i h m"), vt[:])
        p.dma("dbg", nc.sync, o_o.rearrange("c p t -> p c t"), o_fm[:])
        p.wait(nc.sync, "dbg")
    for oc in range(DC):
        p.dma("wo2", nc.gpsimd, wbuf[:, oc % 2],
              woL_d.rearrange("(kc p) o -> p kc o", p=128)
              [:, :, oc * 128:(oc + 1) * 128], wait_chans=("T",))
        p.wait(nc.tensor, "wo2")
        for tg in range(4):
            tsl = slice(tg * 512, (tg + 1) * 512)
            p.dma("xl", nc.sync, xrow[:],
                  x_d.rearrange("(c p) t -> p c t", p=128)[:, oc, tsl],
                  wait_chans=("V",))
            p.wait(nc.tensor, "V")
            for kc in range(DC):
                p.mm(psQ[:], wbuf[:, oc % 2, kc, :], o_fm[:, kc, tsl],
                     start=(kc == 0), stop=(kc == DC - 1))
            p.wait(nc.vector, "T")
            p.wait(nc.vector, "xl")
            p.V("tensor_scalar", scr5f(p)[:, :], psQ[:], bcols[:, 48 + oc:49 + oc],
                None, op0=ALU.add)
            p.V("tensor_add", xrow[:, :], xrow[:, :], scr5f(p)[:, :])
            p.dma("st", nc.sync, xo.rearrange("(c p) t -> p c t", p=128)[:, oc, tsl],
                  xrow[:], wait_chans=("V",))
            p.wait(nc.vector, "st")

    wqc_buf = p.sbuf("wqc_buf", [128, DC, 512], BF16)
    p.wait(nc.tensor, "st")
    for g in range(2):
        gsl = slice(g * 512, (g + 1) * 512)
        p.dma("wc", nc.gpsimd, wqc_buf[:],
              wcls_d.rearrange("(kc p) o -> p kc o", p=128)[:, :, gsl],
              wait_chans=("T",))
        p.wait(nc.tensor, "wc")
        p.wait(nc.tensor, "V")
        for kc in range(DC):
            p.mm(psQ[0:1, :], ln0_sb[:, kc:kc + 1], wqc_buf[:, kc, :],
                 start=(kc == 0), stop=(kc == DC - 1))
        p.wait(nc.vector, "T")
        p.V("tensor_copy", qcls_row[0:1, gsl], psQ[0:1, 0:512])
    p.wait(nc.tensor, "V")
    for cch in range(DC):
        p.tr(psQ[0:1, 0:128], bcols[:, 24 + cch:25 + cch], ident[:])
        p.wait(nc.vector, "T")
        p.V("tensor_add", qcls_row[0:1, cch * 128:(cch + 1) * 128],
            qcls_row[0:1, cch * 128:(cch + 1) * 128], psQ[0:1, 0:128])
        p.wait(nc.tensor, "V")
    for cch in range(DC):
        p.wait(nc.tensor, "V")
        p.tr(psQ[:, 0:1], qcls_row[0:1, cch * 128:(cch + 1) * 128], ident[0:1, 0:1])
        p.wait(nc.vector, "T")
        p.V("tensor_copy", qcol[:, cch:cch + 1], psQ[:, 0:1])
    proj_fm(wcls_d, D, qp, grps_own, 32, "kc", aoff=16)
    v_tiles(wcls_d, grps_own, 24, "c", 16, aoff=16)
    for h in range(16):
        hb = (h % 2) * 64
        for it in range(NTILE):
            p.wait(nc.tensor, "S")
            p.mm(psS[:, 0:1],
                 qp[hb:hb + 64, h // 2, it * 128:(it + 1) * 128],
                 qcol[hb:hb + 64, h // 2:h // 2 + 1], start=True, stop=True)
            p.wait(nc.scalar, "T")
            p.S("activation", out=se[:, it, h:h + 1], in_=psS[:, 0:1], func=AF.Exp,
                scale=sc18[:])
    p.wait(nc.tensor, "S")
    for h in range(16):
        for it in range(NTILE):
            p.mm(psA[:, 0:1], vt[:, it, h, :], se[:, it, h:h + 1],
                 start=(it == 0), stop=(it == NTILE - 1))
        p.wait(nc.vector, "T")
        p.V("tensor_copy", part_sb[:, h:h + 1], psA[:, 0:1])
        p.wait(nc.tensor, "V")
    p.dma("po", nc.sync, po.rearrange("h m -> m h"), part_sb[:], wait_chans=("V",), allow_slow_non_contiguous=True)
    p.wait(nc.sync, "po")
    return p


def scr5f(p):
    if not hasattr(p, "_scr5f"):
        p._scr5f = p.sbuf("scr5f", [128, 512], F32)
    return p._scr5f


def build_l1_attn_combine():
    p = Prog()
    nc = p.nc
    part_d = p.din("parts", [4, 16, 65])
    wo_d = p.din("woC", [D, D])
    bo_d = p.din("boC", [128, DC])
    x0_d = p.din("x0", [D, 1])
    xo = p.dout("x0_new", [D, 1])

    parts = p.sbuf("parts_sb", [65, 4, 16], F32)
    ssum = p.sbuf("ssum", [65, 16], F32)
    dr = p.sbuf("dr", [1, 16], F32)
    ctx_mh = p.sbuf("ctx_mh", [64, 16], F32)
    ctx_col = p.sbuf("ctx_col", [128, DC], BF16)
    w_sb = p.sbuf("w_sb", [128, DC, D], BF16)
    bo_sb = p.sbuf("bo_sb", [128, DC], F32)
    x0_sb = p.sbuf("x0_sb", [128, DC], F32)
    y0 = p.sbuf("y0", [128, DC], F32)
    ones_row = p.sbuf("c_onesrow2", [1, 128], F32)
    psB = p.psum("psB", [64, 16])
    psQ = p.psum("psQ", [128, 1])

    p.dma("ld", nc.sync, parts[:], part_d.rearrange("s h m -> m s h"),
          allow_slow_non_contiguous=True)
    p.dma("ld", nc.gpsimd, w_sb[:], wo_d.rearrange("(kc p) o -> p kc o", p=128))
    p.dma("ld", nc.sync, bo_sb[:], bo_d)
    p.dma("ld", nc.sync, x0_sb[:], x0_d.rearrange("(c p) o -> p c o", p=128), allow_slow_non_contiguous=True)
    for e in (nc.tensor, nc.scalar, nc.vector, nc.gpsimd):
        p.wait(e, "ld")
    p.V("memset", ones_row[:], 1.0)
    p.V("tensor_add", ssum[:], parts[:, 0, :], parts[:, 1, :])
    p.V("tensor_add", ssum[:], ssum[:], parts[:, 2, :])
    p.V("tensor_add", ssum[:], ssum[:], parts[:, 3, :])
    p.V("reciprocal", dr[:], ssum[64:65, :])
    p.wait(nc.tensor, "V")
    p.mm(psB[:], ones_row[0:1, 0:64], dr[:], start=True, stop=True)
    p.wait(nc.vector, "T")
    p.V("tensor_mul", ctx_mh[:], ssum[0:64, :], psB[:, :])
    odd_tmp = p.sbuf("odd_tmp", [64, DC], F32)
    for c in range(DC):
        p.V("tensor_copy", ctx_col[0:64, c:c + 1], ctx_mh[:, 2 * c:2 * c + 1])
        p.V("tensor_copy", odd_tmp[:, c:c + 1], ctx_mh[:, 2 * c + 1:2 * c + 2])
    p.dma("mv", nc.gpsimd, ctx_col[64:128, :], odd_tmp[:], wait_chans=("V",))
    for e in (nc.tensor, nc.vector):
        p.wait(e, "mv")
    for oc in range(DC):
        for kc in range(DC):
            p.mm(psQ[:], w_sb[:, kc, oc * 128:(oc + 1) * 128], ctx_col[:, kc:kc + 1],
                 start=(kc == 0), stop=(kc == DC - 1))
        p.wait(nc.vector, "T")
        p.V("tensor_add", y0[:, oc:oc + 1], psQ[:, 0:1], bo_sb[:, oc:oc + 1])
        p.V("tensor_add", y0[:, oc:oc + 1], y0[:, oc:oc + 1], x0_sb[:, oc:oc + 1])
        p.wait(nc.tensor, "V")
    p.dma("st", nc.sync, xo.rearrange("(c p) o -> p c o", p=128), y0[:],
          wait_chans=("V",), allow_slow_non_contiguous=True)
    p.wait(nc.sync, "st")
    return p


def run_l1_attn(x_shards, inputs, cache):
    import ml_dtypes
    bf16 = ml_dtypes.bfloat16
    if "ln_f32" not in cache:
        cache["ln_f32"] = build_ln_f32()
    pL = cache["ln_f32"]
    res = run_bass_kernel_spmd(pL.nc, [{"x_fm": x_shards[c]} for c in range(NCORE)],
                               core_ids=list(range(NCORE)))
    ln_b = [np.concatenate([np.asarray(res.results[b * 4 + q]["ln_out"]) for q in range(4)],
                           axis=1) for b in range(2)]

    if "l1am" not in cache:
        cache["l1am"] = build_l1_attn_main()
    pM = cache["l1am"]
    wlocT = np.ascontiguousarray(inputs["L1_attn_in_w_loc"].T)
    wclsT = np.ascontiguousarray(inputs["L1_attn_in_w_cls"].T)
    bloc = np.ascontiguousarray(inputs["L1_attn_in_b_loc"].reshape(24, 128).T)
    bcls = np.ascontiguousarray(inputs["L1_attn_in_b_cls"].reshape(24, 128).T)
    woLT = np.ascontiguousarray(inputs["L1_attn_out_w_loc"].T)
    boL = np.ascontiguousarray(inputs["L1_attn_out_b_loc"].reshape(DC, 128).T)
    in_maps = []
    for c in range(NCORE):
        b, q = c // 4, c % 4
        G = q * TPC
        xe = np.zeros((D, XCL), dtype=np.float32)
        lo, hi = G - 16, G + 2064
        s0 = max(0, -lo)
        e0 = XCL - max(0, hi - 8192)
        xe[:, s0:e0] = ln_b[b][:, max(lo, 0):min(hi, 8192)]
        mb = np.zeros((16, 128, 160), dtype=np.float32)
        pp = np.arange(128)[:, None]; jj = np.arange(160)[None, :]
        for i in range(16):
            kglob = G + 128 * i - 16 + jj
            qglob = G + 128 * i + pp
            valid = (kglob >= qglob - 16) & (kglob <= qglob + 16) & \
                    (kglob >= 1) & (kglob < 8192)
            mb[i] = np.where(valid, 0.0, -1e30)
        in_maps.append({
            "x_fm": x_shards[c], "ln_ext": xe,
            "ln0": np.ascontiguousarray(ln_b[b][:, 0:1]),
            "wlocT": wlocT, "bloc": bloc, "wclsT": wclsT, "bcls": bcls,
            "woLT": woLT, "boL": boL, "maskb": mb.astype(bf16),
        })
    resM = run_bass_kernel_spmd(pM.nc, in_maps, core_ids=list(range(NCORE)))

    if "l1ac" not in cache:
        cache["l1ac"] = build_l1_attn_combine()
    pC = cache["l1ac"]
    woC = np.ascontiguousarray(inputs["L1_attn_out_w_cls"].T)
    boC = np.ascontiguousarray(inputs["L1_attn_out_b_cls"].reshape(DC, 128).T)
    in_maps = []
    for c in range(NCORE):
        b = c // 4
        parts = np.stack([np.asarray(resM.results[b * 4 + q]["part"]) for q in range(4)])
        in_maps.append({
            "parts": parts, "woC": woC, "boC": boC,
            "x0": np.ascontiguousarray(x_shards[b * 4][:, 0:1]),
        })
    resC = run_bass_kernel_spmd(pC.nc, in_maps, core_ids=list(range(NCORE)))

    new_shards = [np.asarray(r["x_out"]).copy() for r in resM.results]
    for b in range(2):
        new_shards[b * 4][:, 0:1] = np.asarray(resC.results[b * 4]["x0_new"])
    return new_shards


NSLOT_T = 24
I32 = mybir.dt.int32


def build_l0_moe_sparse(debug=False):
    p = Prog()
    nc = p.nc
    x = p.din("x_fm", [D, TPC])
    ep_d = p.din("ep", [2, 128, NTILE])
    idx_d = p.din("idx", [128, NTILE])
    e_all_d = p.din("e_all", [1, T_ALL])
    s_all_d = p.din("s_all", [1, T_ALL])
    w1_d = p.din("W1", [8 * D, 4096])
    w2_d = p.din("W2", [8 * 4096, D])
    b1_d = p.din("b1r", [256, 128])
    b2_d = p.din("b2r", [64, 128])
    xo = p.dout("x_out", [D, TPC])
    if debug:
        dbg_lng = p.dout("dbg_lng", [128, D], BF16)
        dbg_y0 = p.dout("dbg_y0", [128, D], BF16)
        dbg_h0 = p.dout("dbg_h0", [128, 32, 128], BF16)
        dbg_pos = p.dout("dbg_pos", [128, NTILE])
        dbg_inv = p.dout("dbg_inv", [128, NSLOT_T])
        dbg_et = p.dout("dbg_et", [1, NSLOT_T])
        dbg_fe = p.dout("dbg_fe", [128, NTILE])
        dbg_crow = p.dout("dbg_crow", [1, 8])
        dbg_brow = p.dout("dbg_brow", [1, 8])
    fe_scr = nc.dram_tensor("fe_scr", [NTILE, 128], F32, kind="Internal").ap()
    lnt_d = nc.dram_tensor("lnt_scr", [TPC, D], BF16, kind="Internal").ap()
    inv_d = nc.dram_tensor("inv_scr", [NSLOT_T, 128], F32, kind="Internal").ap()
    yt_d = nc.dram_tensor("yt_scr", [NSLOT_T * 128, D], BF16, kind="Internal").ap()

    x_sb = p.sbuf("x_sb", [128, DC, TPC], F32)
    arena = p.sbuf("arena", [128, 5 * TPC], F32)
    wg = p.sbuf("wg", [128, 16384], BF16)
    h_sb = p.sbuf("h_sb", [128, 32, 128], BF16)
    lng_tm = p.sbuf("lng_tm", [128, D], BF16)
    lng_fm = p.sbuf("lng_fm", [128, DC, 128], BF16)
    y_fm_t = p.sbuf("y_fm_t", [128, DC, 128], BF16)
    y32 = p.sbuf("y32", [128, DC, 128], F32)
    y_tm = p.sbuf("y_tm", [128, D], BF16)
    a_row = p.sbuf("a_row", [1, TPC], F32)
    b_row = p.sbuf("b_row", [1, TPC], F32)
    ec = p.sbuf("ec", [128, NTILE], F32)
    sc = p.sbuf("sc", [128, NTILE], F32)
    ic = p.sbuf("ic", [128, NTILE], F32)
    fe = p.sbuf("fe", [128, NTILE], F32)
    fe_t = p.sbuf("fe_t", [NTILE, 128], F32)
    crow = p.sbuf("crow", [1, 8], F32)
    trow = p.sbuf("trow", [1, 8], F32)
    brow = p.sbuf("brow", [1, 8], F32)
    bcol = p.sbuf("bcol", [8, 1], F32)
    posf = p.sbuf("posf", [128, NTILE], F32)
    posi = p.sbuf("posi", [128, NTILE], I32)
    et_row = p.sbuf("et_row", [1, NSLOT_T], F32)
    e_bcT = p.sbuf("e_bcT", [128, NSLOT_T], F32)
    idx1f = p.sbuf("idx1f", [128, DC, NSLOT_T], F32)
    idx1i = p.sbuf("idx1i", [128, DC, NSLOT_T], I32)
    idx2f = p.sbuf("idx2f", [128, 32, NSLOT_T], F32)
    idx2i = p.sbuf("idx2i", [128, 32, NSLOT_T], I32)
    idb1 = p.sbuf("idb1", [32, NSLOT_T], F32)
    idb1i = p.sbuf("idb1i", [32, NSLOT_T], I32)
    idb2 = p.sbuf("idb2", [8, NSLOT_T], F32)
    idb2i = p.sbuf("idb2i", [8, NSLOT_T], I32)
    tok_col = p.sbuf("tok_col", [128, NTILE], F32)
    inv_sb = p.sbuf("inv_sb", [NSLOT_T, 128], F32)
    invf = p.sbuf("invf", [128, NSLOT_T], F32)
    invi = p.sbuf("invi", [128, NSLOT_T], I32)
    b1c = p.sbuf("b1c", [128, 32], F32)
    b2c = p.sbuf("b2c", [128, 8], F32)
    b1g = p.sbuf("b1g", [32, 128], F32)
    b2g = p.sbuf("b2g", [8, 128], F32)
    zrow = p.sbuf("zrow", [NSLOT_T, 128], F32)
    tstart = p.sbuf("tstart", [8, NSLOT_T], F32)
    ge8 = p.sbuf("ge8", [8, NSLOT_T], F32)

    psP = p.psum("psP", [128, 2, 512])
    psH = p.psum("psH", [128, 512])
    psY2 = p.psum("psY2", [128, DC, 128])
    psTb = p.psum("psTb", [128, 128], BF16)

    p.dma("ld", nc.sync, x_sb[:], x.rearrange("(c p) t -> p c t", p=128))
    p.dma("ld", nc.sync, ec[:], ep_d[0])
    p.dma("ld", nc.sync, sc[:], ep_d[1])
    p.dma("ld", nc.sync, ic[:], idx_d)
    for e in (nc.tensor, nc.scalar, nc.vector, nc.gpsimd):
        p.wait(e, "ld")

    ident, iota_p, ones, ones_row = make_consts(p)
    ident_bf = p.sbuf("ident_bf4", [128, 128], BF16)
    p.V("tensor_copy", ident_bf[:], ident[:])

    scr0 = arena[:, 2 * TPC:3 * TPC]
    emit_ln_stats(p, x_sb, TPC, a_row, b_row, scr0,
                  ps_a=psP[0:1, 0, :], ps_b=psP[0:1, 1, :],
                  mu_row=arena[0:1, 0:TPC])
    a_bc = arena[:, 0 * TPC:1 * TPC]
    b_bc = arena[:, 1 * TPC:2 * TPC]
    emit_bcast_rows(p, a_row[:], a_bc, ones_row, ps=psP[:, 0, :], tag="lna")
    emit_bcast_rows(p, b_row[:], b_bc, ones_row, ps=psP[:, 0, :], tag="lnb")
    lnc_bf = p.sbuf("lnc_bf", [128, 128], BF16)
    for it in range(NTILE):
        tsl = slice(it * 128, (it + 1) * 128)
        for c in range(DC):
            p.V("tensor_mul", scr0[:, 0:128], x_sb[:, c, tsl], a_bc[:, tsl])
            p.V("tensor_add", lnc_bf[:, :], scr0[:, 0:128], b_bc[:, tsl])
            p.wait(nc.tensor, "V")
            p.tr(psTb[:, :], lnc_bf[:, :], ident_bf[:])
            p.wait(nc.vector, "T")
            p.V("tensor_copy", y_tm[:, c * 128:(c + 1) * 128], psTb[:, :])
        p.dma("lt", nc.sync, lnt_d[it * 128:(it + 1) * 128, :], y_tm[:],
              wait_chans=("V",))
        p.wait(nc.vector, "lt")

    emit_moe_rank(p, e_all_d, s_all_d, ec, sc, ic, fe, arena, ones_row, ps_bc=psP[:])
    p.wait(nc.tensor, "V")
    p.tr(psP[0:NTILE, 0, 0:128], fe[:, :], ident[:])
    p.wait(nc.vector, "T")
    p.V("tensor_copy", fe_t[:, :], psP[0:NTILE, 0, 0:128])
    p.dma("fe1", nc.gpsimd, fe_scr, fe_t[:], wait_chans=("V",))
    p.wait(nc.gpsimd, "fe1")
    fe_bc = arena[:, 0 * TPC:1 * TPC]
    p.dma("fe2", nc.gpsimd, fe_bc,
          fe_scr.rearrange("i p -> (i p)").unsqueeze(0).partition_broadcast(128),
          wait_chans=("V",))
    for e in (nc.tensor, nc.vector, nc.gpsimd):
        p.wait(e, "fe2")

    masks8 = arena[0:8, 1 * TPC:2 * TPC]
    pos_row = arena[32:33, 1 * TPC:2 * TPC]
    incl8 = arena[0:8, 2 * TPC:3 * TPC]
    rank8 = arena[0:8, 3 * TPC:4 * TPC]
    p.V("tensor_scalar", masks8, fe_bc[0:8, :], iota_p[0:8, 0:1], None,
        op0=ALU.is_equal)
    p.V("tensor_tensor_scan", incl8, masks8, masks8, 0.0,
        op0=ALU.add, op1=ALU.bypass)
    p.V("tensor_sub", rank8, incl8, masks8)
    p.wait(nc.tensor, "V")
    p.tr(psP[0:1, 0, 0:8], incl8[:, TPC - 1:TPC], ident[0:8, 0:8])
    p.wait(nc.vector, "T")
    p.V("tensor_copy", crow[:], psP[0:1, 0, 0:8])
    p.V("tensor_scalar", trow[:], crow[:], 1.0 / 128.0, 0.49609375,
        op0=ALU.mult, op1=ALU.add)
    ti8 = p.sbuf("ti8", [1, 8], I32)
    p.V("tensor_copy", ti8[:], trow[:])
    p.V("tensor_copy", trow[:], ti8[:])
    p.V("tensor_scalar_mul", trow[:], trow[:], 128.0)
    p.V("tensor_tensor_scan", brow[:], trow[:], trow[:], 0.0,
        op0=ALU.add, op1=ALU.bypass)
    p.V("tensor_sub", brow[:], brow[:], trow[:])
    p.wait(nc.tensor, "V")
    p.tr(psP[0:8, 0, 0:1], brow[:], ident[0:1, 0:1])
    p.wait(nc.vector, "T")
    p.V("tensor_copy", bcol[:], psP[0:8, 0, 0:1])
    p.V("tensor_scalar", rank8, rank8, bcol[0:8, 0:1], None, op0=ALU.add)
    p.V("tensor_mul", rank8, rank8, masks8)
    p.wait(nc.tensor, "V")
    for g in range(4):
        gsl = slice(g * 512, (g + 1) * 512)
        p.mm(psP[0:1, 0, :], ones[0:8, :], rank8[:, gsl], start=True, stop=True)
        p.wait(nc.vector, "T")
        p.V("tensor_copy", pos_row[0:1, gsl], psP[0:1, 0, 0:512])
        p.wait(nc.tensor, "V")
    for it in range(NTILE):
        p.tr(psP[:, 0, it:it + 1], pos_row[0:1, it * 128:(it + 1) * 128],
             ident[32:33, 32:33])
    p.wait(nc.vector, "T")
    p.V("tensor_copy", posf[:], psP[:, 0, 0:NTILE])
    p.V("tensor_copy", posi[:], posf[:])
    p.G("iota", tstart[:], pattern=[[128, NSLOT_T]], base=0, channel_multiplier=0,
        allow_small_or_imprecise_dtypes=True)
    p.wait(nc.vector, "G")
    p.V("tensor_scalar", ge8[:], tstart[:], bcol[0:8, 0:1], None, op0=ALU.is_ge)
    p.wait(nc.tensor, "V")
    p.mm(psP[0:1, 0, 0:NSLOT_T], ones[0:8, :], ge8[:], start=True, stop=True)
    p.wait(nc.vector, "T")
    p.V("tensor_scalar", et_row[:], psP[0:1, 0, 0:NSLOT_T], -1.0, None, op0=ALU.add)
    p.wait(nc.tensor, "V")
    p.mm(psP[:, 0, 0:NSLOT_T], ones_row[:], et_row[:], start=True, stop=True)
    p.wait(nc.vector, "T")
    p.V("tensor_copy", e_bcT[:], psP[:, 0, 0:NSLOT_T])
    for kc in range(DC):
        p.V("tensor_scalar", idx1f[:, kc, :], e_bcT[:], 1024.0, iota_p[:, 0:1],
            op0=ALU.mult, op1=ALU.add)
        p.V("tensor_scalar", idx1f[:, kc, :], idx1f[:, kc, :], float(kc * 128), None,
            op0=ALU.add)
    p.V("tensor_copy", idx1i[:], idx1f[:])
    for fc in range(32):
        p.V("tensor_scalar", idx2f[:, fc, :], e_bcT[:], 4096.0, iota_p[:, 0:1],
            op0=ALU.mult, op1=ALU.add)
        p.V("tensor_scalar", idx2f[:, fc, :], idx2f[:, fc, :], float(fc * 128), None,
            op0=ALU.add)
    p.V("tensor_copy", idx2i[:], idx2f[:])
    p.V("tensor_scalar", idb1[:], e_bcT[0:32, :], 32.0, iota_p[0:32, 0:1],
        op0=ALU.mult, op1=ALU.add)
    p.V("tensor_copy", idb1i[:], idb1[:])
    p.V("tensor_scalar", idb2[:], e_bcT[0:8, :], 8.0, iota_p[0:8, 0:1],
        op0=ALU.mult, op1=ALU.add)
    p.V("tensor_copy", idb2i[:], idb2[:])
    p.G("iota", tok_col[:], pattern=[[128, NTILE]], base=0, channel_multiplier=1,
        allow_small_or_imprecise_dtypes=True)
    p.V("memset", zrow[:], 0.0)
    p.dma("g", nc.gpsimd, inv_d, zrow[:], wait_chans=("V", "G"))
    p.wait(nc.gpsimd, "g")
    inv_flat = inv_d.rearrange("a b -> (a b)").unsqueeze(-1)
    for it in range(NTILE):
        i = nc.gpsimd.indirect_dma_start(
            inv_flat, IndirectOffsetOnAxis(posi[:, it:it + 1], 0),
            tok_col[:, it:it + 1], None)
        p.inc("g", i, 16)
    p.wait(nc.sync, "g")
    p.dma("ld2", nc.sync, inv_sb[:], inv_d)
    p.wait(nc.tensor, "ld2")
    p.tr(psP[:, 0, 0:NSLOT_T], inv_sb[:], ident[0:NSLOT_T, 0:NSLOT_T])
    p.wait(nc.vector, "T")
    p.V("tensor_copy", invf[:], psP[:, 0, 0:NSLOT_T])
    p.V("tensor_copy", invi[:], invf[:])

    if debug:
        p.dma("dbg", nc.sync, dbg_pos, posf[:], wait_chans=("V",))
        p.dma("dbg", nc.sync, dbg_inv, invf[:])
        p.dma("dbg", nc.sync, dbg_et, et_row[:])
        p.dma("dbg", nc.sync, dbg_fe, fe[:])
        p.dma("dbg", nc.sync, dbg_crow, crow[:])
        p.dma("dbg", nc.sync, dbg_brow, brow[:])
        p.wait(nc.sync, "dbg")
    w1v = wg[:].rearrange("p (kc f) -> p kc f", kc=DC)
    w2v = wg[:].rearrange("p (fc o) -> p fc o", fc=16)
    for T in range(NSLOT_T):
        p.wait(nc.gpsimd, "T")
        i = nc.gpsimd.indirect_dma_start(
            lng_tm[:], None, lnt_d, IndirectOffsetOnAxis(invi[:, T:T + 1], 0))
        p.inc("g", i, 16)
        i = nc.gpsimd.indirect_dma_start(
            b1g[:], None, b1_d, IndirectOffsetOnAxis(idb1i[:, T:T + 1], 0))
        p.inc("g", i, 16)
        p.wait(nc.tensor, "g")
        for c in range(DC):
            p.wait(nc.tensor, "V")
            p.tr(psTb[:, :], lng_tm[:, c * 128:(c + 1) * 128], ident_bf[:])
            p.wait(nc.vector, "T")
            p.V("tensor_copy", lng_fm[:, c, :], psTb[:, :])
        p.tr(psP[:, 0, 0:32], b1g[:], ident[0:32, 0:32])
        p.wait(nc.vector, "T")
        p.V("tensor_copy", b1c[:], psP[:, 0, 0:32])
        for half in range(2):
            p.wait(nc.gpsimd, "T")
            for kc in range(DC):
                i = nc.gpsimd.indirect_dma_start(
                    w1v[:, kc, :], None, w1_d,
                    IndirectOffsetOnAxis(idx1i[:, kc, T:T + 1], 0),
                    element_offset=half * 2048)
                p.inc("g", i, 16)
            p.wait(nc.tensor, "g")
            p.wait(nc.tensor, "V")
            for f16 in range(16):
                fc = half * 16 + f16
                p.wait(nc.tensor, "S")
                for kc in range(DC):
                    p.mm(psH[:, 0:128], w1v[:, kc, f16 * 128:(f16 + 1) * 128],
                         lng_fm[:, kc, :], start=(kc == 0), stop=(kc == DC - 1))
                p.wait(nc.scalar, "T")
                p.S("activation", out=h_sb[:, fc, :], in_=psH[:, 0:128], func=AF.Relu,
                    bias=b1c[:, fc:fc + 1])
        i = nc.gpsimd.indirect_dma_start(
            b2g[:], None, b2_d, IndirectOffsetOnAxis(idb2i[:, T:T + 1], 0))
        p.inc("g", i, 16)
        p.wait(nc.tensor, "g")
        p.tr(psP[:, 0, 0:8], b2g[:], ident[0:8, 0:8])
        p.wait(nc.vector, "T")
        p.V("tensor_copy", b2c[:], psP[:, 0, 0:8])
        for half in range(2):
            p.wait(nc.gpsimd, "T")
            for f16 in range(16):
                fc = half * 16 + f16
                i = nc.gpsimd.indirect_dma_start(
                    w2v[:, f16, :], None, w2_d,
                    IndirectOffsetOnAxis(idx2i[:, fc, T:T + 1], 0))
                p.inc("g", i, 16)
            p.wait(nc.tensor, "g")
            p.wait(nc.tensor, "S")
            for oc in range(DC):
                p.wait(nc.tensor, "V")
                for f16 in range(16):
                    p.mm(psY2[:, 0, :], w2v[:, f16, oc * 128:(oc + 1) * 128],
                         h_sb[:, half * 16 + f16, :],
                         start=(f16 == 0), stop=(f16 == 15))
                p.wait(nc.vector, "T")
                if half == 0:
                    p.V("tensor_copy", y32[:, oc, :], psY2[:, 0, :])
                else:
                    p.V("tensor_add", y32[:, oc, :], y32[:, oc, :], psY2[:, 0, :])
        for oc in range(DC):
            p.V("tensor_scalar", y_fm_t[:, oc, :], y32[:, oc, :], b2c[:, oc:oc + 1],
                None, op0=ALU.add)
        for c in range(DC):
            p.wait(nc.tensor, "V")
            p.tr(psTb[:, :], y_fm_t[:, c, :], ident_bf[:])
            p.wait(nc.vector, "T")
            p.V("tensor_copy", y_tm[:, c * 128:(c + 1) * 128], psTb[:, :])
        p.dma("yt", nc.sync, yt_d[T * 128:(T + 1) * 128, :], y_tm[:],
              wait_chans=("V",))
        if debug and T == 0:
            p.dma("yt", nc.sync, dbg_y0, y_tm[:])
            p.dma("yt", nc.sync, dbg_lng, lng_tm[:])
            p.dma("yt", nc.sync, dbg_h0, h_sb[:], wait_chans=("S",))
        p.wait(nc.vector, "yt")
        p.wait(nc.gpsimd, "yt")

    for it in range(NTILE):
        p.wait(nc.gpsimd, "T")
        i = nc.gpsimd.indirect_dma_start(
            lng_tm[:], None, yt_d, IndirectOffsetOnAxis(posi[:, it:it + 1], 0))
        p.inc("g", i, 16)
        p.wait(nc.tensor, "g")
        for c in range(DC):
            p.wait(nc.tensor, "V")
            p.tr(psTb[:, :], lng_tm[:, c * 128:(c + 1) * 128], ident_bf[:])
            p.wait(nc.vector, "T")
            p.V("scalar_tensor_tensor", x_sb[:, c, it * 128:(it + 1) * 128],
                psTb[:, :], 0.5, x_sb[:, c, it * 128:(it + 1) * 128],
                op0=ALU.mult, op1=ALU.add)
        p.wait(nc.gpsimd, "V")

    p.dma("st", nc.sync, xo.rearrange("(c p) t -> p c t", p=128), x_sb[:],
          wait_chans=("V",))
    p.wait(nc.sync, "st")
    return p


# version 05 (final)
# speedup vs baseline: 1.1856x; 1.0132x over previous
import contextlib
import math
import numpy as np
import concourse.bass as bass
from concourse.bass import IndirectOffsetOnAxis
import concourse.mybir as mybir
from concourse.bass_utils import run_bass_kernel_spmd

F32 = mybir.dt.float32
BF16 = mybir.dt.bfloat16
AF = mybir.ActivationFunctionType
ALU = mybir.AluOpType
AX = mybir.AxisListType

NCORE = 8
TPC = 2048
NTILE = TPC // 128
D = 1024
DC = D // 128
T_ALL = 16384


class Prog:

    def __init__(self):
        self.nc = bass.Bass("TRN2", target_bir_lowering=False)
        self.ctx = contextlib.ExitStack()
        self._sems = {}
        self._cnt = {}
        self._cvecs = {}

    def sem(self, name):
        if name not in self._sems:
            self._sems[name] = self.ctx.enter_context(self.nc.semaphore(name))
            self._cnt[name] = 0
        return self._sems[name]

    def inc(self, name, inst, n=1):
        inst.then_inc(self.sem(name), n)
        self._cnt[name] += n
        return inst

    def wait(self, eng, name):
        eng.wait_ge(self.sem(name), self._cnt[name])

    def _op(self, eng, chan, fname, *args, **kw):
        self.wait(eng, chan)
        inst = getattr(eng, fname)(*args, **kw)
        self.inc(chan, inst, 1)
        return inst

    def V(self, fname, *args, **kw):
        return self._op(self.nc.vector, "V", fname, *args, **kw)

    def S(self, fname, *args, **kw):
        return self._op(self.nc.scalar, "S", fname, *args, **kw)

    def G(self, fname, *args, **kw):
        return self._op(self.nc.gpsimd, "G", fname, *args, **kw)

    def mm(self, out, lhsT, rhs, start, stop, inc=None, **kw):
        i = self.nc.tensor.matmul(out, lhsT, rhs, start=start, stop=stop, **kw)
        if inc or (inc is None and stop):
            self.inc("T", i, 1)
        return i

    def tr(self, out, in_, ident):
        i = self.nc.tensor.transpose(out, in_, ident)
        self.inc("T", i, 1)
        return i

    def dma(self, name, eng, out, in_, wait_chans=(), **kw):
        for ch in wait_chans:
            self.wait(eng, ch)
        i = eng.dma_start(out=out, in_=in_, **kw)
        self.inc(name, i, 16)
        return i

    def sbuf(self, name, shape, dtype):
        return self.ctx.enter_context(self.nc.sbuf_tensor(name, shape, dtype))

    def psum(self, name, shape, dtype=F32):
        return self.ctx.enter_context(self.nc.psum_tensor(name, shape, dtype))

    def din(self, name, shape, dtype=F32):
        return self.nc.dram_tensor(name, shape, dtype, kind="ExternalInput").ap()

    def dout(self, name, shape, dtype=F32):
        return self.nc.dram_tensor(name, shape, dtype, kind="ExternalOutput").ap()

    def close(self):
        self.ctx.close()

    def cvec(self, val, parts=128):
        key = (val, parts)
        if key not in self._cvecs:
            t = self.sbuf(f"cv_{len(self._cvecs)}", [parts, 1], F32)
            self.V("memset", t[:], val)
            self._cvecs[key] = t
        return self._cvecs[key]


def make_consts(p):
    ident = p.sbuf("c_ident", [128, 128], F32)
    iota_p = p.sbuf("c_iotap", [128, 1], F32)
    iota_f = p.sbuf("c_iotaf", [128, 128], F32)
    ones = p.sbuf("c_ones", [128, 1], F32)
    ones_row = p.sbuf("c_onesrow", [1, 128], F32)
    p.G("iota", iota_p[:], pattern=[[0, 1]], base=0, channel_multiplier=1,
        allow_small_or_imprecise_dtypes=True)
    p.G("iota", iota_f[:], pattern=[[1, 128]], base=0, channel_multiplier=0,
        allow_small_or_imprecise_dtypes=True)
    p.V("memset", ones[:], 1.0)
    p.V("memset", ones_row[:], 1.0)
    p.wait(p.nc.vector, "G")
    p.V("tensor_scalar", ident[:], iota_f[:], iota_p[:], None, op0=ALU.is_equal)
    return ident, iota_p, ones, ones_row


def emit_ln_stats(p, x_sb, ntok, a_row, b_row, scr, tag="", ps_a=None, ps_b=None, mu_row=None):
    nc = p.nc
    ones = p.cvec(1.0)
    if ps_a is None:
        ps_a = p.psum("lnA" + tag, [1, 512])[:]
        ps_b = p.psum("lnB" + tag, [1, 512])[:]
    if mu_row is None:
        mu_row = p.sbuf("lnmu" + tag, [1, ntok], F32)[:]
    nG = ntok // 512
    p.wait(nc.tensor, "V")
    for g in range(nG):
        sl = slice(g * 512, (g + 1) * 512)
        for c in range(DC):
            p.mm(ps_a, ones[:], x_sb[:, c, sl], start=(c == 0), stop=(c == DC - 1))
        for c in range(DC):
            p.wait(nc.scalar, "T")
            p.S("activation", out=scr[:, 0:512], in_=x_sb[:, c, sl], func=AF.Square)
            p.wait(nc.tensor, "S")
            p.mm(ps_b, ones[:], scr[:, 0:512], start=(c == 0), stop=(c == DC - 1),
                 inc=True)
        p.wait(nc.vector, "T")
        p.V("tensor_scalar_mul", mu_row[0:1, sl], ps_a, 1.0 / D)
        p.V("tensor_scalar_mul", b_row[:, sl], ps_b, 1.0 / D)
        if g < nG - 1:
            p.wait(nc.tensor, "V")
    p.wait(nc.scalar, "V")
    p.S("activation", out=scr[0:1, 0:ntok], in_=mu_row[0:1, 0:ntok], func=AF.Square)
    p.wait(nc.vector, "S")
    p.V("tensor_sub", b_row[:], b_row[:], scr[0:1, 0:ntok])
    eps = p.cvec(1e-5, 1)
    p.wait(nc.scalar, "V")
    p.S("activation", out=scr[0:1, 0:ntok], in_=b_row[:], func=AF.Sqrt, bias=eps[0:1, 0:1])
    p.wait(nc.vector, "S")
    p.V("reciprocal", a_row[:], scr[0:1, 0:ntok])
    p.V("scalar_tensor_tensor", b_row[:], mu_row[0:1, 0:ntok], -1.0, a_row[:],
        op0=ALU.mult, op1=ALU.mult)


def emit_row_to_col(p, row, col, ident, ntile, tag="", ps=None):
    nc = p.nc
    if ps is None:
        ps = p.psum("r2c" + tag, [128, ntile])[:]
    p.wait(nc.tensor, "V")
    for i in range(ntile):
        p.tr(ps[:, i:i + 1], row[:, i * 128:(i + 1) * 128], ident[0:1, 0:1])
    p.wait(nc.vector, "T")
    p.V("tensor_copy", col[:, :], ps[:, 0:ntile])


def emit_bcast_rows(p, rows, out_sb, ones_row, nrow_parts=128, tag="", ps=None):
    nc = p.nc
    n = rows.shape[-1] if hasattr(rows, "shape") else None
    n = out_sb.shape[-1]
    if ps is None:
        ps = p.psum("bc" + tag, [nrow_parts, 512])[:]
    p.wait(nc.tensor, "V")
    for g in range((n + 511) // 512):
        sl = slice(g * 512, min((g + 1) * 512, n))
        w = sl.stop - sl.start
        p.mm(ps[:, 0:w], ones_row[0:1, 0:nrow_parts], rows[:, sl], start=True, stop=True)
        p.wait(nc.vector, "T")
        p.V("tensor_copy", out_sb[:, sl], ps[:, 0:w])
        if g * 512 + 512 < n:
            p.wait(nc.tensor, "V")
    return out_sb



def build_l0_route():
    p = Prog()
    nc = p.nc
    x = p.din("x_fm", [D, TPC])
    rt = p.din("r_T", [D, 8])
    ep = p.dout("ep", [2, 128, NTILE])

    x_sb = p.sbuf("x_sb", [128, DC, TPC], F32)
    r_sb = p.sbuf("r_sb", [128, DC, 8], F32)
    scr = p.sbuf("scr", [128, TPC], F32)
    a_row = p.sbuf("a_row", [1, TPC], F32)
    b_row = p.sbuf("b_row", [1, TPC], F32)
    a_col = p.sbuf("a_col", [128, NTILE], F32)
    b_col = p.sbuf("b_col", [128, NTILE], F32)
    lg = p.sbuf("lg", [128, NTILE, 8], F32)
    csr = p.sbuf("csr", [128, 8], F32)
    iota8 = p.sbuf("iota8", [128, 8], F32)
    mx = p.sbuf("mx", [128, NTILE], F32)
    e_col = p.sbuf("e_col", [128, NTILE], F32)
    s_col = p.sbuf("s_col", [128, NTILE], F32)
    exps = p.sbuf("exps", [128, 8], F32)
    eq = p.sbuf("eq", [128, 8], F32)
    csr_row = p.sbuf("csr_row", [1, 8], F32)

    p.dma("ld", nc.sync, x_sb[:], x.rearrange("(c p) t -> p c t", p=128))
    p.dma("ld", nc.sync, r_sb[:], rt.rearrange("(c p) e -> p c e", p=128))
    for e in (nc.tensor, nc.scalar, nc.vector, nc.gpsimd):
        p.wait(e, "ld")

    ident, iota_p, ones, ones_row = make_consts(p)
    p.G("iota", iota8[:], pattern=[[1, 8]], base=99, channel_multiplier=0,
        allow_small_or_imprecise_dtypes=True)

    emit_ln_stats(p, x_sb, TPC, a_row, b_row, scr)

    ps_lg = p.psum("ps_lg", [128, NTILE, 8])
    for it in range(NTILE):
        for c in range(DC):
            p.mm(ps_lg[:, it, :], x_sb[:, c, it * 128:(it + 1) * 128], r_sb[:, c, :],
                 start=(c == 0), stop=(c == DC - 1))
    ps_c = p.psum("ps_c", [1, 8])
    for c in range(DC):
        p.mm(ps_c[:], ones[:], r_sb[:, c, :], start=(c == 0), stop=(c == DC - 1))
    p.wait(nc.vector, "T")
    p.V("tensor_copy", csr_row[:], ps_c[:])
    ps_b8 = p.psum("ps_b8", [128, 8])
    p.wait(nc.tensor, "V")
    p.mm(ps_b8[:], ones_row[:], csr_row[:], start=True, stop=True)
    p.wait(nc.vector, "T")
    p.V("tensor_copy", csr[:], ps_b8[:])

    emit_row_to_col(p, a_row, a_col, ident, NTILE, tag="a")
    emit_row_to_col(p, b_row, b_col, ident, NTILE, tag="b")

    p.wait(nc.vector, "G")
    for it in range(NTILE):
        p.V("tensor_scalar_mul", lg[:, it, :], ps_lg[:, it, :], a_col[:, it:it + 1])
        p.V("scalar_tensor_tensor", lg[:, it, :], csr[:], b_col[:, it:it + 1],
            lg[:, it, :], op0=ALU.mult, op1=ALU.add)
        p.V("tensor_reduce", mx[:, it:it + 1], lg[:, it, :], axis=AX.X, op=ALU.max,
            negate=True)
        p.wait(nc.scalar, "V")
        p.S("activation", out=exps[:], in_=lg[:, it, :], func=AF.Exp,
            bias=mx[:, it:it + 1], accum_out=s_col[:, it:it + 1])
        p.V("tensor_scalar", eq[:], lg[:, it, :], mx[:, it:it + 1], None, op0=ALU.add)
        p.V("tensor_scalar", eq[:], eq[:], 0.0, None, op0=ALU.is_equal)
        p.V("scalar_tensor_tensor", eq[:], eq[:], -99.0, iota8[:],
            op0=ALU.mult, op1=ALU.add)
        p.V("tensor_reduce", e_col[:, it:it + 1], eq[:], axis=AX.X, op=ALU.min)

    p.dma("st", nc.gpsimd, ep[0], e_col[:], wait_chans=("V",))
    p.dma("st", nc.gpsimd, ep[1], s_col[:], wait_chans=("S",))
    p.wait(nc.gpsimd, "st")
    return p



def _shard_x(x):
    out = []
    for c in range(NCORE):
        b, q = c // 4, c % 4
        out.append(np.ascontiguousarray(x[b, q * TPC:(q + 1) * TPC, :].T))
    return out


def kernel(**inputs):
    shards = _shard_x(inputs["L0_attn_x"])
    cache = {}
    shards = run_l0_ffn(shards, inputs, cache)
    shards = run_l0_attn(shards, inputs, cache)
    shards = run_l0_ffn(shards, inputs, cache)
    shards = run_l1_ffn(shards, inputs, cache)
    shards = run_l1_attn(shards, inputs, cache)
    shards = run_l1_ffn(shards, inputs, cache)
    shards = run_l2_ffn(shards, inputs, cache)
    shards = run_l2_attn(shards, inputs, cache)
    shards = run_l2_ffn(shards, inputs, cache)
    shards = run_l3_ffn(shards, inputs, cache)
    shards = run_l3_attn(shards, inputs, cache)
    shards = run_l3_ffn(shards, inputs, cache)
    out = np.empty((2, 8192, 1024), dtype=np.float32)
    for c in range(NCORE):
        b, q = c // 4, c % 4
        out[b, q * TPC:(q + 1) * TPC, :] = shards[c].T
    return out


CAPACITY = 2560


def emit_moe_rank(p, e_all_d, s_all_d, e_col, s_col, idx_col, fe_col, arena, ones_row, ps_bc=None):
    nc = p.nc
    A = arena
    e_bc = A[:, 0 * TPC:1 * TPC]
    s_bc = A[:, 1 * TPC:2 * TPC]
    scrap = A[:, 2 * TPC:3 * TPC]
    beq = A[:, 3 * TPC:4 * TPC]
    jiota = A[:, 4 * TPC:5 * TPC]
    rank = p.sbuf("rank_col", [128, NTILE], F32)
    r1 = p.sbuf("r1_col", [128, 1], F32)
    r2 = p.sbuf("r2_col", [128, 1], F32)
    ish = p.sbuf("ish_col", [128, 1], F32)

    p.G("iota", jiota[:, :], pattern=[[1, TPC]], base=0, channel_multiplier=0,
        allow_small_or_imprecise_dtypes=True)
    p.V("memset", rank[:], 0.0)
    for jc in range(NCORE):
        sl = slice(jc * TPC, (jc + 1) * TPC)
        p.dma("rk", nc.sync, e_bc, e_all_d[0:1, sl].partition_broadcast(128),
              wait_chans=("V",))
        p.dma("rk", nc.sync, s_bc, s_all_d[0:1, sl].partition_broadcast(128))
        p.wait(nc.vector, "rk")
        for it in range(NTILE):
            p.V("tensor_scalar", beq[:, :], e_bc, e_col[:, it:it + 1], None,
                op0=ALU.is_equal)
            p.V("scalar_tensor_tensor", scrap[:, :], s_bc, s_col[:, it:it + 1],
                beq[:, :], op0=ALU.is_lt, op1=ALU.mult, accum_out=r1[:])
            p.V("scalar_tensor_tensor", beq[:, :], s_bc, s_col[:, it:it + 1],
                beq[:, :], op0=ALU.is_equal, op1=ALU.mult)
            p.V("tensor_scalar", ish[:], idx_col[:, it:it + 1], float(jc * TPC), None,
                op0=ALU.subtract)
            p.V("scalar_tensor_tensor", scrap[:, :], jiota, ish[:], beq[:, :],
                op0=ALU.is_lt, op1=ALU.mult, accum_out=r2[:])
            p.V("tensor_add", r1[:], r1[:], r2[:])
            p.V("tensor_add", rank[:, it:it + 1], rank[:, it:it + 1], r1[:])
    p.V("tensor_scalar", fe_col[:], rank[:], float(CAPACITY), None, op0=ALU.is_lt)
    p.V("tensor_mul", fe_col[:], fe_col[:], e_col[:])
    return fe_col



def build_l0_moe():
    p = Prog()
    nc = p.nc
    x = p.din("x_fm", [D, TPC])
    ep_d = p.din("ep", [2, 128, NTILE])
    idx_d = p.din("idx", [128, NTILE])
    e_all_d = p.din("e_all", [1, T_ALL])
    s_all_d = p.din("s_all", [1, T_ALL])
    w1_d = p.din("W1", [8, D, 4096])
    w2_d = p.din("W2", [8, 4096, D])
    b1_d = p.din("b1c", [128, 8, 32])
    b2_d = p.din("b2c", [128, 8, 8])
    xo = p.dout("x_out", [D, TPC])
    scratch_d = nc.dram_tensor("fe_scr", [NTILE, 128], F32, kind="Internal").ap()

    x_sb = p.sbuf("x_sb", [128, DC, TPC], F32)
    ln_bf = p.sbuf("ln_bf", [128, DC, TPC], BF16)
    arena = p.sbuf("arena", [128, 5 * TPC], F32)
    w1buf = p.sbuf("w1buf", [128, DC, 512], BF16)
    h_tg = p.sbuf("h_tg", [128, 32, 512], BF16)
    w2buf = p.sbuf("w2buf", [128, 1, 32, 128], BF16)
    a_row = p.sbuf("a_row", [1, TPC], F32)
    b_row = p.sbuf("b_row", [1, TPC], F32)
    ec = p.sbuf("ec", [128, NTILE], F32)
    sc = p.sbuf("sc", [128, NTILE], F32)
    ic = p.sbuf("ic", [128, NTILE], F32)
    fe = p.sbuf("fe", [128, NTILE], F32)
    fe_t = p.sbuf("fe_t", [NTILE, 128], F32)
    b1_sb = p.sbuf("b1_sb", [128, 8, 32], F32)
    b2_sb = p.sbuf("b2_sb", [128, 8, 8], F32)

    psP = p.psum("psP", [128, 2, 512])
    psH = p.psum("psH", [128, 2, 512])
    psY = p.psum("psY", [128, 512])

    p.dma("ld", nc.sync, x_sb[:], x.rearrange("(c p) t -> p c t", p=128))
    p.dma("ld", nc.sync, ec[:], ep_d[0])
    p.dma("ld", nc.sync, sc[:], ep_d[1])
    p.dma("ld", nc.sync, ic[:], idx_d)
    p.dma("ld", nc.sync, b1_sb[:], b1_d)
    p.dma("ld", nc.sync, b2_sb[:], b2_d)
    for e in (nc.tensor, nc.scalar, nc.vector, nc.gpsimd):
        p.wait(e, "ld")

    ident, iota_p, ones, ones_row = make_consts(p)

    scr = arena[:, 2 * TPC:3 * TPC]
    emit_ln_stats(p, x_sb, TPC, a_row, b_row, scr,
                  ps_a=psP[0:1, 0, :], ps_b=psP[0:1, 1, :],
                  mu_row=arena[0:1, 3 * TPC:3 * TPC + TPC])
    a_bc = arena[:, 0 * TPC:1 * TPC]
    b_bc = arena[:, 1 * TPC:2 * TPC]
    emit_bcast_rows(p, a_row[:], a_bc, ones_row, ps=psP[:, 0, :], tag="a")
    emit_bcast_rows(p, b_row[:], b_bc, ones_row, ps=psP[:, 0, :], tag="b")
    for c in range(DC):
        p.V("tensor_mul", scr[:, :], x_sb[:, c, :], a_bc)
        p.V("tensor_add", ln_bf[:, c, :], scr[:, :], b_bc)

    emit_moe_rank(p, e_all_d, s_all_d, ec, sc, ic, fe, arena, ones_row, ps_bc=psP[:])

    p.wait(nc.tensor, "V")
    p.tr(psP[0:NTILE, 0, 0:128], fe[:, :], ident[:])
    p.wait(nc.vector, "T")
    p.V("tensor_copy", fe_t[:, :], psP[0:NTILE, 0, 0:128])
    p.dma("fe1", nc.gpsimd, scratch_d, fe_t[:], wait_chans=("V",))
    p.wait(nc.gpsimd, "fe1")
    fe_bc = arena[:, 0 * TPC:1 * TPC]
    p.dma("fe2", nc.gpsimd, fe_bc,
          scratch_d.rearrange("i p -> (i p)").unsqueeze(0).partition_broadcast(128),
          wait_chans=("V",))
    for e in (nc.tensor, nc.vector):
        p.wait(e, "fe2")

    mask_bc = arena[:, 1 * TPC:2 * TPC]
    evac = arena[:, 2 * TPC:3 * TPC]

    for e in range(8):
        p.V("tensor_scalar", mask_bc[:, :], fe_bc, float(e), None, op0=ALU.is_equal)
        for tg in range(4):
            tsl = slice(tg * 512, (tg + 1) * 512)
            for q in range(8):
                p.dma("w1", nc.gpsimd, w1buf[:],
                      w1_d[e].rearrange("(kc p) f -> p kc f", p=128)
                      [:, :, q * 512:(q + 1) * 512],
                      wait_chans=("T",))
                p.wait(nc.tensor, "w1")
                for f8 in range(4):
                    fc = q * 4 + f8
                    hp = psH[:, fc % 2, :]
                    p.wait(nc.tensor, "S")
                    for kc in range(DC):
                        p.mm(hp, w1buf[:, kc, f8 * 128:(f8 + 1) * 128],
                             ln_bf[:, kc, tsl], start=(kc == 0), stop=(kc == DC - 1))
                    p.wait(nc.scalar, "T")
                    p.S("activation", out=h_tg[:, fc, :], in_=hp, func=AF.Relu,
                        bias=b1_sb[:, e, fc:fc + 1])
            for oc in range(8):
                par = 0
                p.dma("w2", nc.gpsimd, w2buf[:, par],
                      w2_d[e].rearrange("(fc p) o -> p fc o", p=128)
                      [:, :, oc * 128:(oc + 1) * 128],
                      wait_chans=("T",))
                p.wait(nc.tensor, "w2")
                p.wait(nc.tensor, "V")
                p.wait(nc.tensor, "S")
                for fc in range(32):
                    p.mm(psY[:], w2buf[:, par, fc, :], h_tg[:, fc, :],
                         start=(fc == 0), stop=(fc == 31))
                p.wait(nc.vector, "T")
                p.V("tensor_scalar", evac[:, 0:512], psY[:], b2_sb[:, e, oc:oc + 1],
                    None, op0=ALU.add)
                p.V("tensor_mul", evac[:, 0:512], evac[:, 0:512], mask_bc[:, tsl])
                p.V("scalar_tensor_tensor", x_sb[:, oc, tsl], evac[:, 0:512], 0.5,
                    x_sb[:, oc, tsl], op0=ALU.mult, op1=ALU.add)

    p.dma("st", nc.sync, xo.rearrange("(c p) t -> p c t", p=128), x_sb[:],
          wait_chans=("V",))
    p.wait(nc.sync, "st")
    return p


def _col_of(v, c):
    return np.ascontiguousarray(v[c * TPC:(c + 1) * TPC].reshape(NTILE, 128).T)


def build_l0_wcast():
    p = Prog()
    nc = p.nc
    w1_d = p.din("w1s", [D, 4096])
    w2_d = p.din("w2s", [4096, D])
    w1_o = p.dout("w1b", [D, 4096], BF16)
    w2_o = p.dout("w2b", [4096, D], BF16)
    bufs = [p.sbuf(f"cv{b}", [128, 4096], BF16) for b in range(2)]
    chunks = []
    v1i = w1_d.rearrange("(c p) f -> p c f", p=128)
    v1o = w1_o.rearrange("(c p) f -> p c f", p=128)
    for c in range(8):
        chunks.append((v1i[:, c, :], v1o[:, c, :]))
    v2i = w2_d.rearrange("(c p) o -> p c o", p=128)
    v2o = w2_o.rearrange("(c p) o -> p c o", p=128)
    for c in range(0, 32, 4):
        chunks.append((v2i[:, c:c + 4, :], v2o[:, c:c + 4, :]))
    for j, (src, dst) in enumerate(chunks):
        b = j % 2
        buf = bufs[b][:] if len(src.shape) == 2 else \
            bufs[b][:].rearrange("p (c o) -> p c o", c=4)
        p.dma(f"l{b}", nc.gpsimd, buf, src, wait_chans=(f"s{b}",))
        p.dma(f"s{b}", nc.sync, dst, buf, wait_chans=(f"l{b}",))
    p.wait(nc.sync, "s0")
    p.wait(nc.sync, "s1")
    return p


def run_l0_ffn(x_shards, inputs, cache):
    rt = np.ascontiguousarray(inputs["L0_ffn_router_w"].T)
    if "l0_route" not in cache:
        cache["l0_route"] = build_l0_route()
    pA = cache["l0_route"]
    in_maps = [{"x_fm": x_shards[c], "r_T": rt} for c in range(NCORE)]
    resA = run_bass_kernel_spmd(pA.nc, in_maps, core_ids=list(range(NCORE)))
    e_all = np.concatenate([r["ep"][0].T.reshape(-1) for r in resA.results])
    s_all = np.concatenate([r["ep"][1].T.reshape(-1) for r in resA.results])

    if "l0_wbf" not in cache:
        if "l0_wcast" not in cache:
            cache["l0_wcast"] = build_l0_wcast()
        pW = cache["l0_wcast"]
        w1f = inputs["L0_ffn_W1"].reshape(8 * D, 4096)
        w2f = inputs["L0_ffn_W2"].reshape(8 * 4096, D)
        wc_maps = [{"w1s": w1f[c * D:(c + 1) * D],
                    "w2s": w2f[c * 4096:(c + 1) * 4096]} for c in range(NCORE)]
        resW = run_bass_kernel_spmd(pW.nc, wc_maps, core_ids=list(range(NCORE)))
        cache["l0_wbf"] = (
            np.concatenate([np.asarray(r["w1b"]) for r in resW.results], axis=0),
            np.concatenate([np.asarray(r["w2b"]) for r in resW.results], axis=0))
    w1b, w2b = cache["l0_wbf"]

    if "l0_moe" not in cache:
        cache["l0_moe"] = build_l0_moe_sparse()
    pB = cache["l0_moe"]
    idx = np.arange(T_ALL, dtype=np.float32)
    in_maps = []
    for c in range(NCORE):
        in_maps.append({
            "x_fm": x_shards[c],
            "ep": np.stack([_col_of(e_all, c), _col_of(s_all, c)]),
            "idx": _col_of(idx, c),
            "e_all": e_all[None, :], "s_all": s_all[None, :],
            "W1": w1b,
            "W2": w2b,
            "b1r": inputs["L0_ffn_b1"].reshape(256, 128),
            "b2r": inputs["L0_ffn_b2"].reshape(64, 128),
        })
    resB = run_bass_kernel_spmd(pB.nc, in_maps, core_ids=list(range(NCORE)))
    return [np.asarray(r["x_out"]) for r in resB.results]



def emit_gemm_stream(p, w_dram, kdim, odim, act_sb, ntok, ps, evac, wbuf, name,
                     tgw=512):
    nc = p.nc
    KC = kdim // 128
    nog = odim // 128
    ntg = ntok // tgw
    for oc in range(nog):
        par = oc % 2
        p.dma("w" + name, nc.gpsimd, wbuf[:, par, 0:KC],
              w_dram.rearrange("(kc p) o -> p kc o", p=128)[:, :, oc * 128:(oc + 1) * 128],
              wait_chans=("T",))
        p.wait(nc.tensor, "w" + name)
        for tg in range(ntg):
            tsl = slice(tg * tgw, (tg + 1) * tgw)
            p.wait(nc.tensor, "V")
            p.wait(nc.tensor, "S")
            for kc in range(KC):
                p.mm(ps, wbuf[:, par, kc, :], act_sb[:, kc, tsl],
                     start=(kc == 0), stop=(kc == KC - 1))
            evac(oc, tg, ps)




def build_l0_attn_c1():
    p = Prog()
    nc = p.nc
    x = p.din("x_fm", [D, TPC])
    wdq_d = p.din("W_DQ", [D, 4096])
    wdkv_d = p.din("W_DKV", [D, 256])
    wuk_d = p.din("W_UK", [256, D])
    wuv_d = p.din("W_UV", [256, D])
    wkr_d = p.din("W_KR", [D, 32])
    scale_d = p.din("scale_fm", [32, TPC], BF16)
    cq_o = p.dout("cq", [32, 128, TPC], BF16)
    kc_o = p.dout("kc", [DC, 128, TPC], BF16)
    v_o = p.dout("v", [NTILE, 128, D], BF16)
    kr_o = p.dout("kr", [32, TPC], BF16)

    x_sb = p.sbuf("x_sb", [128, DC, TPC], F32)
    ln_bf = p.sbuf("ln_bf", [128, DC, TPC], BF16)
    arena = p.sbuf("arena", [128, 3 * TPC], F32)
    a_row = p.sbuf("a_row", [1, TPC], F32)
    b_row = p.sbuf("b_row", [1, TPC], F32)
    wbuf = p.sbuf("wbuf", [128, 2, DC, 128], BF16)
    wkv_sb = p.sbuf("wkv_sb", [128, DC, 256], BF16)
    ckv_bf = p.sbuf("ckv_bf", [128, 2, TPC], BF16)
    scale_sb = p.sbuf("scale_sb", [32, TPC], BF16)
    orow = p.sbuf("orow", [128, TPC], BF16)
    psP = p.psum("psP", [128, 2, 512])
    psQ = p.psum("psQ", [128, 512])

    p.dma("ld", nc.sync, x_sb[:], x.rearrange("(c p) t -> p c t", p=128))
    p.dma("ld", nc.sync, scale_sb[:], scale_d)
    p.dma("ld", nc.gpsimd, wkv_sb[:], wdkv_d.rearrange("(kc p) o -> p kc o", p=128))
    for e in (nc.tensor, nc.scalar, nc.vector, nc.gpsimd):
        p.wait(e, "ld")

    scr = arena[:, 2 * TPC:3 * TPC]
    emit_ln_stats(p, x_sb, TPC, a_row, b_row, scr,
                  ps_a=psP[0:1, 0, :], ps_b=psP[0:1, 1, :],
                  mu_row=arena[0:1, 0:TPC])
    ones_row = p.sbuf("c_onesrow", [1, 128], F32)
    p.V("memset", ones_row[:], 1.0)
    a_bc = arena[:, 0 * TPC:1 * TPC]
    b_bc = arena[:, 1 * TPC:2 * TPC]
    emit_bcast_rows(p, a_row[:], a_bc, ones_row, ps=psP[:, 0, :], tag="a")
    emit_bcast_rows(p, b_row[:], b_bc, ones_row, ps=psP[:, 0, :], tag="b")
    for c in range(DC):
        p.V("tensor_mul", scr[:, :], x_sb[:, c, :], a_bc)
        p.V("tensor_add", ln_bf[:, c, :], scr[:, :], b_bc)

    def evac_cq(oc, tg, ps):
        p.wait(nc.vector, "T")
        if tg == 0:
            p.wait(nc.vector, "cqo")
        p.V("tensor_copy", orow[:, tg * 512:(tg + 1) * 512], ps)
        if tg == 3:
            p.dma("cqo", nc.sync, cq_o[oc], orow[:, :], wait_chans=("V",))
    emit_gemm_stream(p, wdq_d, D, 4096, ln_bf[:], TPC, psQ[:], evac_cq, wbuf, "dq")

    p.wait(nc.tensor, "ld")
    for oc in range(2):
        for tg in range(4):
            tsl = slice(tg * 512, (tg + 1) * 512)
            p.wait(nc.tensor, "V")
            for kc in range(DC):
                p.mm(psQ[:], wkv_sb[:, kc, oc * 128:(oc + 1) * 128], ln_bf[:, kc, tsl],
                     start=(kc == 0), stop=(kc == DC - 1))
            p.wait(nc.vector, "T")
            p.V("tensor_copy", ckv_bf[:, oc, tsl], psQ[:])

    def evac_kc(oc, tg, ps):
        p.wait(nc.vector, "T")
        if tg == 0:
            p.wait(nc.vector, "kco")
        p.V("tensor_copy", orow[:, tg * 512:(tg + 1) * 512], ps)
        if tg == 3:
            p.dma("kco", nc.sync, kc_o[oc], orow[:, :], wait_chans=("V",))
    wbuf2 = p.sbuf("wbuf2", [128, 2, 2, 128], BF16)
    emit_gemm_stream(p, wuk_d, 256, D, ckv_bf[:], TPC, psQ[:], evac_kc, wbuf2, "uk")

    wuv_sb = p.sbuf("wuv_sb", [128, 2, D], BF16)
    p.dma("wuv", nc.gpsimd, wuv_sb[:], wuv_d.rearrange("(kc p) o -> p kc o", p=128))
    p.wait(nc.tensor, "wuv")
    vt_bf = p.sbuf("vt_bf", [128, D], BF16)
    for it in range(NTILE):
        p.wait(nc.vector, "vo")
        for g in range(2):
            osl = slice(g * 512, (g + 1) * 512)
            p.wait(nc.tensor, "V")
            for kc in range(2):
                p.mm(psQ[:], ckv_bf[:, kc, it * 128:(it + 1) * 128], wuv_sb[:, kc, osl],
                     start=(kc == 0), stop=(kc == 1))
            p.wait(nc.vector, "T")
            p.V("tensor_copy", vt_bf[:, osl], psQ[:])
        p.dma("vo", nc.sync, v_o[it], vt_bf[:], wait_chans=("V",))

    wkr_sb = p.sbuf("wkr_sb", [128, DC, 32], BF16)
    p.dma("wkr", nc.gpsimd, wkr_sb[:], wkr_d.rearrange("(kc p) o -> p kc o", p=128))
    p.wait(nc.tensor, "wkr")
    kr_bf = p.sbuf("kr_bf", [32, TPC], BF16)
    for tg in range(4):
        tsl = slice(tg * 512, (tg + 1) * 512)
        p.wait(nc.tensor, "V")
        for kc in range(DC):
            p.mm(psQ[0:32, :], wkr_sb[:, kc, :], ln_bf[:, kc, tsl],
                 start=(kc == 0), stop=(kc == DC - 1))
        p.wait(nc.vector, "T")
        p.V("tensor_mul", kr_bf[0:32, tsl], psQ[0:32, 0:512], scale_sb[0:32, tsl])
    p.dma("kro", nc.sync, kr_o, kr_bf[:], wait_chans=("V",))

    p.wait(nc.sync, "cqo")
    p.wait(nc.sync, "kco")
    p.wait(nc.sync, "vo")
    p.wait(nc.sync, "kro")
    return p


NKB = 64


def build_l0_attn_d():
    p = Prog()
    nc = p.nc
    S = 8192
    cq_d = p.din("cq", [32, 128, S], BF16)
    kc_d = p.din("kch", [4, 64, S], BF16)
    kr_d = p.din("krf", [32, S], BF16)
    v_d = p.din("vph", [S // 128, 128, 4, 65], BF16)
    wuq_d = p.din("wuqh", [4096, 4 * 64])
    wqr_d = p.din("wqrh", [4096, 4 * 32])
    scale_d = p.din("scale_fm", [32, S], BF16)
    mask_d = p.din("masks", [128, 4, 512], BF16)
    o_o = p.dout("o_out", [4, 64, S], BF16)

    q_sb = p.sbuf("q_sb", [128, 4, S], BF16)
    slab = p.sbuf("slab", [128, 32, 512], BF16)
    wuq_sb = p.sbuf("wuq_sb", [128, 32, 256], BF16)
    wqr_sb = p.sbuf("wqr_sb", [128, 32, 128], BF16)
    k_sb = p.sbuf("k_sb", [128, S], BF16)
    v_sb = p.sbuf("v_sb", [128, NKB, 65], BF16)
    o_sb = p.sbuf("o_sb", [64, S], BF16)
    scale_sb = p.sbuf("scale_sb", [32, S], BF16)
    mask_sb = p.sbuf("mask_sb", [128, 4, 512], BF16)
    p_bf = p.sbuf("p_bf", [128, 2, 512], BF16)
    num_sb = p.sbuf("num_sb", [64, 512], F32)
    den_r = p.sbuf("den_r", [1, 512], F32)
    psS = p.psum("psS", [128, 2, 512])
    psO = p.psum("psO", [65, 512])
    psB = p.psum("psB", [64, 512])

    ones_row = p.sbuf("c_onesrow", [1, 128], F32)
    p.V("memset", ones_row[:], 1.0)
    p.V("memset", q_sb[96:128, :, :].rearrange("p h t -> p (h t)"), 0.0)
    p.V("memset", k_sb[96:128, :], 0.0)

    p.dma("ld", nc.gpsimd, wuq_sb[:], wuq_d.rearrange("(kc p) o -> p kc o", p=128))
    p.dma("ld", nc.gpsimd, wqr_sb[:], wqr_d.rearrange("(kc p) o -> p kc o", p=128))
    p.dma("ld", nc.sync, scale_sb[:], scale_d)
    p.dma("ld", nc.sync, mask_sb[:], mask_d)
    for e in (nc.tensor, nc.scalar, nc.vector, nc.gpsimd):
        p.wait(e, "ld")

    for tg in range(16):
        tsl = slice(tg * 512, (tg + 1) * 512)
        p.dma("cq", nc.sync, slab[:], cq_d[:, :, tsl].rearrange("kc p t -> p kc t"),
              wait_chans=("T",))
        p.wait(nc.tensor, "cq")
        for h in range(4):
            ps = psS[:, 0, :]
            p.wait(nc.tensor, "V")
            for kc in range(32):
                p.mm(ps[0:64, :], wuq_sb[:, kc, h * 64:(h + 1) * 64], slab[:, kc, :],
                     start=(kc == 0), stop=(kc == 31))
            for kc in range(32):
                p.mm(ps[64:96, :], wqr_sb[:, kc, h * 32:(h + 1) * 32], slab[:, kc, :],
                     start=(kc == 0), stop=(kc == 31), inc=(kc == 31))
            p.wait(nc.vector, "T")
            p.V("tensor_copy", q_sb[0:64, h, tsl], ps[0:64, :])
            p.V("tensor_mul", q_sb[64:96, h, tsl], ps[64:96, 0:512], scale_sb[0:32, tsl])

    sc = p.cvec(1.0 / math.sqrt(96.0))
    for h in range(4):
        p.dma("kv", nc.sync, k_sb[0:64, :], kc_d[h], wait_chans=("T",))
        p.dma("kv", nc.sync, k_sb[64:96, :], kr_d[:])
        p.dma("kv", nc.sync, v_sb[:],
              v_d[:, :, h, :].rearrange("kb p o -> p kb o"))
        p.wait(nc.tensor, "kv")
        p.wait(nc.vector, "oo")
        for s in range(16):
            qsl = slice(s * 512, (s + 1) * 512)
            nkb = 4 * (s + 1)
            for kb in range(nkb):
                par = kb % 2
                psT = psS[:, par, :]
                p.wait(nc.tensor, "S")
                p.mm(psT, k_sb[:, kb * 128:(kb + 1) * 128], q_sb[:, h, qsl],
                     start=True, stop=True)
                p.wait(nc.scalar, "T")
                p.S("activation", out=p_bf[:, par, :], in_=psT, func=AF.Exp,
                    scale=sc[:])
                diag = kb - (nkb - 4)
                if diag >= 0:
                    p.wait(nc.vector, "S")
                    p.V("tensor_mul", p_bf[:, par, :], p_bf[:, par, :],
                        mask_sb[:, diag, :])
                    p.wait(nc.tensor, "V")
                else:
                    p.wait(nc.tensor, "S")
                p.mm(psO[:], v_sb[:, kb, :], p_bf[:, par, :],
                     start=(kb == 0), stop=(kb == nkb - 1))
            p.wait(nc.vector, "T")
            p.V("tensor_copy", num_sb[:, :], psO[0:64, :])
            p.V("reciprocal", den_r[:], psO[64:65, :])
            p.wait(nc.tensor, "V")
            p.mm(psB[:], ones_row[0:1, 0:64], den_r[:], start=True, stop=True)
            p.wait(nc.vector, "T")
            p.V("tensor_mul", o_sb[:, qsl], num_sb[:, :], psB[:, :])
        p.dma("oo", nc.sync, o_o[h], o_sb[:], wait_chans=("V",))
        p.wait(nc.sync, "oo")
    return p



def build_l0_attn_e():
    p = Prog()
    nc = p.nc
    x = p.din("x_fm", [D, TPC])
    y_d = p.din("y_fm", [D, TPC], BF16)
    wo_d = p.din("W_O", [D, D])
    xo = p.dout("x_out", [D, TPC])

    x_sb = p.sbuf("x_sb", [128, DC, TPC], F32)
    y_sb = p.sbuf("y_sb", [128, DC, TPC], BF16)
    wbuf = p.sbuf("wbuf", [128, 2, DC, 128], BF16)
    psQ = p.psum("psQ", [128, 512])

    p.dma("ld", nc.sync, x_sb[:], x.rearrange("(c p) t -> p c t", p=128))
    p.dma("ld", nc.sync, y_sb[:], y_d.rearrange("(c p) t -> p c t", p=128))
    for e in (nc.tensor, nc.scalar, nc.vector, nc.gpsimd):
        p.wait(e, "ld")

    def evac(oc, tg, ps):
        tsl = slice(tg * 512, (tg + 1) * 512)
        p.wait(nc.vector, "T")
        p.V("tensor_add", x_sb[:, oc, tsl], x_sb[:, oc, tsl], ps)

    emit_gemm_stream(p, wo_d, D, D, y_sb[:], TPC, psQ[:], evac, wbuf, "wo")
    p.dma("st", nc.sync, xo.rearrange("(c p) t -> p c t", p=128), x_sb[:],
          wait_chans=("V",))
    p.wait(nc.sync, "st")
    return p


def _rope_scale_table():
    T, DHR = 8192, 32
    t = np.arange(T, dtype=np.float32)
    inv = 1.0 / 10000.0 ** (np.arange(0, DHR, 2, dtype=np.float32) / DHR)
    freqs = t[:, None] * inv[None, :]
    emb = np.concatenate([freqs, freqs], -1)
    return (np.cos(emb) + np.sin(emb)).astype(np.float32)


def run_l0_attn(x_shards, inputs, cache):
    import ml_dtypes
    bf16 = ml_dtypes.bfloat16
    scale = _rope_scale_table()
    if "l0c1" not in cache:
        cache["l0c1"] = build_l0_attn_c1()
    pC = cache["l0c1"]
    in_maps = []
    for c in range(NCORE):
        q = c % 4
        sl = scale[q * TPC:(q + 1) * TPC].T.astype(bf16)
        in_maps.append({
            "x_fm": x_shards[c], "W_DQ": inputs["L0_attn_W_DQ"],
            "W_DKV": inputs["L0_attn_W_DKV"], "W_UK": inputs["L0_attn_W_UK"],
            "W_UV": inputs["L0_attn_W_UV"], "W_KR": inputs["L0_attn_W_KR"],
            "scale_fm": np.ascontiguousarray(sl),
        })
    resC = run_bass_kernel_spmd(pC.nc, in_maps, core_ids=list(range(NCORE)))

    cq_b, kc_b, kr_b, vp_b = [], [], [], []
    for b in range(2):
        rs = [resC.results[b * 4 + q] for q in range(4)]
        cq = np.concatenate([np.asarray(r["cq"]) for r in rs], axis=2)
        kcf = np.concatenate([np.asarray(r["kc"]) for r in rs], axis=2)
        kcf = kcf.reshape(D, 8192)
        krf = np.concatenate([np.asarray(r["kr"]) for r in rs], axis=1)
        v = np.concatenate([np.asarray(r["v"]) for r in rs], axis=0)
        vp = np.zeros((64, 128, 16, 65), dtype=bf16)
        vp[:, :, :, :64] = v.reshape(64, 128, 16, 64)
        vp[:, :, :, 64] = np.asarray(1.0, dtype=bf16)
        cq_b.append(cq); kc_b.append(kcf); kr_b.append(krf); vp_b.append(vp)

    if "l0d" not in cache:
        cache["l0d"] = build_l0_attn_d()
    pD = cache["l0d"]
    masks = np.zeros((128, 4, 512), dtype=bf16)
    i = np.arange(128)[:, None]
    j = np.arange(512)[None, :]
    for r in range(4):
        masks[:, r, :] = (j >= i + 128 * r).astype(bf16)
    scale_fmT = np.ascontiguousarray(scale.T.astype(bf16))
    in_maps = []
    for c in range(NCORE):
        b = c // 4
        hs = [(c % 4) * 4 + j2 for j2 in range(4)]
        wuq = np.concatenate([inputs["L0_attn_W_UQ"][:, h * 64:(h + 1) * 64] for h in hs], axis=1)
        wqr = np.concatenate([inputs["L0_attn_W_QR"][:, h * 32:(h + 1) * 32] for h in hs], axis=1)
        kch = np.stack([kc_b[b].reshape(16, 64, 8192)[h] for h in hs])
        vph = np.ascontiguousarray(vp_b[b][:, :, hs, :])
        in_maps.append({
            "cq": cq_b[b], "kch": np.ascontiguousarray(kch), "krf": kr_b[b],
            "vph": vph,
            "wuqh": np.ascontiguousarray(wuq), "wqrh": np.ascontiguousarray(wqr),
            "scale_fm": scale_fmT, "masks": masks,
        })
    resD = run_bass_kernel_spmd(pD.nc, in_maps, core_ids=list(range(NCORE)))

    y_b = []
    for b in range(2):
        y = np.zeros((D, 8192), dtype=bf16)
        for q in range(4):
            r = resD.results[b * 4 + q]
            for j2 in range(4):
                h = q * 4 + j2
                y[h * 64:(h + 1) * 64, :] = np.asarray(r["o_out"][j2])
        y_b.append(y)

    if "l0e" not in cache:
        cache["l0e"] = build_l0_attn_e()
    pE = cache["l0e"]
    in_maps = []
    for c in range(NCORE):
        b, q = c // 4, c % 4
        in_maps.append({
            "x_fm": x_shards[c],
            "y_fm": np.ascontiguousarray(y_b[b][:, q * TPC:(q + 1) * TPC]),
            "W_O": inputs["L0_attn_W_O"],
        })
    resE = run_bass_kernel_spmd(pE.nc, in_maps, core_ids=list(range(NCORE)))
    return [np.asarray(r["x_out"]) for r in resE.results]



def emit_ln_norm(p, x_sb, ln_bf, arena, a_row, b_row, ones_row, psP, ntok=TPC):
    nc = p.nc
    scr = arena[:, 2 * ntok:3 * ntok]
    emit_ln_stats(p, x_sb, ntok, a_row, b_row, scr,
                  ps_a=psP[0:1, 0, :], ps_b=psP[0:1, 1, :],
                  mu_row=arena[0:1, 0:ntok])
    a_bc = arena[:, 0 * ntok:1 * ntok]
    b_bc = arena[:, 1 * ntok:2 * ntok]
    emit_bcast_rows(p, a_row[:], a_bc, ones_row, ps=psP[:, 0, :], tag="lna")
    emit_bcast_rows(p, b_row[:], b_bc, ones_row, ps=psP[:, 0, :], tag="lnb")
    for c in range(DC):
        p.V("tensor_mul", scr[:, :], x_sb[:, c, :], a_bc)
        p.V("tensor_add", ln_bf[:, c, :], scr[:, :], b_bc)



def build_ln_only():
    p = Prog()
    nc = p.nc
    x = p.din("x_fm", [D, TPC])
    lo = p.dout("ln_out", [DC, 128, TPC], BF16)
    x_sb = p.sbuf("x_sb", [128, DC, TPC], F32)
    ln_bf = p.sbuf("ln_bf", [128, DC, TPC], BF16)
    arena = p.sbuf("arena", [128, 3 * TPC], F32)
    a_row = p.sbuf("a_row", [1, TPC], F32)
    b_row = p.sbuf("b_row", [1, TPC], F32)
    ones_row = p.sbuf("c_onesrow", [1, 128], F32)
    psP = p.psum("psP", [128, 2, 512])
    p.dma("ld", nc.sync, x_sb[:], x.rearrange("(c p) t -> p c t", p=128))
    for e in (nc.tensor, nc.scalar, nc.vector, nc.gpsimd):
        p.wait(e, "ld")
    p.V("memset", ones_row[:], 1.0)
    emit_ln_norm(p, x_sb, ln_bf, arena, a_row, b_row, ones_row, psP[:])
    p.dma("st", nc.sync, lo.rearrange("c p t -> p c t"), ln_bf[:], wait_chans=("V",))
    p.wait(nc.sync, "st")
    return p


def build_l2_attn_d():
    p = Prog()
    nc = p.nc
    S = 8192
    lnh_d = p.din("lnh", [4, 64, S], BF16)
    wq_d = p.din("WqT", [64, 64])
    wk_d = p.din("WkT", [64, 64])
    wv_d = p.din("WvT", [64, 64])
    mask_d = p.din("masks", [128, 4, 512], BF16)
    o_o = p.dout("o_out", [4, 64, S], BF16)

    lnh = p.sbuf("lnh_sb", [128, 2, S], BF16)
    wq_sb = p.sbuf("wq_sb", [128, 3, 64], BF16)
    q_sb = p.sbuf("q_sb", [64, S], BF16)
    k_sb = p.sbuf("k_sb", [64, S], BF16)
    v_sb = p.sbuf("v_sb", [128, NKB, 65], BF16)
    o_sb = p.sbuf("o_sb", [64, S], BF16)
    mask_sb = p.sbuf("mask_sb", [128, 4, 512], BF16)
    p_bf = p.sbuf("p_bf", [128, 2, 512], BF16)
    num_sb = p.sbuf("num_sb", [64, 512], F32)
    den_r = p.sbuf("den_r", [1, 512], F32)
    ones_row = p.sbuf("c_onesrow", [1, 128], F32)
    psS = p.psum("psS", [128, 2, 512])
    psO = p.psum("psO", [65, 512])
    psB = p.psum("psB", [64, 512])
    psV = p.psum("psV", [128, 2, 65])

    p.V("memset", ones_row[:], 1.0)
    p.V("memset", v_sb[:, :, 64:65], 1.0)
    p.dma("ld", nc.gpsimd, wq_sb[0:64, 0, :], wq_d)
    p.dma("ld", nc.gpsimd, wq_sb[0:64, 1, :], wk_d)
    p.dma("ld", nc.gpsimd, wq_sb[0:64, 2, :], wv_d)
    p.dma("ld", nc.gpsimd, wq_sb[64:128, 0, :], wq_d)
    p.dma("ld", nc.gpsimd, wq_sb[64:128, 1, :], wk_d)
    p.dma("ld", nc.gpsimd, wq_sb[64:128, 2, :], wv_d)
    p.dma("ld", nc.sync, mask_sb[:], mask_d)
    for e in (nc.tensor, nc.scalar, nc.vector, nc.gpsimd):
        p.wait(e, "ld")

    sc = p.cvec(1.0 / 32.0)
    for hp in range(2):
        p.dma("lh", nc.sync, lnh[0:64, hp, :], lnh_d[2 * hp], wait_chans=("T",))
        p.dma("lh", nc.sync, lnh[64:128, hp, :], lnh_d[2 * hp + 1])
    p.wait(nc.tensor, "lh")
    for h in range(4):
        hb = (h % 2) * 64
        lv = lnh[hb:hb + 64, h // 2, :]
        wsl = wq_sb[hb:hb + 64]
        for g in range(S // 512):
            gsl = slice(g * 512, (g + 1) * 512)
            p.wait(nc.tensor, "V")
            p.mm(psS[0:64, 0, :], wsl[:, 0, :], lv[:, gsl], start=True, stop=True)
            p.mm(psS[0:64, 1, :], wsl[:, 1, :], lv[:, gsl], start=True, stop=True)
            p.wait(nc.vector, "T")
            p.V("tensor_copy", q_sb[:, gsl], psS[0:64, 0, 0:512])
            p.V("tensor_copy", k_sb[:, gsl], psS[0:64, 1, 0:512])
        for kb in range(NKB):
            par = kb % 2
            p.wait(nc.tensor, "V")
            p.mm(psV[:, par, 0:64], lv[:, kb * 128:(kb + 1) * 128], wsl[:, 2, :],
                 start=True, stop=True)
            p.wait(nc.vector, "T")
            p.V("tensor_copy", v_sb[:, kb, 0:64], psV[:, par, 0:64])
        for s in range(16):
            qsl = slice(s * 512, (s + 1) * 512)
            nkb = 4 * (s + 1)
            for kb in range(nkb):
                par = kb % 2
                psT = psS[:, par, :]
                p.wait(nc.tensor, "S")
                p.mm(psT, k_sb[:, kb * 128:(kb + 1) * 128], q_sb[:, qsl],
                     start=True, stop=True)
                p.wait(nc.scalar, "T")
                p.S("activation", out=p_bf[:, par, :], in_=psT, func=AF.Exp,
                    scale=sc[:])
                diag = kb - (nkb - 4)
                if diag >= 0:
                    p.wait(nc.vector, "S")
                    p.V("tensor_mul", p_bf[:, par, :], p_bf[:, par, :],
                        mask_sb[:, diag, :])
                    p.wait(nc.tensor, "V")
                else:
                    p.wait(nc.tensor, "S")
                p.mm(psO[:], v_sb[:, kb, :], p_bf[:, par, :],
                     start=(kb == 0), stop=(kb == nkb - 1))
            p.wait(nc.vector, "T")
            p.V("tensor_copy", num_sb[:, :], psO[0:64, :])
            p.V("reciprocal", den_r[:], psO[64:65, :])
            p.wait(nc.tensor, "V")
            p.mm(psB[:], ones_row[0:1, 0:64], den_r[:], start=True, stop=True)
            p.wait(nc.vector, "T")
            p.V("tensor_mul", o_sb[:, qsl], num_sb[:, :], psB[:, :])
        p.dma("oo", nc.sync, o_o[h], o_sb[:], wait_chans=("V",))
        p.wait(nc.sync, "oo")
        p.wait(nc.vector, "oo")
    return p


def build_proj_res():
    p = Prog()
    nc = p.nc
    x = p.din("x_fm", [D, TPC])
    y_d = p.din("y_fm", [D, TPC], BF16)
    w_d = p.din("W", [D, D])
    b_d = p.din("bcol", [128, DC])
    xo = p.dout("x_out", [D, TPC])

    x_sb = p.sbuf("x_sb", [128, DC, TPC], F32)
    y_sb = p.sbuf("y_sb", [128, DC, TPC], BF16)
    b_sb = p.sbuf("b_sb", [128, DC], F32)
    wbuf = p.sbuf("wbuf", [128, 2, DC, 128], BF16)
    scr = p.sbuf("scr", [128, 512], F32)
    psQ = p.psum("psQ", [128, 512])

    p.dma("ld", nc.sync, x_sb[:], x.rearrange("(c p) t -> p c t", p=128))
    p.dma("ld", nc.sync, y_sb[:], y_d.rearrange("(c p) t -> p c t", p=128))
    p.dma("ld", nc.sync, b_sb[:], b_d)
    for e in (nc.tensor, nc.scalar, nc.vector, nc.gpsimd):
        p.wait(e, "ld")

    def evac(oc, tg, ps):
        tsl = slice(tg * 512, (tg + 1) * 512)
        p.wait(nc.vector, "T")
        p.V("tensor_scalar", scr[:, :], ps, b_sb[:, oc:oc + 1], None, op0=ALU.add)
        p.V("tensor_add", x_sb[:, oc, tsl], x_sb[:, oc, tsl], scr[:, :])

    emit_gemm_stream(p, w_d, D, D, y_sb[:], TPC, psQ[:], evac, wbuf, "w")
    p.dma("st", nc.sync, xo.rearrange("(c p) t -> p c t", p=128), x_sb[:],
          wait_chans=("V",))
    p.wait(nc.sync, "st")
    return p


def run_l2_attn(x_shards, inputs, cache):
    import ml_dtypes
    bf16 = ml_dtypes.bfloat16
    if "ln_only" not in cache:
        cache["ln_only"] = build_ln_only()
    pC = cache["ln_only"]
    in_maps = [{"x_fm": x_shards[c]} for c in range(NCORE)]
    resC = run_bass_kernel_spmd(pC.nc, in_maps, core_ids=list(range(NCORE)))
    ln_b = []
    for b in range(2):
        ln = np.concatenate([np.asarray(resC.results[b * 4 + q]["ln_out"]) for q in range(4)],
                            axis=2)
        ln_b.append(ln.reshape(D, 8192))

    if "l2d" not in cache:
        cache["l2d"] = build_l2_attn_d()
    pD = cache["l2d"]
    masks = np.zeros((128, 4, 512), dtype=bf16)
    i = np.arange(128)[:, None]
    j = np.arange(512)[None, :]
    for r in range(4):
        masks[:, r, :] = (j >= i + 128 * r).astype(bf16)
    in_maps = []
    for c in range(NCORE):
        b = c // 4
        hs = [(c % 4) * 4 + j2 for j2 in range(4)]
        lnh = np.stack([ln_b[b].reshape(16, 64, 8192)[h] for h in hs])
        in_maps.append({
            "lnh": np.ascontiguousarray(lnh),
            "WqT": np.ascontiguousarray(inputs["L2_attn_Wq"].T),
            "WkT": np.ascontiguousarray(inputs["L2_attn_Wk"].T),
            "WvT": np.ascontiguousarray(inputs["L2_attn_Wv"].T),
            "masks": masks,
        })
    resD = run_bass_kernel_spmd(pD.nc, in_maps, core_ids=list(range(NCORE)))
    y_b = []
    for b in range(2):
        y = np.zeros((D, 8192), dtype=bf16)
        for q in range(4):
            r = resD.results[b * 4 + q]
            for j2 in range(4):
                h = q * 4 + j2
                y[h * 64:(h + 1) * 64, :] = np.asarray(r["o_out"][j2])
        y_b.append(y)

    if "proj_res" not in cache:
        cache["proj_res"] = build_proj_res()
    pE = cache["proj_res"]
    bu = inputs["L2_attn_bu"].reshape(DC, 128).T
    in_maps = []
    for c in range(NCORE):
        b, q = c // 4, c % 4
        in_maps.append({
            "x_fm": x_shards[c],
            "y_fm": np.ascontiguousarray(y_b[b][:, q * TPC:(q + 1) * TPC]),
            "W": np.ascontiguousarray(inputs["L2_attn_Wu"].T),
            "bcol": np.ascontiguousarray(bu),
        })
    resE = run_bass_kernel_spmd(pE.nc, in_maps, core_ids=list(range(NCORE)))
    return [np.asarray(r["x_out"]) for r in resE.results]



def build_l2_f1():
    p = Prog()
    nc = p.nc
    x = p.din("x_fm", [D, TPC])
    po = p.dout("part", [128, DC])
    x_sb = p.sbuf("x_sb", [128, DC, TPC], F32)
    arena = p.sbuf("arena", [128, 3 * TPC], F32)
    a_row = p.sbuf("a_row", [1, TPC], F32)
    b_row = p.sbuf("b_row", [1, TPC], F32)
    part = p.sbuf("part_sb1", [128, DC], F32)
    ones_row = p.sbuf("c_onesrow", [1, 128], F32)
    psP = p.psum("psP", [128, 2, 512])
    p.dma("ld", nc.sync, x_sb[:], x.rearrange("(c p) t -> p c t", p=128))
    for e in (nc.tensor, nc.scalar, nc.vector, nc.gpsimd):
        p.wait(e, "ld")
    p.V("memset", ones_row[:], 1.0)
    scr = arena[:, 2 * TPC:3 * TPC]
    emit_ln_stats(p, x_sb, TPC, a_row, b_row, scr,
                  ps_a=psP[0:1, 0, :], ps_b=psP[0:1, 1, :],
                  mu_row=arena[0:1, 0:TPC])
    a_bc = arena[:, 0 * TPC:1 * TPC]
    b_bc = arena[:, 1 * TPC:2 * TPC]
    emit_bcast_rows(p, a_row[:], a_bc, ones_row, ps=psP[:, 0, :], tag="a")
    emit_bcast_rows(p, b_row[:], b_bc, ones_row, ps=psP[:, 0, :], tag="b")
    for c in range(DC):
        p.V("tensor_mul", scr[:, :], x_sb[:, c, :], a_bc)
        p.V("tensor_add", scr[:, :], scr[:, :], b_bc)
        p.V("tensor_reduce", part[:, c:c + 1], scr[:, :], axis=AX.X, op=ALU.add)
    p.dma("st", nc.sync, po, part[:], wait_chans=("V",))
    p.wait(nc.sync, "st")
    return p


def build_l2_f2():
    p = Prog()
    nc = p.nc
    x = p.din("x_fm", [D, TPC])
    part_d = p.din("parts", [4, 128, DC])
    cnw1_d = p.din("cnw1", [D, D])
    cnw2_d = p.din("cnw2", [D, 512])
    cnw3_d = p.din("cnw3", [512, 256])
    cnw4_d = p.din("cnw4", [256, 1])
    cnb1_d = p.din("cnb1", [128, 8])
    cnb2_d = p.din("cnb2", [128, 4])
    cnb3_d = p.din("cnb3", [128, 2])
    cnb4_d = p.din("cnb4", [1, 1])
    gcw1_d = p.din("gcw1", [D, 512])
    gcw2_d = p.din("gcw2", [512, 256])
    gcw3_d = p.din("gcw3", [256, 1])
    gcb1_d = p.din("gcb1", [128, 4])
    gcb2_d = p.din("gcb2", [128, 2])
    gcb3_d = p.din("gcb3", [1, 1])
    ar_d = p.din("ar", [1, 1])
    xo = p.dout("x_out", [D, TPC])

    out32 = p.sbuf("out32", [128, DC, TPC], F32)
    arena = p.sbuf("arena", [128, 3 * TPC], F32)
    out_bf = p.sbuf("out_bf", [128, DC, TPC], BF16)
    h1c = p.sbuf("h1c", [128, DC, TPC], BF16)
    h2c = p.sbuf("h2c", [128, 4, TPC], BF16)
    h3c = p.sbuf("h3c", [128, 2, TPC], BF16)
    a_row = p.sbuf("a_row", [1, TPC], F32)
    b_row = p.sbuf("b_row", [1, TPC], F32)
    part = p.sbuf("part_sb", [128, 4, DC], F32)
    gm_bf = p.sbuf("gm_bf", [128, DC], BF16)
    gm32 = p.sbuf("gm32", [128, DC], F32)
    gcw1_sb = p.sbuf("gcw1_sb", [128, DC, 512], BF16)
    gcw2_sb = p.sbuf("gcw2_sb", [128, 4, 256], BF16)
    gcw3_sb = p.sbuf("gcw3_sb", [128, 2, 1], BF16)
    cnw4_sb = p.sbuf("cnw4_sb", [128, 2, 1], BF16)
    gh1 = p.sbuf("gh1", [128, 4], BF16)
    gh2 = p.sbuf("gh2", [128, 2], BF16)
    gf_sc = p.sbuf("gf_sc", [1, 4], F32)
    bias_sc = p.sbuf("bias_sc", [1, 3], F32)
    cnb1 = p.sbuf("cnb1_sb", [128, 8], F32)
    cnb2 = p.sbuf("cnb2_sb", [128, 4], F32)
    cnb3 = p.sbuf("cnb3_sb", [128, 2], F32)
    gcb1 = p.sbuf("gcb1_sb", [128, 4], F32)
    gcb2 = p.sbuf("gcb2_sb", [128, 2], F32)
    wbuf = p.sbuf("wbuf", [128, 2, DC, 128], BF16)
    ones_row = p.sbuf("c_onesrow", [1, 128], F32)
    psP = p.psum("psP", [128, 2, 512])
    psQ = p.psum("psQ", [128, 512])
    psG = p.psum("psG", [128, 1])

    p.dma("ld", nc.sync, out32[:], x.rearrange("(c p) t -> p c t", p=128))
    p.dma("ld", nc.sync, part[:], part_d.rearrange("q p c -> p q c"))
    p.dma("ld", nc.gpsimd, gcw1_sb[:], gcw1_d.rearrange("(kc p) o -> p kc o", p=128))
    p.dma("ld", nc.gpsimd, gcw2_sb[:], gcw2_d.rearrange("(kc p) o -> p kc o", p=128))
    p.dma("ld", nc.gpsimd, gcw3_sb[:], gcw3_d.rearrange("(kc p) o -> p kc o", p=128), allow_slow_non_contiguous=True)
    p.dma("ld", nc.gpsimd, cnw4_sb[:], cnw4_d.rearrange("(kc p) o -> p kc o", p=128), allow_slow_non_contiguous=True)
    p.dma("ld", nc.sync, cnb1[:], cnb1_d)
    p.dma("ld", nc.sync, cnb2[:], cnb2_d)
    p.dma("ld", nc.sync, cnb3[:], cnb3_d)
    p.dma("ld", nc.sync, bias_sc[0:1, 0:1], cnb4_d)
    p.dma("ld", nc.sync, gcb1[:], gcb1_d)
    p.dma("ld", nc.sync, gcb2[:], gcb2_d)
    p.dma("ld", nc.sync, bias_sc[0:1, 1:2], gcb3_d)
    p.dma("ld", nc.sync, bias_sc[0:1, 2:3], ar_d)
    for e in (nc.tensor, nc.scalar, nc.vector, nc.gpsimd):
        p.wait(e, "ld")
    p.V("memset", ones_row[:], 1.0)

    scr = arena[:, 2 * TPC:3 * TPC]
    emit_ln_stats(p, out32, TPC, a_row, b_row, scr,
                  ps_a=psP[0:1, 0, :], ps_b=psP[0:1, 1, :],
                  mu_row=arena[0:1, 0:TPC])
    a_bc = arena[:, 0 * TPC:1 * TPC]
    b_bc = arena[:, 1 * TPC:2 * TPC]
    emit_bcast_rows(p, a_row[:], a_bc, ones_row, ps=psP[:, 0, :], tag="a")
    emit_bcast_rows(p, b_row[:], b_bc, ones_row, ps=psP[:, 0, :], tag="b")
    for c in range(DC):
        p.V("tensor_mul", out32[:, c, :], out32[:, c, :], a_bc)
        p.V("tensor_add", out32[:, c, :], out32[:, c, :], b_bc)

    tf_row = arena[0:1, 1 * TPC:2 * TPC]
    cp_row = arena[0:1, 2 * TPC:3 * TPC]
    p.V("tensor_add", part[:, 0, :], part[:, 0, :], part[:, 1, :])
    p.V("tensor_add", part[:, 2, :], part[:, 2, :], part[:, 3, :])
    p.V("tensor_add", gm32[:], part[:, 0, :], part[:, 2, :])
    p.V("tensor_scalar_mul", gm32[:], gm32[:], 1.0 / 8192.0)
    p.V("tensor_copy", gm_bf[:], gm32[:])

    p.wait(nc.tensor, "V")
    for oc in range(4):
        for kc in range(DC):
            p.mm(psG[:], gcw1_sb[:, kc, oc * 128:(oc + 1) * 128], gm_bf[:, kc:kc + 1],
                 start=(kc == 0), stop=(kc == DC - 1))
        p.wait(nc.scalar, "T")
        p.S("activation", out=gh1[:, oc:oc + 1], in_=psG[:], func=AF.Gelu,
            bias=gcb1[:, oc:oc + 1])
        p.wait(nc.tensor, "S")
    for oc in range(2):
        for kc in range(4):
            p.mm(psG[:], gcw2_sb[:, kc, oc * 128:(oc + 1) * 128], gh1[:, kc:kc + 1],
                 start=(kc == 0), stop=(kc == 3))
        p.wait(nc.scalar, "T")
        p.S("activation", out=gh2[:, oc:oc + 1], in_=psG[:], func=AF.Gelu,
            bias=gcb2[:, oc:oc + 1])
        p.wait(nc.tensor, "S")
    for kc in range(2):
        p.mm(psG[0:1, :], gcw3_sb[:, kc, :], gh2[:, kc:kc + 1],
             start=(kc == 0), stop=(kc == 1))
    p.wait(nc.scalar, "T")
    p.S("activation", out=gf_sc[0:1, 0:1], in_=psG[0:1, :], func=AF.Sigmoid,
        bias=bias_sc[0:1, 1:2])
    p.wait(nc.vector, "S")
    p.V("tensor_scalar", gf_sc[0:1, 1:2], gf_sc[0:1, 0:1], 0.7, -0.5,
        op0=ALU.mult, op1=ALU.add)
    p.V("tensor_scalar_mul", gf_sc[0:1, 2:3], bias_sc[0:1, 2:3], 2.0)

    for cyc in range(3):
        for c in range(DC):
            p.V("tensor_copy", out_bf[:, c, :], out32[:, c, :])

        def evac_h1(oc, tg, ps):
            tsl = slice(tg * 512, (tg + 1) * 512)
            p.wait(nc.scalar, "T")
            p.S("activation", out=h1c[:, oc, tsl], in_=ps, func=AF.Gelu,
                bias=cnb1[:, oc:oc + 1])
        emit_gemm_stream(p, cnw1_d, D, D, out_bf[:], TPC, psQ[:], evac_h1, wbuf,
                         "c1_%d" % cyc)

        def evac_h2(oc, tg, ps):
            tsl = slice(tg * 512, (tg + 1) * 512)
            p.wait(nc.scalar, "T")
            p.S("activation", out=h2c[:, oc, tsl], in_=ps, func=AF.Gelu,
                bias=cnb2[:, oc:oc + 1])
        emit_gemm_stream(p, cnw2_d, D, 512, h1c[:], TPC, psQ[:], evac_h2, wbuf,
                         "c2_%d" % cyc)

        def evac_h3(oc, tg, ps):
            tsl = slice(tg * 512, (tg + 1) * 512)
            p.wait(nc.scalar, "T")
            p.S("activation", out=h3c[:, oc, tsl], in_=ps, func=AF.Gelu,
                bias=cnb3[:, oc:oc + 1])
        emit_gemm_stream(p, cnw3_d, 512, 256, h2c[:], TPC, psQ[:], evac_h3, wbuf,
                         "c3_%d" % cyc)

        for tg in range(4):
            tsl = slice(tg * 512, (tg + 1) * 512)
            p.wait(nc.tensor, "S")
            for kc in range(2):
                p.mm(psQ[0:1, :], cnw4_sb[:, kc, :], h3c[:, kc, tsl],
                     start=(kc == 0), stop=(kc == 1))
            p.wait(nc.scalar, "T")
            p.S("activation", out=tf_row[0:1, tsl], in_=psQ[0:1, 0:512],
                func=AF.Sigmoid, bias=bias_sc[0:1, 0:1])
        p.wait(nc.vector, "S")
        p.V("tensor_scalar", cp_row[:], tf_row[:], 0.3, gf_sc[0:1, 1:2],
            op0=ALU.mult, op1=ALU.add)
        p.V("tensor_scalar", cp_row[:], cp_row[:], gf_sc[0:1, 2:3], 0.1,
            op0=ALU.mult, op1=ALU.add)
        cp_bc = arena[:, 0 * TPC:1 * TPC]
        emit_bcast_rows(p, cp_row[:], cp_bc, ones_row, ps=psP[:, 0, :],
                        tag="cp%d" % cyc)
        tw = arena[:, 1 * TPC:2 * TPC]
        tw2 = arena[:, 2 * TPC:3 * TPC]
        for c in range(4):
            sgn_top = 1.0 if c < 2 else -1.0
            p.V("tensor_mul", tw[:, :], out32[:, c + 4, :], cp_bc)
            p.V("tensor_mul", tw2[:, :], out32[:, c, :], cp_bc)
            if c < 2:
                p.V("tensor_add", out32[:, c, :], out32[:, c, :], tw[:, :])
                p.V("tensor_sub", out32[:, c + 4, :], out32[:, c + 4, :], tw2[:, :])
            else:
                p.V("tensor_sub", out32[:, c, :], out32[:, c, :], tw[:, :])
                p.V("tensor_add", out32[:, c + 4, :], out32[:, c + 4, :], tw2[:, :])

    xrow = arena[:, 0 * TPC:1 * TPC]
    for c in range(DC):
        p.dma("xl", nc.sync, xrow,
              x.rearrange("(c p) t -> p c t", p=128)[:, c, :], wait_chans=("V",))
        p.wait(nc.vector, "xl")
        p.V("scalar_tensor_tensor", out32[:, c, :], out32[:, c, :], 0.5, xrow,
            op0=ALU.mult, op1=ALU.add)
    p.dma("st", nc.sync, xo.rearrange("(c p) t -> p c t", p=128), out32[:],
          wait_chans=("V",))
    p.wait(nc.sync, "st")
    return p


def run_l2_ffn(x_shards, inputs, cache):
    if "l2f1" not in cache:
        cache["l2f1"] = build_l2_f1()
    p1 = cache["l2f1"]
    res1 = run_bass_kernel_spmd(p1.nc, [{"x_fm": x_shards[c]} for c in range(NCORE)],
                                core_ids=list(range(NCORE)))
    parts = [np.asarray(r["part"]) for r in res1.results]
    if "l2f2" not in cache:
        cache["l2f2"] = build_l2_f2()
    p2 = cache["l2f2"]
    in_maps = []
    for c in range(NCORE):
        b = c // 4
        in_maps.append({
            "x_fm": x_shards[c],
            "parts": np.stack([parts[b * 4 + q] for q in range(4)]),
            "cnw1": inputs["L2_ffn_cn_w1"], "cnw2": inputs["L2_ffn_cn_w2"],
            "cnw3": inputs["L2_ffn_cn_w3"], "cnw4": inputs["L2_ffn_cn_w4"],
            "cnb1": np.ascontiguousarray(inputs["L2_ffn_cn_b1"].reshape(8, 128).T),
            "cnb2": np.ascontiguousarray(inputs["L2_ffn_cn_b2"].reshape(4, 128).T),
            "cnb3": np.ascontiguousarray(inputs["L2_ffn_cn_b3"].reshape(2, 128).T),
            "cnb4": inputs["L2_ffn_cn_b4"].reshape(1, 1),
            "gcw1": inputs["L2_ffn_gc_w1"], "gcw2": inputs["L2_ffn_gc_w2"],
            "gcw3": inputs["L2_ffn_gc_w3"],
            "gcb1": np.ascontiguousarray(inputs["L2_ffn_gc_b1"].reshape(4, 128).T),
            "gcb2": np.ascontiguousarray(inputs["L2_ffn_gc_b2"].reshape(2, 128).T),
            "gcb3": inputs["L2_ffn_gc_b3"].reshape(1, 1),
            "ar": inputs["L2_ffn_adaptive_range"].reshape(1, 1),
        })
    res2 = run_bass_kernel_spmd(p2.nc, in_maps, core_ids=list(range(NCORE)))
    return [np.asarray(r["x_out"]) for r in res2.results]



def build_l3_ffn():
    p = Prog()
    nc = p.nc
    x = p.din("x_fm", [D, TPC])
    wt_d = p.din("w_T", [D, D])
    alpha_d = p.din("alpha", [1, 1])
    b_d = p.din("bcol", [128, DC])
    xo = p.dout("x_out", [D, TPC])

    x_sb = p.sbuf("x_sb", [128, DC, TPC], F32)
    ln_bf = p.sbuf("ln_bf", [128, DC, TPC], BF16)
    arena = p.sbuf("arena", [128, 3 * TPC], F32)
    a_row = p.sbuf("a_row", [1, TPC], F32)
    b_row = p.sbuf("b_row", [1, TPC], F32)
    w_sb = p.sbuf("w_sb", [128, DC, D], F32)
    wq_bf = p.sbuf("wq_bf", [128, DC, D], BF16)
    b_sb = p.sbuf("b_sb", [128, DC], F32)
    asum = p.sbuf("asum", [128, 1], F32)
    dcol = p.sbuf("dcol", [128, 2], F32)
    srow = p.sbuf("srow", [1, 2], F32)
    alpha_sb = p.sbuf("alpha_sb", [1, 1], F32)
    scr5 = p.sbuf("scr5", [128, 512], F32)
    psP = p.psum("psP", [128, 2, 512])
    psQ = p.psum("psQ", [128, 512])

    p.dma("ld", nc.sync, x_sb[:], x.rearrange("(c p) t -> p c t", p=128))
    p.dma("ld", nc.sync, w_sb[:], wt_d.rearrange("(kc p) o -> p kc o", p=128))
    p.dma("ld", nc.sync, b_sb[:], b_d)
    p.dma("ld", nc.sync, alpha_sb[:], alpha_d)
    for e in (nc.tensor, nc.scalar, nc.vector, nc.gpsimd):
        p.wait(e, "ld")
    ident, _, _, ones_row = make_consts(p)

    p.V("tensor_reduce", asum[:], w_sb[:].rearrange("p c o -> p (c o)"), axis=AX.X,
        op=ALU.add, apply_absolute_value=True)
    p.wait(nc.tensor, "V")
    p.tr(psP[0:1, 0, 0:128], asum[:], ident[:])
    p.wait(nc.vector, "T")
    p.V("tensor_reduce", srow[0:1, 0:1], psP[0:1, 0, 0:128], axis=AX.X, op=ALU.add)
    p.V("tensor_scalar_mul", srow[0:1, 0:1], srow[0:1, 0:1], 0.7 / (D * D))
    p.V("tensor_scalar_mul", srow[0:1, 1:2], srow[0:1, 0:1], -1.0)
    p.wait(nc.tensor, "V")
    p.mm(psP[:, 0, 0:2], ones_row[:], srow[:], start=True, stop=True)
    p.wait(nc.vector, "T")
    p.V("tensor_copy", dcol[:], psP[:, 0, 0:2])
    acol = p.sbuf("acol", [128, 1], F32)
    p.mm(psP[:, 0, 2:3], ones_row[:], alpha_sb[:], start=True, stop=True)
    p.wait(nc.vector, "T")
    p.V("tensor_copy", acol[:], psP[:, 0, 2:3])
    nacol = p.sbuf("nacol", [128, 1], F32)
    p.wait(nc.vector, "T")
    p.V("tensor_scalar_mul", nacol[:], acol[:], -1.0)
    for c in range(DC):
        for g in range(2):
            sl = slice(g * 512, (g + 1) * 512)
            p.V("tensor_scalar", scr5[:, :], w_sb[:, c, sl], dcol[:, 0:1], None,
                op0=ALU.is_gt)
            p.V("scalar_tensor_tensor", scr5[:, :], w_sb[:, c, sl], dcol[:, 1:2],
                scr5[:, :], op0=ALU.is_lt, op1=ALU.subtract)
            p.V("tensor_scalar", wq_bf[:, c, sl], scr5[:, :], nacol[:, 0:1], None,
                op0=ALU.mult)

    emit_ln_norm(p, x_sb, ln_bf, arena, a_row, b_row, ones_row, psP[:])

    for oc in range(DC):
        for tg in range(4):
            tsl = slice(tg * 512, (tg + 1) * 512)
            p.wait(nc.tensor, "V")
            for kc in range(DC):
                p.mm(psQ[:], wq_bf[:, kc, oc * 128:(oc + 1) * 128], ln_bf[:, kc, tsl],
                     start=(kc == 0), stop=(kc == DC - 1))
            p.wait(nc.vector, "T")
            p.V("tensor_scalar", scr5[:, :], psQ[:], b_sb[:, oc:oc + 1], None,
                op0=ALU.add)
            p.V("scalar_tensor_tensor", x_sb[:, oc, tsl], scr5[:, :], 0.5,
                x_sb[:, oc, tsl], op0=ALU.mult, op1=ALU.add)
    p.dma("st", nc.sync, xo.rearrange("(c p) t -> p c t", p=128), x_sb[:],
          wait_chans=("V",))
    p.wait(nc.sync, "st")
    return p


_IDENT_CACHE = {}


def ident_view(p):
    if not hasattr(p, "_ident1"):
        t = p.sbuf("ident1", [1, 1], F32)
        p.V("memset", t[:], 1.0)
        p._ident1 = t
    return p._ident1[0:1, 0:1]


def run_l3_ffn(x_shards, inputs, cache):
    if "l3f" not in cache:
        cache["l3f"] = build_l3_ffn()
    pf = cache["l3f"]
    bcol = np.ascontiguousarray(inputs["L3_ffn_bias"].reshape(DC, 128).T)
    in_maps = [{
        "x_fm": x_shards[c],
        "w_T": np.ascontiguousarray(inputs["L3_ffn_weight"].T),
        "alpha": inputs["L3_ffn_alpha"].reshape(1, 1),
        "bcol": bcol,
    } for c in range(NCORE)]
    res = run_bass_kernel_spmd(pf.nc, in_maps, core_ids=list(range(NCORE)))
    return [np.asarray(r["x_out"]) for r in res.results]



def build_l3_a1():
    p = Prog()
    nc = p.nc
    x = p.din("x_fm", [D, TPC])
    wqkv_d = p.din("w_qkv", [D, 3 * D])
    bqkv_d = p.din("bqkv", [128, 24])
    proj_d = p.din("proj", [16, 64, 64])
    qp_o = p.dout("qp", [16, 64, TPC], BF16)
    kpf_o = p.dout("kpf", [16, 64, TPC], BF16)
    kpt_o = p.dout("kpt", [16, NTILE, 128, 64], BF16)
    vt_o = p.dout("vt", [16, NTILE, 128, 65], BF16)
    st_o = p.dout("st", [16, 64, 65])

    x_sb = p.sbuf("x_sb", [128, DC, TPC], F32)
    ln_bf = p.sbuf("ln_bf", [128, DC, TPC], BF16)
    arena = p.sbuf("arena", [128, 3 * TPC], F32)
    a_row = p.sbuf("a_row", [1, TPC], F32)
    b_row = p.sbuf("b_row", [1, TPC], F32)
    qkv_bf = p.sbuf("qkv_bf", [128, 12, TPC], BF16)
    bqkv_sb = p.sbuf("bqkv_sb", [128, 24], F32)
    proj_sb = p.sbuf("proj_sb", [128, 8, 64], BF16)
    wbuf = p.sbuf("wbuf", [128, 2, DC, 128], BF16)
    qp_sb = p.sbuf("qp_sb", [64, TPC], BF16)
    kpf_sb = p.sbuf("kpf_sb", [64, TPC], BF16)
    kpt_sb = p.sbuf("kpt_sb", [128, NTILE, 64], BF16)
    vt_sb = p.sbuf("vt_sb", [128, NTILE, 65], BF16)
    st_sb = p.sbuf("st_sb", [64, 65], F32)
    psP = p.psum("psP", [128, 2, 512])
    psQ = p.psum("psQ", [128, 512])
    psS = p.psum("psS", [64, 65])
    psT = p.psum("psT", [128, 64], BF16)
    ident_bf = p.sbuf("ident_bf", [128, 128], BF16)
    ones_row = None

    p.dma("ld", nc.sync, x_sb[:], x.rearrange("(c p) t -> p c t", p=128))
    p.dma("ld", nc.sync, bqkv_sb[:], bqkv_d)
    p.dma("ld", nc.gpsimd, proj_sb[0:64, :, :],
          proj_d.rearrange("(hp two) d m -> two d hp m", two=2)[0])
    p.dma("ld", nc.gpsimd, proj_sb[64:128, :, :],
          proj_d.rearrange("(hp two) d m -> two d hp m", two=2)[1])
    for e in (nc.tensor, nc.scalar, nc.vector, nc.gpsimd):
        p.wait(e, "ld")
    ident, _, _, ones_row = make_consts(p)
    p.V("tensor_copy", ident_bf[:], ident[:])

    emit_ln_norm(p, x_sb, ln_bf, arena, a_row, b_row, ones_row, psP[:])

    sc18 = p.cvec(1.0 / 8.0)
    for half in range(2):
      for part3 in range(3):
        def evac_qkv(oc, tg, ps, part3=part3, half=half):
            tsl = slice(tg * 512, (tg + 1) * 512)
            p.wait(nc.scalar, "T")
            goc = part3 * 8 + half * 4 + oc
            p.S("activation", out=qkv_bf[:, part3 * 4 + oc, tsl], in_=ps,
                func=AF.Identity, bias=bqkv_sb[:, goc:goc + 1])
        emit_gemm_stream(p, wqkv_d[:, (part3 * 8 + half * 4) * 128:
                                   (part3 * 8 + half * 4 + 4) * 128],
                         D, 512, ln_bf[:], TPC, psQ[:], evac_qkv, wbuf,
                         "qkv%d_%d" % (half, part3))

      for hh in range(8):
        h = half * 8 + hh
        hb = (h % 2) * 64
        prj = proj_sb[hb:hb + 64, :, :]
        pc = (h // 2)
        lc = hh // 2
        qf = qkv_bf[hb:hb + 64, lc, :]
        kf = qkv_bf[hb:hb + 64, 4 + lc, :]
        vf = qkv_bf[hb:hb + 64, 8 + lc, :]
        for g in range(4):
            gsl = slice(g * 512, (g + 1) * 512)
            p.wait(nc.tensor, "S")
            p.mm(psQ[0:64, :], prj[:, pc, :], qf[:, gsl], start=True, stop=True)
            p.wait(nc.scalar, "T")
            p.S("activation", out=qp_sb[:, gsl], in_=psQ[0:64, 0:512], func=AF.Relu,
                scale=sc18[0:64, :])
            p.wait(nc.tensor, "S")
            p.mm(psQ[0:64, :], prj[:, pc, :], kf[:, gsl], start=True, stop=True)
            p.wait(nc.scalar, "T")
            p.S("activation", out=kpf_sb[:, gsl], in_=psQ[0:64, 0:512], func=AF.Relu,
                scale=sc18[0:64, :])
        for it in range(NTILE):
            tsl = slice(it * 128, (it + 1) * 128)
            p.wait(nc.tensor, "S")
            p.wait(nc.tensor, "V")
            p.mm(psP[:, 0, 0:64], kf[:, tsl], prj[:, pc, :], start=True, stop=True)
            p.tr(psT[:, 0:64], vf[:, tsl], ident_bf[hb:hb + 64, hb:hb + 64])
            p.wait(nc.scalar, "T")
            p.S("activation", out=kpt_sb[:, it, :], in_=psP[:, 0, 0:64], func=AF.Relu,
                scale=sc18[:])
            p.wait(nc.vector, "T")
            p.V("tensor_copy", vt_sb[:, it, 0:64], psT[:, 0:64])
        p.V("memset", vt_sb[:, :, 64:65], 1.0)
        p.wait(nc.tensor, "S")
        p.wait(nc.tensor, "V")
        for it in range(NTILE):
            p.mm(psS[:], kpt_sb[:, it, :], vt_sb[:, it, :],
                 start=(it == 0), stop=(it == NTILE - 1))
        p.wait(nc.vector, "T")
        p.V("tensor_copy", st_sb[:], psS[:])
        p.dma("st", nc.sync, qp_o[h], qp_sb[:], wait_chans=("S",))
        p.dma("st", nc.sync, kpf_o[h], kpf_sb[:])
        p.dma("st", nc.sync, kpt_o[h].rearrange("i p m -> p i m"), kpt_sb[:])
        p.dma("st", nc.sync, vt_o[h].rearrange("i p m -> p i m"), vt_sb[:],
              wait_chans=("V",))
        p.dma("st", nc.sync, st_o[h], st_sb[:])
        p.wait(nc.tensor, "st")
        p.wait(nc.scalar, "st")
        p.wait(nc.vector, "st")
    p.wait(nc.sync, "st")
    return p


def build_l3_a2():
    p = Prog()
    nc = p.nc
    x = p.din("x_fm", [D, TPC])
    qp_d = p.din("qp", [16, 64, TPC], BF16)
    kpf_d = p.din("kpf", [16, 64, TPC], BF16)
    kpt_d = p.din("kpt", [16, NTILE, 128, 64], BF16)
    vt_d = p.din("vt", [16, NTILE, 128, 65], BF16)
    prev_d = p.din("prev", [3, 16, 64, 65])
    wout_d = p.din("w_out", [D, D])
    bout_d = p.din("bcol", [128, DC])
    xo = p.dout("x_out", [D, TPC])

    x_sb = p.sbuf("x_sb", [128, DC, TPC], F32)
    qp_sb = p.sbuf("qp_sb", [64, TPC], BF16)
    kpf_sb = p.sbuf("kpf_sb", [64, TPC], BF16)
    kpt_sb = p.sbuf("kpt_sb", [128, NTILE, 64], BF16)
    vt_sb = p.sbuf("vt_sb", [128, NTILE, 65], BF16)
    prev_sb = p.sbuf("prev_sb", [64, 3, 16, 65], F32)
    st32 = p.sbuf("st32", [64, 65], F32)
    st_bf = p.sbuf("st_bf", [64, 65], BF16)
    ao_sb = p.sbuf("ao_sb", [128, DC, TPC], BF16)
    g_bf = p.sbuf("g_bf", [128, 128], BF16)
    num_sb = p.sbuf("num_sb", [64, 128], F32)
    den_r = p.sbuf("den_r", [1, 128], F32)
    tri = p.sbuf("tri", [128, 128], BF16)
    b_sb = p.sbuf("b_sb", [128, DC], F32)
    wbuf = p.sbuf("wbuf", [128, 2, DC, 128], BF16)
    scr5 = p.sbuf("scr5", [128, 512], F32)
    psG = p.psum("psG", [128, 2, 128])
    psA = p.psum("psA", [65, 128])
    psU = p.psum("psU", [64, 65])
    psB = p.psum("psB", [64, 128])
    psQ = p.psum("psQ", [128, 512])

    p.dma("ld", nc.sync, x_sb[:], x.rearrange("(c p) t -> p c t", p=128))
    p.dma("ld", nc.sync, prev_sb[:], prev_d.rearrange("s h d m -> d s h m"))
    p.dma("ld", nc.sync, b_sb[:], bout_d)
    for e in (nc.tensor, nc.scalar, nc.vector, nc.gpsimd):
        p.wait(e, "ld")
    ident, _, ones, ones_row = make_consts(p)
    iota_p2 = p.sbuf("iota_p2", [128, 1], F32)
    iota_f2 = p.sbuf("iota_f2", [128, 128], F32)
    p.G("iota", iota_p2[:], pattern=[[0, 1]], base=0, channel_multiplier=1,
        allow_small_or_imprecise_dtypes=True)
    p.G("iota", iota_f2[:], pattern=[[1, 128]], base=0, channel_multiplier=0,
        allow_small_or_imprecise_dtypes=True)
    p.wait(nc.vector, "G")
    p.V("tensor_scalar", tri[:], iota_f2[:], iota_p2[:], None, op0=ALU.is_ge)
    eps = p.cvec(1e-6, 1)

    for h in range(16):
        p.dma("lh", nc.sync, qp_sb[:], qp_d[h], wait_chans=("T",))
        p.dma("lh", nc.sync, kpf_sb[:], kpf_d[h])
        p.dma("lh", nc.sync, kpt_sb[:], kpt_d[h].rearrange("i p m -> p i m"))
        p.dma("lh", nc.sync, vt_sb[:], vt_d[h].rearrange("i p m -> p i m"))
        p.wait(nc.vector, "lh")
        hsl = [prev_sb[:, s, h, :] for s in range(3)]
        p.V("tensor_add", st32[:], hsl[0], hsl[1])
        p.V("tensor_add", st32[:], st32[:], hsl[2])
        p.wait(nc.vector, "lh")
        for it in range(NTILE):
            tsl = slice(it * 128, (it + 1) * 128)
            p.wait(nc.vector, "T")
            p.V("tensor_copy", st_bf[:], st32[:])
            p.wait(nc.tensor, "V")
            p.wait(nc.tensor, "S")
            p.mm(psG[:, 0, :], kpf_sb[:, tsl], qp_sb[:, tsl], start=True, stop=True)
            p.wait(nc.vector, "T")
            p.V("tensor_mul", g_bf[:], psG[:, 0, 0:128], tri[:])
            p.wait(nc.tensor, "V")
            p.mm(psA[:], vt_sb[:, it, :], g_bf[:], start=True, stop=False)
            p.mm(psA[:], st_bf[:], qp_sb[:, tsl], start=False, stop=True)
            p.wait(nc.vector, "T")
            p.V("tensor_copy", num_sb[:], psA[0:64, :])
            p.V("tensor_scalar", den_r[:], psA[64:65, :], eps[0:1, 0:1], None,
                op0=ALU.add)
            p.V("reciprocal", den_r[:], den_r[:])
            p.wait(nc.tensor, "V")
            p.mm(psB[:], ones_row[0:1, 0:64], den_r[:], start=True, stop=True)
            p.wait(nc.vector, "T")
            hb = (h % 2) * 64
            p.V("tensor_mul", ao_sb[hb:hb + 64, h // 2, tsl], num_sb[:], psB[:, :])
            p.mm(psU[:], kpt_sb[:, it, :], vt_sb[:, it, :], start=True, stop=True)
            p.wait(nc.vector, "T")
            p.V("tensor_add", st32[:], st32[:], psU[:, :])

    def evac(oc, tg, ps):
        tsl = slice(tg * 512, (tg + 1) * 512)
        p.wait(nc.vector, "T")
        p.V("tensor_scalar", scr5[:, :], ps, b_sb[:, oc:oc + 1], None, op0=ALU.add)
        p.V("tensor_add", x_sb[:, oc, tsl], x_sb[:, oc, tsl], scr5[:, :])
    emit_gemm_stream(p, wout_d, D, D, ao_sb[:], TPC, psQ[:], evac, wbuf, "wo")
    p.dma("st", nc.sync, xo.rearrange("(c p) t -> p c t", p=128), x_sb[:],
          wait_chans=("V",))
    p.wait(nc.sync, "st")
    return p


def run_l3_attn(x_shards, inputs, cache):
    if "l3a1" not in cache:
        cache["l3a1"] = build_l3_a1()
    p1 = cache["l3a1"]
    bq = np.ascontiguousarray(inputs["L3_attn_b_qkv"].reshape(24, 128).T)
    in_maps = [{
        "x_fm": x_shards[c], "w_qkv": inputs["L3_attn_w_qkv"],
        "bqkv": bq, "proj": inputs["L3_attn_proj"],
    } for c in range(NCORE)]
    res1 = run_bass_kernel_spmd(p1.nc, in_maps, core_ids=list(range(NCORE)))
    parts = [np.asarray(r["st"]) for r in res1.results]

    if "l3a2" not in cache:
        cache["l3a2"] = build_l3_a2()
    p2 = cache["l3a2"]
    in_maps = []
    for c in range(NCORE):
        b, q = c // 4, c % 4
        prev = np.zeros((3, 16, 64, 65), dtype=np.float32)
        for j2 in range(q):
            prev[j2] = parts[b * 4 + j2]
        r = res1.results[c]
        in_maps.append({
            "x_fm": x_shards[c],
            "qp": np.asarray(r["qp"]), "kpf": np.asarray(r["kpf"]),
            "kpt": np.asarray(r["kpt"]), "vt": np.asarray(r["vt"]),
            "prev": prev,
            "w_out": inputs["L3_attn_w_out"],
            "bcol": np.ascontiguousarray(inputs["L3_attn_b_out"].reshape(DC, 128).T),
        })
    res2 = run_bass_kernel_spmd(p2.nc, in_maps, core_ids=list(range(NCORE)))
    return [np.asarray(r["x_out"]) for r in res2.results]


EXTA = 2688
XCA = EXTA + 1


ILA_GRPS = [(g * 512, 512) for g in range(5)] + [(2560, 128)]


def build_l1_ila():
    p = Prog()
    nc = p.nc
    x_d = p.din("x_ext", [D, XCA])
    wqkv_d = p.din("wqkv", [D, 1536])
    qs_d = p.din("qs", [64, 1])
    ks_d = p.din("ks", [64, 1])
    cosS_d = p.din("cosS", [64, EXTA], BF16)
    sinS_d = p.din("sinS", [64, EXTA], BF16)
    cosI_d = p.din("cosI", [64, EXTA], BF16)
    sinI_d = p.din("sinI", [64, EXTA], BF16)
    mask_d = p.din("maskb", [16, 128, 640], BF16)
    wo_d = p.din("wo", [512, D])
    rt_d = p.din("rotT", [64, 64], BF16)
    xo = p.dout("x_out", [D, TPC])

    A1 = p.sbuf("A1", [128, 8 * XCA], BF16)
    qn = p.sbuf("qn", [64, 8, EXTA], BF16)
    kn = p.sbuf("kn", [64, 8, EXTA], BF16)
    vp = p.sbuf("vp", [128, 4, EXTA], BF16)
    vt = p.sbuf("vt", [128, 21, 8, 65], BF16)
    wbuf = p.sbuf("wbuf", [128, 2, DC, 128], BF16)
    ssq_row = p.sbuf("ssq_row", [1, EXTA], F32)
    r_row = p.sbuf("r_row", [1, EXTA], F32)
    scrS = p.sbuf("scrS", [64, EXTA], BF16)
    p_bf = p.sbuf("p_bf", [128, 640], BF16)
    pt_bf = p.sbuf("pt_bf", [128, 128], BF16)
    num_sb = p.sbuf("num_sb", [64, 128], F32)
    den_r = p.sbuf("den_r", [1, 128], F32)
    xrow = p.sbuf("xrow", [128, 512], F32)
    qs_sb = p.sbuf("qs_sb", [64, 2], F32)
    ident_bf = p.sbuf("ident_bf2", [128, 128], BF16)
    psS = p.psum("psS", [128, 768])
    psQ = p.psum("psQ", [128, 512])
    psA = p.psum("psA", [65, 128])
    psB = p.psum("psB", [64, 128])
    psT = p.psum("psT2", [128, 64], BF16)
    psTP = p.psum("psTP", [128, 128], BF16)
    psR = p.psum("psR", [64, 512])

    xq = A1[:, 0:8 * XCA].rearrange("p (c t) -> p c t", c=8)
    masks = A1[:, 0:16 * 640].rearrange("p (i k) -> p i k", i=16)
    o_fm = A1[:, 16 * 640:16 * 640 + 4 * TPC].rearrange("p (c t) -> p c t", c=4)
    tabs = vp[0:64, :, :]

    p.dma("ld", nc.gpsimd, xq, x_d.rearrange("(c p) t -> p c t", p=128))
    p.dma("ld", nc.sync, qs_sb[:, 0:1], qs_d)
    p.dma("ld", nc.sync, qs_sb[:, 1:2], ks_d)
    rt_bf = p.sbuf("rt_bf", [64, 64], BF16)
    p.dma("ld", nc.sync, rt_bf[:], rt_d)
    for e in (nc.tensor, nc.scalar, nc.vector, nc.gpsimd):
        p.wait(e, "ld")
    ident, _, ones, ones_row = make_consts(p)
    p.V("tensor_copy", ident_bf[:], ident[:])

    def act_fn(kc, tsl):
        off = 1 if kc < 4 else 0
        return xq[:, kc, off + tsl.start: off + tsl.stop]
    for part3 in range(3):
        wsl = wqkv_d[:, part3 * 512:(part3 + 1) * 512]
        if part3 < 2:
            dst = qn if part3 == 0 else kn
            for oc in range(8):
                p.dma("w", nc.gpsimd, wbuf[:, oc % 2, :, 0:64],
                      wsl.rearrange("(kc p) o -> p kc o", p=128)
                      [:, :, oc * 64:(oc + 1) * 64], wait_chans=("T",))
                p.wait(nc.tensor, "w")
                for g0, gw in ILA_GRPS:
                    tsl = slice(g0, g0 + gw)
                    p.wait(nc.tensor, "V")
                    for kc in range(DC):
                        p.mm(psR[:, 0:gw], wbuf[:, oc % 2, kc, 0:64], act_fn(kc, tsl),
                             start=(kc == 0), stop=(kc == DC - 1))
                    p.wait(nc.vector, "T")
                    p.V("tensor_copy", dst[:, oc, tsl], psR[:, 0:gw])
        else:
            for oc in range(4):
                p.dma("w", nc.gpsimd, wbuf[:, oc % 2],
                      wsl.rearrange("(kc p) o -> p kc o", p=128)
                      [:, :, oc * 128:(oc + 1) * 128], wait_chans=("T",))
                p.wait(nc.tensor, "w")
                for g0, gw in ILA_GRPS:
                    tsl = slice(g0, g0 + gw)
                    p.wait(nc.tensor, "V")
                    for kc in range(DC):
                        p.mm(psQ[:, 0:gw], wbuf[:, oc % 2, kc, :], act_fn(kc, tsl),
                             start=(kc == 0), stop=(kc == DC - 1))
                    p.wait(nc.vector, "T")
                    p.V("tensor_copy", vp[:, oc, tsl], psQ[:, 0:gw])

    for h in range(8):
        hb = (h % 2) * 64
        for it in range(21):
            tsl = slice(it * 128, (it + 1) * 128)
            p.wait(nc.tensor, "V")
            p.tr(psT[:, 0:64], vp[hb:hb + 64, h // 2, tsl],
                 ident_bf[hb:hb + 64, hb:hb + 64])
            p.wait(nc.vector, "T")
            p.V("tensor_copy", vt[:, it, h, 0:64], psT[:, 0:64])
    p.V("memset", vt[:, :, :, 64:65], 1.0)

    p.dma("tb", nc.sync, tabs[:, 0, :], cosS_d, wait_chans=("V",))
    p.dma("tb", nc.sync, tabs[:, 1, :], sinS_d)
    p.dma("tb", nc.sync, tabs[:, 2, :], cosI_d)
    p.dma("tb", nc.sync, tabs[:, 3, :], sinI_d)
    p.wait(nc.vector, "tb")
    p.wait(nc.scalar, "tb")

    sq_ones = p.sbuf("sq1b", [64, 1], BF16)
    p.V("memset", sq_ones[:], 1.0)
    clipv = p.cvec(1e-12, 1)
    for which, dst, scol, ctab, stab in ((0, qn, qs_sb[:, 0:1], 0, 1),
                                         (1, kn, qs_sb[:, 1:2], 2, 3)):
        for h in range(8):
            hv = dst[:, h, :]
            p.wait(nc.scalar, "T")
            p.S("activation", out=scrS[:, :], in_=hv, func=AF.Square)
            p.wait(nc.tensor, "S")
            for g0, gw in ILA_GRPS:
                gsl = slice(g0, g0 + gw)
                p.mm(psR[0:1, 0:gw], sq_ones[:], scrS[:, gsl], start=True,
                     stop=True, inc=True)
                p.wait(nc.vector, "T")
                p.V("tensor_copy", ssq_row[0:1, gsl], psR[0:1, 0:gw])
                p.wait(nc.tensor, "V")
            p.wait(nc.scalar, "V")
            p.S("activation", out=r_row[:], in_=ssq_row[:], func=AF.Sqrt)
            p.wait(nc.vector, "S")
            p.V("tensor_scalar", r_row[:], r_row[:], clipv[0:1, 0:1], None, op0=ALU.max)
            p.V("reciprocal", r_row[:], r_row[:])
            p.wait(nc.tensor, "V")
            for g0, gw in ILA_GRPS:
                gsl = slice(g0, g0 + gw)
                p.mm(psR[:, 0:gw], ones_row[0:1, 0:64], r_row[0:1, gsl], start=True,
                     stop=True, inc=True)
                p.wait(nc.vector, "T")
                p.V("tensor_mul", hv[:, gsl], hv[:, gsl], psR[:, 0:gw])
                p.wait(nc.tensor, "V")
            p.V("tensor_scalar", hv[:, :], hv[:, :], scol, None, op0=ALU.mult)
            ct = tabs[:, ctab, :]
            st = tabs[:, stab, :]
            for g0, gw in ILA_GRPS:
                gsl = slice(g0, g0 + gw)
                p.wait(nc.tensor, "V")
                p.mm(psR[:, 0:gw], rt_bf[:], hv[:, gsl], start=True, stop=True,
                     inc=True)
                p.wait(nc.vector, "T")
                p.V("tensor_mul", scrS[:, gsl], psR[:, 0:gw], st[:, gsl])
                p.V("tensor_mul", hv[:, gsl], hv[:, gsl], ct[:, gsl])
                p.V("tensor_add", hv[:, gsl], hv[:, gsl], scrS[:, gsl])

    p.dma("mk", nc.sync, masks, mask_d.rearrange("i p k -> p i k"), wait_chans=("V",))
    for e in (nc.vector, nc.scalar, nc.tensor):
        p.wait(e, "mk")

    sc8 = p.cvec(8.0)
    b8 = p.sbuf("b8col", [128, 1], F32)
    mcol = p.sbuf("mcol", [128, 1], F32)
    for h in range(8):
        hb = (h % 2) * 64
        for i in range(16):
            qsl = slice(640 + i * 128, 640 + (i + 1) * 128)
            k0 = i * 128 + 128
            p.wait(nc.tensor, "V")
            p.wait(nc.tensor, "S")
            p.mm(psS[:, 0:512], qn[:, h, qsl], kn[:, h, k0:k0 + 512],
                 start=True, stop=True, inc=False)
            p.mm(psS[:, 512:640], qn[:, h, qsl], kn[:, h, k0 + 512:k0 + 640],
                 start=True, stop=True)
            p.wait(nc.vector, "T")
            p.V("tensor_add", psS[:, 0:640], psS[:, 0:640], masks[:, i, :])
            p.V("tensor_reduce", mcol[:], psS[:, 0:640], axis=AX.X, op=ALU.max,
                negate=True)
            p.wait(nc.vector, "S")
            p.V("tensor_scalar_mul", b8[:], mcol[:], 8.0)
            p.wait(nc.scalar, "V")
            p.S("activation", out=p_bf[:, :], in_=psS[:, 0:640], func=AF.Exp,
                scale=sc8[:], bias=b8[:])
            for kb in range(5):
                p.wait(nc.tensor, "S" if kb == 0 else "V")
                p.tr(psTP[:, :], p_bf[:, kb * 128:(kb + 1) * 128], ident_bf[:])
                p.wait(nc.vector, "T")
                p.V("tensor_copy", pt_bf[:, :], psTP[:, :])
                p.wait(nc.tensor, "V")
                p.mm(psA[:], vt[:, i + 1 + kb, h, :], pt_bf[:, 0:128],
                     start=(kb == 0), stop=(kb == 4))
            p.wait(nc.vector, "T")
            p.V("tensor_copy", num_sb[:], psA[0:64, :])
            p.V("reciprocal", den_r[:], psA[64:65, :])
            p.wait(nc.tensor, "V")
            p.mm(psB[:], ones_row[0:1, 0:64], den_r[:], start=True, stop=True)
            p.wait(nc.vector, "T")
            p.V("tensor_mul", o_fm[hb:hb + 64, h // 2, i * 128:(i + 1) * 128],
                num_sb[:], psB[:, :])

    for oc in range(DC):
        p.dma("w2", nc.gpsimd, wbuf[:, oc % 2, 0:4],
              wo_d.rearrange("(kc p) o -> p kc o", p=128)[:, :, oc * 128:(oc + 1) * 128],
              wait_chans=("T",))
        p.wait(nc.tensor, "w2")
        for tg in range(4):
            tsl = slice(tg * 512, (tg + 1) * 512)
            p.dma("xl", nc.sync, xrow[:],
                  x_d.rearrange("(c p) t -> p c t", p=128)[:, oc, 641 + tg * 512:
                                                           641 + (tg + 1) * 512],
                  wait_chans=("V",))
            p.wait(nc.tensor, "V")
            for kc in range(4):
                p.mm(psQ[:], wbuf[:, oc % 2, kc, :], o_fm[:, kc, tsl],
                     start=(kc == 0), stop=(kc == 3))
            p.wait(nc.vector, "T")
            p.wait(nc.vector, "xl")
            p.V("tensor_add", xrow[:, :], xrow[:, :], psQ[:, 0:512])
            p.dma("st", nc.sync, xo.rearrange("(c p) t -> p c t", p=128)[:, oc, tsl],
                  xrow[:], wait_chans=("V",))
            p.wait(nc.vector, "st")
    p.wait(nc.sync, "st")
    return p


HIDP = 2816


def build_l1_ilb():
    p = Prog()
    nc = p.nc
    x_d = p.din("x_ext", [D, TPC + 1])
    w1a_d = p.din("w1a", [D, HIDP])
    w1g_d = p.din("w1g", [D, HIDP])
    w2_d = p.din("w2p", [HIDP, D])
    b1a_d = p.din("b1a", [128, 22])
    b1g_d = p.din("b1g", [128, 22])
    b2_d = p.din("b2c", [128, DC])
    lg_d = p.din("lgc", [128, DC])
    lb_d = p.din("lbc", [128, DC])
    xo = p.dout("x_out", [D, TPC])

    x_sb = p.sbuf("x_sb", [128, DC, TPC + 1], F32)
    lnf = p.sbuf("lnf", [128, DC, TPC], BF16)
    geglu = p.sbuf("geglu", [128, 22, 512], BF16)
    gel_s = p.sbuf("gel_s", [128, 512], F32)
    a_s = p.sbuf("a_s", [128, 512], F32)
    arena = p.sbuf("arena", [128, 3 * TPC], F32)
    a_row = p.sbuf("a_row", [1, TPC], F32)
    b_row = p.sbuf("b_row", [1, TPC], F32)
    wbuf = p.sbuf("wbuf", [128, 2, DC, 128], BF16)
    w2buf = p.sbuf("w2buf", [128, 2, 22, 128], BF16)
    bcols = p.sbuf("bcols", [128, 22 * 2 + DC * 3], F32)
    scr5 = p.sbuf("scr5", [128, 512], F32)
    psP = p.psum("psP", [128, 2, 512])
    psQ = p.psum("psQ", [128, 512])
    psH = p.psum("psH", [128, 2, 512])

    b1a = bcols[:, 0:22]
    b1g = bcols[:, 22:44]
    b2c = bcols[:, 44:44 + DC]
    lgc = bcols[:, 44 + DC:44 + 2 * DC]
    lbc = bcols[:, 44 + 2 * DC:44 + 3 * DC]

    p.dma("ld", nc.sync, x_sb[:], x_d.rearrange("(c p) t -> p c t", p=128))
    p.dma("ld", nc.sync, b1a, b1a_d)
    p.dma("ld", nc.sync, b1g, b1g_d)
    p.dma("ld", nc.sync, b2c, b2_d)
    p.dma("ld", nc.sync, lgc, lg_d)
    p.dma("ld", nc.sync, lbc, lb_d)
    for e in (nc.tensor, nc.scalar, nc.vector, nc.gpsimd):
        p.wait(e, "ld")
    ident, _, ones, ones_row = make_consts(p)

    def hview(c, tsl):
        off = 1 if c < 4 else 0
        return x_sb[:, c, off + tsl.start: off + tsl.stop]

    scr = arena[:, 2 * TPC:3 * TPC]
    mu_row = arena[0:1, 0:TPC]
    ps_a, ps_b = psP[0:1, 0, :], psP[0:1, 1, :]
    p.wait(nc.tensor, "V")
    for g in range(4):
        sl = slice(g * 512, (g + 1) * 512)
        for c in range(DC):
            p.mm(ps_a, ones[:], hview(c, sl), start=(c == 0), stop=(c == DC - 1))
        for c in range(DC):
            p.wait(nc.scalar, "T")
            p.S("activation", out=scr[:, 0:512], in_=hview(c, sl), func=AF.Square)
            p.wait(nc.tensor, "S")
            p.mm(ps_b, ones[:], scr[:, 0:512], start=(c == 0), stop=(c == DC - 1),
                 inc=True)
        p.wait(nc.vector, "T")
        p.V("tensor_scalar_mul", mu_row[0:1, sl], ps_a, 1.0 / D)
        p.V("tensor_scalar_mul", b_row[:, sl], ps_b, 1.0 / D)
        if g < 3:
            p.wait(nc.tensor, "V")
    p.wait(nc.scalar, "V")
    p.S("activation", out=scr[0:1, 0:TPC], in_=mu_row[0:1, 0:TPC], func=AF.Square)
    p.wait(nc.vector, "S")
    p.V("tensor_sub", b_row[:], b_row[:], scr[0:1, 0:TPC])
    eps = p.cvec(1e-5, 1)
    p.wait(nc.scalar, "V")
    p.S("activation", out=scr[0:1, 0:TPC], in_=b_row[:], func=AF.Sqrt,
        bias=eps[0:1, 0:1])
    p.wait(nc.vector, "S")
    p.V("reciprocal", a_row[:], scr[0:1, 0:TPC])
    p.V("scalar_tensor_tensor", b_row[:], mu_row[0:1, 0:TPC], -1.0, a_row[:],
        op0=ALU.mult, op1=ALU.mult)
    a_bc = arena[:, 0 * TPC:1 * TPC]
    b_bc = arena[:, 1 * TPC:2 * TPC]
    emit_bcast_rows(p, a_row[:], a_bc, ones_row, ps=psP[:, 0, :], tag="a")
    emit_bcast_rows(p, b_row[:], b_bc, ones_row, ps=psP[:, 0, :], tag="b")
    full = slice(0, TPC)
    for c in range(DC):
        p.V("tensor_mul", scr[:, 0:TPC], hview(c, full), a_bc)
        p.V("tensor_add", scr[:, 0:TPC], scr[:, 0:TPC], b_bc)
        p.V("tensor_scalar", lnf[:, c, :], scr[:, 0:TPC], lgc[:, c:c + 1],
            lbc[:, c:c + 1], op0=ALU.mult, op1=ALU.add)

    for tg in range(4):
        tsl = slice(tg * 512, (tg + 1) * 512)
        for fc in range(22):
            para = fc % 2
            p.dma("wa", nc.gpsimd, wbuf[:, para],
                  w1a_d.rearrange("(kc p) o -> p kc o", p=128)
                  [:, :, fc * 128:(fc + 1) * 128], wait_chans=("T",))
            p.dma("wg", nc.gpsimd, w2buf[:, para, 0:DC],
                  w1g_d.rearrange("(kc p) o -> p kc o", p=128)
                  [:, :, fc * 128:(fc + 1) * 128], wait_chans=("T",))
            p.wait(nc.tensor, "wa")
            p.wait(nc.tensor, "wg")
            p.wait(nc.tensor, "S")
            p.wait(nc.tensor, "V")
            for kc in range(DC):
                p.mm(psH[:, 0, :], wbuf[:, para, kc, :], lnf[:, kc, tsl],
                     start=(kc == 0), stop=(kc == DC - 1))
            for kc in range(DC):
                p.mm(psH[:, 1, :], w2buf[:, para, kc, :], lnf[:, kc, tsl],
                     start=(kc == 0), stop=(kc == DC - 1), inc=(kc == DC - 1))
            p.wait(nc.scalar, "T")
            p.S("activation", out=gel_s[:, :], in_=psH[:, 1, :], func=AF.Gelu,
                bias=b1g[:, fc:fc + 1])
            p.wait(nc.vector, "T")
            p.wait(nc.vector, "S")
            p.V("tensor_scalar", a_s[:, :], psH[:, 0, :], b1a[:, fc:fc + 1], None,
                op0=ALU.add)
            p.V("tensor_mul", geglu[:, fc, :], a_s[:, :], gel_s[:, :])
        for oc in range(DC):
            paro = oc % 2
            p.dma("w2", nc.gpsimd, w2buf[:, paro],
                  w2_d.rearrange("(kc p) o -> p kc o", p=128)
                  [:, :, oc * 128:(oc + 1) * 128], wait_chans=("T",))
            p.wait(nc.tensor, "w2")
            p.wait(nc.tensor, "V")
            for kc in range(22):
                p.mm(psQ[:], w2buf[:, paro, kc, :], geglu[:, kc, :],
                     start=(kc == 0), stop=(kc == 21))
            p.wait(nc.vector, "T")
            p.V("tensor_scalar", scr5[:, :], psQ[:], b2c[:, oc:oc + 1], None,
                op0=ALU.add)
            p.V("tensor_add", x_sb[:, oc, 1 + tsl.start:1 + tsl.stop],
                x_sb[:, oc, 1 + tsl.start:1 + tsl.stop], scr5[:, :])
    p.dma("st", nc.sync, xo.rearrange("(c p) t -> p c t", p=128), x_sb[:, :, 1:],
          wait_chans=("V",))
    p.wait(nc.sync, "st")
    return p


def build_ln_f32():
    p = Prog()
    nc = p.nc
    x = p.din("x_fm", [D, TPC])
    lo = p.dout("ln_out", [D, TPC])
    x_sb = p.sbuf("x_sb", [128, DC, TPC], F32)
    arena = p.sbuf("arena", [128, 3 * TPC], F32)
    a_row = p.sbuf("a_row", [1, TPC], F32)
    b_row = p.sbuf("b_row", [1, TPC], F32)
    ones_row = p.sbuf("c_onesrow", [1, 128], F32)
    psP = p.psum("psP", [128, 2, 512])
    p.dma("ld", nc.sync, x_sb[:], x.rearrange("(c p) t -> p c t", p=128))
    for e in (nc.tensor, nc.scalar, nc.vector, nc.gpsimd):
        p.wait(e, "ld")
    p.V("memset", ones_row[:], 1.0)
    scr = arena[:, 2 * TPC:3 * TPC]
    emit_ln_stats(p, x_sb, TPC, a_row, b_row, scr,
                  ps_a=psP[0:1, 0, :], ps_b=psP[0:1, 1, :],
                  mu_row=arena[0:1, 0:TPC])
    a_bc = arena[:, 0 * TPC:1 * TPC]
    b_bc = arena[:, 1 * TPC:2 * TPC]
    emit_bcast_rows(p, a_row[:], a_bc, ones_row, ps=psP[:, 0, :], tag="a")
    emit_bcast_rows(p, b_row[:], b_bc, ones_row, ps=psP[:, 0, :], tag="b")
    for c in range(DC):
        p.V("tensor_mul", x_sb[:, c, :], x_sb[:, c, :], a_bc)
        p.V("tensor_add", x_sb[:, c, :], x_sb[:, c, :], b_bc)
    p.dma("st", nc.sync, lo.rearrange("(c p) t -> p c t", p=128), x_sb[:],
          wait_chans=("V",))
    p.wait(nc.sync, "st")
    return p


def build_axpy():
    p = Prog()
    nc = p.nc
    x = p.din("x_fm", [D, TPC])
    y = p.din("y32", [D, TPC])
    xo = p.dout("x_out", [D, TPC])
    x_sb = p.sbuf("x_sb", [128, DC, TPC], F32)
    y_sb = p.sbuf("y_sb", [128, DC, TPC], F32)
    p.dma("ld", nc.sync, x_sb[:], x.rearrange("(c p) t -> p c t", p=128))
    p.dma("ld", nc.sync, y_sb[:], y.rearrange("(c p) t -> p c t", p=128))
    p.wait(nc.vector, "ld")
    for c in range(DC):
        p.V("scalar_tensor_tensor", x_sb[:, c, :], y_sb[:, c, :], 0.5, x_sb[:, c, :],
            op0=ALU.mult, op1=ALU.add)
    p.dma("st", nc.sync, xo.rearrange("(c p) t -> p c t", p=128), x_sb[:],
          wait_chans=("V",))
    p.wait(nc.sync, "st")
    return p


def _xpos_tables(ext_lo, n=8192):
    DH = 64
    W2 = 256
    t = np.arange(ext_lo, ext_lo + EXTA, dtype=np.float32)
    t = np.maximum(t, 0.0)
    half = np.arange(0, DH, 2, dtype=np.float32)
    inv = 1.0 / 10000.0 ** (half / DH)
    fr = t[:, None] * inv[None, :]
    fr = np.concatenate([fr, fr], 1)
    base = (half + 0.4 * DH) / (1.4 * DH)
    power = (t - n // 2) / float(W2)
    sc = base[None, :] ** power[:, None]
    sc = np.concatenate([sc, sc], 1)
    cos, sin = np.cos(fr), np.sin(fr)
    return (cos * sc).T, (sin * sc).T, (cos / sc).T, (sin / sc).T


def run_l1_ffn(x_shards, inputs, cache, call_tag=""):
    import ml_dtypes
    bf16 = ml_dtypes.bfloat16
    if "ln_f32" not in cache:
        cache["ln_f32"] = build_ln_f32()
    pL = cache["ln_f32"]
    res = run_bass_kernel_spmd(pL.nc, [{"x_fm": x_shards[c]} for c in range(NCORE)],
                               core_ids=list(range(NCORE)))
    y_b = [np.concatenate([np.asarray(res.results[b * 4 + q]["ln_out"]) for q in range(4)],
                          axis=1) for b in range(2)]

    if "il_a" not in cache:
        cache["il_a"] = build_l1_ila()
    if "il_b" not in cache:
        cache["il_b"] = build_l1_ilb()
    pA, pB = cache["il_a"], cache["il_b"]

    for l in range(2):
        wqkv = inputs["L1_ffn_wqkv"][l]
        qs = inputs["L1_ffn_q_scale"][l].reshape(64, 1)
        ks = inputs["L1_ffn_k_scale"][l].reshape(64, 1)
        wo = inputs["L1_ffn_wo"][l]
        in_maps = []
        for c in range(NCORE):
            b, q = c // 4, c % 4
            G = q * TPC
            xe = np.zeros((D, XCA), dtype=np.float32)
            lo = G - 641
            s0 = max(0, -lo)
            xe[:, s0:] = y_b[b][:, max(lo, 0):G + TPC]
            cS, sS, cI, sI = _xpos_tables(G - 640)
            mb = np.zeros((16, 128, 640), dtype=np.float32)
            pp = np.arange(128)[:, None]
            jj = np.arange(640)[None, :]
            for i in range(16):
                valid = (jj >= pp) & (jj <= pp + 512) & (G + 128 * i - 512 + jj >= 0)
                mb[i] = np.where(valid, 0.0, -1e30)
            rotT = np.zeros((64, 64), dtype=np.float32)
            for o in range(32):
                rotT[32 + o, o] = -1.0
                rotT[o, 32 + o] = 1.0
            in_maps.append({
                "x_ext": xe, "wqkv": wqkv, "qs": qs, "ks": ks,
                "cosS": cS.astype(bf16), "sinS": sS.astype(bf16),
                "cosI": cI.astype(bf16), "sinI": sI.astype(bf16),
                "maskb": mb.astype(bf16), "wo": wo, "rotT": rotT.astype(bf16),
            })
        resA = run_bass_kernel_spmd(pA.nc, in_maps, core_ids=list(range(NCORE)))
        for b in range(2):
            y_b[b] = np.concatenate(
                [np.asarray(resA.results[b * 4 + q]["x_out"]) for q in range(4)], axis=1)
        w1 = inputs["L1_ffn_w1"][l]
        w1a = np.zeros((D, HIDP), dtype=np.float32); w1a[:, :2730] = w1[:, :2730]
        w1g = np.zeros((D, HIDP), dtype=np.float32); w1g[:, :2730] = w1[:, 2730:]
        w2p = np.zeros((HIDP, D), dtype=np.float32)
        w2p[:2730] = inputs["L1_ffn_w2"][l]
        b1 = inputs["L1_ffn_b1"][l]
        b1a = np.zeros((128, 22), dtype=np.float32)
        b1g = np.zeros((128, 22), dtype=np.float32)
        b1a.T.reshape(-1)[:2730] = b1[:2730]
        b1g.T.reshape(-1)[:2730] = b1[2730:]
        in_maps = []
        for c in range(NCORE):
            b, q = c // 4, c % 4
            G = q * TPC
            xe = np.zeros((D, TPC + 1), dtype=np.float32)
            lo = G - 1
            s0 = max(0, -lo)
            xe[:, s0:] = y_b[b][:, max(lo, 0):G + TPC]
            in_maps.append({
                "x_ext": xe, "w1a": w1a, "w1g": w1g, "w2p": w2p,
                "b1a": b1a, "b1g": b1g,
                "b2c": np.ascontiguousarray(inputs["L1_ffn_b2"][l].reshape(DC, 128).T),
                "lgc": np.ascontiguousarray(inputs["L1_ffn_ln_g"][l].reshape(DC, 128).T),
                "lbc": np.ascontiguousarray(inputs["L1_ffn_ln_b"][l].reshape(DC, 128).T),
            })
        resB = run_bass_kernel_spmd(pB.nc, in_maps, core_ids=list(range(NCORE)))
        for b in range(2):
            y_b[b] = np.concatenate(
                [np.asarray(resB.results[b * 4 + q]["x_out"]) for q in range(4)], axis=1)

    if "axpy" not in cache:
        cache["axpy"] = build_axpy()
    pX = cache["axpy"]
    in_maps = []
    for c in range(NCORE):
        b, q = c // 4, c % 4
        in_maps.append({"x_fm": x_shards[c],
                        "y32": np.ascontiguousarray(y_b[b][:, q * TPC:(q + 1) * TPC])})
    resX = run_bass_kernel_spmd(pX.nc, in_maps, core_ids=list(range(NCORE)))
    return [np.asarray(r["x_out"]) for r in resX.results]


XCL = 2080


def build_l1_attn_main(debug=False):
    p = Prog()
    nc = p.nc
    x_d = p.din("x_fm", [D, TPC])
    ln_d = p.din("ln_ext", [D, XCL])
    ln0_d = p.din("ln0", [D, 1])
    wloc_d = p.din("wlocT", [D, 3 * D])
    bloc_d = p.din("bloc", [128, 24])
    wcls_d = p.din("wclsT", [D, 3 * D])
    bcls_d = p.din("bcls", [128, 24])
    woL_d = p.din("woLT", [D, D])
    boL_d = p.din("boL", [128, DC])
    mask_d = p.din("maskb", [16, 128, 160], BF16)
    xo = p.dout("x_out", [D, TPC])
    po = p.dout("part", [16, 65])
    if debug:
        qp_o = p.dout("qp_o", [DC, 128, TPC], BF16)
        kp_o = p.dout("kp_o", [DC, 128, XCL], BF16)
        vt_o = p.dout("vt_o", [17, 128, 16, 65], BF16)
        o_o = p.dout("o_o", [DC, 128, TPC], BF16)

    lnx = p.sbuf("lnx", [128, DC, XCL], BF16)
    kp = p.sbuf("kp", [128, DC, XCL], BF16)
    vt = p.sbuf("vt", [128, 17, 16, 65], BF16)
    qp = p.sbuf("qp", [128, DC, TPC], BF16)
    o_fm = p.sbuf("o_fm", [128, DC, TPC], BF16)
    masks = p.sbuf("masks", [128, 16, 160], BF16)
    wbuf = p.sbuf("wbuf", [128, 2, DC, 128], BF16)
    bcols = p.sbuf("bcols", [128, 24 * 2 + DC], F32)
    vrow_bf = p.sbuf("vrow_bf", [128, 512], BF16)
    p_bf = p.sbuf("p_bf", [128, 160], BF16)
    pt_bf = p.sbuf("pt_bf", [128, 128], BF16)
    num_sb = p.sbuf("num_sb", [64, 128], F32)
    den_r = p.sbuf("den_r", [1, 128], F32)
    mcol = p.sbuf("mcol", [128, 1], F32)
    b8 = p.sbuf("b8c", [128, 1], F32)
    xrow = p.sbuf("xrow", [128, 512], F32)
    qcls_row = p.sbuf("qcls_row", [1, D], F32)
    qcol = p.sbuf("qcol", [128, DC], BF16)
    se = p.sbuf("se", [128, NTILE, 16], BF16)
    part_sb = p.sbuf("part_sb", [65, 16], F32)
    ident_bf = p.sbuf("ident_bf3", [128, 128], BF16)
    ln0_sb = p.sbuf("ln0_sb", [128, DC], BF16)
    psS = p.psum("psS", [128, 512])
    psA = p.psum("psA", [65, 128])
    psB = p.psum("psB", [64, 128])
    psTP = p.psum("psTP", [128, 128], BF16)
    psQ = p.psum("psQ", [128, 512])
    psC = p.psum("psC", [128, 16])

    p.dma("ld", nc.gpsimd, lnx[:], ln_d.rearrange("(c p) t -> p c t", p=128))
    p.dma("ld", nc.gpsimd, ln0_sb[:], ln0_d.rearrange("(c p) o -> p c o", p=128), allow_slow_non_contiguous=True)
    p.dma("ld", nc.sync, bcols[:, 0:24], bloc_d)
    p.dma("ld", nc.sync, bcols[:, 24:48], bcls_d)
    p.dma("ld", nc.sync, bcols[:, 48:48 + DC], boL_d)
    p.dma("ld", nc.sync, masks[:], mask_d.rearrange("i p k -> p i k"))
    for e in (nc.tensor, nc.scalar, nc.vector, nc.gpsimd):
        p.wait(e, "ld")
    ident, _, ones, ones_row = make_consts(p)
    p.V("tensor_copy", ident_bf[:], ident[:])

    grps_ext = [(0, 512), (512, 512), (1024, 512), (1536, 512), (2048, 32)]
    grps_own = [(g * 512, 512) for g in range(4)]

    def proj_fm(w_d, col0, dst, grps, bofs, name, aoff=0):
        for oc in range(DC):
            p.dma("w" + name, nc.gpsimd, wbuf[:, oc % 2],
                  w_d.rearrange("(kc p) o -> p kc o", p=128)
                  [:, :, col0 + oc * 128:col0 + (oc + 1) * 128], wait_chans=("T",))
            p.wait(nc.tensor, "w" + name)
            for g0, gw in grps:
                p.wait(nc.tensor, "S")
                for kc in range(DC):
                    p.mm(psQ[:, 0:gw], wbuf[:, oc % 2, kc, :],
                         lnx[:, kc, aoff + g0:aoff + g0 + gw],
                         start=(kc == 0), stop=(kc == DC - 1))
                p.wait(nc.scalar, "T")
                p.S("activation", out=dst[:, oc, g0:g0 + gw], in_=psQ[:, 0:gw],
                    func=AF.Identity, bias=bcols[:, bofs + oc:bofs + oc + 1])

    def v_tiles(w_d, grps, bofs, name, ntl, aoff=0):
        for oc in range(DC):
            p.dma("wv" + name, nc.gpsimd, wbuf[:, oc % 2],
                  w_d.rearrange("(kc p) o -> p kc o", p=128)
                  [:, :, 2 * D + oc * 128:2 * D + (oc + 1) * 128], wait_chans=("T",))
            p.wait(nc.tensor, "wv" + name)
            for g0, gw in grps:
                p.wait(nc.tensor, "S")
                p.wait(nc.tensor, "V")
                for kc in range(DC):
                    p.mm(psQ[:, 0:gw], wbuf[:, oc % 2, kc, :],
                         lnx[:, kc, aoff + g0:aoff + g0 + gw],
                         start=(kc == 0), stop=(kc == DC - 1))
                p.wait(nc.scalar, "T")
                p.S("activation", out=vrow_bf[:, 0:gw], in_=psQ[:, 0:gw],
                    func=AF.Identity, bias=bcols[:, bofs + 16 + oc:bofs + 17 + oc])
                nt = (gw + 127) // 128
                for t2 in range(nt):
                    it = (g0 + t2 * 128) // 128
                    tw = min(128, gw - t2 * 128)
                    p.wait(nc.tensor, "S")
                    p.wait(nc.tensor, "V")
                    p.tr(psTP[0:tw, :], vrow_bf[:, t2 * 128:t2 * 128 + tw],
                         ident_bf[:])
                    p.wait(nc.vector, "T")
                    p.V("tensor_copy", vt[0:tw, it, 2 * oc, 0:64], psTP[0:tw, 0:64])
                    p.V("tensor_copy", vt[0:tw, it, 2 * oc + 1, 0:64],
                        psTP[0:tw, 64:128])
        p.V("memset", vt[:, :, :, 64:65], 1.0)

    proj_fm(wloc_d, D, kp, grps_ext, 8, "kl")
    v_tiles(wloc_d, grps_ext, 0, "l", 17)
    proj_fm(wloc_d, 0, qp, grps_own, 0, "ql", aoff=16)

    sc18 = p.cvec(0.125)
    for h in range(16):
        hb = (h % 2) * 64
        for i in range(16):
            p.wait(nc.tensor, "V")
            p.wait(nc.tensor, "S")
            k0 = i * 128
            p.mm(psS[:, 0:160], qp[hb:hb + 64, h // 2, i * 128:(i + 1) * 128],
                 kp[hb:hb + 64, h // 2, k0:k0 + 160], start=True, stop=True)
            p.wait(nc.vector, "T")
            p.V("tensor_add", psS[:, 0:160], psS[:, 0:160], masks[:, i, :])
            p.V("tensor_reduce", mcol[:], psS[:, 0:160], axis=AX.X, op=ALU.max,
                negate=True)
            p.wait(nc.vector, "S")
            p.V("tensor_scalar_mul", b8[:], mcol[:], 0.125)
            p.wait(nc.scalar, "V")
            p.S("activation", out=p_bf[:, :], in_=psS[:, 0:160], func=AF.Exp,
                scale=sc18[:], bias=b8[:])
            for kb in range(2):
                p.wait(nc.tensor, "S" if kb == 0 else "V")
                kw = 128 if kb == 0 else 32
                p.tr(psTP[0:kw, :], p_bf[:, kb * 128:kb * 128 + kw], ident_bf[:])
                p.wait(nc.vector, "T")
                p.V("tensor_copy", pt_bf[0:kw, :], psTP[0:kw, 0:128])
                p.wait(nc.tensor, "V")
                p.mm(psA[:], vt[0:kw, i + kb, h, :], pt_bf[0:kw, 0:128],
                     start=(kb == 0), stop=(kb == 1))
            p.wait(nc.vector, "T")
            p.V("tensor_copy", num_sb[:], psA[0:64, :])
            p.V("reciprocal", den_r[:], psA[64:65, :])
            p.wait(nc.tensor, "V")
            p.mm(psB[:], ones_row[0:1, 0:64], den_r[:], start=True, stop=True)
            p.wait(nc.vector, "T")
            p.V("tensor_mul", o_fm[hb:hb + 64, h // 2, i * 128:(i + 1) * 128],
                num_sb[:], psB[:, :])

    if debug:
        p.dma("dbg", nc.sync, qp_o.rearrange("c p t -> p c t"), qp[:],
              wait_chans=("V", "S"))
        p.dma("dbg", nc.sync, kp_o.rearrange("c p t -> p c t"), kp[:])
        p.dma("dbg", nc.sync, vt_o.rearrange("i p h m -> p i h m"), vt[:])
        p.dma("dbg", nc.sync, o_o.rearrange("c p t -> p c t"), o_fm[:])
        p.wait(nc.sync, "dbg")
    for oc in range(DC):
        p.dma("wo2", nc.gpsimd, wbuf[:, oc % 2],
              woL_d.rearrange("(kc p) o -> p kc o", p=128)
              [:, :, oc * 128:(oc + 1) * 128], wait_chans=("T",))
        p.wait(nc.tensor, "wo2")
        for tg in range(4):
            tsl = slice(tg * 512, (tg + 1) * 512)
            p.dma("xl", nc.sync, xrow[:],
                  x_d.rearrange("(c p) t -> p c t", p=128)[:, oc, tsl],
                  wait_chans=("V",))
            p.wait(nc.tensor, "V")
            for kc in range(DC):
                p.mm(psQ[:], wbuf[:, oc % 2, kc, :], o_fm[:, kc, tsl],
                     start=(kc == 0), stop=(kc == DC - 1))
            p.wait(nc.vector, "T")
            p.wait(nc.vector, "xl")
            p.V("tensor_scalar", scr5f(p)[:, :], psQ[:], bcols[:, 48 + oc:49 + oc],
                None, op0=ALU.add)
            p.V("tensor_add", xrow[:, :], xrow[:, :], scr5f(p)[:, :])
            p.dma("st", nc.sync, xo.rearrange("(c p) t -> p c t", p=128)[:, oc, tsl],
                  xrow[:], wait_chans=("V",))
            p.wait(nc.vector, "st")

    wqc_buf = p.sbuf("wqc_buf", [128, DC, 512], BF16)
    p.wait(nc.tensor, "st")
    for g in range(2):
        gsl = slice(g * 512, (g + 1) * 512)
        p.dma("wc", nc.gpsimd, wqc_buf[:],
              wcls_d.rearrange("(kc p) o -> p kc o", p=128)[:, :, gsl],
              wait_chans=("T",))
        p.wait(nc.tensor, "wc")
        p.wait(nc.tensor, "V")
        for kc in range(DC):
            p.mm(psQ[0:1, :], ln0_sb[:, kc:kc + 1], wqc_buf[:, kc, :],
                 start=(kc == 0), stop=(kc == DC - 1))
        p.wait(nc.vector, "T")
        p.V("tensor_copy", qcls_row[0:1, gsl], psQ[0:1, 0:512])
    p.wait(nc.tensor, "V")
    for cch in range(DC):
        p.tr(psQ[0:1, 0:128], bcols[:, 24 + cch:25 + cch], ident[:])
        p.wait(nc.vector, "T")
        p.V("tensor_add", qcls_row[0:1, cch * 128:(cch + 1) * 128],
            qcls_row[0:1, cch * 128:(cch + 1) * 128], psQ[0:1, 0:128])
        p.wait(nc.tensor, "V")
    for cch in range(DC):
        p.wait(nc.tensor, "V")
        p.tr(psQ[:, 0:1], qcls_row[0:1, cch * 128:(cch + 1) * 128], ident[0:1, 0:1])
        p.wait(nc.vector, "T")
        p.V("tensor_copy", qcol[:, cch:cch + 1], psQ[:, 0:1])
    proj_fm(wcls_d, D, qp, grps_own, 32, "kc", aoff=16)
    v_tiles(wcls_d, grps_own, 24, "c", 16, aoff=16)
    for h in range(16):
        hb = (h % 2) * 64
        for it in range(NTILE):
            p.wait(nc.tensor, "S")
            p.mm(psS[:, 0:1],
                 qp[hb:hb + 64, h // 2, it * 128:(it + 1) * 128],
                 qcol[hb:hb + 64, h // 2:h // 2 + 1], start=True, stop=True)
            p.wait(nc.scalar, "T")
            p.S("activation", out=se[:, it, h:h + 1], in_=psS[:, 0:1], func=AF.Exp,
                scale=sc18[:])
    p.wait(nc.tensor, "S")
    for h in range(16):
        for it in range(NTILE):
            p.mm(psA[:, 0:1], vt[:, it, h, :], se[:, it, h:h + 1],
                 start=(it == 0), stop=(it == NTILE - 1))
        p.wait(nc.vector, "T")
        p.V("tensor_copy", part_sb[:, h:h + 1], psA[:, 0:1])
        p.wait(nc.tensor, "V")
    p.dma("po", nc.sync, po.rearrange("h m -> m h"), part_sb[:], wait_chans=("V",), allow_slow_non_contiguous=True)
    p.wait(nc.sync, "po")
    return p


def scr5f(p):
    if not hasattr(p, "_scr5f"):
        p._scr5f = p.sbuf("scr5f", [128, 512], F32)
    return p._scr5f


def build_l1_attn_combine():
    p = Prog()
    nc = p.nc
    part_d = p.din("parts", [4, 16, 65])
    wo_d = p.din("woC", [D, D])
    bo_d = p.din("boC", [128, DC])
    x0_d = p.din("x0", [D, 1])
    xo = p.dout("x0_new", [D, 1])

    parts = p.sbuf("parts_sb", [65, 4, 16], F32)
    ssum = p.sbuf("ssum", [65, 16], F32)
    dr = p.sbuf("dr", [1, 16], F32)
    ctx_mh = p.sbuf("ctx_mh", [64, 16], F32)
    ctx_col = p.sbuf("ctx_col", [128, DC], BF16)
    w_sb = p.sbuf("w_sb", [128, DC, D], BF16)
    bo_sb = p.sbuf("bo_sb", [128, DC], F32)
    x0_sb = p.sbuf("x0_sb", [128, DC], F32)
    y0 = p.sbuf("y0", [128, DC], F32)
    ones_row = p.sbuf("c_onesrow2", [1, 128], F32)
    psB = p.psum("psB", [64, 16])
    psQ = p.psum("psQ", [128, 1])

    p.dma("ld", nc.sync, parts[:], part_d.rearrange("s h m -> m s h"),
          allow_slow_non_contiguous=True)
    p.dma("ld", nc.gpsimd, w_sb[:], wo_d.rearrange("(kc p) o -> p kc o", p=128))
    p.dma("ld", nc.sync, bo_sb[:], bo_d)
    p.dma("ld", nc.sync, x0_sb[:], x0_d.rearrange("(c p) o -> p c o", p=128), allow_slow_non_contiguous=True)
    for e in (nc.tensor, nc.scalar, nc.vector, nc.gpsimd):
        p.wait(e, "ld")
    p.V("memset", ones_row[:], 1.0)
    p.V("tensor_add", ssum[:], parts[:, 0, :], parts[:, 1, :])
    p.V("tensor_add", ssum[:], ssum[:], parts[:, 2, :])
    p.V("tensor_add", ssum[:], ssum[:], parts[:, 3, :])
    p.V("reciprocal", dr[:], ssum[64:65, :])
    p.wait(nc.tensor, "V")
    p.mm(psB[:], ones_row[0:1, 0:64], dr[:], start=True, stop=True)
    p.wait(nc.vector, "T")
    p.V("tensor_mul", ctx_mh[:], ssum[0:64, :], psB[:, :])
    odd_tmp = p.sbuf("odd_tmp", [64, DC], F32)
    for c in range(DC):
        p.V("tensor_copy", ctx_col[0:64, c:c + 1], ctx_mh[:, 2 * c:2 * c + 1])
        p.V("tensor_copy", odd_tmp[:, c:c + 1], ctx_mh[:, 2 * c + 1:2 * c + 2])
    p.dma("mv", nc.gpsimd, ctx_col[64:128, :], odd_tmp[:], wait_chans=("V",))
    for e in (nc.tensor, nc.vector):
        p.wait(e, "mv")
    for oc in range(DC):
        for kc in range(DC):
            p.mm(psQ[:], w_sb[:, kc, oc * 128:(oc + 1) * 128], ctx_col[:, kc:kc + 1],
                 start=(kc == 0), stop=(kc == DC - 1))
        p.wait(nc.vector, "T")
        p.V("tensor_add", y0[:, oc:oc + 1], psQ[:, 0:1], bo_sb[:, oc:oc + 1])
        p.V("tensor_add", y0[:, oc:oc + 1], y0[:, oc:oc + 1], x0_sb[:, oc:oc + 1])
        p.wait(nc.tensor, "V")
    p.dma("st", nc.sync, xo.rearrange("(c p) o -> p c o", p=128), y0[:],
          wait_chans=("V",), allow_slow_non_contiguous=True)
    p.wait(nc.sync, "st")
    return p


def run_l1_attn(x_shards, inputs, cache):
    import ml_dtypes
    bf16 = ml_dtypes.bfloat16
    if "ln_f32" not in cache:
        cache["ln_f32"] = build_ln_f32()
    pL = cache["ln_f32"]
    res = run_bass_kernel_spmd(pL.nc, [{"x_fm": x_shards[c]} for c in range(NCORE)],
                               core_ids=list(range(NCORE)))
    ln_b = [np.concatenate([np.asarray(res.results[b * 4 + q]["ln_out"]) for q in range(4)],
                           axis=1) for b in range(2)]

    if "l1am" not in cache:
        cache["l1am"] = build_l1_attn_main()
    pM = cache["l1am"]
    wlocT = np.ascontiguousarray(inputs["L1_attn_in_w_loc"].T)
    wclsT = np.ascontiguousarray(inputs["L1_attn_in_w_cls"].T)
    bloc = np.ascontiguousarray(inputs["L1_attn_in_b_loc"].reshape(24, 128).T)
    bcls = np.ascontiguousarray(inputs["L1_attn_in_b_cls"].reshape(24, 128).T)
    woLT = np.ascontiguousarray(inputs["L1_attn_out_w_loc"].T)
    boL = np.ascontiguousarray(inputs["L1_attn_out_b_loc"].reshape(DC, 128).T)
    in_maps = []
    for c in range(NCORE):
        b, q = c // 4, c % 4
        G = q * TPC
        xe = np.zeros((D, XCL), dtype=np.float32)
        lo, hi = G - 16, G + 2064
        s0 = max(0, -lo)
        e0 = XCL - max(0, hi - 8192)
        xe[:, s0:e0] = ln_b[b][:, max(lo, 0):min(hi, 8192)]
        mb = np.zeros((16, 128, 160), dtype=np.float32)
        pp = np.arange(128)[:, None]; jj = np.arange(160)[None, :]
        for i in range(16):
            kglob = G + 128 * i - 16 + jj
            qglob = G + 128 * i + pp
            valid = (kglob >= qglob - 16) & (kglob <= qglob + 16) & \
                    (kglob >= 1) & (kglob < 8192)
            mb[i] = np.where(valid, 0.0, -1e30)
        in_maps.append({
            "x_fm": x_shards[c], "ln_ext": xe,
            "ln0": np.ascontiguousarray(ln_b[b][:, 0:1]),
            "wlocT": wlocT, "bloc": bloc, "wclsT": wclsT, "bcls": bcls,
            "woLT": woLT, "boL": boL, "maskb": mb.astype(bf16),
        })
    resM = run_bass_kernel_spmd(pM.nc, in_maps, core_ids=list(range(NCORE)))

    if "l1ac" not in cache:
        cache["l1ac"] = build_l1_attn_combine()
    pC = cache["l1ac"]
    woC = np.ascontiguousarray(inputs["L1_attn_out_w_cls"].T)
    boC = np.ascontiguousarray(inputs["L1_attn_out_b_cls"].reshape(DC, 128).T)
    in_maps = []
    for c in range(NCORE):
        b = c // 4
        parts = np.stack([np.asarray(resM.results[b * 4 + q]["part"]) for q in range(4)])
        in_maps.append({
            "parts": parts, "woC": woC, "boC": boC,
            "x0": np.ascontiguousarray(x_shards[b * 4][:, 0:1]),
        })
    resC = run_bass_kernel_spmd(pC.nc, in_maps, core_ids=list(range(NCORE)))

    new_shards = [np.asarray(r["x_out"]).copy() for r in resM.results]
    for b in range(2):
        new_shards[b * 4][:, 0:1] = np.asarray(resC.results[b * 4]["x0_new"])
    return new_shards


NSLOT_T = 24
I32 = mybir.dt.int32


def build_l0_moe_sparse(debug=False):
    p = Prog()
    nc = p.nc
    x = p.din("x_fm", [D, TPC])
    ep_d = p.din("ep", [2, 128, NTILE])
    idx_d = p.din("idx", [128, NTILE])
    e_all_d = p.din("e_all", [1, T_ALL])
    s_all_d = p.din("s_all", [1, T_ALL])
    w1_d = p.din("W1", [8 * D, 4096], BF16)
    w2_d = p.din("W2", [8 * 4096, D], BF16)
    b1_d = p.din("b1r", [256, 128])
    b2_d = p.din("b2r", [64, 128])
    xo = p.dout("x_out", [D, TPC])
    if debug:
        dbg_lng = p.dout("dbg_lng", [128, D], BF16)
        dbg_y0 = p.dout("dbg_y0", [128, D], BF16)
        dbg_h0 = p.dout("dbg_h0", [128, 32, 128], BF16)
        dbg_pos = p.dout("dbg_pos", [128, NTILE])
        dbg_inv = p.dout("dbg_inv", [128, NSLOT_T])
        dbg_et = p.dout("dbg_et", [1, NSLOT_T])
        dbg_fe = p.dout("dbg_fe", [128, NTILE])
        dbg_crow = p.dout("dbg_crow", [1, 8])
        dbg_brow = p.dout("dbg_brow", [1, 8])
    fe_scr = nc.dram_tensor("fe_scr", [NTILE, 128], F32, kind="Internal").ap()
    lnt_d = nc.dram_tensor("lnt_scr", [TPC, D], BF16, kind="Internal").ap()
    inv_d = nc.dram_tensor("inv_scr", [NSLOT_T, 128], F32, kind="Internal").ap()
    yt_d = nc.dram_tensor("yt_scr", [NSLOT_T * 128, D], BF16, kind="Internal").ap()

    x_sb = p.sbuf("x_sb", [128, DC, TPC], F32)
    arena = p.sbuf("arena", [128, 5 * TPC], F32)
    wg = p.sbuf("wg", [128, 16384], BF16)
    h_sb = p.sbuf("h_sb", [128, 32, 128], BF16)
    lng_tm = p.sbuf("lng_tm", [128, D], BF16)
    lng_fm = p.sbuf("lng_fm", [128, DC, 128], BF16)
    y_fm_t = p.sbuf("y_fm_t", [128, DC, 128], BF16)
    y32 = p.sbuf("y32", [128, DC, 128], F32)
    y_tm = p.sbuf("y_tm", [128, D], BF16)
    a_row = p.sbuf("a_row", [1, TPC], F32)
    b_row = p.sbuf("b_row", [1, TPC], F32)
    ec = p.sbuf("ec", [128, NTILE], F32)
    sc = p.sbuf("sc", [128, NTILE], F32)
    ic = p.sbuf("ic", [128, NTILE], F32)
    fe = p.sbuf("fe", [128, NTILE], F32)
    fe_t = p.sbuf("fe_t", [NTILE, 128], F32)
    crow = p.sbuf("crow", [1, 8], F32)
    trow = p.sbuf("trow", [1, 8], F32)
    brow = p.sbuf("brow", [1, 8], F32)
    bcol = p.sbuf("bcol", [8, 1], F32)
    posf = p.sbuf("posf", [128, NTILE], F32)
    posi = p.sbuf("posi", [128, NTILE], I32)
    et_row = p.sbuf("et_row", [1, NSLOT_T], F32)
    e_bcT = p.sbuf("e_bcT", [128, NSLOT_T], F32)
    idx1f = p.sbuf("idx1f", [128, DC, NSLOT_T], F32)
    idx1i = p.sbuf("idx1i", [128, DC, NSLOT_T], I32)
    idx2f = p.sbuf("idx2f", [128, 32, NSLOT_T], F32)
    idx2i = p.sbuf("idx2i", [128, 32, NSLOT_T], I32)
    idb1 = p.sbuf("idb1", [32, NSLOT_T], F32)
    idb1i = p.sbuf("idb1i", [32, NSLOT_T], I32)
    idb2 = p.sbuf("idb2", [8, NSLOT_T], F32)
    idb2i = p.sbuf("idb2i", [8, NSLOT_T], I32)
    tok_col = p.sbuf("tok_col", [128, NTILE], F32)
    inv_sb = p.sbuf("inv_sb", [NSLOT_T, 128], F32)
    invf = p.sbuf("invf", [128, NSLOT_T], F32)
    invi = p.sbuf("invi", [128, NSLOT_T], I32)
    b1c = p.sbuf("b1c", [128, 32], F32)
    b2c = p.sbuf("b2c", [128, 8], F32)
    b1g = p.sbuf("b1g", [32, 128], F32)
    b2g = p.sbuf("b2g", [8, 128], F32)
    zrow = p.sbuf("zrow", [NSLOT_T, 128], F32)
    tstart = p.sbuf("tstart", [8, NSLOT_T], F32)
    ge8 = p.sbuf("ge8", [8, NSLOT_T], F32)

    psP = p.psum("psP", [128, 2, 512])
    psH = p.psum("psH", [128, 512])
    psY2 = p.psum("psY2", [128, DC, 128])
    psTb = p.psum("psTb", [128, 128], BF16)

    p.dma("ld", nc.sync, x_sb[:], x.rearrange("(c p) t -> p c t", p=128))
    p.dma("ld", nc.sync, ec[:], ep_d[0])
    p.dma("ld", nc.sync, sc[:], ep_d[1])
    p.dma("ld", nc.sync, ic[:], idx_d)
    for e in (nc.tensor, nc.scalar, nc.vector, nc.gpsimd):
        p.wait(e, "ld")

    ident, iota_p, ones, ones_row = make_consts(p)
    ident_bf = p.sbuf("ident_bf4", [128, 128], BF16)
    p.V("tensor_copy", ident_bf[:], ident[:])

    scr0 = arena[:, 2 * TPC:3 * TPC]
    emit_ln_stats(p, x_sb, TPC, a_row, b_row, scr0,
                  ps_a=psP[0:1, 0, :], ps_b=psP[0:1, 1, :],
                  mu_row=arena[0:1, 0:TPC])
    a_bc = arena[:, 0 * TPC:1 * TPC]
    b_bc = arena[:, 1 * TPC:2 * TPC]
    emit_bcast_rows(p, a_row[:], a_bc, ones_row, ps=psP[:, 0, :], tag="lna")
    emit_bcast_rows(p, b_row[:], b_bc, ones_row, ps=psP[:, 0, :], tag="lnb")
    lnc_bf = p.sbuf("lnc_bf", [128, 128], BF16)
    for it in range(NTILE):
        tsl = slice(it * 128, (it + 1) * 128)
        for c in range(DC):
            p.V("tensor_mul", scr0[:, 0:128], x_sb[:, c, tsl], a_bc[:, tsl])
            p.V("tensor_add", lnc_bf[:, :], scr0[:, 0:128], b_bc[:, tsl])
            p.wait(nc.tensor, "V")
            p.tr(psTb[:, :], lnc_bf[:, :], ident_bf[:])
            p.wait(nc.vector, "T")
            p.V("tensor_copy", y_tm[:, c * 128:(c + 1) * 128], psTb[:, :])
        p.dma("lt", nc.sync, lnt_d[it * 128:(it + 1) * 128, :], y_tm[:],
              wait_chans=("V",))
        p.wait(nc.vector, "lt")

    emit_moe_rank(p, e_all_d, s_all_d, ec, sc, ic, fe, arena, ones_row, ps_bc=psP[:])
    p.wait(nc.tensor, "V")
    p.tr(psP[0:NTILE, 0, 0:128], fe[:, :], ident[:])
    p.wait(nc.vector, "T")
    p.V("tensor_copy", fe_t[:, :], psP[0:NTILE, 0, 0:128])
    p.dma("fe1", nc.gpsimd, fe_scr, fe_t[:], wait_chans=("V",))
    p.wait(nc.gpsimd, "fe1")
    fe_bc = arena[:, 0 * TPC:1 * TPC]
    p.dma("fe2", nc.gpsimd, fe_bc,
          fe_scr.rearrange("i p -> (i p)").unsqueeze(0).partition_broadcast(128),
          wait_chans=("V",))
    for e in (nc.tensor, nc.vector, nc.gpsimd):
        p.wait(e, "fe2")

    masks8 = arena[0:8, 1 * TPC:2 * TPC]
    pos_row = arena[32:33, 1 * TPC:2 * TPC]
    incl8 = arena[0:8, 2 * TPC:3 * TPC]
    rank8 = arena[0:8, 3 * TPC:4 * TPC]
    p.V("tensor_scalar", masks8, fe_bc[0:8, :], iota_p[0:8, 0:1], None,
        op0=ALU.is_equal)
    p.V("tensor_tensor_scan", incl8, masks8, masks8, 0.0,
        op0=ALU.add, op1=ALU.bypass)
    p.V("tensor_sub", rank8, incl8, masks8)
    p.wait(nc.tensor, "V")
    p.tr(psP[0:1, 0, 0:8], incl8[:, TPC - 1:TPC], ident[0:8, 0:8])
    p.wait(nc.vector, "T")
    p.V("tensor_copy", crow[:], psP[0:1, 0, 0:8])
    p.V("tensor_scalar", trow[:], crow[:], 1.0 / 128.0, 0.49609375,
        op0=ALU.mult, op1=ALU.add)
    ti8 = p.sbuf("ti8", [1, 8], I32)
    p.V("tensor_copy", ti8[:], trow[:])
    p.V("tensor_copy", trow[:], ti8[:])
    p.V("tensor_scalar_mul", trow[:], trow[:], 128.0)
    p.V("tensor_tensor_scan", brow[:], trow[:], trow[:], 0.0,
        op0=ALU.add, op1=ALU.bypass)
    p.V("tensor_sub", brow[:], brow[:], trow[:])
    p.wait(nc.tensor, "V")
    p.tr(psP[0:8, 0, 0:1], brow[:], ident[0:1, 0:1])
    p.wait(nc.vector, "T")
    p.V("tensor_copy", bcol[:], psP[0:8, 0, 0:1])
    p.V("tensor_scalar", rank8, rank8, bcol[0:8, 0:1], None, op0=ALU.add)
    p.V("tensor_mul", rank8, rank8, masks8)
    p.wait(nc.tensor, "V")
    for g in range(4):
        gsl = slice(g * 512, (g + 1) * 512)
        p.mm(psP[0:1, 0, :], ones[0:8, :], rank8[:, gsl], start=True, stop=True)
        p.wait(nc.vector, "T")
        p.V("tensor_copy", pos_row[0:1, gsl], psP[0:1, 0, 0:512])
        p.wait(nc.tensor, "V")
    for it in range(NTILE):
        p.tr(psP[:, 0, it:it + 1], pos_row[0:1, it * 128:(it + 1) * 128],
             ident[32:33, 32:33])
    p.wait(nc.vector, "T")
    p.V("tensor_copy", posf[:], psP[:, 0, 0:NTILE])
    p.V("tensor_copy", posi[:], posf[:])
    p.G("iota", tstart[:], pattern=[[128, NSLOT_T]], base=0, channel_multiplier=0,
        allow_small_or_imprecise_dtypes=True)
    p.wait(nc.vector, "G")
    p.V("tensor_scalar", ge8[:], tstart[:], bcol[0:8, 0:1], None, op0=ALU.is_ge)
    p.wait(nc.tensor, "V")
    p.mm(psP[0:1, 0, 0:NSLOT_T], ones[0:8, :], ge8[:], start=True, stop=True)
    p.wait(nc.vector, "T")
    p.V("tensor_scalar", et_row[:], psP[0:1, 0, 0:NSLOT_T], -1.0, None, op0=ALU.add)
    p.wait(nc.tensor, "V")
    p.mm(psP[:, 0, 0:NSLOT_T], ones_row[:], et_row[:], start=True, stop=True)
    p.wait(nc.vector, "T")
    p.V("tensor_copy", e_bcT[:], psP[:, 0, 0:NSLOT_T])
    for kc in range(DC):
        p.V("tensor_scalar", idx1f[:, kc, :], e_bcT[:], 1024.0, iota_p[:, 0:1],
            op0=ALU.mult, op1=ALU.add)
        p.V("tensor_scalar", idx1f[:, kc, :], idx1f[:, kc, :], float(kc * 128), None,
            op0=ALU.add)
    p.V("tensor_copy", idx1i[:], idx1f[:])
    for fc in range(32):
        p.V("tensor_scalar", idx2f[:, fc, :], e_bcT[:], 4096.0, iota_p[:, 0:1],
            op0=ALU.mult, op1=ALU.add)
        p.V("tensor_scalar", idx2f[:, fc, :], idx2f[:, fc, :], float(fc * 128), None,
            op0=ALU.add)
    p.V("tensor_copy", idx2i[:], idx2f[:])
    p.V("tensor_scalar", idb1[:], e_bcT[0:32, :], 32.0, iota_p[0:32, 0:1],
        op0=ALU.mult, op1=ALU.add)
    p.V("tensor_copy", idb1i[:], idb1[:])
    p.V("tensor_scalar", idb2[:], e_bcT[0:8, :], 8.0, iota_p[0:8, 0:1],
        op0=ALU.mult, op1=ALU.add)
    p.V("tensor_copy", idb2i[:], idb2[:])
    p.G("iota", tok_col[:], pattern=[[128, NTILE]], base=0, channel_multiplier=1,
        allow_small_or_imprecise_dtypes=True)
    p.V("memset", zrow[:], 0.0)
    p.dma("g", nc.gpsimd, inv_d, zrow[:], wait_chans=("V", "G"))
    p.wait(nc.gpsimd, "g")
    inv_flat = inv_d.rearrange("a b -> (a b)").unsqueeze(-1)
    for it in range(NTILE):
        i = nc.gpsimd.indirect_dma_start(
            inv_flat, IndirectOffsetOnAxis(posi[:, it:it + 1], 0),
            tok_col[:, it:it + 1], None)
        p.inc("g", i, 16)
    p.wait(nc.sync, "g")
    p.dma("ld2", nc.sync, inv_sb[:], inv_d)
    p.wait(nc.tensor, "ld2")
    p.tr(psP[:, 0, 0:NSLOT_T], inv_sb[:], ident[0:NSLOT_T, 0:NSLOT_T])
    p.wait(nc.vector, "T")
    p.V("tensor_copy", invf[:], psP[:, 0, 0:NSLOT_T])
    p.V("tensor_copy", invi[:], invf[:])

    if debug:
        p.dma("dbg", nc.sync, dbg_pos, posf[:], wait_chans=("V",))
        p.dma("dbg", nc.sync, dbg_inv, invf[:])
        p.dma("dbg", nc.sync, dbg_et, et_row[:])
        p.dma("dbg", nc.sync, dbg_fe, fe[:])
        p.dma("dbg", nc.sync, dbg_crow, crow[:])
        p.dma("dbg", nc.sync, dbg_brow, brow[:])
        p.wait(nc.sync, "dbg")
    w1v = wg[:].rearrange("p (kc f) -> p kc f", kc=DC)
    w2v = wg[:].rearrange("p (fc o) -> p fc o", fc=16)
    for T in range(NSLOT_T):
        p.wait(nc.gpsimd, "T")
        i = nc.gpsimd.indirect_dma_start(
            lng_tm[:], None, lnt_d, IndirectOffsetOnAxis(invi[:, T:T + 1], 0))
        p.inc("g", i, 16)
        i = nc.gpsimd.indirect_dma_start(
            b1g[:], None, b1_d, IndirectOffsetOnAxis(idb1i[:, T:T + 1], 0))
        p.inc("g", i, 16)
        p.wait(nc.tensor, "g")
        for c in range(DC):
            p.wait(nc.tensor, "V")
            p.tr(psTb[:, :], lng_tm[:, c * 128:(c + 1) * 128], ident_bf[:])
            p.wait(nc.vector, "T")
            p.V("tensor_copy", lng_fm[:, c, :], psTb[:, :])
        p.tr(psP[:, 0, 0:32], b1g[:], ident[0:32, 0:32])
        p.wait(nc.vector, "T")
        p.V("tensor_copy", b1c[:], psP[:, 0, 0:32])
        for half in range(2):
            p.wait(nc.gpsimd, "T")
            for kc in range(DC):
                i = nc.gpsimd.indirect_dma_start(
                    w1v[:, kc, :], None, w1_d,
                    IndirectOffsetOnAxis(idx1i[:, kc, T:T + 1], 0),
                    element_offset=half * 2048)
                p.inc("g", i, 16)
            p.wait(nc.tensor, "g")
            p.wait(nc.tensor, "V")
            for f16 in range(16):
                fc = half * 16 + f16
                p.wait(nc.tensor, "S")
                for kc in range(DC):
                    p.mm(psH[:, 0:128], w1v[:, kc, f16 * 128:(f16 + 1) * 128],
                         lng_fm[:, kc, :], start=(kc == 0), stop=(kc == DC - 1))
                p.wait(nc.scalar, "T")
                p.S("activation", out=h_sb[:, fc, :], in_=psH[:, 0:128], func=AF.Relu,
                    bias=b1c[:, fc:fc + 1])
        i = nc.gpsimd.indirect_dma_start(
            b2g[:], None, b2_d, IndirectOffsetOnAxis(idb2i[:, T:T + 1], 0))
        p.inc("g", i, 16)
        p.wait(nc.tensor, "g")
        p.tr(psP[:, 0, 0:8], b2g[:], ident[0:8, 0:8])
        p.wait(nc.vector, "T")
        p.V("tensor_copy", b2c[:], psP[:, 0, 0:8])
        for half in range(2):
            p.wait(nc.gpsimd, "T")
            for f16 in range(16):
                fc = half * 16 + f16
                i = nc.gpsimd.indirect_dma_start(
                    w2v[:, f16, :], None, w2_d,
                    IndirectOffsetOnAxis(idx2i[:, fc, T:T + 1], 0))
                p.inc("g", i, 16)
            p.wait(nc.tensor, "g")
            p.wait(nc.tensor, "S")
            for oc in range(DC):
                p.wait(nc.tensor, "V")
                for f16 in range(16):
                    p.mm(psY2[:, 0, :], w2v[:, f16, oc * 128:(oc + 1) * 128],
                         h_sb[:, half * 16 + f16, :],
                         start=(f16 == 0), stop=(f16 == 15))
                p.wait(nc.vector, "T")
                if half == 0:
                    p.V("tensor_copy", y32[:, oc, :], psY2[:, 0, :])
                else:
                    p.V("tensor_add", y32[:, oc, :], y32[:, oc, :], psY2[:, 0, :])
        for oc in range(DC):
            p.V("tensor_scalar", y_fm_t[:, oc, :], y32[:, oc, :], b2c[:, oc:oc + 1],
                None, op0=ALU.add)
        for c in range(DC):
            p.wait(nc.tensor, "V")
            p.tr(psTb[:, :], y_fm_t[:, c, :], ident_bf[:])
            p.wait(nc.vector, "T")
            p.V("tensor_copy", y_tm[:, c * 128:(c + 1) * 128], psTb[:, :])
        p.dma("yt", nc.sync, yt_d[T * 128:(T + 1) * 128, :], y_tm[:],
              wait_chans=("V",))
        if debug and T == 0:
            p.dma("yt", nc.sync, dbg_y0, y_tm[:])
            p.dma("yt", nc.sync, dbg_lng, lng_tm[:])
            p.dma("yt", nc.sync, dbg_h0, h_sb[:], wait_chans=("S",))
        p.wait(nc.vector, "yt")
        p.wait(nc.gpsimd, "yt")

    for it in range(NTILE):
        p.wait(nc.gpsimd, "T")
        i = nc.gpsimd.indirect_dma_start(
            lng_tm[:], None, yt_d, IndirectOffsetOnAxis(posi[:, it:it + 1], 0))
        p.inc("g", i, 16)
        p.wait(nc.tensor, "g")
        for c in range(DC):
            p.wait(nc.tensor, "V")
            p.tr(psTb[:, :], lng_tm[:, c * 128:(c + 1) * 128], ident_bf[:])
            p.wait(nc.vector, "T")
            p.V("scalar_tensor_tensor", x_sb[:, c, it * 128:(it + 1) * 128],
                psTb[:, :], 0.5, x_sb[:, c, it * 128:(it + 1) * 128],
                op0=ALU.mult, op1=ALU.add)
        p.wait(nc.gpsimd, "V")

    p.dma("st", nc.sync, xo.rearrange("(c p) t -> p c t", p=128), x_sb[:],
          wait_chans=("V",))
    p.wait(nc.sync, "st")
    return p
